# version 06
# baseline (speedup 1.0000x reference)
import numpy as np
import ml_dtypes
import concourse.bass as bass
import concourse.mybir as mybir
from concourse.bass_utils import run_bass_kernel_spmd

F32 = mybir.dt.float32
BF16 = mybir.dt.bfloat16
AF = mybir.ActivationFunctionType
ALU = mybir.AluOpType
AX = mybir.AxisListType
bf16 = ml_dtypes.bfloat16

NCORES = 8
T = 2048
D = 4096
NT = T // 128
ND = D // 128

H = 128; P = 64; G = 8; N = 128; CS = 128; K4 = 4
INTER = H * P
GN = G * N
CONV_DIM = INTER + 2 * GN
CPROJ = INTER + CONV_DIM + H
NCHUNK = T // CS
NEXP = 64; TOPK = 16; RANK = 16
KR = NEXP * RANK

INV_SQRT_QK = float(1.0 / np.sqrt(128.0))

ENGINES = ("tensor", "vector", "scalar", "gpsimd", "sync")
NDMACH = 4
CHANNELS = ([(e, "c") for e in ENGINES]
            + [(e, f"d{i}") for e in ("sync", "gpsimd", "scalar", "vector")
               for i in range(NDMACH)])


class Emitter:

    def __init__(self, nc, strict=False):
        self.nc = nc
        self.strict = strict
        self.steps = {e: [] for e in ENGINES}
        self.count = {ch: 0 for ch in CHANNELS}
        self.know = {e: {ch: 0 for ch in CHANNELS} for e in ENGINES}
        self.snap = {ch: {0: {c2: 0 for c2 in CHANNELS}} for ch in CHANNELS}
        self.last_writer = {}
        self.readers = {}
        self.dma_idx = {e: 0 for e in ENGINES}
        self.n_waits = 0
        self.audit = []

    def step(self, eng, fn, reads=(), writes=(), dma=False):
        if dma:
            sub = self.dma_idx[eng] % NDMACH
            self.dma_idx[eng] += 1
            ch = (eng, f"d{sub}")
        else:
            ch = (eng, "c")
        waits = {}
        def add_need(src):
            (sch, c) = src
            if sch == ch and not (self.strict and sch[1] == "c" and eng != "tensor"):
                return
            if sch[0] == eng and sch[1] == "c" and not dma and (
                    not self.strict or eng == "tensor"):
                return
            waits[sch] = max(waits.get(sch, 0), c)
        for t_ in reads:
            lw = self.last_writer.get(t_)
            if lw is not None:
                add_need(lw)
        for t_ in writes:
            lw = self.last_writer.get(t_)
            if lw is not None:
                add_need(lw)
            for r in self.readers.get(t_, ()):
                add_need(r)
        if dma and self.count[ch] > 0:
            waits[ch] = max(waits.get(ch, 0), self.count[ch])
        kn = self.know[eng]
        eff = {ch2: c for ch2, c in waits.items() if kn[ch2] < c}
        for ch2, c in eff.items():
            s = self.snap[ch2][c]
            for g in CHANNELS:
                kn[g] = max(kn[g], s[g])
            kn[ch2] = max(kn[ch2], c)
        amount = 16 if dma else 1
        self.count[ch] += amount
        cnt = self.count[ch]
        self.steps[eng].append((fn, eff, amount, ch))
        self.n_waits += len(eff)
        for t_ in writes:
            self.last_writer[t_] = (ch, cnt)
            self.readers[t_] = []
        for t_ in reads:
            self.readers.setdefault(t_, []).append((ch, cnt))
        snap = dict(kn)
        snap[ch] = cnt
        self.snap[ch][cnt] = snap
        self.audit.append((eng, len(self.steps[eng]) - 1, tuple(reads),
                           tuple(writes), ch, cnt, dma))
        return cnt

    def dma(self, queue, out, in_, reads=(), writes=(), **kw):
        self.step(queue, lambda e: e.dma_start(out=out, in_=in_, **kw),
                  reads=reads, writes=writes, dma=True)

    def barrier_step(self, eng="sync"):
        targets = {ch: c for ch, c in self.count.items() if c > 0}
        kn = self.know[eng]
        eff = {ch: c for ch, c in targets.items()
               if kn[ch] < c and not (ch[0] == eng and ch[1] == "c")}
        for ch2, c in eff.items():
            s = self.snap[ch2][c]
            for g in CHANNELS:
                kn[g] = max(kn[g], s[g])
            kn[ch2] = max(kn[ch2], c)
        ch = (eng, "c")
        self.count[ch] += 1
        cnt = self.count[ch]
        self.steps[eng].append((lambda e: e.nop(), eff, 1, ch))
        snap = dict(kn)
        snap[ch] = cnt
        self.snap[ch][cnt] = snap

    def realize(self):
        nc = self.nc
        self._cms = []
        sems = {}
        for ch in CHANNELS:
            cm = nc.semaphore(f"sem_{ch[0]}_{ch[1]}")
            sems[ch] = cm.__enter__()
            self._cms.append(cm)
        block_cm = nc.Block()
        block = block_cm.__enter__()
        self._cms.append(block_cm)

        def emit_for(e):
            def body(eng):
                for (fn, waits, amount, ch) in self.steps[e]:
                    for ch2, c in waits.items():
                        eng.wait_ge(sems[ch2], c)
                    inst = fn(eng)
                    inst.then_inc(sems[ch], amount)
            return body

        for e in ENGINES:
            if self.steps[e]:
                getattr(block, e)(emit_for(e))
        return block

    def finish(self):
        for cm in reversed(self._cms):
            cm.__exit__(None, None, None)



_CONSTS = None

def build_consts():
    global _CONSTS
    if _CONSTS is not None:
        return _CONSTS
    c = {}
    c["eyeb"] = np.eye(128, dtype=bf16)
    c["eyef"] = np.eye(128, dtype=np.float32)
    c["onesf"] = np.ones((1, 128), dtype=np.float32)
    sel = np.zeros((128, 32 * 128), dtype=bf16)
    for q in range(32):
        for f in range(128):
            sel[4 * q + f // 32, q * 128 + f] = 1
    c["selq"] = sel
    e8 = np.zeros((8, 128), dtype=bf16)
    for j in range(128):
        e8[j // 16, j] = 1
    c["e8"] = e8
    e64 = np.zeros((64, 1024), dtype=bf16)
    for j in range(1024):
        e64[j // 16, j] = 1
    c["e64"] = e64
    c["cumU"] = np.triu(np.ones((128, 128), dtype=np.float32))
    c["mask01"] = np.triu(np.ones((128, 128), dtype=np.float32))
    _CONSTS = {k: np.ascontiguousarray(v) for k, v in c.items()}
    return _CONSTS


CONST_SPECS = {
    "eyeb": BF16, "eyef": F32, "onesf": F32, "selq": BF16,
    "e8": BF16, "e64": BF16, "cumU": F32, "mask01": F32,
}


def lowprec(e, f):
    with e.bass.allow_low_precision(reason="bf16 weight fold / operand cast"):
        return f()


def declare_consts(nc, names):
    cvals = build_consts()
    return {k: nc.dram_tensor("c_" + k, list(cvals[k].shape), CONST_SPECS[k],
                              kind="ExternalInput").ap() for k in names}


def const_inputs(names):
    cvals = build_consts()
    return {"c_" + k: cvals[k] for k in names}


class Alloc:
    _n = 0

    def __init__(self, nc):
        self.nc = nc
        self.cms = []
        Alloc._n += 1
        self.prefix = f"a{Alloc._n}_"

    def sb(self, name, shape, dt):
        cm = self.nc.sbuf_tensor(self.prefix + name, shape, dt)
        t = cm.__enter__(); self.cms.append(cm); return t

    def ps(self, name, shape, dt=F32):
        cm = self.nc.psum_tensor(self.prefix + name, shape, dt)
        t = cm.__enter__(); self.cms.append(cm); return t

    def finish(self):
        for cm in reversed(self.cms):
            cm.__exit__(None, None, None)



L0_CONSTS = ("eyeb", "eyef", "selq", "e8")

def emit_L0(em, io):
    nc = em.nc
    C = io
    x0 = io["x0"]
    Wq = io["Wq"]
    bq = io["bq"]
    Wk = io["Wk"]
    bk = io["bk"]
    Wv = io["Wv"]
    bv = io["bv"]
    A0 = io["A0"]
    B0 = io["B0"]
    Wr0 = io["Wr0"]
    br0 = io["br0"]
    x1_o = io["x1"]
    xn1T_o = io["xn1T"]

    al = Alloc(nc)
    sb, ps = al.sb, al.ps

    eyeb = sb("s_eyeb", [128, 128], BF16)
    eyef = sb("s_eyef", [128, 128], F32)
    selq = sb("s_selq", [128, 32 * 128], BF16)
    e8 = sb("s_e8", [8, 128], BF16)
    wqs = sb("s_wqs", [128, ND * 128], BF16)
    wkT = sb("s_wkT", [128, ND * 128], BF16)
    wvT = sb("s_wvT", [128, ND * 128], BF16)
    a0T = sb("s_a0T", [128, ND * 128], BF16)
    b0T = sb("s_b0T", [128, ND * 128], BF16)
    wr0T = sb("s_wr0T", [128, ND * 8], BF16)
    bqs = sb("s_bqs", [128, 1], F32)
    bkc = sb("s_bkc", [128, 1], F32)
    bvc = sb("s_bvc", [128, 1], F32)
    br0c = sb("s_br0c", [8, 1], F32)
    bq_stage = sb("s_bqstage", [128, 32], F32)
    epscol = sb("s_epscol", [128, 1], F32)

    mat = sb("s_mat", [128, 128 * 128], BF16)
    wq_stage = [mat[:, 0:4096], mat[:, 4096:8192]]
    a0_stage = mat[:, 8192:12288]
    wr0_stage = mat[0:8, 12288:16384]
    b0st = sb("s_b0st", [128, 256], BF16)
    PREP_TAGS = ("mq0", "mq1", "ma", "mw")

    x0t = [sb(f"s_x0t{i}", [128, D], F32) for i in range(2)]
    stat = sb("s_stat", [128, 8], F32)
    g0T = sb("s_g0T", [128, ND * 128], F32)
    d0T = sb("s_d0T", [128, ND * 128], F32)
    xnb = sb("s_xnb", [128, D], BF16)
    xnT = sb("s_xnT", [128, ND * 128], BF16)
    a_T = sb("s_aT", [128, 128], F32)
    k_T = sb("s_kT", [128, 128], F32)
    v_T = sb("s_vT", [128, 128], F32)
    a_tm = sb("s_atm", [128, 128], F32)
    k_tm = sb("s_ktm", [128, 128], F32)
    v_tm = sb("s_vtm", [128, 128], F32)
    kmm = sb("s_kmm", [128, 4], F32)
    negm = sb("s_negm", [128, 128], F32)
    s1col = sb("s_s1", [128, 128], F32)
    s2col = sb("s_s2", [128, 128], F32)
    ttr_scr = sb("s_ttrscr", [128, 128], F32)
    o_tm = sb("s_otm", [128, 128], F32)
    o_Tb = sb("s_oTb", [128, 128], BF16)
    rt_f = sb("s_rtf", [8, 128], F32)
    rt_tm = sb("s_rttm", [128, 8], F32)
    rt_scr = sb("s_rtscr", [128, 8], F32)
    g8T = sb("s_g8T", [8, 128], BF16)
    scoreB = sb("s_scoreB", [128, 128], F32)
    glow = sb("s_glow", [128, 128], BF16)
    x1t = sb("s_x1t", [128, D], F32)
    xn1b = sb("s_xn1b", [128, D], BF16)
    xn1Tt = sb("s_xn1Tt", [128, ND * 128], BF16)

    pA = ps("p_A", [128, 512])
    pB = ps("p_B", [128, 512])
    pC = ps("p_C", [128, 512])
    pD = ps("p_D", [128, 512])
    pT1 = ps("p_T1", [128, 512])
    pT2 = ps("p_T2", [128, 512])
    pb1 = ps("p_b1", [128, 1024], BF16)
    pb2 = ps("p_b2", [128, 1024], BF16)

    em.step("gpsimd", lambda e: e.memset(epscol[:], 1e-5), writes=("epscol",))
    em.dma("sync", eyeb[:], C["eyeb"][:], writes=("eyeb",))
    em.dma("sync", eyef[:], C["eyef"][:], writes=("eyef",))
    em.dma("sync", selq[:], C["selq"][:], writes=("selq",))
    em.dma("sync", e8[:], C["e8"][:], writes=("e8",))
    em.dma("gpsimd", bq_stage[:], bq[:], writes=("bq_stage",))
    em.step("vector", lambda e: e.reduce_sum(bqs[:], bq_stage[:], axis=AX.X),
            reads=("bq_stage",), writes=("bqs",))
    em.dma("gpsimd", bkc[:], bk[:], writes=("bkc",))
    em.dma("gpsimd", bvc[:], bv[:], writes=("bvc",))
    em.dma("gpsimd", br0c[:], br0[:].rearrange("h -> h ()"), writes=("br0c",))

    for kc in range(ND):
        wst = wq_stage[kc % 2]
        wtg = f"mq{kc % 2}"
        em.dma("gpsimd", wst[:],
               Wq[kc * 128:(kc + 1) * 128, :, :].rearrange("p d h -> p (d h)"),
               reads=(wtg,), writes=(wtg,))
        em.step("vector", lambda e, kc=kc, wst=wst: lowprec(e, lambda: e.reduce_sum(
            wqs[:, kc * 128:(kc + 1) * 128].rearrange("p (d a) -> p d a", a=1),
            wst[:].rearrange("p (d h) -> p d h", h=32), axis=AX.X)),
            reads=(wtg,), writes=("wqs",))
        em.dma("gpsimd", wkT[:, kc * 128:(kc + 1) * 128],
               Wk[kc * 128:(kc + 1) * 128, :, 0], writes=("wkT",))
        em.dma("gpsimd", wvT[:, kc * 128:(kc + 1) * 128],
               Wv[kc * 128:(kc + 1) * 128, :, 0], writes=("wvT",))

    em.dma("gpsimd", a0_stage[:], A0[:].rearrange("h r d -> (h r) d"),
           writes=("ma",))
    for kc in range(ND):
        pt, ptag = (pb1, "pb1") if kc % 2 == 0 else (pb2, "pb2")
        em.step("tensor", lambda e, kc=kc, pt=pt: e.transpose(
            pt[:, 0:128], a0_stage[:, kc * 128:(kc + 1) * 128], eyeb[:]),
            reads=("ma", "eyeb", ptag), writes=(ptag,))
        em.step("vector", lambda e, kc=kc, pt=pt: e.tensor_copy(
            a0T[:, kc * 128:(kc + 1) * 128], pt[:, 0:128]),
            reads=(ptag,), writes=("a0T", ptag))
    em.dma("gpsimd", wr0_stage[:], Wr0[:], writes=("mw",))
    for kc in range(ND):
        pt, ptag = (pb1, "pb1") if kc % 2 == 0 else (pb2, "pb2")
        em.step("tensor", lambda e, kc=kc, pt=pt: e.transpose(
            pt[0:128, 0:8], wr0_stage[:, kc * 128:(kc + 1) * 128], eyeb[0:8, 0:8]),
            reads=("mw", "eyeb", ptag), writes=(ptag,))
        em.step("vector", lambda e, kc=kc, pt=pt: e.tensor_copy(
            wr0T[:, kc * 8:(kc + 1) * 8], pt[0:128, 0:8]),
            reads=(ptag,), writes=("wr0T", ptag))
    b0src = B0.rearrange("h (oc p) r -> oc p h r", p=128)
    for oc in range(ND):
        b0stage = b0st[:, (oc % 2) * 128:(oc % 2) * 128 + 128]
        btg = f"mb{oc % 2}"
        em.dma("gpsimd", b0stage.rearrange("p (h r) -> p h r", r=16),
               b0src[oc, :, :, :], reads=(btg,), writes=(btg,))
        pt, ptag = (pb1, "pb1") if oc % 2 == 0 else (pb2, "pb2")
        em.step("tensor", lambda e, pt=pt, b0stage=b0stage, btg=btg: e.transpose(
            pt[:, 0:128], b0stage, eyeb[:]),
            reads=(btg, "eyeb", ptag), writes=(ptag,))
        em.step("vector", lambda e, oc=oc, pt=pt: e.tensor_copy(
            b0T[:, oc * 128:(oc + 1) * 128], pt[:, 0:128]),
            reads=(ptag,), writes=("b0T", ptag))

    def emit_layernorm(src, out_bf, tag_src, tag_out, dump, tag_dump):
        em.step("vector", lambda e: e.reduce_sum(stat[:, 0:1], src[:], axis=AX.X),
                reads=(tag_src,), writes=("stat",))
        em.step("scalar", lambda e: e.activation(
            dump[:], src[:], AF.Square, accum_out=stat[:, 1:2]),
            reads=(tag_src,), writes=("stat", tag_dump))
        em.step("vector", lambda e: e.tensor_scalar(
            stat[:, 2:3], stat[:, 0:1], 1.0 / D, None, ALU.mult),
            reads=("stat",), writes=("stat",))
        em.step("vector", lambda e: e.tensor_mul(stat[:, 5:6], stat[:, 2:3], stat[:, 2:3]),
                reads=("stat",), writes=("stat",))
        em.step("vector", lambda e: e.scalar_tensor_tensor(
            stat[:, 3:4], stat[:, 1:2], 1.0 / D, stat[:, 5:6], ALU.mult, ALU.subtract),
            reads=("stat",), writes=("stat",))
        em.step("scalar", lambda e: e.activation(
            stat[:, 6:7], stat[:, 3:4], AF.Sqrt, bias=epscol[:]),
            reads=("stat", "epscol"), writes=("stat",))
        em.step("vector", lambda e: e.reciprocal(stat[:, 4:5], stat[:, 6:7]),
                reads=("stat",), writes=("stat",))
        em.step("vector", lambda e: e.tensor_scalar(
            out_bf[:], src[:], stat[:, 2:3], stat[:, 4:5], ALU.subtract, ALU.mult),
            reads=(tag_src, "stat"), writes=(tag_out,))

    def emit_transpose_tm2fm(src_bf, dst, tag_src, tag_dst):
        for dc in range(ND):
            pt, ptag = (pb1, "pb1") if dc % 2 == 0 else (pb2, "pb2")
            em.step("tensor", lambda e, dc=dc, pt=pt: e.transpose(
                pt[:, 0:128], src_bf[:, dc * 128:(dc + 1) * 128], eyeb[:]),
                reads=(tag_src, "eyeb", ptag), writes=(ptag,))
            em.step("vector", lambda e, dc=dc, pt=pt: e.tensor_copy(
                dst[:, dc * 128:(dc + 1) * 128], pt[:, 0:128]),
                reads=(ptag,), writes=(tag_dst, ptag))

    for i in range(NT):
        xb = x0t[i % 2]
        xtag = f"x0t{i % 2}"
        em.dma("sync", xb[:], x0[i * 128:(i + 1) * 128, :], writes=(xtag,))
        emit_layernorm(xb, xnb, xtag, "xnb", g0T, "g0T")
        emit_transpose_tm2fm(xnb, xnT, "xnb", "xnT")

        def mm_acc(psum, pslice, lhs_buf, tag_lhs, lhs_w, m_width, ptag):
            for kc in range(ND):
                em.step("tensor", lambda e, kc=kc: e.matmul(
                    psum[0:m_width, pslice[0]:pslice[1]],
                    lhs_buf[:, kc * lhs_w: kc * lhs_w + m_width],
                    xnT[:, kc * 128:(kc + 1) * 128],
                    start=(kc == 0), stop=(kc == ND - 1)),
                    reads=(tag_lhs, "xnT") + ((ptag,) if kc == 0 else ()),
                    writes=(ptag,))
        mm_acc(pA, (0, 128), wqs, "wqs", 128, 128, "pA")
        mm_acc(pB, (0, 128), wkT, "wkT", 128, 128, "pB")
        mm_acc(pC, (0, 128), wvT, "wvT", 128, 128, "pC")
        em.step("vector", lambda e: e.tensor_scalar(
            a_T[:], pA[:, 0:128], bqs[:], INV_SQRT_QK, ALU.add, ALU.mult),
            reads=("pA", "bqs"), writes=("a_T",))
        em.step("vector", lambda e: e.tensor_scalar(
            k_T[:], pB[:, 0:128], bkc[:], None, ALU.add),
            reads=("pB", "bkc"), writes=("k_T",))
        em.step("vector", lambda e: e.tensor_scalar(
            v_T[:], pC[:, 0:128], bvc[:], None, ALU.add),
            reads=("pC", "bvc"), writes=("v_T",))
        for (srcb, dstb, st, dt_) in ((a_T, a_tm, "a_T", "a_tm"),
                                      (k_T, k_tm, "k_T", "k_tm"),
                                      (v_T, v_tm, "v_T", "v_tm")):
            em.step("tensor", lambda e, srcb=srcb: e.transpose(
                pT1[:, 0:128], srcb[:], eyef[:]),
                reads=(st, "eyef", "pT1"), writes=("pT1",))
            em.step("vector", lambda e, dstb=dstb: e.tensor_copy(dstb[:], pT1[:, 0:128]),
                    reads=("pT1",), writes=(dt_, "pT1"))
        em.step("vector", lambda e: e.reduce_max(kmm[:, 0:1], k_tm[:], axis=AX.X),
                reads=("k_tm",), writes=("kmm",))
        em.step("vector", lambda e: e.tensor_reduce(kmm[:, 1:2], k_tm[:], AX.X, ALU.min),
                reads=("k_tm",), writes=("kmm",))
        em.step("vector", lambda e: e.tensor_scalar(
            kmm[:, 2:4], kmm[:, 0:2], -1.0, None, ALU.mult),
            reads=("kmm",), writes=("kmm",))
        em.step("vector", lambda e: e.tensor_scalar_mul(negm[:], a_tm[:], kmm[:, 2:3]),
                reads=("a_tm", "kmm"), writes=("negm",))
        em.step("vector", lambda e: e.scalar_tensor_tensor(
            negm[:], a_tm[:], kmm[:, 3:4], negm[:], ALU.mult, ALU.min),
            reads=("a_tm", "kmm", "negm"), writes=("negm",))
        for d in range(128):
            wtags_d = ("mat", "s1col") + (
                PREP_TAGS + ("mb0", "mb1") if (i == 0 and d == 0) else ())
            em.step("scalar", lambda e, d=d: e.activation(
                mat[:, d * 128:(d + 1) * 128], k_tm[:], AF.Exp,
                bias=negm[:, d:d + 1], scale=a_tm[:, d:d + 1],
                accum_out=s1col[:, d:d + 1]),
                reads=("k_tm", "negm", "a_tm"), writes=wtags_d)
        for d in range(128):
            em.step("vector", lambda e, d=d: e.tensor_mul(
                ttr_scr[:], mat[:, d * 128:(d + 1) * 128], v_tm[:]),
                reads=("mat", "v_tm"), writes=("ttr_scr",))
            em.step("vector", lambda e, d=d: e.reduce_sum(
                s2col[:, d:d + 1], ttr_scr[:], axis=AX.X),
                reads=("ttr_scr",), writes=("s2col",))
        em.step("vector", lambda e: e.reciprocal(s1col[:], s1col[:]),
                reads=("s1col",), writes=("s1col",))
        em.step("vector", lambda e: e.tensor_mul(o_tm[:], s2col[:], s1col[:]),
                reads=("s2col", "s1col"), writes=("o_tm",))
        em.step("tensor", lambda e: e.transpose(pT1[:, 0:128], o_tm[:], eyef[:]),
                reads=("o_tm", "eyef", "pT1"), writes=("pT1",))
        em.step("vector", lambda e: e.tensor_copy(o_Tb[:], pT1[:, 0:128]),
                reads=("pT1",), writes=("o_Tb", "pT1"))

        mm_acc(pD, (0, 128), wr0T, "wr0T", 8, 8, "pD")
        em.step("vector", lambda e: e.tensor_scalar(
            rt_f[:], pD[0:8, 0:128], br0c[:], None, ALU.add),
            reads=("pD", "br0c"), writes=("rt_f",))
        em.step("tensor", lambda e: e.transpose(pT1[0:128, 0:8], rt_f[:], eyef[0:8, 0:8]),
                reads=("rt_f", "eyef", "pT1"), writes=("pT1",))
        em.step("vector", lambda e: e.tensor_copy(rt_tm[:], pT1[0:128, 0:8]),
                reads=("pT1",), writes=("rt_tm", "pT1"))
        em.step("vector", lambda e: e.reduce_max(stat[:, 7:8], rt_tm[:], axis=AX.X),
                reads=("rt_tm",), writes=("stat",))
        em.step("vector", lambda e: e.tensor_scalar(
            stat[:, 7:8], stat[:, 7:8], -1.0, None, ALU.mult),
            reads=("stat",), writes=("stat",))
        em.step("scalar", lambda e: e.activation(
            rt_scr[:], rt_tm[:], AF.Exp, bias=stat[:, 7:8], accum_out=stat[:, 6:7]),
            reads=("rt_tm", "stat"), writes=("rt_scr", "stat"))
        em.step("vector", lambda e: e.reciprocal(stat[:, 6:7], stat[:, 6:7]),
                reads=("stat",), writes=("stat",))
        em.step("vector", lambda e: e.tensor_scalar(
            rt_scr[:], rt_scr[:], stat[:, 6:7], 8.0, ALU.mult, ALU.mult),
            reads=("rt_scr", "stat"), writes=("rt_scr",))
        em.step("tensor", lambda e: e.transpose(pT1[0:8, 0:128], rt_scr[:], eyef[:]),
                reads=("rt_scr", "eyef", "pT1"), writes=("pT1",))
        em.step("vector", lambda e: e.tensor_copy(g8T[:], pT1[0:8, 0:128]),
                reads=("pT1",), writes=("g8T", "pT1"))
        em.step("tensor", lambda e: e.matmul(
            pD[:, 128:256], e8[:], g8T[:], start=True, stop=True),
            reads=("e8", "g8T", "pD"), writes=("pD",))
        em.step("vector", lambda e: e.tensor_copy(scoreB[:], pD[:, 128:256]),
                reads=("pD",), writes=("scoreB", "pD"))
        mm_acc(pA, (128, 256), a0T, "a0T", 128, 128, "pA")
        em.step("vector", lambda e: e.tensor_mul(glow[:], pA[:, 128:256], scoreB[:]),
                reads=("pA", "scoreB"), writes=("glow",))
        for oc in range(ND):
            pt, ptg = (pA, "pA") if oc % 2 == 0 else (pB, "pB")
            em.step("tensor", lambda e, oc=oc, pt=pt: e.matmul(
                pt[:, 256:384], b0T[:, oc * 128:(oc + 1) * 128], glow[:],
                start=True, stop=True),
                reads=("b0T", "glow", ptg), writes=(ptg,))
            em.step("scalar", lambda e, oc=oc, pt=pt: e.activation(
                g0T[:, oc * 128:(oc + 1) * 128], pt[:, 256:384], AF.Silu, scale=2.0),
                reads=(ptg,), writes=("g0T",))
        for q in range(ND):
            pt, ptg = (pC, "pC") if q % 2 == 0 else (pD, "pD")
            em.step("tensor", lambda e, q=q, pt=pt: e.matmul(
                pt[:, 256:384], selq[:, q * 128:(q + 1) * 128], o_Tb[:],
                start=True, stop=True),
                reads=("selq", "o_Tb", ptg), writes=(ptg,))
            em.step("vector", lambda e, q=q, pt=pt: e.tensor_mul(
                d0T[:, q * 128:(q + 1) * 128], pt[:, 256:384],
                g0T[:, q * 128:(q + 1) * 128]),
                reads=(ptg, "g0T"), writes=("d0T",))
        em.step("vector", lambda e, xb=xb: e.tensor_copy(x1t[:], xb[:]),
                reads=(xtag,), writes=("x1t",))
        for dc in range(ND):
            pt, ptg = (pT1, "pT1") if dc % 2 == 0 else (pT2, "pT2")
            em.step("tensor", lambda e, dc=dc, pt=pt: e.transpose(
                pt[:, 0:128], d0T[:, dc * 128:(dc + 1) * 128], eyef[:]),
                reads=("d0T", "eyef", ptg), writes=(ptg,))
            em.step("vector", lambda e, dc=dc, pt=pt: e.tensor_add(
                x1t[:, dc * 128:(dc + 1) * 128],
                x1t[:, dc * 128:(dc + 1) * 128], pt[:, 0:128]),
                reads=(ptg, "x1t"), writes=("x1t", ptg))
        em.dma("sync", x1_o[i * 128:(i + 1) * 128, :], x1t[:], reads=("x1t",))
        emit_layernorm(x1t, xn1b, "x1t", "xn1b", g0T, "g0T")
        emit_transpose_tm2fm(xn1b, xn1Tt, "xn1b", "xn1Tt")
        em.dma("sync",
               xn1T_o.rearrange("(c p) t -> p c t", p=128)[:, :, i * 128:(i + 1) * 128],
               xn1Tt[:].rearrange("p (c t) -> p c t", t=128),
               reads=("xn1Tt",))

    al.finish()


def fix_wkv_dma():
    pass


def l0_inputs(core):
    d = const_inputs(L0_CONSTS)
    d.update({
        "x0": core["x0"],
        "Wq": core["L0_attn_Wq"], "bq": core["L0_attn_bq"],
        "Wk": core["L0_attn_Wk"], "bk": core["L0_attn_bk"],
        "Wv": core["L0_attn_Wv"], "bv": core["L0_attn_bv"],
        "A0": core["L0_ffn_A"], "B0": core["L0_ffn_B"],
        "Wr0": core["L0_ffn_Wr"], "br0": core["L0_ffn_br"],
    })
    return d


def kernel(**inputs):
    raise NotImplementedError("under construction")



PRE_CONSTS = ()

def emit_PREFETCH(em, io):
    Wp = io["Wp"]
    Wo = io["Wo"]
    A1 = io["A1"]
    wbscr = io["wbscr"]
    woscr = io["woscr"]
    wascr = io["wascr"]
    ncast = (CPROJ + CAST_ROWS - 1) // CAST_ROWS
    for ci in range(ncast):
        r0 = ci * CAST_ROWS
        r1 = min(CPROJ, r0 + CAST_ROWS)
        em.dma("gpsimd", wbscr[r0:r1, :], Wp[r0:r1, :], writes=(f"wbscr{ci}",))
    for ci in range(8):
        em.dma("gpsimd", woscr[ci * 512:(ci + 1) * 512, :],
               Wo[ci * 512:(ci + 1) * 512, :], writes=(f"woscr{ci}",))
    a1v = A1.rearrange("k r d -> (k r) d")
    em.dma("gpsimd", wascr[0:512, :], a1v[0:512, :], writes=("wascr0",))
    em.dma("gpsimd", wascr[512:1024, :], a1v[512:1024, :], writes=("wascr1",))



INPROJ_CONSTS = ("eyef",)
NM_PROJ = CPROJ // 128
CAST_ROWS = 512

def emit_INPROJ(em, io):
    nc = em.nc
    C = io
    xn1T = io["xn1T"]
    Wp = io["Wp"]
    convw = io["convw"]
    convb = io["convb"]
    dtb = io["dtb"]
    nw = io["nw"]
    sg_o = io["sg"]
    cvx_o = io["cvx"]
    cvB_o = io["cvB"]
    cvC_o = io["cvC"]
    dtv_o = io["dtv"]
    wbscr = io["wbscr"]

    al = Alloc(nc)
    sb, ps = al.sb, al.ps
    eyef = sb("s_eyef", [128, 128], F32)
    xn1sb = sb("s_xn1sb", [128, ND * T], BF16)
    lbuf = [sb(f"s_lbuf{i}", [128, ND * 256], BF16) for i in range(2)]
    cvpad = [sb(f"s_cvpad{i}", [128, T + 3], F32) for i in range(2)]
    acc = [sb(f"s_acc{i}", [128, T], F32) for i in range(2)]
    otile = [sb(f"s_otile{i}", [128, T], BF16) for i in range(2)]
    cw = sb("s_cw", [128, 80 * 4], F32)
    cb = sb("s_cb", [128, 80], F32)
    nwc = sb("s_nwc", [128, 64], F32)
    dtbc = sb("s_dtbc", [128, 1], F32)
    onec = sb("s_onec", [128, 1], F32)
    stage = sb("s_stage", [128, 128], F32)
    pP = [ps(f"p_P{i}", [128, 512]) for i in range(8)]

    em.dma("sync", eyef[:], C["eyef"][:], writes=("eyef",))
    for i in range(2):
        em.step("gpsimd", lambda e, i=i: e.memset(cvpad[i][:, 0:3], 0.0),
                writes=(f"cvp{i}",))
    em.step("gpsimd", lambda e: e.memset(onec[:], 1.0), writes=("onec",))
    xv = xn1T.rearrange("(c p) t -> p c t", p=128)
    for j in range(4):
        em.dma("sync", xn1sb[:].rearrange("p (c t) -> p c t", t=T)[:, j * 8:(j + 1) * 8, :],
               xv[:, j * 8:(j + 1) * 8, :], writes=("xn1sb",))
    for j in range(80):
        em.dma("scalar", cw[:, j * 4:(j + 1) * 4],
               convw[:, 0, :].rearrange("(n p) j -> p n j", p=128)[:, j, :],
               writes=("cw",))
    em.dma("gpsimd", stage[0:80, :], convb[:].rearrange("(n p) -> n p", p=128),
           writes=("stage",))
    em.step("tensor", lambda e: e.transpose(pP[0][0:128, 0:80], stage[0:80, :], eyef[0:80, 0:80]),
            reads=("stage", "eyef", "pP0"), writes=("pP0",))
    em.step("vector", lambda e: e.tensor_copy(cb[:], pP[0][0:128, 0:80]),
            reads=("pP0",), writes=("cb", "pP0"))
    em.dma("gpsimd", stage[0:1, 0:128], dtb[:].rearrange("h -> () h"), writes=("stage",))
    em.step("tensor", lambda e: e.transpose(pP[0][0:128, 0:1], stage[0:1, 0:128], eyef[0:1, 0:1]),
            reads=("stage", "eyef", "pP0"), writes=("pP0",))
    em.step("vector", lambda e: e.tensor_copy(dtbc[:], pP[0][0:128, 0:1]),
            reads=("pP0",), writes=("dtbc", "pP0"))
    em.dma("gpsimd", stage[0:64, :], nw[:].rearrange("(n p) -> n p", p=128),
           writes=("stage",))
    em.step("tensor", lambda e: e.transpose(pP[0][0:128, 0:64], stage[0:64, :], eyef[0:64, 0:64]),
            reads=("stage", "eyef", "pP0"), writes=("pP0",))
    em.step("vector", lambda e: e.tensor_copy(nwc[:], pP[0][0:128, 0:64]),
            reads=("pP0",), writes=("nwc", "pP0"))

    ngrp = (NM_PROJ + 1) // 2
    for g in range(ngrp):
        gm = 2 if (g + 1) * 2 <= NM_PROJ else NM_PROJ - g * 2
        lb = lbuf[g % 2]
        ltag = f"lbuf{g % 2}"
        wtags = tuple({f"wbscr{(g * 256) // CAST_ROWS}",
                       f"wbscr{(g * 256 + gm * 128 - 1) // CAST_ROWS}"})
        for kc in range(ND):
            q = "sync" if kc % 2 == 0 else "scalar"
            em.dma(q, lb[:, kc * 256: kc * 256 + gm * 128],
                   wbscr[g * 256: g * 256 + gm * 128, kc * 128:(kc + 1) * 128],
                   reads=wtags + ((ltag,) if kc < 2 else ()),
                   writes=(ltag,), transpose=True)
        for mi in range(gm):
            m = g * 2 + mi
            cv = cvpad[m % 2]
            cvt = f"cvp{m % 2}"
            ac = acc[m % 2]
            act = f"acc{m % 2}"
            ot = otile[m % 2]
            ott = f"otile{m % 2}"
            for kc in range(ND):
                for nb in range(4):
                    pi = (m % 2) * 4 + nb
                    em.step("tensor", lambda e, kc=kc, nb=nb, lb=lb, mi=mi, pi=pi: e.matmul(
                        pP[pi][:, :], lb[:, kc * 256 + mi * 128: kc * 256 + mi * 128 + 128],
                        xn1sb[:, kc * T + nb * 512: kc * T + (nb + 1) * 512],
                        start=(kc == 0), stop=(kc == ND - 1)),
                        reads=(ltag, "xn1sb") + ((f"pP{pi}",) if kc == 0 else ()),
                        writes=(f"pP{pi}",))
            for nb in range(4):
                pi = (m % 2) * 4 + nb
                if nb < 2:
                    em.step("vector", lambda e, nb=nb, pi=pi, cv=cv: e.tensor_copy(
                        cv[:, 3 + nb * 512: 3 + (nb + 1) * 512], pP[pi][:, :]),
                        reads=(f"pP{pi}",), writes=(cvt, f"pP{pi}"))
                else:
                    em.step("scalar", lambda e, nb=nb, pi=pi, cv=cv: e.copy(
                        cv[:, 3 + nb * 512: 3 + (nb + 1) * 512], pP[pi][:, :]),
                        reads=(f"pP{pi}",), writes=(cvt, f"pP{pi}"))
            if m < 64:
                em.step("scalar", lambda e, cv=cv, ac=ac: e.activation(
                    ac[:], cv[:, 3:3 + T], AF.Silu),
                    reads=(cvt,), writes=(act,))
                em.step("vector", lambda e, m=m, ot=ot, ac=ac: e.tensor_scalar_mul(
                    ot[:], ac[:], nwc[:, m:m + 1]),
                    reads=(act, "nwc"), writes=(ott,))
                em.dma("gpsimd", sg_o[m * 128:(m + 1) * 128, :], ot[:],
                       reads=(ott,))
            elif m < 144:
                j = m - 64
                em.step("vector", lambda e, j=j, cv=cv, ac=ac: e.tensor_scalar_mul(
                    ac[:], cv[:, 3:3 + T], cw[:, j * 4 + 3: j * 4 + 4]),
                    reads=(cvt, "cw"), writes=(act,))
                for tap in (2, 1, 0):
                    em.step("vector", lambda e, j=j, tap=tap, cv=cv, ac=ac: e.scalar_tensor_tensor(
                        ac[:], cv[:, tap: tap + T], cw[:, j * 4 + tap: j * 4 + tap + 1],
                        ac[:], ALU.mult, ALU.add),
                        reads=(cvt, "cw", act), writes=(act,))
                em.step("scalar", lambda e, j=j, ac=ac, ot=ot: e.activation(
                    ot[:], ac[:], AF.Silu, bias=cb[:, j:j + 1]),
                    reads=(act, "cb"), writes=(ott,))
                if j < 64:
                    em.dma("gpsimd", cvx_o[j * 128:(j + 1) * 128, :], ot[:],
                           reads=(ott,))
                elif j < 72:
                    em.dma("gpsimd", cvB_o[(j - 64) * 128:(j - 63) * 128, :], ot[:],
                           reads=(ott,))
                else:
                    em.dma("gpsimd", cvC_o[(j - 72) * 128:(j - 71) * 128, :], ot[:],
                           reads=(ott,))
            else:
                rl = acc[1]
                rlt = "acc1"
                em.step("vector", lambda e, cv=cv, ac=ac: e.tensor_scalar_add(
                    ac[:], cv[:, 3:3 + T], dtbc[:]),
                    reads=(cvt, "dtbc"), writes=(act,))
                em.step("scalar", lambda e, ac=ac, rl=rl: e.activation(rl[:], ac[:], AF.Relu),
                        reads=(act,), writes=(rlt,))
                em.step("scalar", lambda e, ac=ac: e.activation(ac[:], ac[:], AF.Abs),
                        reads=(act,), writes=(act,))
                em.step("scalar", lambda e, ac=ac: e.activation(ac[:], ac[:], AF.Exp, scale=-1.0),
                        reads=(act,), writes=(act,))
                em.step("scalar", lambda e, ac=ac: e.activation(ac[:], ac[:], AF.Ln, bias=onec[:]),
                        reads=(act, "onec"), writes=(act,))
                em.step("vector", lambda e, ac=ac, rl=rl: e.tensor_add(ac[:], ac[:], rl[:]),
                        reads=(act, rlt), writes=(act,))
                em.dma("gpsimd", dtv_o[:], ac[:], reads=(act,))

    al.finish()


def inproj_inputs(core, xn1T):
    d = const_inputs(INPROJ_CONSTS)
    d.update({
        "xn1T": xn1T,
        "Wp": core["L1_attn_in_proj_w"],
        "convw": core["L1_attn_conv_w"],
        "convb": core["L1_attn_conv_b"],
        "dtb": core["L1_attn_dt_bias"],
        "nw": core["L1_attn_norm_w"],
    })
    return d



SSD_CONSTS = ("eyef", "eyeb", "onesf", "cumU", "mask01")

def emit_SSD(em, io):
    nc = em.nc
    C = io
    cvx = io["cvx"]
    cvB = io["cvB"]
    cvC = io["cvC"]
    dtv = io["dtv"]
    sg = io["sg"]
    alog = io["alog"]
    Din = io["Din"]
    ygT_o = io["ygT"]
    rowscr = io["rowscr"]

    al = Alloc(nc)
    sb, ps = al.sb, al.ps
    eyef = sb("s_eyef", [128, 128], F32)
    eyeb = sb("s_eyeb", [128, 128], BF16)
    onesf = sb("s_onesf", [1, 128], F32)
    cumU = sb("s_cumU", [128, 128], F32)
    mask01 = sb("s_mask01", [128, 128], F32)
    onescb = sb("s_onescb", [128, 1], BF16)
    eps1 = sb("s_eps1", [1, 1], F32)
    acol = sb("s_acol", [128, 1], F32)
    dball = sb("s_dball", [128, 128], F32)
    dtv_sb = sb("s_dtvsb", [128, T], F32)
    adt = sb("s_adt", [128, T], F32)
    s_state = sb("s_state", [128, H * 64], F32)
    xcm = sb("s_xcm", [128, 64 * 128], BF16)
    xtk = sb("s_xtk", [128, 64 * 128], BF16)
    bcm = sb("s_bcm", [128, 8 * 128], BF16)
    ccm = sb("s_ccm", [128, 8 * 128], BF16)
    bgT = sb("s_bgT", [128, 8 * 128], BF16)
    sgt = sb("s_sgt", [128, 64 * 128], BF16)
    ybuf = sb("s_ybuf", [128, 64 * 128], F32)
    y2b = sb("s_y2b", [128, 64 * 128], BF16)
    mgmT = sb("s_mgmT", [128, 8 * 128], BF16)
    adtT = sb("s_adtT", [128, 128], F32)
    csT = sb("s_csT", [128, 128], F32)
    acs = sb("s_acs", [128, 128], F32)
    dbfm = sb("s_dbfm", [128, 128], F32)
    dbT = sb("s_dbT", [128, 128], F32)
    dtT = sb("s_dtT", [128, 128], F32)
    gbc = sb("s_gbc", [128, 128], F32)
    rstdB = sb("s_rstdB", [128, 128], F32)
    rr = sb("s_rr", [1, 128], F32)
    csr4 = sb("s_csr4", [1, 512], F32)
    sd4 = sb("s_sd4", [1, 512], F32)
    grow = sb("s_grow", [1, 128], F32)
    sdB4 = sb("s_sdB4", [128, 512], F32)
    dscr = [sb(f"s_dscr{i}", [128, 128], F32) for i in range(2)]
    ltb = [sb(f"s_ltb{i}", [128, 128], BF16) for i in range(2)]
    mtb = [sb(f"s_mtb{i}", [128, 128], BF16) for i in range(2)]
    xdt = [sb(f"s_xdt{i}", [128, 64], BF16) for i in range(2)]
    bdb = [sb(f"s_bdb{i}", [128, 128], BF16) for i in range(2)]
    sbst = [sb(f"s_sbst{i}", [128, 64], BF16) for i in range(2)]
    o1b = [sb(f"s_o1b{i}", [128, 128], F32) for i in range(2)]
    o2b = [sb(f"s_o2b{i}", [128, 128], F32) for i in range(2)]
    ytmp = sb("s_ytmp", [128, 128], F32)
    stage = sb("s_stage", [1, 128], F32)

    b0 = ps("p_b0", [128, 512])
    b1 = ps("p_b1", [128, 512])
    b2 = ps("p_b2", [128, 512])
    b5 = ps("p_b5", [128, 512])
    b6 = ps("p_b6", [128, 512])
    b7 = ps("p_b7", [128, 1024], BF16)
    b8 = ps("p_b8", [128, 1024], BF16)
    b3 = ps("p_b3", [128, 512])

    em.dma("sync", eyef[:], C["eyef"][:], writes=("eyef",))
    em.dma("sync", eyeb[:], C["eyeb"][:], writes=("eyeb",))
    em.dma("sync", onesf[:], C["onesf"][:], writes=("onesf",))
    em.dma("sync", cumU[:], C["cumU"][:], writes=("cumU",))
    em.dma("sync", mask01[:], C["mask01"][:], writes=("mask01",))
    em.step("gpsimd", lambda e: e.memset(onescb[:], 1.0), writes=("onescb",))
    em.step("gpsimd", lambda e: e.memset(eps1[:], 1e-5), writes=("eps1",))
    for j in range(8):
        em.step("gpsimd", lambda e, j=j: e.memset(s_state[:, j * 1024:(j + 1) * 1024], 0.0),
                writes=("s_state",))
    em.dma("gpsimd", dtv_sb[:], dtv[:], writes=("dtvsb",))
    em.dma("gpsimd", stage[:], alog[:].rearrange("h -> () h"), writes=("stage",))
    em.step("tensor", lambda e: e.transpose(b0[0:128, 0:1], stage[:], eyef[0:1, 0:1]),
            reads=("stage", "eyef", "b0"), writes=("b0",))
    em.step("scalar", lambda e: e.activation(acol[:], b0[0:128, 0:1], AF.Exp),
            reads=("b0",), writes=("acol",))
    em.step("vector", lambda e: e.tensor_scalar_mul(acol[:], acol[:], -1.0),
            reads=("acol",), writes=("acol",))
    em.step("vector", lambda e: e.tensor_scalar_mul(adt[:], dtv_sb[:], acol[:]),
            reads=("dtvsb", "acol"), writes=("adt",))
    em.dma("gpsimd", stage[:], Din[:].rearrange("h -> () h"), writes=("stage",))
    em.step("tensor", lambda e: e.matmul(b0[:, 128:256], onesf[:], stage[:],
                                         start=True, stop=True),
            reads=("onesf", "stage", "b0"), writes=("b0",))
    em.step("vector", lambda e: e.tensor_copy(dball[:], b0[:, 128:256]),
            reads=("b0",), writes=("dball", "b0"))

    cvxv = cvx.rearrange("(c p) t -> p c t", p=128)
    cvBv = cvB.rearrange("(g p) t -> p g t", p=128)
    cvCv = cvC.rearrange("(g p) t -> p g t", p=128)
    sgv = sg.rearrange("(c p) t -> p c t", p=128)
    ygv = ygT_o.rearrange("(c p) t -> p c t", p=128)

    for c in range(NCHUNK):
        tsl = slice(c * CS, (c + 1) * CS)
        em.dma("sync", xcm[:].rearrange("p (a t) -> p a t", t=128), cvxv[:, :, tsl],
               reads=("xcm",), writes=("xcm",))
        em.dma("gpsimd", bcm[:].rearrange("p (a t) -> p a t", t=128), cvBv[:, :, tsl],
               reads=("bcm",), writes=("bcm",))
        em.dma("gpsimd", ccm[:].rearrange("p (a t) -> p a t", t=128), cvCv[:, :, tsl],
               reads=("ccm",), writes=("ccm",))
        em.dma("scalar", sgt[:].rearrange("p (a t) -> p a t", t=128), sgv[:, :, tsl],
               reads=("sgt",), writes=("sgt",))
        em.step("tensor", lambda e, c=c: e.transpose(b0[:, 0:128], adt[:, c * CS:(c + 1) * CS], eyef[:]),
                reads=("adt", "eyef", "b0"), writes=("b0",))
        em.step("vector", lambda e: e.tensor_copy(adtT[:], b0[:, 0:128]),
                reads=("b0",), writes=("adtT", "b0"))
        em.step("tensor", lambda e: e.matmul(b0[:, 128:256], cumU[:], adtT[:], start=True, stop=True),
                reads=("cumU", "adtT", "b0"), writes=("b0",))
        em.step("vector", lambda e: e.tensor_copy(csT[:], b0[:, 128:256]),
                reads=("b0",), writes=("csT", "b0"))
        em.step("tensor", lambda e: e.matmul(b0[:, 256:384], adtT[:], cumU[:], start=True, stop=True),
                reads=("adtT", "cumU", "b0"), writes=("b0",))
        em.step("vector", lambda e: e.tensor_copy(acs[:], b0[:, 256:384]),
                reads=("b0",), writes=("acs", "b0"))
        em.dma("gpsimd", rowscr[:], acs[:], reads=("acs",), writes=("rowscr",))
        em.dma("gpsimd", grow[:], rowscr[:, 127:128].rearrange("p n -> n p"),
               reads=("rowscr",), writes=("grow",), allow_slow_non_contiguous=True)
        em.step("scalar", lambda e: e.activation(grow[:], grow[:], AF.Exp),
                reads=("grow",), writes=("grow",))
        em.step("scalar", lambda e: e.activation(dbfm[:], acs[:], AF.Exp,
                                                 scale=-1.0, bias=acs[:, 127:128]),
                reads=("acs",), writes=("dbfm",))
        em.step("tensor", lambda e: e.transpose(b0[:, 384:512], dbfm[:], eyef[:]),
                reads=("dbfm", "eyef", "b0"), writes=("b0",))
        em.step("vector", lambda e: e.tensor_copy(dbT[:], b0[:, 384:512]),
                reads=("b0",), writes=("dbT", "b0"))
        em.step("tensor", lambda e, c=c: e.transpose(b0[:, 0:128], dtv_sb[:, c * CS:(c + 1) * CS], eyef[:]),
                reads=("dtvsb", "eyef", "b0"), writes=("b0",))
        em.step("vector", lambda e: e.tensor_copy(dtT[:], b0[:, 0:128]),
                reads=("b0",), writes=("dtT", "b0"))
        em.step("tensor", lambda e: e.matmul(
            b3[:, 0:128], onesf[:], grow[:], start=True, stop=True),
            reads=("onesf", "grow", "b3"), writes=("b3",))
        em.step("vector", lambda e: e.tensor_copy(gbc[:], b3[:, 0:128]),
                reads=("b3",), writes=("gbc", "b3"))
        for g in range(8):
            pg, ptag = (b1, "b1") if g < 4 else (b2, "b2")
            off = (g % 4) * 128
            em.step("tensor", lambda e, g=g, pg=pg, off=off: e.matmul(
                pg[:, off:off + 128], bcm[:, g * 128:(g + 1) * 128],
                ccm[:, g * 128:(g + 1) * 128], start=True, stop=True),
                reads=("bcm", "ccm", ptag), writes=(ptag,))
        for g in range(8):
            pg, ptag = (b1, "b1") if g < 4 else (b2, "b2")
            off = (g % 4) * 128
            em.step("vector", lambda e, g=g, pg=pg, off=off: e.tensor_mul(
                mgmT[:, g * 128:(g + 1) * 128], pg[:, off:off + 128], mask01[:]),
                reads=(ptag, "mask01"), writes=("mgmT",))
        for c2 in range(64):
            pb, ptag = (b7, "b7") if c2 % 2 == 0 else (b8, "b8")
            em.step("tensor", lambda e, c2=c2, pb=pb: e.transpose(
                pb[:, 0:128], xcm[:, c2 * 128:(c2 + 1) * 128], eyeb[:]),
                reads=("xcm", "eyeb", ptag), writes=(ptag,))
            em.step("vector", lambda e, c2=c2, pb=pb: e.tensor_copy(
                xtk[:, c2 * 128:(c2 + 1) * 128], pb[:, 0:128]),
                reads=(ptag,), writes=("xtk", ptag))
        for g in range(8):
            pb, ptag = (b7, "b7") if g % 2 == 0 else (b8, "b8")
            em.step("tensor", lambda e, g=g, pb=pb: e.transpose(
                pb[:, 128:256], bcm[:, g * 128:(g + 1) * 128], eyeb[:]),
                reads=("bcm", "eyeb", ptag), writes=(ptag,))
            em.step("vector", lambda e, g=g, pb=pb: e.tensor_copy(
                bgT[:, g * 128:(g + 1) * 128], pb[:, 128:256]),
                reads=(ptag,), writes=("bgT", ptag))
        for hg in range(32):
            em.dma("gpsimd", csr4[:],
                   rowscr[hg * 4:(hg + 1) * 4, :].rearrange("p n -> () (p n)"),
                   reads=("rowscr", "csr4"), writes=("csr4",))
            em.step("scalar", lambda e: e.activation(sd4[:], csr4[:], AF.Exp),
                    reads=("csr4",), writes=("sd4",))
            em.step("tensor", lambda e: e.matmul(
                b1[:, :], onesf[:], csr4[:], start=True, stop=True),
                reads=("onesf", "csr4", "b1"), writes=("b1",))
            em.step("tensor", lambda e: e.matmul(
                b2[:, :], onesf[:], sd4[:], start=True, stop=True),
                reads=("onesf", "sd4", "b2"), writes=("b2",))
            em.step("scalar", lambda e: e.copy(sdB4[:], b2[:, :]),
                    reads=("b2",), writes=("sdB4",))
            for j in range(4):
                h = hg * 4 + j
                g = h // 16
                pM, mtag = (b5, "b5") if h % 2 == 0 else (b6, "b6")
                pr = h % 2
                dv, lv, mv, xv2, bv, sv = dscr[pr], ltb[pr], mtb[pr], xdt[pr], bdb[pr], sbst[pr]
                dtag, lttag, mttag, xtag2, btag, stag2 = (f"dscr{pr}", f"ltb{pr}", f"mtb{pr}",
                                                          f"xdt{pr}", f"bdb{pr}", f"sbst{pr}")
                em.step("vector", lambda e, j=j, dv=dv, h=h: e.tensor_scalar(
                    dv[:], b1[:, j * 128:(j + 1) * 128], csT[:, h:h + 1], 0.0,
                    ALU.subtract, ALU.min),
                    reads=("b1", "csT", dtag), writes=(dtag,))
                em.step("scalar", lambda e, dv=dv, lv=lv: e.activation(lv[:], dv[:], AF.Exp),
                        reads=(dtag,), writes=(lttag,))
                em.step("vector", lambda e, lv=lv, mv=mv, g=g: e.tensor_mul(
                    mv[:], lv[:], mgmT[:, g * 128:(g + 1) * 128]),
                    reads=(lttag, "mgmT"), writes=(mttag,))
                em.step("vector", lambda e, xv2=xv2, h=h: e.tensor_scalar_mul(
                    xv2[:], xtk[:, h * 64:(h + 1) * 64], dtT[:, h:h + 1]),
                    reads=("xtk", "dtT", xtag2), writes=(xtag2,))
                em.step("vector", lambda e, bv=bv, g=g, h=h: e.tensor_scalar_mul(
                    bv[:], bgT[:, g * 128:(g + 1) * 128], dbT[:, h:h + 1]),
                    reads=("bgT", "dbT", btag), writes=(btag,))
                if c > 0:
                    em.step("vector", lambda e, sv=sv, h=h: lowprec(e, lambda: e.tensor_copy(
                        sv[:], s_state[:, h * 64:(h + 1) * 64])),
                        reads=("s_state", stag2), writes=(stag2,))
                B_ = (h % 2) * 64
                em.step("tensor", lambda e, pM=pM, xv2=xv2, mv=mv, B_=B_: e.matmul(
                    pM[B_:B_ + 64, 0:128], xv2[:], mv[:], start=True, stop=True),
                    reads=(xtag2, mttag, mtag), writes=(mtag,))
                if c > 0:
                    em.step("tensor", lambda e, pM=pM, sv=sv, g=g, B_=B_: e.matmul(
                        pM[B_:B_ + 64, 128:256], sv[:], ccm[:, g * 128:(g + 1) * 128],
                        start=True, stop=True),
                        reads=(stag2, "ccm", mtag), writes=(mtag,))
                em.step("tensor", lambda e, pM=pM, bv=bv, xv2=xv2: e.matmul(
                    pM[0:128, 256:320], bv[:], xv2[:], start=True, stop=True),
                    reads=(btag, xtag2, mtag), writes=(mtag,))
                o1v, o2v = o1b[pr], o2b[pr]
                o1tag, o2tag = f"o1b{pr}", f"o2b{pr}"
                if c > 0:
                    em.step("vector", lambda e, o1v=o1v, pM=pM, j=j, B_=B_: e.tensor_mul(
                        o1v[B_:B_ + 64, :], pM[B_:B_ + 64, 128:256],
                        sdB4[B_:B_ + 64, j * 128:(j + 1) * 128]),
                        reads=(mtag, "sdB4", o1tag), writes=(o1tag,))
                    em.step("vector", lambda e, o1v=o1v, o2v=o2v, pM=pM, B_=B_: e.tensor_add(
                        o2v[B_:B_ + 64, :], o1v[B_:B_ + 64, :], pM[B_:B_ + 64, 0:128]),
                        reads=(o1tag, mtag, o2tag), writes=(o2tag,))
                else:
                    em.step("vector", lambda e, o2v=o2v, pM=pM, B_=B_: e.tensor_copy(
                        o2v[B_:B_ + 64, :], pM[B_:B_ + 64, 0:128]),
                        reads=(mtag, o2tag), writes=(o2tag,))
                em.step("vector", lambda e, h=h, o2v=o2v, B_=B_: e.scalar_tensor_tensor(
                    ybuf[B_:B_ + 64, (h // 2) * 128:(h // 2) * 128 + 128],
                    xcm[B_:B_ + 64, (h // 2) * 128:(h // 2) * 128 + 128],
                    dball[B_:B_ + 64, h:h + 1], o2v[B_:B_ + 64, :], ALU.mult, ALU.add),
                    reads=("xcm", "dball", o2tag, "ybuf"), writes=("ybuf",))
                em.step("vector", lambda e, h=h, pM=pM: e.scalar_tensor_tensor(
                    s_state[:, h * 64:(h + 1) * 64], s_state[:, h * 64:(h + 1) * 64],
                    gbc[:, h:h + 1], pM[0:128, 256:320], ALU.mult, ALU.add),
                    reads=("s_state", "gbc", mtag), writes=("s_state",))
        em.step("scalar", lambda e: e.activation(y2b[:], ybuf[:], AF.Square),
                reads=("ybuf",), writes=("y2b",))
        for c2 in range(64):
            em.step("tensor", lambda e, c2=c2: e.matmul(
                b0[0:1, 0:128], onescb[:], y2b[:, c2 * 128:(c2 + 1) * 128],
                start=(c2 == 0), stop=(c2 == 63)),
                reads=("onescb", "y2b") + (("b0",) if c2 == 0 else ()), writes=("b0",))
        em.step("scalar", lambda e: e.activation(rr[:], b0[0:1, 0:128], AF.Sqrt,
                                                 scale=1.0 / INTER, bias=eps1[:]),
                reads=("b0", "eps1"), writes=("rr",))
        em.step("vector", lambda e: e.reciprocal(rr[:], rr[:]),
                reads=("rr",), writes=("rr",))
        em.step("tensor", lambda e: e.matmul(b0[:, 128:256], onesf[:], rr[:],
                                             start=True, stop=True),
                reads=("onesf", "rr", "b0"), writes=("b0",))
        em.step("vector", lambda e: e.tensor_copy(rstdB[:], b0[:, 128:256]),
                reads=("b0",), writes=("rstdB", "b0"))
        for c2 in range(64):
            em.step("vector", lambda e, c2=c2: e.tensor_mul(
                ytmp[:], ybuf[:, c2 * 128:(c2 + 1) * 128], rstdB[:]),
                reads=("ybuf", "rstdB", "ytmp"), writes=("ytmp",))
            em.step("vector", lambda e, c2=c2: e.tensor_mul(
                y2b[:, c2 * 128:(c2 + 1) * 128], ytmp[:], sgt[:, c2 * 128:(c2 + 1) * 128]),
                reads=("ytmp", "sgt"), writes=("y2b",))
        em.dma("sync", ygv[:, :, tsl], y2b[:].rearrange("p (a t) -> p a t", t=128),
               reads=("y2b",))

    al.finish()


def ssd_inputs(core, cvx, cvB, cvC, dtv, sg):
    d = const_inputs(SSD_CONSTS)
    d.update({
        "cvx": cvx, "cvB": cvB, "cvC": cvC, "dtv": dtv, "sg": sg,
        "alog": core["L1_attn_A_log"], "Din": core["L1_attn_D"],
    })
    return d



NKO = INTER // 128
OPROJ_CONSTS = ()

def emit_OPROJ(em, io):
    nc = em.nc
    ygT = io["ygT"]
    Wo = io["Wo"]
    s1T_o = io["s1T"]
    woscr = io["woscr"]

    al = Alloc(nc)
    sb, ps = al.sb, al.ps
    ygb = sb("s_ygb", [128, NKO * 1024], BF16)
    lbuf = [sb(f"s_lb{i}", [128, 32 * 256], BF16) for i in range(2)]
    ev = [sb(f"s_ev{i}", [128, 1024], F32) for i in range(2)]
    pO = [ps(f"p_O{i}", [128, 512]) for i in range(8)]

    wtags = tuple(f"woscr{ci}" for ci in range(8))
    ygvv = ygT.rearrange("(c p) t -> p c t", p=128)

    segs = []
    for th in range(2):
        for mg in range(16):
            for sg_ in range(2):
                segs.append((th, mg, sg_))

    for th in range(2):
        tsl = slice(th * 1024, (th + 1) * 1024)
        for j in range(8):
            em.dma("sync", ygb[:].rearrange("p (c t) -> p c t", t=1024)[:, j * 8:(j + 1) * 8, :],
                   ygvv[:, j * 8:(j + 1) * 8, tsl], reads=("ygb",), writes=("ygb",))
        for mg in range(16):
            for sg_ in range(2):
                si = (th * 32 + mg * 2 + sg_)
                lb = lbuf[si % 2]
                ltag = f"lb{si % 2}"
                for kc in range(sg_ * 32, sg_ * 32 + 32):
                    q = "sync" if kc % 2 == 0 else "scalar"
                    em.dma(q, lb[:, (kc % 32) * 256:(kc % 32) * 256 + 256],
                           woscr[mg * 256:(mg + 1) * 256, kc * 128:(kc + 1) * 128],
                           reads=wtags + ((ltag,) if kc % 32 < 2 else ()),
                           writes=(ltag,), transpose=True)
                for mi in range(2):
                    m = mg * 2 + mi
                    for kc in range(sg_ * 32, sg_ * 32 + 32):
                        for nb in range(2):
                            pi = (mg % 2) * 4 + mi * 2 + nb
                            em.step("tensor", lambda e, kc=kc, nb=nb, lb=lb, mi=mi, pi=pi: e.matmul(
                                pO[pi][:, :],
                                lb[:, (kc % 32) * 256 + mi * 128:(kc % 32) * 256 + mi * 128 + 128],
                                ygb[:, kc * 1024 + nb * 512: kc * 1024 + (nb + 1) * 512],
                                start=(kc == 0), stop=(kc == NKO - 1)),
                                reads=(ltag, "ygb") + ((f"pO{pi}",) if kc == 0 else ()),
                                writes=(f"pO{pi}",))
            for mi in range(2):
                m = mg * 2 + mi
                evb = ev[m % 2]
                evt = f"ev{m % 2}"
                for nb in range(2):
                    pi = (mg % 2) * 4 + mi * 2 + nb
                    if nb == 0:
                        em.step("vector", lambda e, pi=pi, nb=nb, evb=evb: e.tensor_copy(
                            evb[:, nb * 512:(nb + 1) * 512], pO[pi][:, :]),
                            reads=(f"pO{pi}", evt), writes=(evt, f"pO{pi}"))
                    else:
                        em.step("scalar", lambda e, pi=pi, nb=nb, evb=evb: e.copy(
                            evb[:, nb * 512:(nb + 1) * 512], pO[pi][:, :]),
                            reads=(f"pO{pi}", evt), writes=(evt, f"pO{pi}"))
                em.dma("gpsimd", s1T_o[m * 128:(m + 1) * 128, tsl], evb[:],
                       reads=(evt,))

    al.finish()


def oproj_inputs(core, ygT):
    return {"ygT": ygT, "Wo": core["L1_attn_out_proj_w"]}



L1FFN_CONSTS = ("eyef", "e64")

def emit_L1FFN(em, io):
    nc = em.nc
    C = io
    xn1T = io["xn1T"]
    A1 = io["A1"]
    B1 = io["B1"]
    Wr1 = io["Wr1"]
    g1T_o = io["g1T"]
    wascr = io["wascr"]

    al = Alloc(nc)
    sb, ps = al.sb, al.ps
    eyef = sb("s_eyef", [128, 128], F32)
    e64 = sb("s_e64", [64, 1024], BF16)
    a1T = sb("s_a1T", [128, ND * KR], BF16)
    b1T = sb("s_b1T", [128, 8 * D], BF16)
    wr1T = sb("s_wr1T", [128, ND * 64], BF16)
    hgT = sb("s_hgT", [128, 8 * T], BF16)
    hsb = sb("s_hsb", [128, 512], F32)
    otile = sb("s_ot", [128, 512], BF16)
    xstream = [sb(f"s_xs{i}", [128, 512], BF16) for i in range(2)]
    wstage = sb("s_wstage", [128, 4096], F32)
    sc_f = sb("s_scf", [64, T], F32)
    sc_tm = sb("s_sctm", [128, 64], F32)
    mx16 = sb("s_mx16", [128, 16], F32)
    mrep = sb("s_mrep", [128, 64], F32)
    thcol = sb("s_thcol", [128, 1], F32)
    t1col = sb("s_t1col", [128, 1], F32)
    maskb = sb("s_maskb", [128, 64], F32)
    gtm = sb("s_gtm", [128, 64], F32)
    gT = sb("s_gT", [64, T], BF16)
    pH = [ps(f"p_H{i}", [128, 512]) for i in range(8)]

    em.dma("sync", eyef[:], C["eyef"][:], writes=("eyef",))
    em.dma("sync", e64[:], C["e64"][:], writes=("e64",))
    for kc in range(ND):
        em.dma("sync", a1T[:, kc * KR:(kc + 1) * KR],
               wascr[:, kc * 128:(kc + 1) * 128],
               reads=("wascr0", "wascr1"), writes=("a1T",), transpose=True)
    em.dma("sync", wstage[0:64, :], Wr1[:], writes=("wstage",))
    for kc in range(ND):
        pb, ptag = (pH[0], "pH0") if kc % 2 == 0 else (pH[1], "pH1")
        em.step("tensor", lambda e, kc=kc, pb=pb: e.transpose(
            pb[0:128, 0:64], wstage[0:64, kc * 128:(kc + 1) * 128], eyef[0:64, 0:64]),
            reads=("wstage", "eyef", ptag), writes=(ptag,))
        em.step("vector", lambda e, kc=kc, pb=pb: e.tensor_copy(
            wr1T[:, kc * 64:(kc + 1) * 64], pb[0:128, 0:64]),
            reads=(ptag,), writes=("wr1T", ptag))
    b1src = B1.rearrange("k (oc p) r -> oc p k r", p=128)
    for kc2 in range(8):
        for oc in range(ND):
            pb, ptag = (pH[0], "pH0") if oc % 2 == 0 else (pH[1], "pH1")
            em.dma("gpsimd", wstage[:, 0:128].rearrange("p (k r) -> p k r", r=16),
                   b1src[oc, :, kc2 * 8:(kc2 + 1) * 8, :],
                   reads=("wstage",), writes=("wstage",))
            em.step("tensor", lambda e, pb=pb: e.transpose(
                pb[:, 0:128], wstage[:, 0:128], eyef[:]),
                reads=("wstage", "eyef", ptag), writes=(ptag,))
            em.step("vector", lambda e, kc2=kc2, oc=oc, pb=pb: e.tensor_copy(
                b1T[:, kc2 * D + oc * 128: kc2 * D + (oc + 1) * 128],
                pb[:, 0:128]),
                reads=(ptag,), writes=("b1T", ptag))

    xv = xn1T.rearrange("(c p) t -> p c t", p=128)
    for kc in range(ND):
        xs = xstream[kc % 2]
        xst = f"xs{kc % 2}"
        for nb in range(4):
            em.dma("scalar", xs[:], xv[:, kc, nb * 512:(nb + 1) * 512],
                   reads=(xst,), writes=(xst,))
            em.step("tensor", lambda e, kc=kc, nb=nb, xs=xs: e.matmul(
                pH[nb][0:64, :], wr1T[:, kc * 64:(kc + 1) * 64], xs[:],
                start=(kc == 0), stop=(kc == ND - 1)),
                reads=("wr1T", xst) + ((f"pH{nb}",) if kc == 0 else ()),
                writes=(f"pH{nb}",))
    for nb in range(4):
        em.step("vector", lambda e, nb=nb: e.tensor_copy(
            sc_f[0:64, nb * 512:(nb + 1) * 512], pH[nb][0:64, :]),
            reads=(f"pH{nb}", "sc_f"), writes=("sc_f", f"pH{nb}"))
    for i in range(NT):
        nb = i // 4
        off = (i % 4) * 128
        em.step("tensor", lambda e, nb=nb, off=off: e.transpose(
            pH[4][0:128, 0:64], sc_f[0:64, nb * 512 + off: nb * 512 + off + 128],
            eyef[0:64, 0:64]),
            reads=("sc_f", "eyef", "pH4"), writes=("pH4",))
        em.step("vector", lambda e: e.tensor_copy(sc_tm[:], pH[4][0:128, 0:64]),
                reads=("pH4",), writes=("sc_tm", "pH4"))
        em.step("vector", lambda e: e.max(out=mx16[:, 0:8], in_=sc_tm[:]),
                reads=("sc_tm",), writes=("mx16",))
        em.step("vector", lambda e: e.match_replace(
            out=mrep[:], in_to_replace=mx16[:, 0:8], in_values=sc_tm[:], imm_value=-1e30),
            reads=("sc_tm", "mx16"), writes=("mrep",))
        em.step("vector", lambda e: e.max(out=mx16[:, 8:16], in_=mrep[:]),
                reads=("mrep",), writes=("mx16",))
        em.step("vector", lambda e: e.tensor_reduce(thcol[:], mx16[:], AX.X, ALU.min),
                reads=("mx16",), writes=("thcol",))
        em.step("vector", lambda e: e.tensor_scalar(
            t1col[:], mx16[:, 0:1], -1.0, None, ALU.mult),
            reads=("mx16",), writes=("t1col",))
        em.step("vector", lambda e: e.tensor_scalar(
            maskb[:], sc_tm[:], thcol[:], None, ALU.is_ge),
            reads=("sc_tm", "thcol"), writes=("maskb",))
        em.step("scalar", lambda e: e.activation(
            gtm[:], sc_tm[:], AF.Exp, bias=t1col[:]),
            reads=("sc_tm", "t1col"), writes=("gtm",))
        em.step("vector", lambda e: e.tensor_mul(gtm[:], gtm[:], maskb[:]),
                reads=("gtm", "maskb"), writes=("gtm",))
        em.step("vector", lambda e: e.reduce_sum(thcol[:], gtm[:], axis=AX.X),
                reads=("gtm",), writes=("thcol",))
        em.step("vector", lambda e: e.reciprocal(thcol[:], thcol[:]),
                reads=("thcol",), writes=("thcol",))
        em.step("vector", lambda e: e.tensor_scalar_mul(gtm[:], gtm[:], thcol[:]),
                reads=("gtm", "thcol"), writes=("gtm",))
        em.step("tensor", lambda e: e.transpose(
            pH[5][0:64, 0:128], gtm[:], eyef[:]),
            reads=("gtm", "eyef", "pH5"), writes=("pH5",))
        em.step("vector", lambda e, i=i: lowprec(e, lambda: e.tensor_copy(
            gT[:, i * 128:(i + 1) * 128], pH[5][0:64, 0:128])),
            reads=("pH5",), writes=("gT", "pH5"))
    for nb in range(4):
        for kc in range(ND):
            xs = xstream[kc % 2]
            xst = f"xs{kc % 2}"
            em.dma("scalar", xs[:], xv[:, kc, nb * 512:(nb + 1) * 512],
                   reads=(xst,), writes=(xst,))
            for m in range(8):
                em.step("tensor", lambda e, kc=kc, m=m, xs=xs: e.matmul(
                    pH[m][:, :], a1T[:, kc * KR + m * 128: kc * KR + (m + 1) * 128],
                    xs[:], start=(kc == 0), stop=(kc == ND - 1)),
                    reads=("a1T", xst) + ((f"pH{m}",) if kc == 0 else ()),
                    writes=(f"pH{m}",))
        for m in range(8):
            em.step("vector", lambda e, m=m: e.tensor_copy(hsb[:], pH[m][:, :]),
                    reads=(f"pH{m}", "hsb"), writes=("hsb", f"pH{m}"))
            em.step("tensor", lambda e, m=m, nb=nb: e.matmul(
                pH[m][:, :], e64[:, m * 128:(m + 1) * 128], gT[:, nb * 512:(nb + 1) * 512],
                start=True, stop=True),
                reads=("e64", "gT", f"pH{m}"), writes=(f"pH{m}",))
            em.step("vector", lambda e, m=m, nb=nb: e.tensor_mul(
                hgT[:, m * T + nb * 512: m * T + (nb + 1) * 512], hsb[:], pH[m][:, :]),
                reads=("hsb", f"pH{m}"), writes=("hgT",))
    for nb in range(4):
        for m2 in range(ND):
            pi = 6 + (m2 % 2)
            for kc2 in range(8):
                em.step("tensor", lambda e, m2=m2, kc2=kc2, nb=nb, pi=pi: e.matmul(
                    pH[pi][:, :], b1T[:, kc2 * D + m2 * 128: kc2 * D + (m2 + 1) * 128],
                    hgT[:, kc2 * T + nb * 512: kc2 * T + (nb + 1) * 512],
                    start=(kc2 == 0), stop=(kc2 == 7)),
                    reads=("b1T", "hgT") + ((f"pH{pi}",) if kc2 == 0 else ()),
                    writes=(f"pH{pi}",))
            em.step("scalar", lambda e, pi=pi: e.activation(
                otile[:], pH[pi][:, :], AF.Silu, scale=2.0),
                reads=(f"pH{pi}", "otile"), writes=("otile", f"pH{pi}"))
            em.dma("gpsimd", g1T_o[m2 * 128:(m2 + 1) * 128, nb * 512:(nb + 1) * 512],
                   otile[:], reads=("otile",))

    al.finish()


def l1ffn_inputs(core, xn1T):
    d = const_inputs(L1FFN_CONSTS)
    d.update({"xn1T": xn1T, "A1": core["L1_ffn_A"], "B1": core["L1_ffn_B"],
              "Wr1": core["L1_ffn_Wr"]})
    return d



COMB_CONSTS = ("eyef",)

def emit_COMB(em, io):
    nc = em.nc
    C = io
    x1 = io["x1"]
    s1T = io["s1T"]
    g1T = io["g1T"]
    out_o = io["out"]

    al = Alloc(nc)
    sb, ps = al.sb, al.ps
    eyef = sb("s_eyef", [128, 128], F32)
    x1t = [sb(f"s_x1t{i}", [128, D], F32) for i in range(2)]
    sTt = [sb(f"s_sTt{i}", [128, ND * 128], F32) for i in range(2)]
    gTt = [sb(f"s_gTt{i}", [128, ND * 128], BF16) for i in range(2)]
    dT = sb("s_dT", [128, 128], F32)
    pT = [ps(f"p_T{i}", [128, 512]) for i in range(2)]

    em.dma("sync", eyef[:], C["eyef"][:], writes=("eyef",))
    sv = s1T.rearrange("(c p) t -> p c t", p=128)
    gv = g1T.rearrange("(c p) t -> p c t", p=128)
    for i in range(NT):
        xb, xtag = x1t[i % 2], f"x1t{i % 2}"
        st, stag = sTt[i % 2], f"sTt{i % 2}"
        gt, gtag = gTt[i % 2], f"gTt{i % 2}"
        em.dma("sync", xb[:], x1[i * 128:(i + 1) * 128, :], reads=(xtag,), writes=(xtag,))
        em.dma("gpsimd", st[:].rearrange("p (c t) -> p c t", t=128),
               sv[:, :, i * 128:(i + 1) * 128], reads=(stag,), writes=(stag,))
        em.dma("scalar", gt[:].rearrange("p (c t) -> p c t", t=128),
               gv[:, :, i * 128:(i + 1) * 128], reads=(gtag,), writes=(gtag,))
        for dc in range(ND):
            pb, ptag = (pT[0], "pT0") if dc % 2 == 0 else (pT[1], "pT1")
            em.step("vector", lambda e, dc=dc, st=st, gt=gt: e.tensor_mul(
                dT[:], st[:, dc * 128:(dc + 1) * 128], gt[:, dc * 128:(dc + 1) * 128]),
                reads=(stag, gtag, "dT"), writes=("dT",))
            em.step("tensor", lambda e, pb=pb: e.transpose(pb[:, 0:128], dT[:], eyef[:]),
                    reads=("dT", "eyef", ptag), writes=(ptag,))
            em.step("vector", lambda e, dc=dc, pb=pb, xb=xb: e.tensor_add(
                xb[:, dc * 128:(dc + 1) * 128], xb[:, dc * 128:(dc + 1) * 128], pb[:, 0:128]),
                reads=(ptag, xtag), writes=(xtag, ptag))
        em.dma("sync", out_o[i * 128:(i + 1) * 128, :], xb[:], reads=(xtag,))

    al.finish()


def comb_inputs(x1, s1T, g1T):
    d = const_inputs(COMB_CONSTS)
    d.update({"x1": x1, "s1T": s1T, "g1T": g1T})
    return d



_PROGS = {}

def _prog(name, builder):
    if name not in _PROGS:
        nc = bass.Bass("TRN2", target_bir_lowering=False)
        builder(nc, strict=True)
        _PROGS[name] = nc
    return _PROGS[name]


def _run(name, builder, in_maps):
    nc = _prog(name, builder)
    res = run_bass_kernel_spmd(nc, in_maps, core_ids=list(range(NCORES)))
    return res.results


def kernel(**inputs):
    inputs = {k: np.ascontiguousarray(v) for k, v in inputs.items()}
    x_full = inputs["L0_attn_x"]

    maps = []
    for b in range(NCORES):
        core = dict(inputs)
        core["x0"] = np.ascontiguousarray(x_full[b])
        maps.append(l0_inputs(core))
    r = _run("L0", build_L0, maps)
    x1 = [r[b]["x1"] for b in range(NCORES)]
    xn1T = [r[b]["xn1T"] for b in range(NCORES)]

    maps = [inproj_inputs(inputs, xn1T[b]) for b in range(NCORES)]
    r = _run("INPROJ", build_INPROJ, maps)
    sg = [r[b]["sg"] for b in range(NCORES)]
    cvx = [r[b]["cvx"] for b in range(NCORES)]
    cvB = [r[b]["cvB"] for b in range(NCORES)]
    cvC = [r[b]["cvC"] for b in range(NCORES)]
    dtv = [r[b]["dtv"] for b in range(NCORES)]

    maps = [ssd_inputs(inputs, cvx[b], cvB[b], cvC[b], dtv[b], sg[b])
            for b in range(NCORES)]
    r = _run("SSD", build_SSD, maps)
    ygT = [r[b]["ygT"] for b in range(NCORES)]

    maps = [oproj_inputs(inputs, ygT[b]) for b in range(NCORES)]
    r = _run("OPROJ", build_OPROJ, maps)
    s1T = [r[b]["s1T"] for b in range(NCORES)]

    maps = [l1ffn_inputs(inputs, xn1T[b]) for b in range(NCORES)]
    r = _run("L1FFN", build_L1FFN, maps)
    g1T = [r[b]["g1T"] for b in range(NCORES)]

    maps = [comb_inputs(x1[b], s1T[b], g1T[b]) for b in range(NCORES)]
    r = _run("COMB", build_COMB, maps)
    out = np.stack([r[b]["out"] for b in range(NCORES)], axis=0)
    return out



PHASE_SPECS = {
    "PREFETCH": dict(
        consts=PRE_CONSTS,
        ins={"Wp": ([CPROJ, D], F32), "Wo": ([D, INTER], F32),
             "A1": ([NEXP, RANK, D], F32)},
        outs={},
        internal={"wbscr": ([CPROJ, D], BF16), "woscr": ([D, INTER], BF16),
                  "wascr": ([KR, D], BF16)}),
    "L0": dict(
        consts=L0_CONSTS,
        ins={"x0": ([T, D], F32), "Wq": ([D, 128, 32], F32), "bq": ([128, 32], F32),
             "Wk": ([D, 128, 1], F32), "bk": ([128, 1], F32), "Wv": ([D, 128, 1], F32),
             "bv": ([128, 1], F32), "A0": ([8, 16, D], F32), "B0": ([8, D, 16], F32),
             "Wr0": ([8, D], F32), "br0": ([8], F32)},
        outs={"x1": ([T, D], F32), "xn1T": ([D, T], BF16)},
        internal={}),
    "INPROJ": dict(
        consts=INPROJ_CONSTS,
        ins={"xn1T": ([D, T], BF16), "Wp": ([CPROJ, D], F32),
             "convw": ([CONV_DIM, 1, 4], F32), "convb": ([CONV_DIM], F32),
             "dtb": ([H], F32), "nw": ([INTER], F32)},
        outs={"sg": ([INTER, T], BF16), "cvx": ([INTER, T], BF16),
              "cvB": ([GN, T], BF16), "cvC": ([GN, T], BF16), "dtv": ([H, T], F32)},
        internal={"wbscr": ([CPROJ, D], BF16)}),
    "SSD": dict(
        consts=SSD_CONSTS,
        ins={"cvx": ([INTER, T], BF16), "cvB": ([GN, T], BF16), "cvC": ([GN, T], BF16),
             "dtv": ([H, T], F32), "sg": ([INTER, T], BF16),
             "alog": ([H], F32), "Din": ([H], F32)},
        outs={"ygT": ([INTER, T], BF16)},
        internal={"rowscr": ([128, 128], F32)}),
    "OPROJ": dict(
        consts=OPROJ_CONSTS,
        ins={"ygT": ([INTER, T], BF16), "Wo": ([D, INTER], F32)},
        outs={"s1T": ([D, T], F32)},
        internal={"woscr": ([D, INTER], BF16)}),
    "L1FFN": dict(
        consts=L1FFN_CONSTS,
        ins={"xn1T": ([D, T], BF16), "A1": ([NEXP, RANK, D], F32),
             "B1": ([NEXP, D, RANK], F32), "Wr1": ([NEXP, D], F32)},
        outs={"g1T": ([D, T], BF16)},
        internal={"wascr": ([KR, D], BF16)}),
    "COMB": dict(
        consts=COMB_CONSTS,
        ins={"x1": ([T, D], F32), "s1T": ([D, T], F32), "g1T": ([D, T], BF16)},
        outs={"out": ([T, D], F32)},
        internal={}),
}

EMITTERS = {"PREFETCH": emit_PREFETCH, "L0": emit_L0, "INPROJ": emit_INPROJ,
            "SSD": emit_SSD, "OPROJ": emit_OPROJ, "L1FFN": emit_L1FFN,
            "COMB": emit_COMB}


def full_barrier(em, targets=None):
    if targets is None:
        targets = {ch: c for ch, c in em.count.items() if c > 0}
    else:
        targets = {ch: c for ch, c in targets.items() if c > 0}
    for e in ENGINES:
        kn = em.know[e]
        eff = {}
        for ch2, c in targets.items():
            if ch2 == (e, "c"):
                continue
            if kn[ch2] < c:
                eff[ch2] = c
        for ch2, c in eff.items():
            s = em.snap[ch2][c]
            for g in CHANNELS:
                kn[g] = max(kn[g], s[g])
            kn[ch2] = max(kn[ch2], c)
        ch = (e, "c")
        em.count[ch] += 1
        cnt = em.count[ch]
        em.steps[e].append(((lambda e2: e2.nop()), eff, 1, ch))
        snap = dict(kn)
        snap[ch] = cnt
        em.snap[ch][cnt] = snap


def build_phase(nc, phase, strict=True):
    em = Emitter(nc, strict)
    spec = PHASE_SPECS[phase]
    io = declare_consts(nc, spec["consts"])
    for k, (shape, dt) in spec["ins"].items():
        io[k] = nc.dram_tensor(k, shape, dt, kind="ExternalInput").ap()
    for k, (shape, dt) in spec["outs"].items():
        io[k] = nc.dram_tensor(k, shape, dt, kind="ExternalOutput").ap()
    for k, (shape, dt) in spec["internal"].items():
        io[k] = nc.dram_tensor(k, shape, dt, kind="Internal").ap()
    EMITTERS[phase](em, io)
    em.barrier_step("sync")
    em.realize()
    em.finish()
    return em


PHASE_ORDER = ["PREFETCH", "L0", "INPROJ", "SSD", "OPROJ", "L1FFN", "COMB"]


def build_FUSED(nc, strict=True):
    em = Emitter(nc, strict)
    io = declare_consts(nc, tuple(CONST_SPECS.keys()))
    declared = {}
    produced = set()
    for ph in PHASE_ORDER:
        for k in PHASE_SPECS[ph]["outs"]:
            produced.add(k)
    produced.discard("out")
    for ph in PHASE_ORDER:
        spec = PHASE_SPECS[ph]
        for k, (shape, dt) in spec["ins"].items():
            if k not in declared:
                kind = "Internal" if k in produced else "ExternalInput"
                declared[k] = nc.dram_tensor(k, shape, dt, kind=kind).ap()
        for k, (shape, dt) in spec["outs"].items():
            if k not in declared:
                kind = "ExternalOutput" if k == "out" else "Internal"
                declared[k] = nc.dram_tensor(k, shape, dt, kind=kind).ap()
        for k, (shape, dt) in spec["internal"].items():
            if k not in declared:
                declared[k] = nc.dram_tensor(k, shape, dt, kind="Internal").ap()
    io.update(declared)
    EMITTERS["L0"](em, io)
    snap0 = dict(em.count)
    EMITTERS["PREFETCH"](em, io)
    full_barrier(em, targets=snap0)
    for ph in PHASE_ORDER:
        if ph in ("PREFETCH", "L0"):
            continue
        EMITTERS[ph](em, io)
        full_barrier(em)
    em.barrier_step("sync")
    em.realize()
    em.finish()
    return em


FUSED_INPUT_MAP = {
    "x0": None,
    "Wq": "L0_attn_Wq", "bq": "L0_attn_bq", "Wk": "L0_attn_Wk", "bk": "L0_attn_bk",
    "Wv": "L0_attn_Wv", "bv": "L0_attn_bv", "A0": "L0_ffn_A", "B0": "L0_ffn_B",
    "Wr0": "L0_ffn_Wr", "br0": "L0_ffn_br",
    "Wp": "L1_attn_in_proj_w", "convw": "L1_attn_conv_w", "convb": "L1_attn_conv_b",
    "dtb": "L1_attn_dt_bias", "nw": "L1_attn_norm_w",
    "alog": "L1_attn_A_log", "Din": "L1_attn_D", "Wo": "L1_attn_out_proj_w",
    "A1": "L1_ffn_A", "B1": "L1_ffn_B", "Wr1": "L1_ffn_Wr",
}


def kernel(**inputs):
    inputs = {k: np.ascontiguousarray(v) for k, v in inputs.items()}
    x_full = inputs["L0_attn_x"]
    if "FUSED" not in _PROGS:
        nc = bass.Bass("TRN2", target_bir_lowering=False)
        build_FUSED(nc, strict=True)
        _PROGS["FUSED"] = nc
    nc = _PROGS["FUSED"]
    base = const_inputs(tuple(CONST_SPECS.keys()))
    for k, src in FUSED_INPUT_MAP.items():
        if src is not None:
            base[k] = inputs[src]
    in_maps = []
    for b in range(NCORES):
        m = dict(base)
        m["x0"] = np.ascontiguousarray(x_full[b])
        in_maps.append(m)
    res = run_bass_kernel_spmd(nc, in_maps, core_ids=list(range(NCORES)))
    return np.stack([res.results[b]["out"] for b in range(NCORES)], axis=0)


# version 07
# speedup vs baseline: 1.2186x; 1.2186x over previous
import numpy as np
import ml_dtypes
import concourse.bass as bass
import concourse.mybir as mybir
from concourse.bass_utils import run_bass_kernel_spmd

F32 = mybir.dt.float32
BF16 = mybir.dt.bfloat16
AF = mybir.ActivationFunctionType
ALU = mybir.AluOpType
AX = mybir.AxisListType
bf16 = ml_dtypes.bfloat16

NCORES = 8
T = 2048
D = 4096
NT = T // 128
ND = D // 128

H = 128; P = 64; G = 8; N = 128; CS = 128; K4 = 4
INTER = H * P
GN = G * N
CONV_DIM = INTER + 2 * GN
CPROJ = INTER + CONV_DIM + H
NCHUNK = T // CS
NEXP = 64; TOPK = 16; RANK = 16
KR = NEXP * RANK

INV_SQRT_QK = float(1.0 / np.sqrt(128.0))

ENGINES = ("tensor", "vector", "scalar", "gpsimd", "sync")
NDMACH = 4
CHANNELS = ([(e, "c") for e in ENGINES]
            + [(e, f"d{i}") for e in ("sync", "gpsimd", "scalar", "vector")
               for i in range(NDMACH)])


class Emitter:

    def __init__(self, nc, strict=False):
        self.nc = nc
        self.strict = strict
        self.steps = {e: [] for e in ENGINES}
        self.count = {ch: 0 for ch in CHANNELS}
        self.know = {e: {ch: 0 for ch in CHANNELS} for e in ENGINES}
        self.snap = {ch: {0: {c2: 0 for c2 in CHANNELS}} for ch in CHANNELS}
        self.last_writer = {}
        self.readers = {}
        self.dma_idx = {e: 0 for e in ENGINES}
        self.n_waits = 0
        self.audit = []

    def step(self, eng, fn, reads=(), writes=(), dma=False):
        if dma:
            sub = self.dma_idx[eng] % NDMACH
            self.dma_idx[eng] += 1
            ch = (eng, f"d{sub}")
        else:
            ch = (eng, "c")
        waits = {}
        def add_need(src):
            (sch, c) = src
            if sch == ch and not (self.strict and sch[1] == "c" and eng != "tensor"):
                return
            if sch[0] == eng and sch[1] == "c" and not dma and (
                    not self.strict or eng == "tensor"):
                return
            waits[sch] = max(waits.get(sch, 0), c)
        for t_ in reads:
            lw = self.last_writer.get(t_)
            if lw is not None:
                add_need(lw)
        for t_ in writes:
            lw = self.last_writer.get(t_)
            if lw is not None:
                add_need(lw)
            for r in self.readers.get(t_, ()):
                add_need(r)
        if dma and self.count[ch] > 0:
            waits[ch] = max(waits.get(ch, 0), self.count[ch])
        kn = self.know[eng]
        eff = {ch2: c for ch2, c in waits.items() if kn[ch2] < c}
        for ch2, c in eff.items():
            s = self.snap[ch2][c]
            for g in CHANNELS:
                kn[g] = max(kn[g], s[g])
            kn[ch2] = max(kn[ch2], c)
        amount = 16 if dma else 1
        self.count[ch] += amount
        cnt = self.count[ch]
        self.steps[eng].append((fn, eff, amount, ch))
        self.n_waits += len(eff)
        for t_ in writes:
            self.last_writer[t_] = (ch, cnt)
            self.readers[t_] = []
        for t_ in reads:
            self.readers.setdefault(t_, []).append((ch, cnt))
        snap = dict(kn)
        snap[ch] = cnt
        self.snap[ch][cnt] = snap
        self.audit.append((eng, len(self.steps[eng]) - 1, tuple(reads),
                           tuple(writes), ch, cnt, dma))
        return cnt

    def dma(self, queue, out, in_, reads=(), writes=(), **kw):
        self.step(queue, lambda e: e.dma_start(out=out, in_=in_, **kw),
                  reads=reads, writes=writes, dma=True)

    def barrier_step(self, eng="sync"):
        targets = {ch: c for ch, c in self.count.items() if c > 0}
        kn = self.know[eng]
        eff = {ch: c for ch, c in targets.items()
               if kn[ch] < c and not (ch[0] == eng and ch[1] == "c")}
        for ch2, c in eff.items():
            s = self.snap[ch2][c]
            for g in CHANNELS:
                kn[g] = max(kn[g], s[g])
            kn[ch2] = max(kn[ch2], c)
        ch = (eng, "c")
        self.count[ch] += 1
        cnt = self.count[ch]
        self.steps[eng].append((lambda e: e.nop(), eff, 1, ch))
        snap = dict(kn)
        snap[ch] = cnt
        self.snap[ch][cnt] = snap

    def realize(self):
        nc = self.nc
        self._cms = []
        sems = {}
        for ch in CHANNELS:
            cm = nc.semaphore(f"sem_{ch[0]}_{ch[1]}")
            sems[ch] = cm.__enter__()
            self._cms.append(cm)
        block_cm = nc.Block()
        block = block_cm.__enter__()
        self._cms.append(block_cm)

        def emit_for(e):
            def body(eng):
                for (fn, waits, amount, ch) in self.steps[e]:
                    for ch2, c in waits.items():
                        eng.wait_ge(sems[ch2], c)
                    inst = fn(eng)
                    inst.then_inc(sems[ch], amount)
            return body

        for e in ENGINES:
            if self.steps[e]:
                getattr(block, e)(emit_for(e))
        return block

    def finish(self):
        for cm in reversed(self._cms):
            cm.__exit__(None, None, None)



_CONSTS = None

def build_consts():
    global _CONSTS
    if _CONSTS is not None:
        return _CONSTS
    c = {}
    c["eyeb"] = np.eye(128, dtype=bf16)
    c["eyef"] = np.eye(128, dtype=np.float32)
    c["onesf"] = np.ones((1, 128), dtype=np.float32)
    sel = np.zeros((128, 32 * 128), dtype=bf16)
    for q in range(32):
        for f in range(128):
            sel[4 * q + f // 32, q * 128 + f] = 1
    c["selq"] = sel
    e8 = np.zeros((8, 128), dtype=bf16)
    for j in range(128):
        e8[j // 16, j] = 1
    c["e8"] = e8
    e64 = np.zeros((64, 1024), dtype=bf16)
    for j in range(1024):
        e64[j // 16, j] = 1
    c["e64"] = e64
    c["cumU"] = np.triu(np.ones((128, 128), dtype=np.float32))
    c["mask01"] = np.triu(np.ones((128, 128), dtype=np.float32))
    _CONSTS = {k: np.ascontiguousarray(v) for k, v in c.items()}
    return _CONSTS


CONST_SPECS = {
    "eyeb": BF16, "eyef": F32, "onesf": F32, "selq": BF16,
    "e8": BF16, "e64": BF16, "cumU": F32, "mask01": F32,
}


def lowprec(e, f):
    with e.bass.allow_low_precision(reason="bf16 weight fold / operand cast"):
        return f()


def declare_consts(nc, names):
    cvals = build_consts()
    return {k: nc.dram_tensor("c_" + k, list(cvals[k].shape), CONST_SPECS[k],
                              kind="ExternalInput").ap() for k in names}


def const_inputs(names):
    cvals = build_consts()
    return {"c_" + k: cvals[k] for k in names}


class Alloc:
    _n = 0

    def __init__(self, nc):
        self.nc = nc
        self.cms = []
        Alloc._n += 1
        self.prefix = f"a{Alloc._n}_"

    def sb(self, name, shape, dt):
        cm = self.nc.sbuf_tensor(self.prefix + name, shape, dt)
        t = cm.__enter__(); self.cms.append(cm); return t

    def ps(self, name, shape, dt=F32):
        cm = self.nc.psum_tensor(self.prefix + name, shape, dt)
        t = cm.__enter__(); self.cms.append(cm); return t

    def finish(self):
        for cm in reversed(self.cms):
            cm.__exit__(None, None, None)



L0_CONSTS = ("eyeb", "eyef", "selq", "e8")

def emit_L0(em, io):
    nc = em.nc
    C = io
    x0 = io["x0"]
    Wq = io["Wq"]
    bq = io["bq"]
    Wk = io["Wk"]
    bk = io["bk"]
    Wv = io["Wv"]
    bv = io["bv"]
    A0T = io["A0T"]
    B0T = io["B0T"]
    Wr0 = io["Wr0"]
    br0 = io["br0"]
    x1_o = io["x1"]
    xn1T_o = io["xn1T"]

    al = Alloc(nc)
    sb, ps = al.sb, al.ps

    eyeb = sb("s_eyeb", [128, 128], BF16)
    eyef = sb("s_eyef", [128, 128], F32)
    selq = sb("s_selq", [128, 32 * 128], BF16)
    e8 = sb("s_e8", [8, 128], BF16)
    wqs = sb("s_wqs", [128, ND * 128], BF16)
    wkT = sb("s_wkT", [128, ND * 128], BF16)
    wvT = sb("s_wvT", [128, ND * 128], BF16)
    a0T = sb("s_a0T", [128, ND * 128], BF16)
    b0T = sb("s_b0T", [128, ND * 128], BF16)
    wr0T = sb("s_wr0T", [128, ND * 8], BF16)
    bqs = sb("s_bqs", [128, 1], F32)
    bkc = sb("s_bkc", [128, 1], F32)
    bvc = sb("s_bvc", [128, 1], F32)
    br0c = sb("s_br0c", [8, 1], F32)
    bq_stage = sb("s_bqstage", [128, 32], F32)
    epscol = sb("s_epscol", [128, 1], F32)

    mat = sb("s_mat", [128, 128 * 128], BF16)
    wq_stage = [mat[:, 0:4096], mat[:, 4096:8192]]
    wr0_stage = mat[0:8, 12288:16384]
    PREP_TAGS = ("mq0", "mq1", "mw")

    x0t = [sb(f"s_x0t{i}", [128, D], F32) for i in range(2)]
    stat = sb("s_stat", [128, 8], F32)
    g0T = sb("s_g0T", [128, ND * 128], F32)
    d0T = sb("s_d0T", [128, ND * 128], F32)
    xnb = sb("s_xnb", [128, D], BF16)
    xnT = sb("s_xnT", [128, ND * 128], BF16)
    a_T = sb("s_aT", [128, 128], F32)
    k_T = sb("s_kT", [128, 128], F32)
    v_T = sb("s_vT", [128, 128], F32)
    a_tm = sb("s_atm", [128, 128], F32)
    k_tm = sb("s_ktm", [128, 128], F32)
    v_tm = sb("s_vtm", [128, 128], F32)
    kmm = sb("s_kmm", [128, 4], F32)
    negm = sb("s_negm", [128, 128], F32)
    s1col = sb("s_s1", [128, 128], F32)
    s2col = sb("s_s2", [128, 128], F32)
    ttr_scr = sb("s_ttrscr", [128, 128], F32)
    o_tm = sb("s_otm", [128, 128], F32)
    o_Tb = sb("s_oTb", [128, 128], BF16)
    rt_f = sb("s_rtf", [8, 128], F32)
    rt_tm = sb("s_rttm", [128, 8], F32)
    rt_scr = sb("s_rtscr", [128, 8], F32)
    g8T = sb("s_g8T", [8, 128], BF16)
    scoreB = sb("s_scoreB", [128, 128], F32)
    glow = sb("s_glow", [128, 128], BF16)
    x1t = sb("s_x1t", [128, D], F32)
    xn1b = sb("s_xn1b", [128, D], BF16)
    xn1Tt = sb("s_xn1Tt", [128, ND * 128], BF16)

    pA = ps("p_A", [128, 512])
    pB = ps("p_B", [128, 512])
    pC = ps("p_C", [128, 512])
    pD = ps("p_D", [128, 512])
    pT1 = ps("p_T1", [128, 512])
    pT2 = ps("p_T2", [128, 512])
    pb1 = ps("p_b1", [128, 1024], BF16)
    pb2 = ps("p_b2", [128, 1024], BF16)

    em.step("gpsimd", lambda e: e.memset(epscol[:], 1e-5), writes=("epscol",))
    em.dma("sync", eyeb[:], C["eyeb"][:], writes=("eyeb",))
    em.dma("sync", eyef[:], C["eyef"][:], writes=("eyef",))
    em.dma("sync", selq[:], C["selq"][:], writes=("selq",))
    em.dma("sync", e8[:], C["e8"][:], writes=("e8",))
    em.dma("gpsimd", bq_stage[:], bq[:], writes=("bq_stage",))
    em.step("vector", lambda e: e.reduce_sum(bqs[:], bq_stage[:], axis=AX.X),
            reads=("bq_stage",), writes=("bqs",))
    em.dma("gpsimd", bkc[:], bk[:], writes=("bkc",))
    em.dma("gpsimd", bvc[:], bv[:], writes=("bvc",))
    em.dma("gpsimd", br0c[:], br0[:].rearrange("h -> h ()"), writes=("br0c",))

    for kc in range(ND):
        wst = wq_stage[kc % 2]
        wtg = f"mq{kc % 2}"
        em.dma("gpsimd", wst[:],
               Wq[kc * 128:(kc + 1) * 128, :, :].rearrange("p d h -> p (d h)"),
               reads=(wtg,), writes=(wtg,))
        em.step("vector", lambda e, kc=kc, wst=wst: lowprec(e, lambda: e.reduce_sum(
            wqs[:, kc * 128:(kc + 1) * 128].rearrange("p (d a) -> p d a", a=1),
            wst[:].rearrange("p (d h) -> p d h", h=32), axis=AX.X)),
            reads=(wtg,), writes=("wqs",))
        em.dma("gpsimd", wkT[:, kc * 128:(kc + 1) * 128],
               Wk[kc * 128:(kc + 1) * 128, :, 0], writes=("wkT",))
        em.dma("gpsimd", wvT[:, kc * 128:(kc + 1) * 128],
               Wv[kc * 128:(kc + 1) * 128, :, 0], writes=("wvT",))

    em.dma("gpsimd", a0T[:].rearrange("p (c h) -> p c h", h=128),
           A0T.rearrange("(c p) h -> p c h", p=128), writes=("a0T",))
    em.dma("gpsimd", b0T[:], B0T[:], writes=("b0T",))
    em.dma("gpsimd", wr0_stage[:], Wr0[:], writes=("mw",))
    for kc in range(ND):
        pt, ptag = (pb1, "pb1") if kc % 2 == 0 else (pb2, "pb2")
        em.step("tensor", lambda e, kc=kc, pt=pt: e.transpose(
            pt[0:128, 0:8], wr0_stage[:, kc * 128:(kc + 1) * 128], eyeb[0:8, 0:8]),
            reads=("mw", "eyeb", ptag), writes=(ptag,))
        em.step("vector", lambda e, kc=kc, pt=pt: e.tensor_copy(
            wr0T[:, kc * 8:(kc + 1) * 8], pt[0:128, 0:8]),
            reads=(ptag,), writes=("wr0T", ptag))

    def emit_layernorm(src, out_bf, tag_src, tag_out, dump, tag_dump):
        em.step("vector", lambda e: e.reduce_sum(stat[:, 0:1], src[:], axis=AX.X),
                reads=(tag_src,), writes=("stat",))
        em.step("scalar", lambda e: e.activation(
            dump[:], src[:], AF.Square, accum_out=stat[:, 1:2]),
            reads=(tag_src,), writes=("stat", tag_dump))
        em.step("vector", lambda e: e.tensor_scalar(
            stat[:, 2:3], stat[:, 0:1], 1.0 / D, None, ALU.mult),
            reads=("stat",), writes=("stat",))
        em.step("vector", lambda e: e.tensor_mul(stat[:, 5:6], stat[:, 2:3], stat[:, 2:3]),
                reads=("stat",), writes=("stat",))
        em.step("vector", lambda e: e.scalar_tensor_tensor(
            stat[:, 3:4], stat[:, 1:2], 1.0 / D, stat[:, 5:6], ALU.mult, ALU.subtract),
            reads=("stat",), writes=("stat",))
        em.step("scalar", lambda e: e.activation(
            stat[:, 6:7], stat[:, 3:4], AF.Sqrt, bias=epscol[:]),
            reads=("stat", "epscol"), writes=("stat",))
        em.step("vector", lambda e: e.reciprocal(stat[:, 4:5], stat[:, 6:7]),
                reads=("stat",), writes=("stat",))
        em.step("vector", lambda e: e.tensor_scalar(
            out_bf[:], src[:], stat[:, 2:3], stat[:, 4:5], ALU.subtract, ALU.mult),
            reads=(tag_src, "stat"), writes=(tag_out,))

    def emit_transpose_tm2fm(src_bf, dst, tag_src, tag_dst):
        for dc in range(ND):
            pt, ptag = (pb1, "pb1") if dc % 2 == 0 else (pb2, "pb2")
            em.step("tensor", lambda e, dc=dc, pt=pt: e.transpose(
                pt[:, 0:128], src_bf[:, dc * 128:(dc + 1) * 128], eyeb[:]),
                reads=(tag_src, "eyeb", ptag), writes=(ptag,))
            em.step("vector", lambda e, dc=dc, pt=pt: e.tensor_copy(
                dst[:, dc * 128:(dc + 1) * 128], pt[:, 0:128]),
                reads=(ptag,), writes=(tag_dst, ptag))

    for i in range(NT):
        xb = x0t[i % 2]
        xtag = f"x0t{i % 2}"
        em.dma("sync", xb[:], x0[i * 128:(i + 1) * 128, :], writes=(xtag,))
        emit_layernorm(xb, xnb, xtag, "xnb", g0T, "g0T")
        emit_transpose_tm2fm(xnb, xnT, "xnb", "xnT")

        def mm_acc(psum, pslice, lhs_buf, tag_lhs, lhs_w, m_width, ptag):
            for kc in range(ND):
                em.step("tensor", lambda e, kc=kc: e.matmul(
                    psum[0:m_width, pslice[0]:pslice[1]],
                    lhs_buf[:, kc * lhs_w: kc * lhs_w + m_width],
                    xnT[:, kc * 128:(kc + 1) * 128],
                    start=(kc == 0), stop=(kc == ND - 1)),
                    reads=(tag_lhs, "xnT") + ((ptag,) if kc == 0 else ()),
                    writes=(ptag,))
        mm_acc(pA, (0, 128), wqs, "wqs", 128, 128, "pA")
        mm_acc(pB, (0, 128), wkT, "wkT", 128, 128, "pB")
        mm_acc(pC, (0, 128), wvT, "wvT", 128, 128, "pC")
        em.step("vector", lambda e: e.tensor_scalar(
            a_T[:], pA[:, 0:128], bqs[:], INV_SQRT_QK, ALU.add, ALU.mult),
            reads=("pA", "bqs"), writes=("a_T",))
        em.step("vector", lambda e: e.tensor_scalar(
            k_T[:], pB[:, 0:128], bkc[:], None, ALU.add),
            reads=("pB", "bkc"), writes=("k_T",))
        em.step("vector", lambda e: e.tensor_scalar(
            v_T[:], pC[:, 0:128], bvc[:], None, ALU.add),
            reads=("pC", "bvc"), writes=("v_T",))
        for (srcb, dstb, st, dt_) in ((a_T, a_tm, "a_T", "a_tm"),
                                      (k_T, k_tm, "k_T", "k_tm"),
                                      (v_T, v_tm, "v_T", "v_tm")):
            em.step("tensor", lambda e, srcb=srcb: e.transpose(
                pT1[:, 0:128], srcb[:], eyef[:]),
                reads=(st, "eyef", "pT1"), writes=("pT1",))
            em.step("vector", lambda e, dstb=dstb: e.tensor_copy(dstb[:], pT1[:, 0:128]),
                    reads=("pT1",), writes=(dt_, "pT1"))
        em.step("vector", lambda e: e.reduce_max(kmm[:, 0:1], k_tm[:], axis=AX.X),
                reads=("k_tm",), writes=("kmm",))
        em.step("vector", lambda e: e.tensor_reduce(kmm[:, 1:2], k_tm[:], AX.X, ALU.min),
                reads=("k_tm",), writes=("kmm",))
        em.step("vector", lambda e: e.tensor_scalar(
            kmm[:, 2:4], kmm[:, 0:2], -1.0, None, ALU.mult),
            reads=("kmm",), writes=("kmm",))
        em.step("vector", lambda e: e.tensor_scalar_mul(negm[:], a_tm[:], kmm[:, 2:3]),
                reads=("a_tm", "kmm"), writes=("negm",))
        em.step("vector", lambda e: e.scalar_tensor_tensor(
            negm[:], a_tm[:], kmm[:, 3:4], negm[:], ALU.mult, ALU.min),
            reads=("a_tm", "kmm", "negm"), writes=("negm",))
        for d in range(128):
            wtags_d = ("mat", "s1col") + (
                PREP_TAGS if (i == 0 and d == 0) else ())
            em.step("scalar", lambda e, d=d: e.activation(
                mat[:, d * 128:(d + 1) * 128], k_tm[:], AF.Exp,
                bias=negm[:, d:d + 1], scale=a_tm[:, d:d + 1],
                accum_out=s1col[:, d:d + 1]),
                reads=("k_tm", "negm", "a_tm"), writes=wtags_d)
        for d in range(128):
            em.step("vector", lambda e, d=d: e.tensor_mul(
                ttr_scr[:], mat[:, d * 128:(d + 1) * 128], v_tm[:]),
                reads=("mat", "v_tm"), writes=("ttr_scr",))
            em.step("vector", lambda e, d=d: e.reduce_sum(
                s2col[:, d:d + 1], ttr_scr[:], axis=AX.X),
                reads=("ttr_scr",), writes=("s2col",))
        em.step("vector", lambda e: e.reciprocal(s1col[:], s1col[:]),
                reads=("s1col",), writes=("s1col",))
        em.step("vector", lambda e: e.tensor_mul(o_tm[:], s2col[:], s1col[:]),
                reads=("s2col", "s1col"), writes=("o_tm",))
        em.step("tensor", lambda e: e.transpose(pT1[:, 0:128], o_tm[:], eyef[:]),
                reads=("o_tm", "eyef", "pT1"), writes=("pT1",))
        em.step("vector", lambda e: e.tensor_copy(o_Tb[:], pT1[:, 0:128]),
                reads=("pT1",), writes=("o_Tb", "pT1"))

        mm_acc(pD, (0, 128), wr0T, "wr0T", 8, 8, "pD")
        em.step("vector", lambda e: e.tensor_scalar(
            rt_f[:], pD[0:8, 0:128], br0c[:], None, ALU.add),
            reads=("pD", "br0c"), writes=("rt_f",))
        em.step("tensor", lambda e: e.transpose(pT1[0:128, 0:8], rt_f[:], eyef[0:8, 0:8]),
                reads=("rt_f", "eyef", "pT1"), writes=("pT1",))
        em.step("vector", lambda e: e.tensor_copy(rt_tm[:], pT1[0:128, 0:8]),
                reads=("pT1",), writes=("rt_tm", "pT1"))
        em.step("vector", lambda e: e.reduce_max(stat[:, 7:8], rt_tm[:], axis=AX.X),
                reads=("rt_tm",), writes=("stat",))
        em.step("vector", lambda e: e.tensor_scalar(
            stat[:, 7:8], stat[:, 7:8], -1.0, None, ALU.mult),
            reads=("stat",), writes=("stat",))
        em.step("scalar", lambda e: e.activation(
            rt_scr[:], rt_tm[:], AF.Exp, bias=stat[:, 7:8], accum_out=stat[:, 6:7]),
            reads=("rt_tm", "stat"), writes=("rt_scr", "stat"))
        em.step("vector", lambda e: e.reciprocal(stat[:, 6:7], stat[:, 6:7]),
                reads=("stat",), writes=("stat",))
        em.step("vector", lambda e: e.tensor_scalar(
            rt_scr[:], rt_scr[:], stat[:, 6:7], 8.0, ALU.mult, ALU.mult),
            reads=("rt_scr", "stat"), writes=("rt_scr",))
        em.step("tensor", lambda e: e.transpose(pT1[0:8, 0:128], rt_scr[:], eyef[:]),
                reads=("rt_scr", "eyef", "pT1"), writes=("pT1",))
        em.step("vector", lambda e: e.tensor_copy(g8T[:], pT1[0:8, 0:128]),
                reads=("pT1",), writes=("g8T", "pT1"))
        em.step("tensor", lambda e: e.matmul(
            pD[:, 128:256], e8[:], g8T[:], start=True, stop=True),
            reads=("e8", "g8T", "pD"), writes=("pD",))
        em.step("vector", lambda e: e.tensor_copy(scoreB[:], pD[:, 128:256]),
                reads=("pD",), writes=("scoreB", "pD"))
        mm_acc(pA, (128, 256), a0T, "a0T", 128, 128, "pA")
        em.step("vector", lambda e: e.tensor_mul(glow[:], pA[:, 128:256], scoreB[:]),
                reads=("pA", "scoreB"), writes=("glow",))
        for oc in range(ND):
            pt, ptg = (pA, "pA") if oc % 2 == 0 else (pB, "pB")
            em.step("tensor", lambda e, oc=oc, pt=pt: e.matmul(
                pt[:, 256:384], b0T[:, oc * 128:(oc + 1) * 128], glow[:],
                start=True, stop=True),
                reads=("b0T", "glow", ptg), writes=(ptg,))
            em.step("scalar", lambda e, oc=oc, pt=pt: e.activation(
                g0T[:, oc * 128:(oc + 1) * 128], pt[:, 256:384], AF.Silu, scale=2.0),
                reads=(ptg,), writes=("g0T",))
        for q in range(ND):
            pt, ptg = (pC, "pC") if q % 2 == 0 else (pD, "pD")
            em.step("tensor", lambda e, q=q, pt=pt: e.matmul(
                pt[:, 256:384], selq[:, q * 128:(q + 1) * 128], o_Tb[:],
                start=True, stop=True),
                reads=("selq", "o_Tb", ptg), writes=(ptg,))
            em.step("vector", lambda e, q=q, pt=pt: e.tensor_mul(
                d0T[:, q * 128:(q + 1) * 128], pt[:, 256:384],
                g0T[:, q * 128:(q + 1) * 128]),
                reads=(ptg, "g0T"), writes=("d0T",))
        em.step("vector", lambda e, xb=xb: e.tensor_copy(x1t[:], xb[:]),
                reads=(xtag,), writes=("x1t",))
        for dc in range(ND):
            pt, ptg = (pT1, "pT1") if dc % 2 == 0 else (pT2, "pT2")
            em.step("tensor", lambda e, dc=dc, pt=pt: e.transpose(
                pt[:, 0:128], d0T[:, dc * 128:(dc + 1) * 128], eyef[:]),
                reads=("d0T", "eyef", ptg), writes=(ptg,))
            em.step("vector", lambda e, dc=dc, pt=pt: e.tensor_add(
                x1t[:, dc * 128:(dc + 1) * 128],
                x1t[:, dc * 128:(dc + 1) * 128], pt[:, 0:128]),
                reads=(ptg, "x1t"), writes=("x1t", ptg))
        em.dma("sync", x1_o[i * 128:(i + 1) * 128, :], x1t[:], reads=("x1t",))
        emit_layernorm(x1t, xn1b, "x1t", "xn1b", g0T, "g0T")
        emit_transpose_tm2fm(xn1b, xn1Tt, "xn1b", "xn1Tt")
        em.dma("sync",
               xn1T_o.rearrange("(c p) t -> p c t", p=128)[:, :, i * 128:(i + 1) * 128],
               xn1Tt[:].rearrange("p (c t) -> p c t", t=128),
               reads=("xn1Tt",))

    al.finish()


def fix_wkv_dma():
    pass


def l0_inputs(core):
    d = const_inputs(L0_CONSTS)
    d.update({
        "x0": core["x0"],
        "Wq": core["L0_attn_Wq"], "bq": core["L0_attn_bq"],
        "Wk": core["L0_attn_Wk"], "bk": core["L0_attn_bk"],
        "Wv": core["L0_attn_Wv"], "bv": core["L0_attn_bv"],
        "A0": core["L0_ffn_A"], "B0": core["L0_ffn_B"],
        "Wr0": core["L0_ffn_Wr"], "br0": core["L0_ffn_br"],
    })
    return d


def kernel(**inputs):
    raise NotImplementedError("under construction")



INPROJ_CONSTS = ("eyef",)
NM_PROJ = CPROJ // 128
CAST_ROWS = 512

def emit_INPROJ(em, io):
    nc = em.nc
    C = io
    xn1T = io["xn1T"]
    WpT = io["WpT"]
    convw = io["convw"]
    convb = io["convb"]
    dtb = io["dtb"]
    nw = io["nw"]
    sg_o = io["sg"]
    cvx_o = io["cvx"]
    cvB_o = io["cvB"]
    cvC_o = io["cvC"]
    dtv_o = io["dtv"]

    al = Alloc(nc)
    sb, ps = al.sb, al.ps
    eyef = sb("s_eyef", [128, 128], F32)
    xn1sb = sb("s_xn1sb", [128, ND * T], BF16)
    lbuf = [sb(f"s_lbuf{i}", [128, ND * 256], BF16) for i in range(2)]
    cvpad = [sb(f"s_cvpad{i}", [128, T + 3], F32) for i in range(2)]
    acc = [sb(f"s_acc{i}", [128, T], F32) for i in range(2)]
    otile = [sb(f"s_otile{i}", [128, T], BF16) for i in range(2)]
    cw = sb("s_cw", [128, 80 * 4], F32)
    cb = sb("s_cb", [128, 80], F32)
    nwc = sb("s_nwc", [128, 64], F32)
    dtbc = sb("s_dtbc", [128, 1], F32)
    onec = sb("s_onec", [128, 1], F32)
    stage = sb("s_stage", [128, 128], F32)
    pP = [ps(f"p_P{i}", [128, 512]) for i in range(8)]

    em.dma("sync", eyef[:], C["eyef"][:], writes=("eyef",))
    for i in range(2):
        em.step("gpsimd", lambda e, i=i: e.memset(cvpad[i][:, 0:3], 0.0),
                writes=(f"cvp{i}",))
    em.step("gpsimd", lambda e: e.memset(onec[:], 1.0), writes=("onec",))
    xv = xn1T.rearrange("(c p) t -> p c t", p=128)
    for j in range(4):
        em.dma("sync", xn1sb[:].rearrange("p (c t) -> p c t", t=T)[:, j * 8:(j + 1) * 8, :],
               xv[:, j * 8:(j + 1) * 8, :], writes=("xn1sb",))
    for j in range(80):
        em.dma("scalar", cw[:, j * 4:(j + 1) * 4],
               convw[:, 0, :].rearrange("(n p) j -> p n j", p=128)[:, j, :],
               writes=("cw",))
    em.dma("gpsimd", stage[0:80, :], convb[:].rearrange("(n p) -> n p", p=128),
           writes=("stage",))
    em.step("tensor", lambda e: e.transpose(pP[0][0:128, 0:80], stage[0:80, :], eyef[0:80, 0:80]),
            reads=("stage", "eyef", "pP0"), writes=("pP0",))
    em.step("vector", lambda e: e.tensor_copy(cb[:], pP[0][0:128, 0:80]),
            reads=("pP0",), writes=("cb", "pP0"))
    em.dma("gpsimd", stage[0:1, 0:128], dtb[:].rearrange("h -> () h"), writes=("stage",))
    em.step("tensor", lambda e: e.transpose(pP[0][0:128, 0:1], stage[0:1, 0:128], eyef[0:1, 0:1]),
            reads=("stage", "eyef", "pP0"), writes=("pP0",))
    em.step("vector", lambda e: e.tensor_copy(dtbc[:], pP[0][0:128, 0:1]),
            reads=("pP0",), writes=("dtbc", "pP0"))
    em.dma("gpsimd", stage[0:64, :], nw[:].rearrange("(n p) -> n p", p=128),
           writes=("stage",))
    em.step("tensor", lambda e: e.transpose(pP[0][0:128, 0:64], stage[0:64, :], eyef[0:64, 0:64]),
            reads=("stage", "eyef", "pP0"), writes=("pP0",))
    em.step("vector", lambda e: e.tensor_copy(nwc[:], pP[0][0:128, 0:64]),
            reads=("pP0",), writes=("nwc", "pP0"))

    ngrp = (NM_PROJ + 1) // 2
    for g in range(ngrp):
        gm = 2 if (g + 1) * 2 <= NM_PROJ else NM_PROJ - g * 2
        lb = lbuf[g % 2]
        ltag = f"lbuf{g % 2}"
        if gm == 2:
            for kc0 in range(0, ND, 4):
                em.dma("gpsimd",
                       lb[:, kc0 * 256:(kc0 + 4) * 256].rearrange(
                           "p (f m) -> p f m", f=4),
                       WpT[kc0 * 128:(kc0 + 4) * 128,
                           g * 256: g * 256 + 256].rearrange(
                           "(f p) m -> p f m", p=128),
                       reads=((ltag,) if kc0 == 0 else ()),
                       writes=(ltag,))
        else:
            for kc in range(ND):
                em.dma("gpsimd", lb[:, kc * 256: kc * 256 + gm * 128],
                       WpT[kc * 128:(kc + 1) * 128, g * 256: g * 256 + gm * 128],
                       reads=((ltag,) if kc < 2 else ()),
                       writes=(ltag,))
        for mi in range(gm):
            m = g * 2 + mi
            cv = cvpad[m % 2]
            cvt = f"cvp{m % 2}"
            ac = acc[m % 2]
            act = f"acc{m % 2}"
            ot = otile[m % 2]
            ott = f"otile{m % 2}"
            for kc in range(ND):
                for nb in range(4):
                    pi = (m % 2) * 4 + nb
                    em.step("tensor", lambda e, kc=kc, nb=nb, lb=lb, mi=mi, pi=pi: e.matmul(
                        pP[pi][:, :], lb[:, kc * 256 + mi * 128: kc * 256 + mi * 128 + 128],
                        xn1sb[:, kc * T + nb * 512: kc * T + (nb + 1) * 512],
                        start=(kc == 0), stop=(kc == ND - 1)),
                        reads=(ltag, "xn1sb") + ((f"pP{pi}",) if kc == 0 else ()),
                        writes=(f"pP{pi}",))
            for nb in range(4):
                pi = (m % 2) * 4 + nb
                if nb < 2:
                    em.step("vector", lambda e, nb=nb, pi=pi, cv=cv: e.tensor_copy(
                        cv[:, 3 + nb * 512: 3 + (nb + 1) * 512], pP[pi][:, :]),
                        reads=(f"pP{pi}",), writes=(cvt, f"pP{pi}"))
                else:
                    em.step("scalar", lambda e, nb=nb, pi=pi, cv=cv: e.copy(
                        cv[:, 3 + nb * 512: 3 + (nb + 1) * 512], pP[pi][:, :]),
                        reads=(f"pP{pi}",), writes=(cvt, f"pP{pi}"))
            if m < 64:
                em.step("scalar", lambda e, cv=cv, ac=ac: e.activation(
                    ac[:], cv[:, 3:3 + T], AF.Silu),
                    reads=(cvt,), writes=(act,))
                em.step("vector", lambda e, m=m, ot=ot, ac=ac: e.tensor_scalar_mul(
                    ot[:], ac[:], nwc[:, m:m + 1]),
                    reads=(act, "nwc"), writes=(ott,))
                em.dma("gpsimd", sg_o[m * 128:(m + 1) * 128, :], ot[:],
                       reads=(ott,))
            elif m < 144:
                j = m - 64
                em.step("vector", lambda e, j=j, cv=cv, ac=ac: e.tensor_scalar_mul(
                    ac[:], cv[:, 3:3 + T], cw[:, j * 4 + 3: j * 4 + 4]),
                    reads=(cvt, "cw"), writes=(act,))
                for tap in (2, 1, 0):
                    em.step("vector", lambda e, j=j, tap=tap, cv=cv, ac=ac: e.scalar_tensor_tensor(
                        ac[:], cv[:, tap: tap + T], cw[:, j * 4 + tap: j * 4 + tap + 1],
                        ac[:], ALU.mult, ALU.add),
                        reads=(cvt, "cw", act), writes=(act,))
                em.step("scalar", lambda e, j=j, ac=ac, ot=ot: e.activation(
                    ot[:], ac[:], AF.Silu, bias=cb[:, j:j + 1]),
                    reads=(act, "cb"), writes=(ott,))
                if j < 64:
                    em.dma("gpsimd", cvx_o[j * 128:(j + 1) * 128, :], ot[:],
                           reads=(ott,))
                elif j < 72:
                    em.dma("gpsimd", cvB_o[(j - 64) * 128:(j - 63) * 128, :], ot[:],
                           reads=(ott,))
                else:
                    em.dma("gpsimd", cvC_o[(j - 72) * 128:(j - 71) * 128, :], ot[:],
                           reads=(ott,))
            else:
                rl = acc[1]
                rlt = "acc1"
                em.step("vector", lambda e, cv=cv, ac=ac: e.tensor_scalar_add(
                    ac[:], cv[:, 3:3 + T], dtbc[:]),
                    reads=(cvt, "dtbc"), writes=(act,))
                em.step("scalar", lambda e, ac=ac, rl=rl: e.activation(rl[:], ac[:], AF.Relu),
                        reads=(act,), writes=(rlt,))
                em.step("scalar", lambda e, ac=ac: e.activation(ac[:], ac[:], AF.Abs),
                        reads=(act,), writes=(act,))
                em.step("scalar", lambda e, ac=ac: e.activation(ac[:], ac[:], AF.Exp, scale=-1.0),
                        reads=(act,), writes=(act,))
                em.step("scalar", lambda e, ac=ac: e.activation(ac[:], ac[:], AF.Ln, bias=onec[:]),
                        reads=(act, "onec"), writes=(act,))
                em.step("vector", lambda e, ac=ac, rl=rl: e.tensor_add(ac[:], ac[:], rl[:]),
                        reads=(act, rlt), writes=(act,))
                em.dma("gpsimd", dtv_o[:], ac[:], reads=(act,))

    al.finish()


def inproj_inputs(core, xn1T):
    d = const_inputs(INPROJ_CONSTS)
    d.update({
        "xn1T": xn1T,
        "Wp": core["L1_attn_in_proj_w"],
        "convw": core["L1_attn_conv_w"],
        "convb": core["L1_attn_conv_b"],
        "dtb": core["L1_attn_dt_bias"],
        "nw": core["L1_attn_norm_w"],
    })
    return d



SSD_CONSTS = ("eyef", "eyeb", "onesf", "cumU", "mask01")

def emit_SSD(em, io):
    nc = em.nc
    C = io
    cvx = io["cvx"]
    cvB = io["cvB"]
    cvC = io["cvC"]
    dtv = io["dtv"]
    sg = io["sg"]
    alog = io["alog"]
    Din = io["Din"]
    ygT_o = io["ygT"]
    rowscr = io["rowscr"]

    al = Alloc(nc)
    sb, ps = al.sb, al.ps
    eyef = sb("s_eyef", [128, 128], F32)
    eyeb = sb("s_eyeb", [128, 128], BF16)
    onesf = sb("s_onesf", [1, 128], F32)
    cumU = sb("s_cumU", [128, 128], F32)
    mask01 = sb("s_mask01", [128, 128], F32)
    onescb = sb("s_onescb", [128, 1], BF16)
    eps1 = sb("s_eps1", [1, 1], F32)
    acol = sb("s_acol", [128, 1], F32)
    dball = sb("s_dball", [128, 128], F32)
    dtv_sb = sb("s_dtvsb", [128, T], F32)
    adt = sb("s_adt", [128, T], F32)
    s_state = sb("s_state", [128, H * 64], F32)
    xcm = sb("s_xcm", [128, 64 * 128], BF16)
    xtk = sb("s_xtk", [128, 64 * 128], BF16)
    bcm = sb("s_bcm", [128, 8 * 128], BF16)
    ccm = sb("s_ccm", [128, 8 * 128], BF16)
    bgT = sb("s_bgT", [128, 8 * 128], BF16)
    sgt = sb("s_sgt", [128, 64 * 128], BF16)
    ybuf = sb("s_ybuf", [128, 64 * 128], F32)
    y2b = sb("s_y2b", [128, 64 * 128], BF16)
    mgmT = sb("s_mgmT", [128, 8 * 128], BF16)
    adtT = sb("s_adtT", [128, 128], F32)
    csT = sb("s_csT", [128, 128], F32)
    acs = sb("s_acs", [128, 128], F32)
    dbfm = sb("s_dbfm", [128, 128], F32)
    dbT = sb("s_dbT", [128, 128], F32)
    dtT = sb("s_dtT", [128, 128], F32)
    gbc = sb("s_gbc", [128, 128], F32)
    rstdB = sb("s_rstdB", [128, 128], F32)
    rr = sb("s_rr", [1, 128], F32)
    csr4 = sb("s_csr4", [1, 512], F32)
    sd4 = sb("s_sd4", [1, 512], F32)
    grow = sb("s_grow", [1, 128], F32)
    sdB4 = sb("s_sdB4", [128, 512], F32)
    dscr = [sb(f"s_dscr{i}", [128, 128], F32) for i in range(2)]
    ltb = [sb(f"s_ltb{i}", [128, 128], BF16) for i in range(2)]
    mtb = [sb(f"s_mtb{i}", [128, 128], BF16) for i in range(2)]
    xdt = [sb(f"s_xdt{i}", [128, 64], BF16) for i in range(2)]
    bdb = [sb(f"s_bdb{i}", [128, 128], BF16) for i in range(2)]
    sbst = [sb(f"s_sbst{i}", [128, 64], BF16) for i in range(2)]
    o1b = [sb(f"s_o1b{i}", [128, 128], F32) for i in range(2)]
    o2b = [sb(f"s_o2b{i}", [128, 128], F32) for i in range(2)]
    ytmp = sb("s_ytmp", [128, 128], F32)
    stage = sb("s_stage", [1, 128], F32)

    b0 = ps("p_b0", [128, 512])
    b1 = ps("p_b1", [128, 512])
    b2 = ps("p_b2", [128, 512])
    b5 = ps("p_b5", [128, 512])
    b6 = ps("p_b6", [128, 512])
    b7 = ps("p_b7", [128, 1024], BF16)
    b8 = ps("p_b8", [128, 1024], BF16)
    b3 = ps("p_b3", [128, 512])

    em.dma("sync", eyef[:], C["eyef"][:], writes=("eyef",))
    em.dma("sync", eyeb[:], C["eyeb"][:], writes=("eyeb",))
    em.dma("sync", onesf[:], C["onesf"][:], writes=("onesf",))
    em.dma("sync", cumU[:], C["cumU"][:], writes=("cumU",))
    em.dma("sync", mask01[:], C["mask01"][:], writes=("mask01",))
    em.step("gpsimd", lambda e: e.memset(onescb[:], 1.0), writes=("onescb",))
    em.step("gpsimd", lambda e: e.memset(eps1[:], 1e-5), writes=("eps1",))
    for j in range(8):
        em.step("gpsimd", lambda e, j=j: e.memset(s_state[:, j * 1024:(j + 1) * 1024], 0.0),
                writes=("s_state",))
    em.dma("gpsimd", dtv_sb[:], dtv[:], writes=("dtvsb",))
    em.dma("gpsimd", stage[:], alog[:].rearrange("h -> () h"), writes=("stage",))
    em.step("tensor", lambda e: e.transpose(b0[0:128, 0:1], stage[:], eyef[0:1, 0:1]),
            reads=("stage", "eyef", "b0"), writes=("b0",))
    em.step("scalar", lambda e: e.activation(acol[:], b0[0:128, 0:1], AF.Exp),
            reads=("b0",), writes=("acol",))
    em.step("vector", lambda e: e.tensor_scalar_mul(acol[:], acol[:], -1.0),
            reads=("acol",), writes=("acol",))
    em.step("vector", lambda e: e.tensor_scalar_mul(adt[:], dtv_sb[:], acol[:]),
            reads=("dtvsb", "acol"), writes=("adt",))
    em.dma("gpsimd", stage[:], Din[:].rearrange("h -> () h"), writes=("stage",))
    em.step("tensor", lambda e: e.matmul(b0[:, 128:256], onesf[:], stage[:],
                                         start=True, stop=True),
            reads=("onesf", "stage", "b0"), writes=("b0",))
    em.step("vector", lambda e: e.tensor_copy(dball[:], b0[:, 128:256]),
            reads=("b0",), writes=("dball", "b0"))

    cvxv = cvx.rearrange("(c p) t -> p c t", p=128)
    cvBv = cvB.rearrange("(g p) t -> p g t", p=128)
    cvCv = cvC.rearrange("(g p) t -> p g t", p=128)
    sgv = sg.rearrange("(c p) t -> p c t", p=128)
    ygv = ygT_o.rearrange("(c p) t -> p c t", p=128)

    for c in range(NCHUNK):
        tsl = slice(c * CS, (c + 1) * CS)
        em.dma("sync", xcm[:].rearrange("p (a t) -> p a t", t=128), cvxv[:, :, tsl],
               reads=("xcm",), writes=("xcm",))
        em.dma("gpsimd", bcm[:].rearrange("p (a t) -> p a t", t=128), cvBv[:, :, tsl],
               reads=("bcm",), writes=("bcm",))
        em.dma("gpsimd", ccm[:].rearrange("p (a t) -> p a t", t=128), cvCv[:, :, tsl],
               reads=("ccm",), writes=("ccm",))
        em.dma("scalar", sgt[:].rearrange("p (a t) -> p a t", t=128), sgv[:, :, tsl],
               reads=("sgt",), writes=("sgt",))
        em.step("tensor", lambda e, c=c: e.transpose(b0[:, 0:128], adt[:, c * CS:(c + 1) * CS], eyef[:]),
                reads=("adt", "eyef", "b0"), writes=("b0",))
        em.step("vector", lambda e: e.tensor_copy(adtT[:], b0[:, 0:128]),
                reads=("b0",), writes=("adtT", "b0"))
        em.step("tensor", lambda e: e.matmul(b0[:, 128:256], cumU[:], adtT[:], start=True, stop=True),
                reads=("cumU", "adtT", "b0"), writes=("b0",))
        em.step("vector", lambda e: e.tensor_copy(csT[:], b0[:, 128:256]),
                reads=("b0",), writes=("csT", "b0"))
        em.step("tensor", lambda e: e.matmul(b0[:, 256:384], adtT[:], cumU[:], start=True, stop=True),
                reads=("adtT", "cumU", "b0"), writes=("b0",))
        em.step("vector", lambda e: e.tensor_copy(acs[:], b0[:, 256:384]),
                reads=("b0",), writes=("acs", "b0"))
        em.dma("gpsimd", rowscr[:], acs[:], reads=("acs",), writes=("rowscr",))
        em.dma("gpsimd", grow[:], rowscr[:, 127:128].rearrange("p n -> n p"),
               reads=("rowscr",), writes=("grow",), allow_slow_non_contiguous=True)
        em.step("scalar", lambda e: e.activation(grow[:], grow[:], AF.Exp),
                reads=("grow",), writes=("grow",))
        em.step("scalar", lambda e: e.activation(dbfm[:], acs[:], AF.Exp,
                                                 scale=-1.0, bias=acs[:, 127:128]),
                reads=("acs",), writes=("dbfm",))
        em.step("tensor", lambda e: e.transpose(b0[:, 384:512], dbfm[:], eyef[:]),
                reads=("dbfm", "eyef", "b0"), writes=("b0",))
        em.step("vector", lambda e: e.tensor_copy(dbT[:], b0[:, 384:512]),
                reads=("b0",), writes=("dbT", "b0"))
        em.step("tensor", lambda e, c=c: e.transpose(b0[:, 0:128], dtv_sb[:, c * CS:(c + 1) * CS], eyef[:]),
                reads=("dtvsb", "eyef", "b0"), writes=("b0",))
        em.step("vector", lambda e: e.tensor_copy(dtT[:], b0[:, 0:128]),
                reads=("b0",), writes=("dtT", "b0"))
        em.step("tensor", lambda e: e.matmul(
            b3[:, 0:128], onesf[:], grow[:], start=True, stop=True),
            reads=("onesf", "grow", "b3"), writes=("b3",))
        em.step("vector", lambda e: e.tensor_copy(gbc[:], b3[:, 0:128]),
                reads=("b3",), writes=("gbc", "b3"))
        for g in range(8):
            pg, ptag = (b1, "b1") if g < 4 else (b2, "b2")
            off = (g % 4) * 128
            em.step("tensor", lambda e, g=g, pg=pg, off=off: e.matmul(
                pg[:, off:off + 128], bcm[:, g * 128:(g + 1) * 128],
                ccm[:, g * 128:(g + 1) * 128], start=True, stop=True),
                reads=("bcm", "ccm", ptag), writes=(ptag,))
        for g in range(8):
            pg, ptag = (b1, "b1") if g < 4 else (b2, "b2")
            off = (g % 4) * 128
            em.step("vector", lambda e, g=g, pg=pg, off=off: e.tensor_mul(
                mgmT[:, g * 128:(g + 1) * 128], pg[:, off:off + 128], mask01[:]),
                reads=(ptag, "mask01"), writes=("mgmT",))
        for c2 in range(64):
            pb, ptag = (b7, "b7") if c2 % 2 == 0 else (b8, "b8")
            em.step("tensor", lambda e, c2=c2, pb=pb: e.transpose(
                pb[:, 0:128], xcm[:, c2 * 128:(c2 + 1) * 128], eyeb[:]),
                reads=("xcm", "eyeb", ptag), writes=(ptag,))
            em.step("vector", lambda e, c2=c2, pb=pb: e.tensor_copy(
                xtk[:, c2 * 128:(c2 + 1) * 128], pb[:, 0:128]),
                reads=(ptag,), writes=("xtk", ptag))
        for g in range(8):
            pb, ptag = (b7, "b7") if g % 2 == 0 else (b8, "b8")
            em.step("tensor", lambda e, g=g, pb=pb: e.transpose(
                pb[:, 128:256], bcm[:, g * 128:(g + 1) * 128], eyeb[:]),
                reads=("bcm", "eyeb", ptag), writes=(ptag,))
            em.step("vector", lambda e, g=g, pb=pb: e.tensor_copy(
                bgT[:, g * 128:(g + 1) * 128], pb[:, 128:256]),
                reads=(ptag,), writes=("bgT", ptag))
        for hg in range(32):
            em.dma("gpsimd", csr4[:],
                   rowscr[hg * 4:(hg + 1) * 4, :].rearrange("p n -> () (p n)"),
                   reads=("rowscr", "csr4"), writes=("csr4",))
            em.step("scalar", lambda e: e.activation(sd4[:], csr4[:], AF.Exp),
                    reads=("csr4",), writes=("sd4",))
            em.step("tensor", lambda e: e.matmul(
                b1[:, :], onesf[:], csr4[:], start=True, stop=True),
                reads=("onesf", "csr4", "b1"), writes=("b1",))
            em.step("tensor", lambda e: e.matmul(
                b2[:, :], onesf[:], sd4[:], start=True, stop=True),
                reads=("onesf", "sd4", "b2"), writes=("b2",))
            em.step("scalar", lambda e: e.copy(sdB4[:], b2[:, :]),
                    reads=("b2",), writes=("sdB4",))
            for j in range(4):
                h = hg * 4 + j
                g = h // 16
                pM, mtag = (b5, "b5") if h % 2 == 0 else (b6, "b6")
                pr = h % 2
                dv, lv, mv, xv2, bv, sv = dscr[pr], ltb[pr], mtb[pr], xdt[pr], bdb[pr], sbst[pr]
                dtag, lttag, mttag, xtag2, btag, stag2 = (f"dscr{pr}", f"ltb{pr}", f"mtb{pr}",
                                                          f"xdt{pr}", f"bdb{pr}", f"sbst{pr}")
                em.step("vector", lambda e, j=j, dv=dv, h=h: e.tensor_scalar(
                    dv[:], b1[:, j * 128:(j + 1) * 128], csT[:, h:h + 1], 0.0,
                    ALU.subtract, ALU.min),
                    reads=("b1", "csT", dtag), writes=(dtag,))
                em.step("scalar", lambda e, dv=dv, lv=lv: e.activation(lv[:], dv[:], AF.Exp),
                        reads=(dtag,), writes=(lttag,))
                em.step("vector", lambda e, lv=lv, mv=mv, g=g: e.tensor_mul(
                    mv[:], lv[:], mgmT[:, g * 128:(g + 1) * 128]),
                    reads=(lttag, "mgmT"), writes=(mttag,))
                em.step("vector", lambda e, xv2=xv2, h=h: e.tensor_scalar_mul(
                    xv2[:], xtk[:, h * 64:(h + 1) * 64], dtT[:, h:h + 1]),
                    reads=("xtk", "dtT", xtag2), writes=(xtag2,))
                em.step("vector", lambda e, bv=bv, g=g, h=h: e.tensor_scalar_mul(
                    bv[:], bgT[:, g * 128:(g + 1) * 128], dbT[:, h:h + 1]),
                    reads=("bgT", "dbT", btag), writes=(btag,))
                if c > 0:
                    em.step("vector", lambda e, sv=sv, h=h: lowprec(e, lambda: e.tensor_copy(
                        sv[:], s_state[:, h * 64:(h + 1) * 64])),
                        reads=("s_state", stag2), writes=(stag2,))
                B_ = (h % 2) * 64
                em.step("tensor", lambda e, pM=pM, xv2=xv2, mv=mv, B_=B_: e.matmul(
                    pM[B_:B_ + 64, 0:128], xv2[:], mv[:], start=True, stop=True),
                    reads=(xtag2, mttag, mtag), writes=(mtag,))
                if c > 0:
                    em.step("tensor", lambda e, pM=pM, sv=sv, g=g, B_=B_: e.matmul(
                        pM[B_:B_ + 64, 128:256], sv[:], ccm[:, g * 128:(g + 1) * 128],
                        start=True, stop=True),
                        reads=(stag2, "ccm", mtag), writes=(mtag,))
                em.step("tensor", lambda e, pM=pM, bv=bv, xv2=xv2: e.matmul(
                    pM[0:128, 256:320], bv[:], xv2[:], start=True, stop=True),
                    reads=(btag, xtag2, mtag), writes=(mtag,))
                o1v, o2v = o1b[pr], o2b[pr]
                o1tag, o2tag = f"o1b{pr}", f"o2b{pr}"
                if c > 0:
                    em.step("vector", lambda e, o1v=o1v, pM=pM, j=j, B_=B_: e.tensor_mul(
                        o1v[B_:B_ + 64, :], pM[B_:B_ + 64, 128:256],
                        sdB4[B_:B_ + 64, j * 128:(j + 1) * 128]),
                        reads=(mtag, "sdB4", o1tag), writes=(o1tag,))
                    em.step("vector", lambda e, o1v=o1v, o2v=o2v, pM=pM, B_=B_: e.tensor_add(
                        o2v[B_:B_ + 64, :], o1v[B_:B_ + 64, :], pM[B_:B_ + 64, 0:128]),
                        reads=(o1tag, mtag, o2tag), writes=(o2tag,))
                else:
                    em.step("vector", lambda e, o2v=o2v, pM=pM, B_=B_: e.tensor_copy(
                        o2v[B_:B_ + 64, :], pM[B_:B_ + 64, 0:128]),
                        reads=(mtag, o2tag), writes=(o2tag,))
                em.step("vector", lambda e, h=h, o2v=o2v, B_=B_: e.scalar_tensor_tensor(
                    ybuf[B_:B_ + 64, (h // 2) * 128:(h // 2) * 128 + 128],
                    xcm[B_:B_ + 64, (h // 2) * 128:(h // 2) * 128 + 128],
                    dball[B_:B_ + 64, h:h + 1], o2v[B_:B_ + 64, :], ALU.mult, ALU.add),
                    reads=("xcm", "dball", o2tag, "ybuf"), writes=("ybuf",))
                em.step("vector", lambda e, h=h, pM=pM: e.scalar_tensor_tensor(
                    s_state[:, h * 64:(h + 1) * 64], s_state[:, h * 64:(h + 1) * 64],
                    gbc[:, h:h + 1], pM[0:128, 256:320], ALU.mult, ALU.add),
                    reads=("s_state", "gbc", mtag), writes=("s_state",))
        em.step("scalar", lambda e: e.activation(y2b[:], ybuf[:], AF.Square),
                reads=("ybuf",), writes=("y2b",))
        for c2 in range(64):
            em.step("tensor", lambda e, c2=c2: e.matmul(
                b0[0:1, 0:128], onescb[:], y2b[:, c2 * 128:(c2 + 1) * 128],
                start=(c2 == 0), stop=(c2 == 63)),
                reads=("onescb", "y2b") + (("b0",) if c2 == 0 else ()), writes=("b0",))
        em.step("scalar", lambda e: e.activation(rr[:], b0[0:1, 0:128], AF.Sqrt,
                                                 scale=1.0 / INTER, bias=eps1[:]),
                reads=("b0", "eps1"), writes=("rr",))
        em.step("vector", lambda e: e.reciprocal(rr[:], rr[:]),
                reads=("rr",), writes=("rr",))
        em.step("tensor", lambda e: e.matmul(b0[:, 128:256], onesf[:], rr[:],
                                             start=True, stop=True),
                reads=("onesf", "rr", "b0"), writes=("b0",))
        em.step("vector", lambda e: e.tensor_copy(rstdB[:], b0[:, 128:256]),
                reads=("b0",), writes=("rstdB", "b0"))
        for c2 in range(64):
            em.step("vector", lambda e, c2=c2: e.tensor_mul(
                ytmp[:], ybuf[:, c2 * 128:(c2 + 1) * 128], rstdB[:]),
                reads=("ybuf", "rstdB", "ytmp"), writes=("ytmp",))
            em.step("vector", lambda e, c2=c2: e.tensor_mul(
                y2b[:, c2 * 128:(c2 + 1) * 128], ytmp[:], sgt[:, c2 * 128:(c2 + 1) * 128]),
                reads=("ytmp", "sgt"), writes=("y2b",))
        em.dma("sync", ygv[:, :, tsl], y2b[:].rearrange("p (a t) -> p a t", t=128),
               reads=("y2b",))

    al.finish()


def ssd_inputs(core, cvx, cvB, cvC, dtv, sg):
    d = const_inputs(SSD_CONSTS)
    d.update({
        "cvx": cvx, "cvB": cvB, "cvC": cvC, "dtv": dtv, "sg": sg,
        "alog": core["L1_attn_A_log"], "Din": core["L1_attn_D"],
    })
    return d



NKO = INTER // 128
OPROJ_CONSTS = ()

def emit_OPROJ(em, io):
    nc = em.nc
    ygT = io["ygT"]
    WoT = io["WoT"]
    s1T_o = io["s1T"]

    al = Alloc(nc)
    sb, ps = al.sb, al.ps
    ygb = sb("s_ygb", [128, NKO * 1024], BF16)
    lbuf = [sb(f"s_lb{i}", [128, 32 * 256], BF16) for i in range(2)]
    ev = [sb(f"s_ev{i}", [128, 1024], F32) for i in range(2)]
    pO = [ps(f"p_O{i}", [128, 512]) for i in range(8)]
    ygvv = ygT.rearrange("(c p) t -> p c t", p=128)

    segs = []
    for th in range(2):
        for mg in range(16):
            for sg_ in range(2):
                segs.append((th, mg, sg_))

    for th in range(2):
        tsl = slice(th * 1024, (th + 1) * 1024)
        for j in range(8):
            em.dma("sync", ygb[:].rearrange("p (c t) -> p c t", t=1024)[:, j * 8:(j + 1) * 8, :],
                   ygvv[:, j * 8:(j + 1) * 8, tsl], reads=("ygb",), writes=("ygb",))
        for mg in range(16):
            for sg_ in range(2):
                si = (th * 32 + mg * 2 + sg_)
                lb = lbuf[si % 2]
                ltag = f"lb{si % 2}"
                for kc0 in range(sg_ * 32, sg_ * 32 + 32, 4):
                    em.dma("gpsimd",
                           lb[:, (kc0 % 32) * 256:((kc0 % 32) + 4) * 256].rearrange(
                               "p (f m) -> p f m", f=4),
                           WoT[kc0 * 128:(kc0 + 4) * 128,
                               mg * 256:(mg + 1) * 256].rearrange(
                               "(f p) m -> p f m", p=128),
                           reads=((ltag,) if kc0 % 32 == 0 else ()),
                           writes=(ltag,))
                for mi in range(2):
                    m = mg * 2 + mi
                    for kc in range(sg_ * 32, sg_ * 32 + 32):
                        for nb in range(2):
                            pi = (mg % 2) * 4 + mi * 2 + nb
                            em.step("tensor", lambda e, kc=kc, nb=nb, lb=lb, mi=mi, pi=pi: e.matmul(
                                pO[pi][:, :],
                                lb[:, (kc % 32) * 256 + mi * 128:(kc % 32) * 256 + mi * 128 + 128],
                                ygb[:, kc * 1024 + nb * 512: kc * 1024 + (nb + 1) * 512],
                                start=(kc == 0), stop=(kc == NKO - 1)),
                                reads=(ltag, "ygb") + ((f"pO{pi}",) if kc == 0 else ()),
                                writes=(f"pO{pi}",))
            for mi in range(2):
                m = mg * 2 + mi
                evb = ev[m % 2]
                evt = f"ev{m % 2}"
                for nb in range(2):
                    pi = (mg % 2) * 4 + mi * 2 + nb
                    if nb == 0:
                        em.step("vector", lambda e, pi=pi, nb=nb, evb=evb: e.tensor_copy(
                            evb[:, nb * 512:(nb + 1) * 512], pO[pi][:, :]),
                            reads=(f"pO{pi}", evt), writes=(evt, f"pO{pi}"))
                    else:
                        em.step("scalar", lambda e, pi=pi, nb=nb, evb=evb: e.copy(
                            evb[:, nb * 512:(nb + 1) * 512], pO[pi][:, :]),
                            reads=(f"pO{pi}", evt), writes=(evt, f"pO{pi}"))
                em.dma("gpsimd", s1T_o[m * 128:(m + 1) * 128, tsl], evb[:],
                       reads=(evt,))

    al.finish()


def oproj_inputs(core, ygT):
    return {"ygT": ygT, "Wo": core["L1_attn_out_proj_w"]}



L1FFN_CONSTS = ("eyef", "e64")

def emit_L1FFN(em, io):
    nc = em.nc
    C = io
    xn1T = io["xn1T"]
    A1T = io["A1T"]
    B1T = io["B1T"]
    Wr1T = io["Wr1T"]
    g1T_o = io["g1T"]

    al = Alloc(nc)
    sb, ps = al.sb, al.ps
    eyef = sb("s_eyef", [128, 128], F32)
    e64 = sb("s_e64", [64, 1024], BF16)
    a1T = sb("s_a1T", [128, ND * KR], BF16)
    b1T = sb("s_b1T", [128, 8 * D], BF16)
    wr1T = sb("s_wr1T", [128, ND * 64], BF16)
    hgT = sb("s_hgT", [128, 8 * T], BF16)
    hsb = sb("s_hsb", [128, 512], F32)
    otile = sb("s_ot", [128, 512], BF16)
    xstream = [sb(f"s_xs{i}", [128, 512], BF16) for i in range(2)]
    sc_f = sb("s_scf", [64, T], F32)
    sc_tm = sb("s_sctm", [128, 64], F32)
    mx16 = sb("s_mx16", [128, 16], F32)
    mrep = sb("s_mrep", [128, 64], F32)
    thcol = sb("s_thcol", [128, 1], F32)
    t1col = sb("s_t1col", [128, 1], F32)
    maskb = sb("s_maskb", [128, 64], F32)
    gtm = sb("s_gtm", [128, 64], F32)
    gT = sb("s_gT", [64, T], BF16)
    pH = [ps(f"p_H{i}", [128, 512]) for i in range(8)]

    em.dma("sync", eyef[:], C["eyef"][:], writes=("eyef",))
    em.dma("sync", e64[:], C["e64"][:], writes=("e64",))
    a1dst = a1T[:].rearrange("p (c k) -> p c k", k=KR)
    a1src = A1T.rearrange("(c p) k -> p c k", p=128)
    for j in range(4):
        em.dma("gpsimd", a1dst[:, j * 8:(j + 1) * 8, :], a1src[:, j * 8:(j + 1) * 8, :],
               writes=("a1T",))
    wrdst = wr1T[:].rearrange("p (c e) -> p c e", e=64)
    wrsrc = Wr1T.rearrange("(c p) e -> p c e", p=128)
    for j in range(2):
        em.dma("gpsimd", wrdst[:, j * 16:(j + 1) * 16, :], wrsrc[:, j * 16:(j + 1) * 16, :],
               writes=("wr1T",))
    b1dst = b1T[:].rearrange("p (c d) -> p c d", d=D)
    b1src = B1T.rearrange("(c p) d -> p c d", p=128)
    for j in range(2):
        em.dma("gpsimd", b1dst[:, j * 4:(j + 1) * 4, :], b1src[:, j * 4:(j + 1) * 4, :],
               writes=("b1T",))

    xv = xn1T.rearrange("(c p) t -> p c t", p=128)
    for kc in range(ND):
        xs = xstream[kc % 2]
        xst = f"xs{kc % 2}"
        for nb in range(4):
            em.dma("scalar", xs[:], xv[:, kc, nb * 512:(nb + 1) * 512],
                   reads=(xst,), writes=(xst,))
            em.step("tensor", lambda e, kc=kc, nb=nb, xs=xs: e.matmul(
                pH[nb][0:64, :], wr1T[:, kc * 64:(kc + 1) * 64], xs[:],
                start=(kc == 0), stop=(kc == ND - 1)),
                reads=("wr1T", xst) + ((f"pH{nb}",) if kc == 0 else ()),
                writes=(f"pH{nb}",))
    for nb in range(4):
        em.step("vector", lambda e, nb=nb: e.tensor_copy(
            sc_f[0:64, nb * 512:(nb + 1) * 512], pH[nb][0:64, :]),
            reads=(f"pH{nb}", "sc_f"), writes=("sc_f", f"pH{nb}"))
    for i in range(NT):
        nb = i // 4
        off = (i % 4) * 128
        em.step("tensor", lambda e, nb=nb, off=off: e.transpose(
            pH[4][0:128, 0:64], sc_f[0:64, nb * 512 + off: nb * 512 + off + 128],
            eyef[0:64, 0:64]),
            reads=("sc_f", "eyef", "pH4"), writes=("pH4",))
        em.step("vector", lambda e: e.tensor_copy(sc_tm[:], pH[4][0:128, 0:64]),
                reads=("pH4",), writes=("sc_tm", "pH4"))
        em.step("vector", lambda e: e.max(out=mx16[:, 0:8], in_=sc_tm[:]),
                reads=("sc_tm",), writes=("mx16",))
        em.step("vector", lambda e: e.match_replace(
            out=mrep[:], in_to_replace=mx16[:, 0:8], in_values=sc_tm[:], imm_value=-1e30),
            reads=("sc_tm", "mx16"), writes=("mrep",))
        em.step("vector", lambda e: e.max(out=mx16[:, 8:16], in_=mrep[:]),
                reads=("mrep",), writes=("mx16",))
        em.step("vector", lambda e: e.tensor_reduce(thcol[:], mx16[:], AX.X, ALU.min),
                reads=("mx16",), writes=("thcol",))
        em.step("vector", lambda e: e.tensor_scalar(
            t1col[:], mx16[:, 0:1], -1.0, None, ALU.mult),
            reads=("mx16",), writes=("t1col",))
        em.step("vector", lambda e: e.tensor_scalar(
            maskb[:], sc_tm[:], thcol[:], None, ALU.is_ge),
            reads=("sc_tm", "thcol"), writes=("maskb",))
        em.step("scalar", lambda e: e.activation(
            gtm[:], sc_tm[:], AF.Exp, bias=t1col[:]),
            reads=("sc_tm", "t1col"), writes=("gtm",))
        em.step("vector", lambda e: e.tensor_mul(gtm[:], gtm[:], maskb[:]),
                reads=("gtm", "maskb"), writes=("gtm",))
        em.step("vector", lambda e: e.reduce_sum(thcol[:], gtm[:], axis=AX.X),
                reads=("gtm",), writes=("thcol",))
        em.step("vector", lambda e: e.reciprocal(thcol[:], thcol[:]),
                reads=("thcol",), writes=("thcol",))
        em.step("vector", lambda e: e.tensor_scalar_mul(gtm[:], gtm[:], thcol[:]),
                reads=("gtm", "thcol"), writes=("gtm",))
        em.step("tensor", lambda e: e.transpose(
            pH[5][0:64, 0:128], gtm[:], eyef[:]),
            reads=("gtm", "eyef", "pH5"), writes=("pH5",))
        em.step("vector", lambda e, i=i: lowprec(e, lambda: e.tensor_copy(
            gT[:, i * 128:(i + 1) * 128], pH[5][0:64, 0:128])),
            reads=("pH5",), writes=("gT", "pH5"))
    for nb in range(4):
        for kc in range(ND):
            xs = xstream[kc % 2]
            xst = f"xs{kc % 2}"
            em.dma("scalar", xs[:], xv[:, kc, nb * 512:(nb + 1) * 512],
                   reads=(xst,), writes=(xst,))
            for m in range(8):
                em.step("tensor", lambda e, kc=kc, m=m, xs=xs: e.matmul(
                    pH[m][:, :], a1T[:, kc * KR + m * 128: kc * KR + (m + 1) * 128],
                    xs[:], start=(kc == 0), stop=(kc == ND - 1)),
                    reads=("a1T", xst) + ((f"pH{m}",) if kc == 0 else ()),
                    writes=(f"pH{m}",))
        for m in range(8):
            em.step("vector", lambda e, m=m: e.tensor_copy(hsb[:], pH[m][:, :]),
                    reads=(f"pH{m}", "hsb"), writes=("hsb", f"pH{m}"))
            em.step("tensor", lambda e, m=m, nb=nb: e.matmul(
                pH[m][:, :], e64[:, m * 128:(m + 1) * 128], gT[:, nb * 512:(nb + 1) * 512],
                start=True, stop=True),
                reads=("e64", "gT", f"pH{m}"), writes=(f"pH{m}",))
            em.step("vector", lambda e, m=m, nb=nb: e.tensor_mul(
                hgT[:, m * T + nb * 512: m * T + (nb + 1) * 512], hsb[:], pH[m][:, :]),
                reads=("hsb", f"pH{m}"), writes=("hgT",))
    for nb in range(4):
        for m2 in range(ND):
            pi = 6 + (m2 % 2)
            for kc2 in range(8):
                em.step("tensor", lambda e, m2=m2, kc2=kc2, nb=nb, pi=pi: e.matmul(
                    pH[pi][:, :], b1T[:, kc2 * D + m2 * 128: kc2 * D + (m2 + 1) * 128],
                    hgT[:, kc2 * T + nb * 512: kc2 * T + (nb + 1) * 512],
                    start=(kc2 == 0), stop=(kc2 == 7)),
                    reads=("b1T", "hgT") + ((f"pH{pi}",) if kc2 == 0 else ()),
                    writes=(f"pH{pi}",))
            em.step("scalar", lambda e, pi=pi: e.activation(
                otile[:], pH[pi][:, :], AF.Silu, scale=2.0),
                reads=(f"pH{pi}", "otile"), writes=("otile", f"pH{pi}"))
            em.dma("gpsimd", g1T_o[m2 * 128:(m2 + 1) * 128, nb * 512:(nb + 1) * 512],
                   otile[:], reads=("otile",))

    al.finish()


def l1ffn_inputs(core, xn1T):
    d = const_inputs(L1FFN_CONSTS)
    d.update({"xn1T": xn1T, "A1": core["L1_ffn_A"], "B1": core["L1_ffn_B"],
              "Wr1": core["L1_ffn_Wr"]})
    return d



COMB_CONSTS = ("eyef",)

def emit_COMB(em, io):
    nc = em.nc
    C = io
    x1 = io["x1"]
    s1T = io["s1T"]
    g1T = io["g1T"]
    out_o = io["out"]

    al = Alloc(nc)
    sb, ps = al.sb, al.ps
    eyef = sb("s_eyef", [128, 128], F32)
    x1t = [sb(f"s_x1t{i}", [128, D], F32) for i in range(2)]
    sTt = [sb(f"s_sTt{i}", [128, ND * 128], F32) for i in range(2)]
    gTt = [sb(f"s_gTt{i}", [128, ND * 128], BF16) for i in range(2)]
    dT = sb("s_dT", [128, 128], F32)
    pT = [ps(f"p_T{i}", [128, 512]) for i in range(2)]

    em.dma("sync", eyef[:], C["eyef"][:], writes=("eyef",))
    sv = s1T.rearrange("(c p) t -> p c t", p=128)
    gv = g1T.rearrange("(c p) t -> p c t", p=128)
    for i in range(NT):
        xb, xtag = x1t[i % 2], f"x1t{i % 2}"
        st, stag = sTt[i % 2], f"sTt{i % 2}"
        gt, gtag = gTt[i % 2], f"gTt{i % 2}"
        em.dma("sync", xb[:], x1[i * 128:(i + 1) * 128, :], reads=(xtag,), writes=(xtag,))
        em.dma("gpsimd", st[:].rearrange("p (c t) -> p c t", t=128),
               sv[:, :, i * 128:(i + 1) * 128], reads=(stag,), writes=(stag,))
        em.dma("scalar", gt[:].rearrange("p (c t) -> p c t", t=128),
               gv[:, :, i * 128:(i + 1) * 128], reads=(gtag,), writes=(gtag,))
        for dc in range(ND):
            pb, ptag = (pT[0], "pT0") if dc % 2 == 0 else (pT[1], "pT1")
            em.step("vector", lambda e, dc=dc, st=st, gt=gt: e.tensor_mul(
                dT[:], st[:, dc * 128:(dc + 1) * 128], gt[:, dc * 128:(dc + 1) * 128]),
                reads=(stag, gtag, "dT"), writes=("dT",))
            em.step("tensor", lambda e, pb=pb: e.transpose(pb[:, 0:128], dT[:], eyef[:]),
                    reads=("dT", "eyef", ptag), writes=(ptag,))
            em.step("vector", lambda e, dc=dc, pb=pb, xb=xb: e.tensor_add(
                xb[:, dc * 128:(dc + 1) * 128], xb[:, dc * 128:(dc + 1) * 128], pb[:, 0:128]),
                reads=(ptag, xtag), writes=(xtag, ptag))
        em.dma("sync", out_o[i * 128:(i + 1) * 128, :], xb[:], reads=(xtag,))

    al.finish()


def comb_inputs(x1, s1T, g1T):
    d = const_inputs(COMB_CONSTS)
    d.update({"x1": x1, "s1T": s1T, "g1T": g1T})
    return d



_PROGS = {}

def _prog(name, builder):
    if name not in _PROGS:
        nc = bass.Bass("TRN2", target_bir_lowering=False)
        builder(nc, strict=True)
        _PROGS[name] = nc
    return _PROGS[name]


def _run(name, builder, in_maps):
    nc = _prog(name, builder)
    res = run_bass_kernel_spmd(nc, in_maps, core_ids=list(range(NCORES)))
    return res.results


def kernel(**inputs):
    inputs = {k: np.ascontiguousarray(v) for k, v in inputs.items()}
    x_full = inputs["L0_attn_x"]

    maps = []
    for b in range(NCORES):
        core = dict(inputs)
        core["x0"] = np.ascontiguousarray(x_full[b])
        maps.append(l0_inputs(core))
    r = _run("L0", build_L0, maps)
    x1 = [r[b]["x1"] for b in range(NCORES)]
    xn1T = [r[b]["xn1T"] for b in range(NCORES)]

    maps = [inproj_inputs(inputs, xn1T[b]) for b in range(NCORES)]
    r = _run("INPROJ", build_INPROJ, maps)
    sg = [r[b]["sg"] for b in range(NCORES)]
    cvx = [r[b]["cvx"] for b in range(NCORES)]
    cvB = [r[b]["cvB"] for b in range(NCORES)]
    cvC = [r[b]["cvC"] for b in range(NCORES)]
    dtv = [r[b]["dtv"] for b in range(NCORES)]

    maps = [ssd_inputs(inputs, cvx[b], cvB[b], cvC[b], dtv[b], sg[b])
            for b in range(NCORES)]
    r = _run("SSD", build_SSD, maps)
    ygT = [r[b]["ygT"] for b in range(NCORES)]

    maps = [oproj_inputs(inputs, ygT[b]) for b in range(NCORES)]
    r = _run("OPROJ", build_OPROJ, maps)
    s1T = [r[b]["s1T"] for b in range(NCORES)]

    maps = [l1ffn_inputs(inputs, xn1T[b]) for b in range(NCORES)]
    r = _run("L1FFN", build_L1FFN, maps)
    g1T = [r[b]["g1T"] for b in range(NCORES)]

    maps = [comb_inputs(x1[b], s1T[b], g1T[b]) for b in range(NCORES)]
    r = _run("COMB", build_COMB, maps)
    out = np.stack([r[b]["out"] for b in range(NCORES)], axis=0)
    return out



PHASE_SPECS = {
    "L0": dict(
        consts=L0_CONSTS,
        ins={"x0": ([T, D], F32), "Wq": ([D, 128, 32], F32), "bq": ([128, 32], F32),
             "Wk": ([D, 128, 1], F32), "bk": ([128, 1], F32), "Wv": ([D, 128, 1], F32),
             "bv": ([128, 1], F32), "A0T": ([D, 128], F32), "B0T": ([128, D], F32),
             "Wr0": ([8, D], F32), "br0": ([8], F32)},
        outs={"x1": ([T, D], F32), "xn1T": ([D, T], BF16)},
        internal={}),
    "INPROJ": dict(
        consts=INPROJ_CONSTS,
        ins={"xn1T": ([D, T], BF16), "WpT": ([D, CPROJ], F32),
             "convw": ([CONV_DIM, 1, 4], F32), "convb": ([CONV_DIM], F32),
             "dtb": ([H], F32), "nw": ([INTER], F32)},
        outs={"sg": ([INTER, T], BF16), "cvx": ([INTER, T], BF16),
              "cvB": ([GN, T], BF16), "cvC": ([GN, T], BF16), "dtv": ([H, T], F32)},
        internal={}),
    "SSD": dict(
        consts=SSD_CONSTS,
        ins={"cvx": ([INTER, T], BF16), "cvB": ([GN, T], BF16), "cvC": ([GN, T], BF16),
             "dtv": ([H, T], F32), "sg": ([INTER, T], BF16),
             "alog": ([H], F32), "Din": ([H], F32)},
        outs={"ygT": ([INTER, T], BF16)},
        internal={"rowscr": ([128, 128], F32)}),
    "OPROJ": dict(
        consts=OPROJ_CONSTS,
        ins={"ygT": ([INTER, T], BF16), "WoT": ([INTER, D], F32)},
        outs={"s1T": ([D, T], F32)},
        internal={}),
    "L1FFN": dict(
        consts=L1FFN_CONSTS,
        ins={"xn1T": ([D, T], BF16), "A1T": ([D, KR], F32),
             "B1T": ([KR, D], F32), "Wr1T": ([D, NEXP], F32)},
        outs={"g1T": ([D, T], BF16)},
        internal={}),
    "COMB": dict(
        consts=COMB_CONSTS,
        ins={"x1": ([T, D], F32), "s1T": ([D, T], F32), "g1T": ([D, T], BF16)},
        outs={"out": ([T, D], F32)},
        internal={}),
}

EMITTERS = {"L0": emit_L0, "INPROJ": emit_INPROJ,
            "SSD": emit_SSD, "OPROJ": emit_OPROJ, "L1FFN": emit_L1FFN,
            "COMB": emit_COMB}


def full_barrier(em, targets=None):
    if targets is None:
        targets = {ch: c for ch, c in em.count.items() if c > 0}
    else:
        targets = {ch: c for ch, c in targets.items() if c > 0}
    for e in ENGINES:
        kn = em.know[e]
        eff = {}
        for ch2, c in targets.items():
            if ch2 == (e, "c"):
                continue
            if kn[ch2] < c:
                eff[ch2] = c
        for ch2, c in eff.items():
            s = em.snap[ch2][c]
            for g in CHANNELS:
                kn[g] = max(kn[g], s[g])
            kn[ch2] = max(kn[ch2], c)
        ch = (e, "c")
        em.count[ch] += 1
        cnt = em.count[ch]
        em.steps[e].append(((lambda e2: e2.nop()), eff, 1, ch))
        snap = dict(kn)
        snap[ch] = cnt
        em.snap[ch][cnt] = snap


def build_phase(nc, phase, strict=True):
    em = Emitter(nc, strict)
    spec = PHASE_SPECS[phase]
    io = declare_consts(nc, spec["consts"])
    for k, (shape, dt) in spec["ins"].items():
        io[k] = nc.dram_tensor(k, shape, dt, kind="ExternalInput").ap()
    for k, (shape, dt) in spec["outs"].items():
        io[k] = nc.dram_tensor(k, shape, dt, kind="ExternalOutput").ap()
    for k, (shape, dt) in spec["internal"].items():
        io[k] = nc.dram_tensor(k, shape, dt, kind="Internal").ap()
    EMITTERS[phase](em, io)
    em.barrier_step("sync")
    em.realize()
    em.finish()
    return em


PHASE_ORDER = ["L0", "INPROJ", "SSD", "OPROJ", "L1FFN", "COMB"]


def build_FUSED(nc, strict=True):
    em = Emitter(nc, strict)
    io = declare_consts(nc, tuple(CONST_SPECS.keys()))
    declared = {}
    produced = set()
    for ph in PHASE_ORDER:
        for k in PHASE_SPECS[ph]["outs"]:
            produced.add(k)
    produced.discard("out")
    for ph in PHASE_ORDER:
        spec = PHASE_SPECS[ph]
        for k, (shape, dt) in spec["ins"].items():
            if k not in declared:
                kind = "Internal" if k in produced else "ExternalInput"
                declared[k] = nc.dram_tensor(k, shape, dt, kind=kind).ap()
        for k, (shape, dt) in spec["outs"].items():
            if k not in declared:
                kind = "ExternalOutput" if k == "out" else "Internal"
                declared[k] = nc.dram_tensor(k, shape, dt, kind=kind).ap()
        for k, (shape, dt) in spec["internal"].items():
            if k not in declared:
                declared[k] = nc.dram_tensor(k, shape, dt, kind="Internal").ap()
    io.update(declared)
    for ph in PHASE_ORDER:
        EMITTERS[ph](em, io)
        full_barrier(em)
    em.barrier_step("sync")
    em.realize()
    em.finish()
    return em


def kernel(**inputs):
    inputs = {k: np.ascontiguousarray(v) for k, v in inputs.items()}
    x_full = inputs["L0_attn_x"]
    if "FUSED" not in _PROGS:
        nc = bass.Bass("TRN2", target_bir_lowering=False)
        build_FUSED(nc, strict=True)
        _PROGS["FUSED"] = nc
    nc = _PROGS["FUSED"]
    base = const_inputs(tuple(CONST_SPECS.keys()))
    ct = np.ascontiguousarray
    base.update({
        "Wq": inputs["L0_attn_Wq"], "bq": inputs["L0_attn_bq"],
        "Wk": inputs["L0_attn_Wk"], "bk": inputs["L0_attn_bk"],
        "Wv": inputs["L0_attn_Wv"], "bv": inputs["L0_attn_bv"],
        "A0T": ct(inputs["L0_ffn_A"].reshape(128, D).T),
        "B0T": ct(inputs["L0_ffn_B"].transpose(0, 2, 1).reshape(128, D)),
        "Wr0": inputs["L0_ffn_Wr"], "br0": inputs["L0_ffn_br"],
        "WpT": ct(inputs["L1_attn_in_proj_w"].T),
        "convw": inputs["L1_attn_conv_w"], "convb": inputs["L1_attn_conv_b"],
        "dtb": inputs["L1_attn_dt_bias"], "nw": inputs["L1_attn_norm_w"],
        "alog": inputs["L1_attn_A_log"], "Din": inputs["L1_attn_D"],
        "WoT": ct(inputs["L1_attn_out_proj_w"].T),
        "A1T": ct(inputs["L1_ffn_A"].reshape(KR, D).T),
        "B1T": ct(inputs["L1_ffn_B"].transpose(0, 2, 1).reshape(KR, D)),
        "Wr1T": ct(inputs["L1_ffn_Wr"].T),
    })
    in_maps = []
    for b in range(NCORES):
        m = dict(base)
        m["x0"] = np.ascontiguousarray(x_full[b])
        in_maps.append(m)
    res = run_bass_kernel_spmd(nc, in_maps, core_ids=list(range(NCORES)))
    return np.stack([res.results[b]["out"] for b in range(NCORES)], axis=0)


# version 09
# speedup vs baseline: 1.3533x; 1.1105x over previous
import numpy as np
import ml_dtypes
import concourse.bass as bass
import concourse.mybir as mybir
from concourse.bass_utils import run_bass_kernel_spmd

F32 = mybir.dt.float32
BF16 = mybir.dt.bfloat16
AF = mybir.ActivationFunctionType
ALU = mybir.AluOpType
AX = mybir.AxisListType
bf16 = ml_dtypes.bfloat16

NCORES = 8
T = 2048
D = 4096
NT = T // 128
ND = D // 128

H = 128; P = 64; G = 8; N = 128; CS = 128; K4 = 4
INTER = H * P
GN = G * N
CONV_DIM = INTER + 2 * GN
CPROJ = INTER + CONV_DIM + H
NCHUNK = T // CS
NEXP = 64; TOPK = 16; RANK = 16
KR = NEXP * RANK

INV_SQRT_QK = float(1.0 / np.sqrt(128.0))

ENGINES = ("tensor", "vector", "scalar", "gpsimd", "sync")
NDMACH = 4
CHANNELS = ([(e, "c") for e in ENGINES]
            + [(e, f"d{i}") for e in ("sync", "gpsimd", "scalar", "vector")
               for i in range(NDMACH)])


class Emitter:

    def __init__(self, nc, strict=False):
        self.nc = nc
        self.strict = strict
        self.steps = {e: [] for e in ENGINES}
        self.count = {ch: 0 for ch in CHANNELS}
        self.know = {e: {ch: 0 for ch in CHANNELS} for e in ENGINES}
        self.snap = {ch: {0: {c2: 0 for c2 in CHANNELS}} for ch in CHANNELS}
        self.last_writer = {}
        self.readers = {}
        self.dma_idx = {e: 0 for e in ENGINES}
        self.n_waits = 0
        self.audit = []

    def step(self, eng, fn, reads=(), writes=(), dma=False):
        if dma:
            sub = self.dma_idx[eng] % NDMACH
            self.dma_idx[eng] += 1
            ch = (eng, f"d{sub}")
        else:
            ch = (eng, "c")
        waits = {}
        def add_need(src):
            (sch, c) = src
            if sch == ch and not (self.strict and sch[1] == "c" and eng != "tensor"):
                return
            if sch[0] == eng and sch[1] == "c" and not dma and (
                    not self.strict or eng == "tensor"):
                return
            waits[sch] = max(waits.get(sch, 0), c)
        for t_ in reads:
            lw = self.last_writer.get(t_)
            if lw is not None:
                add_need(lw)
        for t_ in writes:
            lw = self.last_writer.get(t_)
            if lw is not None:
                add_need(lw)
            for r in self.readers.get(t_, ()):
                add_need(r)
        if dma and self.count[ch] > 0:
            waits[ch] = max(waits.get(ch, 0), self.count[ch])
        kn = self.know[eng]
        eff = {ch2: c for ch2, c in waits.items() if kn[ch2] < c}
        for ch2, c in eff.items():
            s = self.snap[ch2][c]
            for g in CHANNELS:
                kn[g] = max(kn[g], s[g])
            kn[ch2] = max(kn[ch2], c)
        amount = 16 if dma else 1
        self.count[ch] += amount
        cnt = self.count[ch]
        self.steps[eng].append((fn, eff, amount, ch))
        self.n_waits += len(eff)
        for t_ in writes:
            self.last_writer[t_] = (ch, cnt)
            self.readers[t_] = []
        for t_ in reads:
            self.readers.setdefault(t_, []).append((ch, cnt))
        snap = dict(kn)
        snap[ch] = cnt
        self.snap[ch][cnt] = snap
        self.audit.append((eng, len(self.steps[eng]) - 1, tuple(reads),
                           tuple(writes), ch, cnt, dma))
        return cnt

    def dma(self, queue, out, in_, reads=(), writes=(), **kw):
        self.step(queue, lambda e: e.dma_start(out=out, in_=in_, **kw),
                  reads=reads, writes=writes, dma=True)

    def barrier_step(self, eng="sync"):
        targets = {ch: c for ch, c in self.count.items() if c > 0}
        kn = self.know[eng]
        eff = {ch: c for ch, c in targets.items()
               if kn[ch] < c and not (ch[0] == eng and ch[1] == "c")}
        for ch2, c in eff.items():
            s = self.snap[ch2][c]
            for g in CHANNELS:
                kn[g] = max(kn[g], s[g])
            kn[ch2] = max(kn[ch2], c)
        ch = (eng, "c")
        self.count[ch] += 1
        cnt = self.count[ch]
        self.steps[eng].append((lambda e: e.nop(), eff, 1, ch))
        snap = dict(kn)
        snap[ch] = cnt
        self.snap[ch][cnt] = snap

    def realize(self):
        nc = self.nc
        self._cms = []
        sems = {}
        for ch in CHANNELS:
            cm = nc.semaphore(f"sem_{ch[0]}_{ch[1]}")
            sems[ch] = cm.__enter__()
            self._cms.append(cm)
        block_cm = nc.Block()
        block = block_cm.__enter__()
        self._cms.append(block_cm)

        def emit_for(e):
            def body(eng):
                for (fn, waits, amount, ch) in self.steps[e]:
                    for ch2, c in waits.items():
                        eng.wait_ge(sems[ch2], c)
                    inst = fn(eng)
                    inst.then_inc(sems[ch], amount)
            return body

        for e in ENGINES:
            if self.steps[e]:
                getattr(block, e)(emit_for(e))
        return block

    def finish(self):
        for cm in reversed(self._cms):
            cm.__exit__(None, None, None)



_CONSTS = None

def build_consts():
    global _CONSTS
    if _CONSTS is not None:
        return _CONSTS
    c = {}
    c["eyeb"] = np.eye(128, dtype=bf16)
    c["eyef"] = np.eye(128, dtype=np.float32)
    c["onesf"] = np.ones((1, 128), dtype=np.float32)
    sel = np.zeros((128, 32 * 128), dtype=bf16)
    for q in range(32):
        for f in range(128):
            sel[4 * q + f // 32, q * 128 + f] = 1
    c["selq"] = sel
    e8 = np.zeros((8, 128), dtype=bf16)
    for j in range(128):
        e8[j // 16, j] = 1
    c["e8"] = e8
    e64 = np.zeros((64, 1024), dtype=bf16)
    for j in range(1024):
        e64[j // 16, j] = 1
    c["e64"] = e64
    c["cumU"] = np.triu(np.ones((128, 128), dtype=np.float32))
    c["mask01"] = np.triu(np.ones((128, 128), dtype=np.float32))
    _CONSTS = {k: np.ascontiguousarray(v) for k, v in c.items()}
    return _CONSTS


CONST_SPECS = {
    "eyeb": BF16, "eyef": F32, "onesf": F32, "selq": BF16,
    "e8": BF16, "e64": BF16, "cumU": F32, "mask01": F32,
}


def lowprec(e, f):
    with e.bass.allow_low_precision(reason="bf16 weight fold / operand cast"):
        return f()


def declare_consts(nc, names):
    cvals = build_consts()
    return {k: nc.dram_tensor("c_" + k, list(cvals[k].shape), CONST_SPECS[k],
                              kind="ExternalInput").ap() for k in names}


def const_inputs(names):
    cvals = build_consts()
    return {"c_" + k: cvals[k] for k in names}


class Alloc:
    _n = 0

    def __init__(self, nc):
        self.nc = nc
        self.cms = []
        Alloc._n += 1
        self.prefix = f"a{Alloc._n}_"

    def sb(self, name, shape, dt):
        cm = self.nc.sbuf_tensor(self.prefix + name, shape, dt)
        t = cm.__enter__(); self.cms.append(cm); return t

    def ps(self, name, shape, dt=F32):
        cm = self.nc.psum_tensor(self.prefix + name, shape, dt)
        t = cm.__enter__(); self.cms.append(cm); return t

    def finish(self):
        for cm in reversed(self.cms):
            cm.__exit__(None, None, None)



L0_CONSTS = ("eyeb", "eyef", "selq", "e8")

def emit_L0(em, io):
    nc = em.nc
    C = io
    x0 = io["x0"]
    Wq = io["Wq"]
    bq = io["bq"]
    Wk = io["Wk"]
    bk = io["bk"]
    Wv = io["Wv"]
    bv = io["bv"]
    A0T = io["A0T"]
    B0T = io["B0T"]
    Wr0 = io["Wr0"]
    br0 = io["br0"]
    x1_o = io["x1"]
    xn1T_o = io["xn1T"]

    al = Alloc(nc)
    sb, ps = al.sb, al.ps

    eyeb = sb("s_eyeb", [128, 128], BF16)
    eyef = sb("s_eyef", [128, 128], F32)
    selq = sb("s_selq", [128, 32 * 128], BF16)
    e8 = sb("s_e8", [8, 128], BF16)
    wqs = sb("s_wqs", [128, ND * 128], BF16)
    wkT = sb("s_wkT", [128, ND * 128], BF16)
    wvT = sb("s_wvT", [128, ND * 128], BF16)
    a0T = sb("s_a0T", [128, ND * 128], BF16)
    b0T = sb("s_b0T", [128, ND * 128], BF16)
    wr0T = sb("s_wr0T", [128, ND * 8], BF16)
    bqs = sb("s_bqs", [128, 1], F32)
    bkc = sb("s_bkc", [128, 1], F32)
    bvc = sb("s_bvc", [128, 1], F32)
    br0c = sb("s_br0c", [8, 1], F32)
    bq_stage = sb("s_bqstage", [128, 32], F32)
    epscol = sb("s_epscol", [128, 1], F32)

    mat = sb("s_mat", [128, 128 * 128], BF16)
    wq_stage = [mat[:, 0:4096], mat[:, 4096:8192]]
    wr0_stage = mat[0:8, 12288:16384]
    PREP_TAGS = ("mq0", "mq1", "mw")

    x0t = [sb(f"s_x0t{i}", [128, D], F32) for i in range(2)]
    stat = sb("s_stat", [128, 8], F32)
    g0T = sb("s_g0T", [128, ND * 128], F32)
    d0T = sb("s_d0T", [128, ND * 128], F32)
    xnb = sb("s_xnb", [128, D], BF16)
    xnT = sb("s_xnT", [128, ND * 128], BF16)
    a_T = sb("s_aT", [128, 128], F32)
    k_T = sb("s_kT", [128, 128], F32)
    v_T = sb("s_vT", [128, 128], F32)
    a_tm = sb("s_atm", [128, 128], F32)
    k_tm = sb("s_ktm", [128, 128], F32)
    v_tm = sb("s_vtm", [128, 128], F32)
    kmm = sb("s_kmm", [128, 4], F32)
    negm = sb("s_negm", [128, 128], F32)
    s1col = sb("s_s1", [128, 128], F32)
    s2col = sb("s_s2", [128, 128], F32)
    ttg = [sb(f"s_ttg{i}", [128, 128], F32) for i in range(4)]
    o_tm = sb("s_otm", [128, 128], F32)
    o_Tb = sb("s_oTb", [128, 128], BF16)
    rt_f = sb("s_rtf", [8, 128], F32)
    rt_tm = sb("s_rttm", [128, 8], F32)
    rt_scr = sb("s_rtscr", [128, 8], F32)
    g8T = sb("s_g8T", [8, 128], BF16)
    scoreB = sb("s_scoreB", [128, 128], F32)
    glow = sb("s_glow", [128, 128], BF16)
    x1t = sb("s_x1t", [128, D], F32)
    xn1b = sb("s_xn1b", [128, D], BF16)
    xn1Tt = sb("s_xn1Tt", [128, ND * 128], BF16)

    pA = ps("p_A", [128, 512])
    pB = ps("p_B", [128, 512])
    pC = ps("p_C", [128, 512])
    pD = ps("p_D", [128, 512])
    pT1 = ps("p_T1", [128, 512])
    pT2 = ps("p_T2", [128, 512])
    pb1 = ps("p_b1", [128, 1024], BF16)
    pb2 = ps("p_b2", [128, 1024], BF16)

    em.step("gpsimd", lambda e: e.memset(epscol[:], 1e-5), writes=("epscol",))
    em.dma("sync", eyeb[:], C["eyeb"][:], writes=("eyeb",))
    em.dma("sync", eyef[:], C["eyef"][:], writes=("eyef",))
    em.dma("sync", selq[:], C["selq"][:], writes=("selq",))
    em.dma("sync", e8[:], C["e8"][:], writes=("e8",))
    em.dma("gpsimd", bq_stage[:], bq[:], writes=("bq_stage",))
    em.step("vector", lambda e: e.reduce_sum(bqs[:], bq_stage[:], axis=AX.X),
            reads=("bq_stage",), writes=("bqs",))
    em.dma("gpsimd", bkc[:], bk[:], writes=("bkc",))
    em.dma("gpsimd", bvc[:], bv[:], writes=("bvc",))
    em.dma("gpsimd", br0c[:], br0[:].rearrange("h -> h ()"), writes=("br0c",))

    for kc in range(ND):
        wst = wq_stage[kc % 2]
        wtg = f"mq{kc % 2}"
        em.dma("gpsimd", wst[:],
               Wq[kc * 128:(kc + 1) * 128, :, :].rearrange("p d h -> p (d h)"),
               reads=(wtg,), writes=(wtg,))
        em.step("vector", lambda e, kc=kc, wst=wst: lowprec(e, lambda: e.reduce_sum(
            wqs[:, kc * 128:(kc + 1) * 128].rearrange("p (d a) -> p d a", a=1),
            wst[:].rearrange("p (d h) -> p d h", h=32), axis=AX.X)),
            reads=(wtg,), writes=("wqs",))
        em.dma("gpsimd", wkT[:, kc * 128:(kc + 1) * 128],
               Wk[kc * 128:(kc + 1) * 128, :, 0], writes=("wkT",))
        em.dma("gpsimd", wvT[:, kc * 128:(kc + 1) * 128],
               Wv[kc * 128:(kc + 1) * 128, :, 0], writes=("wvT",))

    em.dma("gpsimd", a0T[:].rearrange("p (c h) -> p c h", h=128),
           A0T.rearrange("(c p) h -> p c h", p=128), writes=("a0T",))
    em.dma("gpsimd", b0T[:], B0T[:], writes=("b0T",))
    em.dma("gpsimd", wr0_stage[:], Wr0[:], writes=("mw",))
    for kc in range(ND):
        pt, ptag = (pb1, "pb1") if kc % 2 == 0 else (pb2, "pb2")
        em.step("tensor", lambda e, kc=kc, pt=pt: e.transpose(
            pt[0:128, 0:8], wr0_stage[:, kc * 128:(kc + 1) * 128], eyeb[0:8, 0:8]),
            reads=("mw", "eyeb", ptag), writes=(ptag,))
        em.step("vector", lambda e, kc=kc, pt=pt: e.tensor_copy(
            wr0T[:, kc * 8:(kc + 1) * 8], pt[0:128, 0:8]),
            reads=(ptag,), writes=("wr0T", ptag))

    def emit_layernorm(src, out_bf, tag_src, tag_out, dump, tag_dump):
        em.step("vector", lambda e: e.reduce_sum(stat[:, 0:1], src[:], axis=AX.X),
                reads=(tag_src,), writes=("stat",))
        em.step("scalar", lambda e: e.activation(
            dump[:], src[:], AF.Square, accum_out=stat[:, 1:2]),
            reads=(tag_src,), writes=("stat", tag_dump))
        em.step("vector", lambda e: e.tensor_scalar(
            stat[:, 2:3], stat[:, 0:1], 1.0 / D, None, ALU.mult),
            reads=("stat",), writes=("stat",))
        em.step("vector", lambda e: e.tensor_mul(stat[:, 5:6], stat[:, 2:3], stat[:, 2:3]),
                reads=("stat",), writes=("stat",))
        em.step("vector", lambda e: e.scalar_tensor_tensor(
            stat[:, 3:4], stat[:, 1:2], 1.0 / D, stat[:, 5:6], ALU.mult, ALU.subtract),
            reads=("stat",), writes=("stat",))
        em.step("scalar", lambda e: e.activation(
            stat[:, 6:7], stat[:, 3:4], AF.Sqrt, bias=epscol[:]),
            reads=("stat", "epscol"), writes=("stat",))
        em.step("vector", lambda e: e.reciprocal(stat[:, 4:5], stat[:, 6:7]),
                reads=("stat",), writes=("stat",))
        em.step("vector", lambda e: e.tensor_scalar(
            out_bf[:], src[:], stat[:, 2:3], stat[:, 4:5], ALU.subtract, ALU.mult),
            reads=(tag_src, "stat"), writes=(tag_out,))

    def emit_transpose_tm2fm(src_bf, dst, tag_src, tag_dst):
        for dc in range(ND):
            pt, ptag = (pb1, "pb1") if dc % 2 == 0 else (pb2, "pb2")
            em.step("tensor", lambda e, dc=dc, pt=pt: e.transpose(
                pt[:, 0:128], src_bf[:, dc * 128:(dc + 1) * 128], eyeb[:]),
                reads=(tag_src, "eyeb", ptag), writes=(ptag,))
            em.step("vector", lambda e, dc=dc, pt=pt: e.tensor_copy(
                dst[:, dc * 128:(dc + 1) * 128], pt[:, 0:128]),
                reads=(ptag,), writes=(tag_dst, ptag))

    for i in range(NT):
        xb = x0t[i % 2]
        xtag = f"x0t{i % 2}"
        em.dma("sync", xb[:], x0[i * 128:(i + 1) * 128, :], writes=(xtag,))
        emit_layernorm(xb, xnb, xtag, "xnb", g0T, "g0T")
        emit_transpose_tm2fm(xnb, xnT, "xnb", "xnT")

        def mm_acc(psum, pslice, lhs_buf, tag_lhs, lhs_w, m_width, ptag):
            for kc in range(ND):
                em.step("tensor", lambda e, kc=kc: e.matmul(
                    psum[0:m_width, pslice[0]:pslice[1]],
                    lhs_buf[:, kc * lhs_w: kc * lhs_w + m_width],
                    xnT[:, kc * 128:(kc + 1) * 128],
                    start=(kc == 0), stop=(kc == ND - 1)),
                    reads=(tag_lhs, "xnT") + ((ptag,) if kc == 0 else ()),
                    writes=(ptag,))
        mm_acc(pA, (0, 128), wqs, "wqs", 128, 128, "pA")
        mm_acc(pB, (0, 128), wkT, "wkT", 128, 128, "pB")
        mm_acc(pC, (0, 128), wvT, "wvT", 128, 128, "pC")
        em.step("vector", lambda e: e.tensor_scalar(
            a_T[:], pA[:, 0:128], bqs[:], INV_SQRT_QK, ALU.add, ALU.mult),
            reads=("pA", "bqs"), writes=("a_T",))
        em.step("vector", lambda e: e.tensor_scalar(
            k_T[:], pB[:, 0:128], bkc[:], None, ALU.add),
            reads=("pB", "bkc"), writes=("k_T",))
        em.step("vector", lambda e: e.tensor_scalar(
            v_T[:], pC[:, 0:128], bvc[:], None, ALU.add),
            reads=("pC", "bvc"), writes=("v_T",))
        for (srcb, dstb, st, dt_) in ((a_T, a_tm, "a_T", "a_tm"),
                                      (k_T, k_tm, "k_T", "k_tm"),
                                      (v_T, v_tm, "v_T", "v_tm")):
            em.step("tensor", lambda e, srcb=srcb: e.transpose(
                pT1[:, 0:128], srcb[:], eyef[:]),
                reads=(st, "eyef", "pT1"), writes=("pT1",))
            em.step("vector", lambda e, dstb=dstb: e.tensor_copy(dstb[:], pT1[:, 0:128]),
                    reads=("pT1",), writes=(dt_, "pT1"))
        em.step("vector", lambda e: e.reduce_max(kmm[:, 0:1], k_tm[:], axis=AX.X),
                reads=("k_tm",), writes=("kmm",))
        em.step("vector", lambda e: e.tensor_reduce(kmm[:, 1:2], k_tm[:], AX.X, ALU.min),
                reads=("k_tm",), writes=("kmm",))
        em.step("vector", lambda e: e.tensor_scalar(
            kmm[:, 2:4], kmm[:, 0:2], -1.0, None, ALU.mult),
            reads=("kmm",), writes=("kmm",))
        em.step("vector", lambda e: e.tensor_scalar_mul(negm[:], a_tm[:], kmm[:, 2:3]),
                reads=("a_tm", "kmm"), writes=("negm",))
        em.step("vector", lambda e: e.scalar_tensor_tensor(
            negm[:], a_tm[:], kmm[:, 3:4], negm[:], ALU.mult, ALU.min),
            reads=("a_tm", "kmm", "negm"), writes=("negm",))
        for d in range(128):
            mtg = f"mat{d % 8}"
            wtags_d = (mtg, "s1col") + (
                PREP_TAGS if (i == 0 and d == 0) else ())
            em.step("scalar", lambda e, d=d: e.activation(
                mat[:, d * 128:(d + 1) * 128], k_tm[:], AF.Exp,
                bias=negm[:, d:d + 1], scale=a_tm[:, d:d + 1],
                accum_out=s1col[:, d:d + 1]),
                reads=("k_tm", "negm", "a_tm"), writes=wtags_d)
            tg = f"ttg{d % 4}"
            em.step("gpsimd", lambda e, d=d: e.tensor_tensor(
                ttg[d % 4][:], mat[:, d * 128:(d + 1) * 128], v_tm[:], ALU.mult),
                reads=(mtg, "v_tm"), writes=(tg,))
            em.step("vector", lambda e, d=d: e.reduce_sum(
                s2col[:, d:d + 1], ttg[d % 4][:], axis=AX.X),
                reads=(tg,), writes=("s2v",))
        em.step("vector", lambda e: e.reciprocal(s1col[:], s1col[:]),
                reads=("s1col",), writes=("s1col",))
        em.step("vector", lambda e: e.tensor_mul(o_tm[:], s2col[:], s1col[:]),
                reads=("s2v", "s1col"), writes=("o_tm",))
        em.step("tensor", lambda e: e.transpose(pT1[:, 0:128], o_tm[:], eyef[:]),
                reads=("o_tm", "eyef", "pT1"), writes=("pT1",))
        em.step("vector", lambda e: e.tensor_copy(o_Tb[:], pT1[:, 0:128]),
                reads=("pT1",), writes=("o_Tb", "pT1"))

        mm_acc(pD, (0, 128), wr0T, "wr0T", 8, 8, "pD")
        em.step("vector", lambda e: e.tensor_scalar(
            rt_f[:], pD[0:8, 0:128], br0c[:], None, ALU.add),
            reads=("pD", "br0c"), writes=("rt_f",))
        em.step("tensor", lambda e: e.transpose(pT1[0:128, 0:8], rt_f[:], eyef[0:8, 0:8]),
                reads=("rt_f", "eyef", "pT1"), writes=("pT1",))
        em.step("vector", lambda e: e.tensor_copy(rt_tm[:], pT1[0:128, 0:8]),
                reads=("pT1",), writes=("rt_tm", "pT1"))
        em.step("vector", lambda e: e.reduce_max(stat[:, 7:8], rt_tm[:], axis=AX.X),
                reads=("rt_tm",), writes=("stat",))
        em.step("vector", lambda e: e.tensor_scalar(
            stat[:, 7:8], stat[:, 7:8], -1.0, None, ALU.mult),
            reads=("stat",), writes=("stat",))
        em.step("scalar", lambda e: e.activation(
            rt_scr[:], rt_tm[:], AF.Exp, bias=stat[:, 7:8], accum_out=stat[:, 6:7]),
            reads=("rt_tm", "stat"), writes=("rt_scr", "stat"))
        em.step("vector", lambda e: e.reciprocal(stat[:, 6:7], stat[:, 6:7]),
                reads=("stat",), writes=("stat",))
        em.step("vector", lambda e: e.tensor_scalar(
            rt_scr[:], rt_scr[:], stat[:, 6:7], 8.0, ALU.mult, ALU.mult),
            reads=("rt_scr", "stat"), writes=("rt_scr",))
        em.step("tensor", lambda e: e.transpose(pT1[0:8, 0:128], rt_scr[:], eyef[:]),
                reads=("rt_scr", "eyef", "pT1"), writes=("pT1",))
        em.step("vector", lambda e: e.tensor_copy(g8T[:], pT1[0:8, 0:128]),
                reads=("pT1",), writes=("g8T", "pT1"))
        em.step("tensor", lambda e: e.matmul(
            pD[:, 128:256], e8[:], g8T[:], start=True, stop=True),
            reads=("e8", "g8T", "pD"), writes=("pD",))
        em.step("vector", lambda e: e.tensor_copy(scoreB[:], pD[:, 128:256]),
                reads=("pD",), writes=("scoreB", "pD"))
        mm_acc(pA, (128, 256), a0T, "a0T", 128, 128, "pA")
        em.step("vector", lambda e: e.tensor_mul(glow[:], pA[:, 128:256], scoreB[:]),
                reads=("pA", "scoreB"), writes=("glow",))
        for oc in range(ND):
            pt, ptg = (pA, "pA") if oc % 2 == 0 else (pB, "pB")
            em.step("tensor", lambda e, oc=oc, pt=pt: e.matmul(
                pt[:, 256:384], b0T[:, oc * 128:(oc + 1) * 128], glow[:],
                start=True, stop=True),
                reads=("b0T", "glow", ptg), writes=(ptg,))
            em.step("scalar", lambda e, oc=oc, pt=pt: e.activation(
                g0T[:, oc * 128:(oc + 1) * 128], pt[:, 256:384], AF.Silu, scale=2.0),
                reads=(ptg,), writes=("g0T",))
        for q in range(ND):
            pt, ptg = (pC, "pC") if q % 2 == 0 else (pD, "pD")
            em.step("tensor", lambda e, q=q, pt=pt: e.matmul(
                pt[:, 256:384], selq[:, q * 128:(q + 1) * 128], o_Tb[:],
                start=True, stop=True),
                reads=("selq", "o_Tb", ptg), writes=(ptg,))
            em.step("vector", lambda e, q=q, pt=pt: e.tensor_mul(
                d0T[:, q * 128:(q + 1) * 128], pt[:, 256:384],
                g0T[:, q * 128:(q + 1) * 128]),
                reads=(ptg, "g0T"), writes=("d0T",))
        for dc in range(ND):
            pt, ptg = (pT1, "pT1") if dc % 2 == 0 else (pT2, "pT2")
            em.step("tensor", lambda e, dc=dc, pt=pt: e.transpose(
                pt[:, 0:128], d0T[:, dc * 128:(dc + 1) * 128], eyef[:]),
                reads=("d0T", "eyef", ptg), writes=(ptg,))
            em.step("vector", lambda e, dc=dc, pt=pt, xb=xb: e.tensor_add(
                x1t[:, dc * 128:(dc + 1) * 128],
                xb[:, dc * 128:(dc + 1) * 128], pt[:, 0:128]),
                reads=(ptg, xtag, "x1t"), writes=("x1t", ptg))
        em.dma("sync", x1_o[i * 128:(i + 1) * 128, :], x1t[:], reads=("x1t",))
        emit_layernorm(x1t, xn1b, "x1t", "xn1b", g0T, "g0T")
        emit_transpose_tm2fm(xn1b, xn1Tt, "xn1b", "xn1Tt")
        em.dma("sync",
               xn1T_o.rearrange("(c p) t -> p c t", p=128)[:, :, i * 128:(i + 1) * 128],
               xn1Tt[:].rearrange("p (c t) -> p c t", t=128),
               reads=("xn1Tt",))

    al.finish()


def fix_wkv_dma():
    pass


def l0_inputs(core):
    d = const_inputs(L0_CONSTS)
    d.update({
        "x0": core["x0"],
        "Wq": core["L0_attn_Wq"], "bq": core["L0_attn_bq"],
        "Wk": core["L0_attn_Wk"], "bk": core["L0_attn_bk"],
        "Wv": core["L0_attn_Wv"], "bv": core["L0_attn_bv"],
        "A0": core["L0_ffn_A"], "B0": core["L0_ffn_B"],
        "Wr0": core["L0_ffn_Wr"], "br0": core["L0_ffn_br"],
    })
    return d


def kernel(**inputs):
    raise NotImplementedError("under construction")



PRE_CONSTS = ()

def emit_PREFETCH(em, io):
    WpT = io["WpT"]
    WoT = io["WoT"]
    A1T = io["A1T"]
    B1T = io["B1T"]
    wbT = io["wbT"]
    woT = io["woT"]
    waT = io["waT"]
    wb1 = io["wb1"]
    for ci in range(4):
        em.dma("gpsimd", wbT[ci * 1024:(ci + 1) * 1024, :],
               WpT[ci * 1024:(ci + 1) * 1024, :], writes=(f"wbT{ci}",))
    for ci in range(8):
        em.dma("gpsimd", woT[ci * 1024:(ci + 1) * 1024, :],
               WoT[ci * 1024:(ci + 1) * 1024, :], writes=(f"woT{ci}",))
    em.dma("gpsimd", waT[:], A1T[:], writes=("waT",))
    em.dma("gpsimd", wb1[:], B1T[:], writes=("wb1",))



INPROJ_CONSTS = ("eyef",)
NM_PROJ = CPROJ // 128
CAST_ROWS = 512

def emit_INPROJ(em, io):
    nc = em.nc
    C = io
    xn1T = io["xn1T"]
    wbT = io["wbT"]
    convw = io["convw"]
    convb = io["convb"]
    dtb = io["dtb"]
    nw = io["nw"]
    sg_o = io["sg"]
    cvx_o = io["cvx"]
    cvB_o = io["cvB"]
    cvC_o = io["cvC"]
    dtv_o = io["dtv"]

    al = Alloc(nc)
    sb, ps = al.sb, al.ps
    eyef = sb("s_eyef", [128, 128], F32)
    xn1sb = sb("s_xn1sb", [128, ND * T], BF16)
    lbuf = [sb(f"s_lbuf{i}", [128, ND * 256], BF16) for i in range(2)]
    cvpad = [sb(f"s_cvpad{i}", [128, T + 3], F32) for i in range(2)]
    acc = [sb(f"s_acc{i}", [128, T], F32) for i in range(2)]
    otile = [sb(f"s_otile{i}", [128, T], BF16) for i in range(2)]
    cw = sb("s_cw", [128, 80 * 4], F32)
    cb = sb("s_cb", [128, 80], F32)
    nwc = sb("s_nwc", [128, 64], F32)
    dtbc = sb("s_dtbc", [128, 1], F32)
    onec = sb("s_onec", [128, 1], F32)
    stage = sb("s_stage", [128, 128], F32)
    pP = [ps(f"p_P{i}", [128, 512]) for i in range(8)]

    em.dma("sync", eyef[:], C["eyef"][:], writes=("eyef",))
    for i in range(2):
        em.step("gpsimd", lambda e, i=i: e.memset(cvpad[i][:, 0:3], 0.0),
                writes=(f"cvp{i}",))
    em.step("gpsimd", lambda e: e.memset(onec[:], 1.0), writes=("onec",))
    xv = xn1T.rearrange("(c p) t -> p c t", p=128)
    for j in range(4):
        em.dma("sync", xn1sb[:].rearrange("p (c t) -> p c t", t=T)[:, j * 8:(j + 1) * 8, :],
               xv[:, j * 8:(j + 1) * 8, :], writes=("xn1sb",))
    for j in range(80):
        em.dma("scalar", cw[:, j * 4:(j + 1) * 4],
               convw[:, 0, :].rearrange("(n p) j -> p n j", p=128)[:, j, :],
               writes=("cw",))
    em.dma("gpsimd", stage[0:80, :], convb[:].rearrange("(n p) -> n p", p=128),
           writes=("stage",))
    em.step("tensor", lambda e: e.transpose(pP[0][0:128, 0:80], stage[0:80, :], eyef[0:80, 0:80]),
            reads=("stage", "eyef", "pP0"), writes=("pP0",))
    em.step("vector", lambda e: e.tensor_copy(cb[:], pP[0][0:128, 0:80]),
            reads=("pP0",), writes=("cb", "pP0"))
    em.dma("gpsimd", stage[0:1, 0:128], dtb[:].rearrange("h -> () h"), writes=("stage",))
    em.step("tensor", lambda e: e.transpose(pP[0][0:128, 0:1], stage[0:1, 0:128], eyef[0:1, 0:1]),
            reads=("stage", "eyef", "pP0"), writes=("pP0",))
    em.step("vector", lambda e: e.tensor_copy(dtbc[:], pP[0][0:128, 0:1]),
            reads=("pP0",), writes=("dtbc", "pP0"))
    em.dma("gpsimd", stage[0:64, :], nw[:].rearrange("(n p) -> n p", p=128),
           writes=("stage",))
    em.step("tensor", lambda e: e.transpose(pP[0][0:128, 0:64], stage[0:64, :], eyef[0:64, 0:64]),
            reads=("stage", "eyef", "pP0"), writes=("pP0",))
    em.step("vector", lambda e: e.tensor_copy(nwc[:], pP[0][0:128, 0:64]),
            reads=("pP0",), writes=("nwc", "pP0"))

    ngrp = (NM_PROJ + 1) // 2
    for g in range(ngrp):
        gm = 2 if (g + 1) * 2 <= NM_PROJ else NM_PROJ - g * 2
        lb = lbuf[g % 2]
        ltag = f"lbuf{g % 2}"
        if gm == 2:
            for kc0 in range(0, ND, 8):
                q = "sync" if (kc0 // 8) % 2 == 0 else "scalar"
                em.dma(q,
                       lb[:, kc0 * 256:(kc0 + 8) * 256].rearrange(
                           "p (f m) -> p f m", f=8),
                       wbT[kc0 * 128:(kc0 + 8) * 128,
                           g * 256: g * 256 + 256].rearrange(
                           "(f p) m -> p f m", p=128),
                       reads=(f"wbT{kc0 // 8}",) + ((ltag,) if kc0 == 0 else ()),
                       writes=(ltag,))
        else:
            for kc in range(ND):
                em.dma("sync", lb[:, kc * 256: kc * 256 + gm * 128],
                       wbT[kc * 128:(kc + 1) * 128, g * 256: g * 256 + gm * 128],
                       reads=(f"wbT{kc // 8}",) + ((ltag,) if kc < 2 else ()),
                       writes=(ltag,))
        for mi in range(gm):
            m = g * 2 + mi
            cv = cvpad[m % 2]
            cvt = f"cvp{m % 2}"
            ac = acc[m % 2]
            act = f"acc{m % 2}"
            ot = otile[m % 2]
            ott = f"otile{m % 2}"
            for kc in range(ND):
                for nb in range(4):
                    pi = (m % 2) * 4 + nb
                    em.step("tensor", lambda e, kc=kc, nb=nb, lb=lb, mi=mi, pi=pi: e.matmul(
                        pP[pi][:, :], lb[:, kc * 256 + mi * 128: kc * 256 + mi * 128 + 128],
                        xn1sb[:, kc * T + nb * 512: kc * T + (nb + 1) * 512],
                        start=(kc == 0), stop=(kc == ND - 1)),
                        reads=(ltag, "xn1sb") + ((f"pP{pi}",) if kc == 0 else ()),
                        writes=(f"pP{pi}",))
            for nb in range(4):
                pi = (m % 2) * 4 + nb
                if nb < 2:
                    em.step("vector", lambda e, nb=nb, pi=pi, cv=cv: e.tensor_copy(
                        cv[:, 3 + nb * 512: 3 + (nb + 1) * 512], pP[pi][:, :]),
                        reads=(f"pP{pi}",), writes=(cvt, f"pP{pi}"))
                else:
                    em.step("scalar", lambda e, nb=nb, pi=pi, cv=cv: e.copy(
                        cv[:, 3 + nb * 512: 3 + (nb + 1) * 512], pP[pi][:, :]),
                        reads=(f"pP{pi}",), writes=(cvt, f"pP{pi}"))
            if m < 64:
                em.step("scalar", lambda e, cv=cv, ac=ac: e.activation(
                    ac[:], cv[:, 3:3 + T], AF.Silu),
                    reads=(cvt,), writes=(act,))
                em.step("vector", lambda e, m=m, ot=ot, ac=ac: e.tensor_scalar_mul(
                    ot[:], ac[:], nwc[:, m:m + 1]),
                    reads=(act, "nwc"), writes=(ott,))
                em.dma("gpsimd", sg_o[m * 128:(m + 1) * 128, :], ot[:],
                       reads=(ott,))
            elif m < 144:
                j = m - 64
                em.step("vector", lambda e, j=j, cv=cv, ac=ac: e.tensor_scalar_mul(
                    ac[:], cv[:, 3:3 + T], cw[:, j * 4 + 3: j * 4 + 4]),
                    reads=(cvt, "cw"), writes=(act,))
                for tap in (2, 1, 0):
                    em.step("vector", lambda e, j=j, tap=tap, cv=cv, ac=ac: e.scalar_tensor_tensor(
                        ac[:], cv[:, tap: tap + T], cw[:, j * 4 + tap: j * 4 + tap + 1],
                        ac[:], ALU.mult, ALU.add),
                        reads=(cvt, "cw", act), writes=(act,))
                em.step("scalar", lambda e, j=j, ac=ac, ot=ot: e.activation(
                    ot[:], ac[:], AF.Silu, bias=cb[:, j:j + 1]),
                    reads=(act, "cb"), writes=(ott,))
                if j < 64:
                    em.dma("gpsimd", cvx_o[j * 128:(j + 1) * 128, :], ot[:],
                           reads=(ott,))
                elif j < 72:
                    em.dma("gpsimd", cvB_o[(j - 64) * 128:(j - 63) * 128, :], ot[:],
                           reads=(ott,))
                else:
                    em.dma("gpsimd", cvC_o[(j - 72) * 128:(j - 71) * 128, :], ot[:],
                           reads=(ott,))
            else:
                rl = acc[1]
                rlt = "acc1"
                em.step("vector", lambda e, cv=cv, ac=ac: e.tensor_scalar_add(
                    ac[:], cv[:, 3:3 + T], dtbc[:]),
                    reads=(cvt, "dtbc"), writes=(act,))
                em.step("scalar", lambda e, ac=ac, rl=rl: e.activation(rl[:], ac[:], AF.Relu),
                        reads=(act,), writes=(rlt,))
                em.step("scalar", lambda e, ac=ac: e.activation(ac[:], ac[:], AF.Abs),
                        reads=(act,), writes=(act,))
                em.step("scalar", lambda e, ac=ac: e.activation(ac[:], ac[:], AF.Exp, scale=-1.0),
                        reads=(act,), writes=(act,))
                em.step("scalar", lambda e, ac=ac: e.activation(ac[:], ac[:], AF.Ln, bias=onec[:]),
                        reads=(act, "onec"), writes=(act,))
                em.step("vector", lambda e, ac=ac, rl=rl: e.tensor_add(ac[:], ac[:], rl[:]),
                        reads=(act, rlt), writes=(act,))
                em.dma("gpsimd", dtv_o[:], ac[:], reads=(act,))

    al.finish()


def inproj_inputs(core, xn1T):
    d = const_inputs(INPROJ_CONSTS)
    d.update({
        "xn1T": xn1T,
        "Wp": core["L1_attn_in_proj_w"],
        "convw": core["L1_attn_conv_w"],
        "convb": core["L1_attn_conv_b"],
        "dtb": core["L1_attn_dt_bias"],
        "nw": core["L1_attn_norm_w"],
    })
    return d



SSD_CONSTS = ("eyef", "eyeb", "onesf", "cumU", "mask01")

def emit_SSD(em, io):
    nc = em.nc
    C = io
    cvx = io["cvx"]
    cvB = io["cvB"]
    cvC = io["cvC"]
    dtv = io["dtv"]
    sg = io["sg"]
    alog = io["alog"]
    Din = io["Din"]
    ygT_o = io["ygT"]
    rowscr = io["rowscr"]

    al = Alloc(nc)
    sb, ps = al.sb, al.ps
    eyef = sb("s_eyef", [128, 128], F32)
    eyeb = sb("s_eyeb", [128, 128], BF16)
    onesf = sb("s_onesf", [1, 128], F32)
    cumU = sb("s_cumU", [128, 128], F32)
    mask01 = sb("s_mask01", [128, 128], F32)
    onescb = sb("s_onescb", [128, 1], BF16)
    onescf = sb("s_onescf", [128, 1], F32)
    eps1 = sb("s_eps1", [1, 1], F32)
    acol = sb("s_acol", [128, 1], F32)
    dball = sb("s_dball", [128, 128], F32)
    dtv_sb = sb("s_dtvsb", [128, T], F32)
    adt = sb("s_adt", [128, T], F32)
    s_state = sb("s_state", [128, H * 64], F32)
    xcm = sb("s_xcm", [128, 64 * 128], BF16)
    xtk = sb("s_xtk", [128, 64 * 128], BF16)
    bcm = sb("s_bcm", [128, 8 * 128], BF16)
    ccm = sb("s_ccm", [128, 8 * 128], BF16)
    bgT = sb("s_bgT", [128, 8 * 128], BF16)
    sgt = sb("s_sgt", [128, 64 * 128], BF16)
    ybuf = sb("s_ybuf", [128, 64 * 128], F32)
    y2b = sb("s_y2b", [128, 64 * 128], BF16)
    mgmT = sb("s_mgmT", [128, 8 * 128], BF16)
    adtT = sb("s_adtT", [128, 128], F32)
    csT = sb("s_csT", [128, 128], F32)
    acs = sb("s_acs", [128, 128], F32)
    dbfm = sb("s_dbfm", [128, 128], F32)
    dbT = sb("s_dbT", [128, 128], F32)
    dtT = sb("s_dtT", [128, 128], F32)
    gbc = sb("s_gbc", [128, 128], F32)
    rstdB = sb("s_rstdB", [128, 128], F32)
    rr = sb("s_rr", [1, 128], F32)
    csr4 = sb("s_csr4", [1, 512], F32)
    sd4 = sb("s_sd4", [1, 512], F32)
    grow = sb("s_grow", [1, 128], F32)
    sdB4 = sb("s_sdB4", [128, 512], F32)
    dscr = [sb(f"s_dscr{i}", [128, 128], F32) for i in range(2)]
    ltb = [sb(f"s_ltb{i}", [128, 128], BF16) for i in range(2)]
    mtb = [sb(f"s_mtb{i}", [128, 128], BF16) for i in range(2)]
    xdt = [sb(f"s_xdt{i}", [128, 64], BF16) for i in range(2)]
    bdb = [sb(f"s_bdb{i}", [128, 128], BF16) for i in range(2)]
    sbst = [sb(f"s_sbst{i}", [128, 64], BF16) for i in range(2)]
    o1b = [sb(f"s_o1b{i}", [128, 128], F32) for i in range(2)]
    o2b = [sb(f"s_o2b{i}", [128, 128], F32) for i in range(2)]
    ytmp = sb("s_ytmp", [128, 128], F32)
    stage = sb("s_stage", [1, 128], F32)

    b0 = ps("p_b0", [128, 512])
    b1 = ps("p_b1", [128, 512])
    b2 = ps("p_b2", [128, 512])
    b5 = ps("p_b5", [128, 512])
    b6 = ps("p_b6", [128, 512])
    b7 = ps("p_b7", [128, 1024], BF16)
    b8 = ps("p_b8", [128, 1024], BF16)
    b3 = ps("p_b3", [128, 512])

    em.dma("sync", eyef[:], C["eyef"][:], writes=("eyef",))
    em.dma("sync", eyeb[:], C["eyeb"][:], writes=("eyeb",))
    em.dma("sync", onesf[:], C["onesf"][:], writes=("onesf",))
    em.dma("sync", cumU[:], C["cumU"][:], writes=("cumU",))
    em.dma("sync", mask01[:], C["mask01"][:], writes=("mask01",))
    em.step("gpsimd", lambda e: e.memset(onescb[:], 1.0), writes=("onescb",))
    em.step("gpsimd", lambda e: e.memset(onescf[:], 1.0), writes=("onescf",))
    em.step("gpsimd", lambda e: e.memset(eps1[:], 1e-5), writes=("eps1",))
    for j in range(8):
        em.step("gpsimd", lambda e, j=j: e.memset(s_state[:, j * 1024:(j + 1) * 1024], 0.0),
                writes=("s_state",))
    em.dma("gpsimd", dtv_sb[:], dtv[:], writes=("dtvsb",))
    em.dma("gpsimd", stage[:], alog[:].rearrange("h -> () h"), writes=("stage",))
    em.step("tensor", lambda e: e.transpose(b0[0:128, 0:1], stage[:], eyef[0:1, 0:1]),
            reads=("stage", "eyef", "b0"), writes=("b0",))
    em.step("scalar", lambda e: e.activation(acol[:], b0[0:128, 0:1], AF.Exp),
            reads=("b0",), writes=("acol",))
    em.step("vector", lambda e: e.tensor_scalar_mul(acol[:], acol[:], -1.0),
            reads=("acol",), writes=("acol",))
    em.step("vector", lambda e: e.tensor_scalar_mul(adt[:], dtv_sb[:], acol[:]),
            reads=("dtvsb", "acol"), writes=("adt",))
    em.dma("gpsimd", stage[:], Din[:].rearrange("h -> () h"), writes=("stage",))
    em.step("tensor", lambda e: e.matmul(b0[:, 128:256], onesf[:], stage[:],
                                         start=True, stop=True),
            reads=("onesf", "stage", "b0"), writes=("b0",))
    em.step("vector", lambda e: e.tensor_copy(dball[:], b0[:, 128:256]),
            reads=("b0",), writes=("dball", "b0"))

    cvxv = cvx.rearrange("(c p) t -> p c t", p=128)
    cvBv = cvB.rearrange("(g p) t -> p g t", p=128)
    cvCv = cvC.rearrange("(g p) t -> p g t", p=128)
    sgv = sg.rearrange("(c p) t -> p c t", p=128)
    ygv = ygT_o.rearrange("(c p) t -> p c t", p=128)

    for c in range(NCHUNK):
        tsl = slice(c * CS, (c + 1) * CS)
        em.dma("sync", xcm[:].rearrange("p (a t) -> p a t", t=128), cvxv[:, :, tsl],
               reads=("xcm",), writes=("xcm",))
        em.dma("gpsimd", bcm[:].rearrange("p (a t) -> p a t", t=128), cvBv[:, :, tsl],
               reads=("bcm",), writes=("bcm",))
        em.dma("gpsimd", ccm[:].rearrange("p (a t) -> p a t", t=128), cvCv[:, :, tsl],
               reads=("ccm",), writes=("ccm",))
        em.dma("scalar", sgt[:].rearrange("p (a t) -> p a t", t=128), sgv[:, :, tsl],
               reads=("sgt",), writes=("sgt",))
        em.step("tensor", lambda e, c=c: e.transpose(b0[:, 0:128], adt[:, c * CS:(c + 1) * CS], eyef[:]),
                reads=("adt", "eyef", "b0"), writes=("b0",))
        em.step("vector", lambda e: e.tensor_copy(adtT[:], b0[:, 0:128]),
                reads=("b0",), writes=("adtT", "b0"))
        em.step("tensor", lambda e: e.matmul(b0[:, 128:256], cumU[:], adtT[:], start=True, stop=True),
                reads=("cumU", "adtT", "b0"), writes=("b0",))
        em.step("vector", lambda e: e.tensor_copy(csT[:], b0[:, 128:256]),
                reads=("b0",), writes=("csT", "b0"))
        em.step("tensor", lambda e: e.matmul(b0[:, 256:384], adtT[:], cumU[:], start=True, stop=True),
                reads=("adtT", "cumU", "b0"), writes=("b0",))
        em.step("vector", lambda e: e.tensor_copy(acs[:], b0[:, 256:384]),
                reads=("b0",), writes=("acs", "b0"))
        em.dma("gpsimd", rowscr[:], acs[:], reads=("acs",), writes=("rowscr",))
        em.step("tensor", lambda e: e.matmul(b3[0:1, 128:256], onescf[:], adtT[:],
                                             start=True, stop=True),
                reads=("onescf", "adtT", "b3"), writes=("b3",))
        em.step("scalar", lambda e: e.activation(grow[:], b3[0:1, 128:256], AF.Exp),
                reads=("b3",), writes=("grow",))
        em.step("scalar", lambda e: e.activation(dbfm[:], acs[:], AF.Exp,
                                                 scale=-1.0, bias=acs[:, 127:128]),
                reads=("acs",), writes=("dbfm",))
        em.step("tensor", lambda e: e.transpose(b0[:, 384:512], dbfm[:], eyef[:]),
                reads=("dbfm", "eyef", "b0"), writes=("b0",))
        em.step("vector", lambda e: e.tensor_copy(dbT[:], b0[:, 384:512]),
                reads=("b0",), writes=("dbT", "b0"))
        em.step("tensor", lambda e, c=c: e.transpose(b0[:, 0:128], dtv_sb[:, c * CS:(c + 1) * CS], eyef[:]),
                reads=("dtvsb", "eyef", "b0"), writes=("b0",))
        em.step("vector", lambda e: e.tensor_copy(dtT[:], b0[:, 0:128]),
                reads=("b0",), writes=("dtT", "b0"))
        em.step("tensor", lambda e: e.matmul(
            b3[:, 0:128], onesf[:], grow[:], start=True, stop=True),
            reads=("onesf", "grow", "b3"), writes=("b3",))
        em.step("vector", lambda e: e.tensor_copy(gbc[:], b3[:, 0:128]),
                reads=("b3",), writes=("gbc", "b3"))
        for g in range(8):
            pg, ptag = (b1, "b1") if g < 4 else (b2, "b2")
            off = (g % 4) * 128
            em.step("tensor", lambda e, g=g, pg=pg, off=off: e.matmul(
                pg[:, off:off + 128], bcm[:, g * 128:(g + 1) * 128],
                ccm[:, g * 128:(g + 1) * 128], start=True, stop=True),
                reads=("bcm", "ccm", ptag), writes=(ptag,))
        for g in range(8):
            pg, ptag = (b1, "b1") if g < 4 else (b2, "b2")
            off = (g % 4) * 128
            em.step("vector", lambda e, g=g, pg=pg, off=off: e.tensor_mul(
                mgmT[:, g * 128:(g + 1) * 128], pg[:, off:off + 128], mask01[:]),
                reads=(ptag, "mask01"), writes=("mgmT",))
        for c2 in range(64):
            pb, ptag = (b7, "b7") if c2 % 2 == 0 else (b8, "b8")
            em.step("tensor", lambda e, c2=c2, pb=pb: e.transpose(
                pb[:, 0:128], xcm[:, c2 * 128:(c2 + 1) * 128], eyeb[:]),
                reads=("xcm", "eyeb", ptag), writes=(ptag,))
            em.step("vector", lambda e, c2=c2, pb=pb: e.tensor_copy(
                xtk[:, c2 * 128:(c2 + 1) * 128], pb[:, 0:128]),
                reads=(ptag,), writes=("xtk", ptag))
        for g in range(8):
            pb, ptag = (b7, "b7") if g % 2 == 0 else (b8, "b8")
            em.step("tensor", lambda e, g=g, pb=pb: e.transpose(
                pb[:, 128:256], bcm[:, g * 128:(g + 1) * 128], eyeb[:]),
                reads=("bcm", "eyeb", ptag), writes=(ptag,))
            em.step("vector", lambda e, g=g, pb=pb: e.tensor_copy(
                bgT[:, g * 128:(g + 1) * 128], pb[:, 128:256]),
                reads=(ptag,), writes=("bgT", ptag))
        for hg in range(32):
            em.dma("gpsimd", csr4[:],
                   rowscr[hg * 4:(hg + 1) * 4, :].rearrange("p n -> () (p n)"),
                   reads=("rowscr", "csr4"), writes=("csr4",))
            em.step("scalar", lambda e: e.activation(sd4[:], csr4[:], AF.Exp),
                    reads=("csr4",), writes=("sd4",))
            em.step("tensor", lambda e: e.matmul(
                b1[:, :], onesf[:], csr4[:], start=True, stop=True),
                reads=("onesf", "csr4", "b1"), writes=("b1",))
            em.step("tensor", lambda e: e.matmul(
                b2[:, :], onesf[:], sd4[:], start=True, stop=True),
                reads=("onesf", "sd4", "b2"), writes=("b2",))
            em.step("scalar", lambda e: e.copy(sdB4[:], b2[:, :]),
                    reads=("b2",), writes=("sdB4",))
            for j in range(4):
                h = hg * 4 + j
                g = h // 16
                pM, mtag = (b5, "b5") if h % 2 == 0 else (b6, "b6")
                pr = h % 2
                dv, lv, mv, xv2, bv, sv = dscr[pr], ltb[pr], mtb[pr], xdt[pr], bdb[pr], sbst[pr]
                dtag, lttag, mttag, xtag2, btag, stag2 = (f"dscr{pr}", f"ltb{pr}", f"mtb{pr}",
                                                          f"xdt{pr}", f"bdb{pr}", f"sbst{pr}")
                em.step("vector", lambda e, j=j, dv=dv, h=h: e.tensor_scalar(
                    dv[:], b1[:, j * 128:(j + 1) * 128], csT[:, h:h + 1], 0.0,
                    ALU.subtract, ALU.min),
                    reads=("b1", "csT", dtag), writes=(dtag,))
                em.step("scalar", lambda e, dv=dv, lv=lv: e.activation(lv[:], dv[:], AF.Exp),
                        reads=(dtag,), writes=(lttag,))
                em.step("vector", lambda e, lv=lv, mv=mv, g=g: e.tensor_mul(
                    mv[:], lv[:], mgmT[:, g * 128:(g + 1) * 128]),
                    reads=(lttag, "mgmT"), writes=(mttag,))
                em.step("vector", lambda e, xv2=xv2, h=h: e.tensor_scalar_mul(
                    xv2[:], xtk[:, h * 64:(h + 1) * 64], dtT[:, h:h + 1]),
                    reads=("xtk", "dtT", xtag2), writes=(xtag2,))
                em.step("vector", lambda e, bv=bv, g=g, h=h: e.tensor_scalar_mul(
                    bv[:], bgT[:, g * 128:(g + 1) * 128], dbT[:, h:h + 1]),
                    reads=("bgT", "dbT", btag), writes=(btag,))
                if c > 0:
                    em.step("vector", lambda e, sv=sv, h=h: lowprec(e, lambda: e.tensor_copy(
                        sv[:], s_state[:, h * 64:(h + 1) * 64])),
                        reads=("s_state", stag2), writes=(stag2,))
                B_ = (h % 2) * 64
                em.step("tensor", lambda e, pM=pM, xv2=xv2, mv=mv, B_=B_: e.matmul(
                    pM[B_:B_ + 64, 0:128], xv2[:], mv[:], start=True, stop=True),
                    reads=(xtag2, mttag, mtag), writes=(mtag,))
                if c > 0:
                    em.step("tensor", lambda e, pM=pM, sv=sv, g=g, B_=B_: e.matmul(
                        pM[B_:B_ + 64, 128:256], sv[:], ccm[:, g * 128:(g + 1) * 128],
                        start=True, stop=True),
                        reads=(stag2, "ccm", mtag), writes=(mtag,))
                em.step("tensor", lambda e, pM=pM, bv=bv, xv2=xv2: e.matmul(
                    pM[0:128, 256:320], bv[:], xv2[:], start=True, stop=True),
                    reads=(btag, xtag2, mtag), writes=(mtag,))
                o1v, o2v = o1b[pr], o2b[pr]
                o1tag, o2tag = f"o1b{pr}", f"o2b{pr}"
                if c > 0:
                    em.step("vector", lambda e, o1v=o1v, pM=pM, j=j, B_=B_: e.tensor_mul(
                        o1v[B_:B_ + 64, :], pM[B_:B_ + 64, 128:256],
                        sdB4[B_:B_ + 64, j * 128:(j + 1) * 128]),
                        reads=(mtag, "sdB4", o1tag), writes=(o1tag,))
                    em.step("vector", lambda e, o1v=o1v, o2v=o2v, pM=pM, B_=B_: e.tensor_add(
                        o2v[B_:B_ + 64, :], o1v[B_:B_ + 64, :], pM[B_:B_ + 64, 0:128]),
                        reads=(o1tag, mtag, o2tag), writes=(o2tag,))
                else:
                    em.step("vector", lambda e, o2v=o2v, pM=pM, B_=B_: e.tensor_copy(
                        o2v[B_:B_ + 64, :], pM[B_:B_ + 64, 0:128]),
                        reads=(mtag, o2tag), writes=(o2tag,))
                em.step("vector", lambda e, h=h, o2v=o2v, B_=B_: e.scalar_tensor_tensor(
                    ybuf[B_:B_ + 64, (h // 2) * 128:(h // 2) * 128 + 128],
                    xcm[B_:B_ + 64, (h // 2) * 128:(h // 2) * 128 + 128],
                    dball[B_:B_ + 64, h:h + 1], o2v[B_:B_ + 64, :], ALU.mult, ALU.add),
                    reads=("xcm", "dball", o2tag, "ybuf"), writes=("ybuf",))
                em.step("vector", lambda e, h=h, pM=pM: e.scalar_tensor_tensor(
                    s_state[:, h * 64:(h + 1) * 64], s_state[:, h * 64:(h + 1) * 64],
                    gbc[:, h:h + 1], pM[0:128, 256:320], ALU.mult, ALU.add),
                    reads=("s_state", "gbc", mtag), writes=("s_state",))
        em.step("scalar", lambda e: e.activation(y2b[:], ybuf[:], AF.Square),
                reads=("ybuf",), writes=("y2b",))
        for c2 in range(64):
            em.step("tensor", lambda e, c2=c2: e.matmul(
                b0[0:1, 0:128], onescb[:], y2b[:, c2 * 128:(c2 + 1) * 128],
                start=(c2 == 0), stop=(c2 == 63)),
                reads=("onescb", "y2b") + (("b0",) if c2 == 0 else ()), writes=("b0",))
        em.step("scalar", lambda e: e.activation(rr[:], b0[0:1, 0:128], AF.Sqrt,
                                                 scale=1.0 / INTER, bias=eps1[:]),
                reads=("b0", "eps1"), writes=("rr",))
        em.step("vector", lambda e: e.reciprocal(rr[:], rr[:]),
                reads=("rr",), writes=("rr",))
        em.step("tensor", lambda e: e.matmul(b0[:, 128:256], onesf[:], rr[:],
                                             start=True, stop=True),
                reads=("onesf", "rr", "b0"), writes=("b0",))
        em.step("vector", lambda e: e.tensor_copy(rstdB[:], b0[:, 128:256]),
                reads=("b0",), writes=("rstdB", "b0"))
        for c2 in range(64):
            em.step("vector", lambda e, c2=c2: e.tensor_mul(
                ytmp[:], ybuf[:, c2 * 128:(c2 + 1) * 128], rstdB[:]),
                reads=("ybuf", "rstdB", "ytmp"), writes=("ytmp",))
            em.step("vector", lambda e, c2=c2: e.tensor_mul(
                y2b[:, c2 * 128:(c2 + 1) * 128], ytmp[:], sgt[:, c2 * 128:(c2 + 1) * 128]),
                reads=("ytmp", "sgt"), writes=("y2b",))
        em.dma("sync", ygv[:, :, tsl], y2b[:].rearrange("p (a t) -> p a t", t=128),
               reads=("y2b",))

    al.finish()


def ssd_inputs(core, cvx, cvB, cvC, dtv, sg):
    d = const_inputs(SSD_CONSTS)
    d.update({
        "cvx": cvx, "cvB": cvB, "cvC": cvC, "dtv": dtv, "sg": sg,
        "alog": core["L1_attn_A_log"], "Din": core["L1_attn_D"],
    })
    return d



NKO = INTER // 128
OPROJ_CONSTS = ()

def emit_OPROJ(em, io):
    nc = em.nc
    ygT = io["ygT"]
    woT = io["woT"]
    s1T_o = io["s1T"]

    al = Alloc(nc)
    sb, ps = al.sb, al.ps
    ygb = sb("s_ygb", [128, NKO * 1024], BF16)
    lbuf = [sb(f"s_lb{i}", [128, 32 * 256], BF16) for i in range(2)]
    ev = [sb(f"s_ev{i}", [128, 1024], F32) for i in range(2)]
    pO = [ps(f"p_O{i}", [128, 512]) for i in range(8)]
    ygvv = ygT.rearrange("(c p) t -> p c t", p=128)

    segs = []
    for th in range(2):
        for mg in range(16):
            for sg_ in range(2):
                segs.append((th, mg, sg_))

    for th in range(2):
        tsl = slice(th * 1024, (th + 1) * 1024)
        for j in range(8):
            em.dma("sync", ygb[:].rearrange("p (c t) -> p c t", t=1024)[:, j * 8:(j + 1) * 8, :],
                   ygvv[:, j * 8:(j + 1) * 8, tsl], reads=("ygb",), writes=("ygb",))
        for mg in range(16):
            for sg_ in range(2):
                si = (th * 32 + mg * 2 + sg_)
                lb = lbuf[si % 2]
                ltag = f"lb{si % 2}"
                for kc0 in range(sg_ * 32, sg_ * 32 + 32, 8):
                    q = "sync" if ((kc0 // 8) % 2 == 0) else "scalar"
                    em.dma(q,
                           lb[:, (kc0 % 32) * 256:((kc0 % 32) + 8) * 256].rearrange(
                               "p (f m) -> p f m", f=8),
                           woT[kc0 * 128:(kc0 + 8) * 128,
                               mg * 256:(mg + 1) * 256].rearrange(
                               "(f p) m -> p f m", p=128),
                           reads=(f"woT{kc0 // 8}",) + ((ltag,) if kc0 % 32 == 0 else ()),
                           writes=(ltag,))
                for mi in range(2):
                    m = mg * 2 + mi
                    for kc in range(sg_ * 32, sg_ * 32 + 32):
                        for nb in range(2):
                            pi = (mg % 2) * 4 + mi * 2 + nb
                            em.step("tensor", lambda e, kc=kc, nb=nb, lb=lb, mi=mi, pi=pi: e.matmul(
                                pO[pi][:, :],
                                lb[:, (kc % 32) * 256 + mi * 128:(kc % 32) * 256 + mi * 128 + 128],
                                ygb[:, kc * 1024 + nb * 512: kc * 1024 + (nb + 1) * 512],
                                start=(kc == 0), stop=(kc == NKO - 1)),
                                reads=(ltag, "ygb") + ((f"pO{pi}",) if kc == 0 else ()),
                                writes=(f"pO{pi}",))
            for mi in range(2):
                m = mg * 2 + mi
                evb = ev[m % 2]
                evt = f"ev{m % 2}"
                for nb in range(2):
                    pi = (mg % 2) * 4 + mi * 2 + nb
                    if nb == 0:
                        em.step("vector", lambda e, pi=pi, nb=nb, evb=evb: e.tensor_copy(
                            evb[:, nb * 512:(nb + 1) * 512], pO[pi][:, :]),
                            reads=(f"pO{pi}", evt), writes=(evt, f"pO{pi}"))
                    else:
                        em.step("scalar", lambda e, pi=pi, nb=nb, evb=evb: e.copy(
                            evb[:, nb * 512:(nb + 1) * 512], pO[pi][:, :]),
                            reads=(f"pO{pi}", evt), writes=(evt, f"pO{pi}"))
                em.dma("gpsimd", s1T_o[m * 128:(m + 1) * 128, tsl], evb[:],
                       reads=(evt,))

    al.finish()


def oproj_inputs(core, ygT):
    return {"ygT": ygT, "Wo": core["L1_attn_out_proj_w"]}



L1FFN_CONSTS = ("eyef", "e64")

def emit_L1FFN(em, io):
    nc = em.nc
    C = io
    xn1T = io["xn1T"]
    waT = io["waT"]
    wb1 = io["wb1"]
    Wr1T = io["Wr1T"]
    g1T_o = io["g1T"]

    al = Alloc(nc)
    sb, ps = al.sb, al.ps
    eyef = sb("s_eyef", [128, 128], F32)
    e64 = sb("s_e64", [64, 1024], BF16)
    a1T = sb("s_a1T", [128, ND * KR], BF16)
    b1T = sb("s_b1T", [128, 8 * D], BF16)
    wr1T = sb("s_wr1T", [128, ND * 64], BF16)
    hgT = sb("s_hgT", [128, 8 * T], BF16)
    hsb = sb("s_hsb", [128, 512], F32)
    otile = sb("s_ot", [128, 512], BF16)
    xstream = [sb(f"s_xs{i}", [128, 512], BF16) for i in range(2)]
    sc_f = sb("s_scf", [64, T], F32)
    sc_tm = sb("s_sctm", [128, 64], F32)
    mx16 = sb("s_mx16", [128, 16], F32)
    mrep = sb("s_mrep", [128, 64], F32)
    thcol = sb("s_thcol", [128, 1], F32)
    t1col = sb("s_t1col", [128, 1], F32)
    maskb = sb("s_maskb", [128, 64], F32)
    gtm = sb("s_gtm", [128, 64], F32)
    gT = sb("s_gT", [64, T], BF16)
    pH = [ps(f"p_H{i}", [128, 512]) for i in range(8)]

    em.dma("sync", eyef[:], C["eyef"][:], writes=("eyef",))
    em.dma("sync", e64[:], C["e64"][:], writes=("e64",))
    a1dst = a1T[:].rearrange("p (c k) -> p c k", k=KR)
    a1src = waT.rearrange("(c p) k -> p c k", p=128)
    for j in range(4):
        em.dma("sync" if j % 2 == 0 else "scalar",
               a1dst[:, j * 8:(j + 1) * 8, :], a1src[:, j * 8:(j + 1) * 8, :],
               reads=("waT",), writes=("a1T",))
    wrdst = wr1T[:].rearrange("p (c e) -> p c e", e=64)
    wrsrc = Wr1T.rearrange("(c p) e -> p c e", p=128)
    for j in range(2):
        em.dma("gpsimd", wrdst[:, j * 16:(j + 1) * 16, :], wrsrc[:, j * 16:(j + 1) * 16, :],
               writes=("wr1T",))
    b1dst = b1T[:].rearrange("p (c d) -> p c d", d=D)
    b1src = wb1.rearrange("(c p) d -> p c d", p=128)
    for j in range(2):
        em.dma("sync" if j % 2 == 0 else "scalar",
               b1dst[:, j * 4:(j + 1) * 4, :], b1src[:, j * 4:(j + 1) * 4, :],
               reads=("wb1",), writes=("b1T",))

    xv = xn1T.rearrange("(c p) t -> p c t", p=128)
    for kc in range(ND):
        xs = xstream[kc % 2]
        xst = f"xs{kc % 2}"
        for nb in range(4):
            em.dma("scalar", xs[:], xv[:, kc, nb * 512:(nb + 1) * 512],
                   reads=(xst,), writes=(xst,))
            em.step("tensor", lambda e, kc=kc, nb=nb, xs=xs: e.matmul(
                pH[nb][0:64, :], wr1T[:, kc * 64:(kc + 1) * 64], xs[:],
                start=(kc == 0), stop=(kc == ND - 1)),
                reads=("wr1T", xst) + ((f"pH{nb}",) if kc == 0 else ()),
                writes=(f"pH{nb}",))
    for nb in range(4):
        em.step("vector", lambda e, nb=nb: e.tensor_copy(
            sc_f[0:64, nb * 512:(nb + 1) * 512], pH[nb][0:64, :]),
            reads=(f"pH{nb}", "sc_f"), writes=("sc_f", f"pH{nb}"))
    for i in range(NT):
        nb = i // 4
        off = (i % 4) * 128
        em.step("tensor", lambda e, nb=nb, off=off: e.transpose(
            pH[4][0:128, 0:64], sc_f[0:64, nb * 512 + off: nb * 512 + off + 128],
            eyef[0:64, 0:64]),
            reads=("sc_f", "eyef", "pH4"), writes=("pH4",))
        em.step("vector", lambda e: e.tensor_copy(sc_tm[:], pH[4][0:128, 0:64]),
                reads=("pH4",), writes=("sc_tm", "pH4"))
        em.step("vector", lambda e: e.max(out=mx16[:, 0:8], in_=sc_tm[:]),
                reads=("sc_tm",), writes=("mx16",))
        em.step("vector", lambda e: e.match_replace(
            out=mrep[:], in_to_replace=mx16[:, 0:8], in_values=sc_tm[:], imm_value=-1e30),
            reads=("sc_tm", "mx16"), writes=("mrep",))
        em.step("vector", lambda e: e.max(out=mx16[:, 8:16], in_=mrep[:]),
                reads=("mrep",), writes=("mx16",))
        em.step("vector", lambda e: e.tensor_reduce(thcol[:], mx16[:], AX.X, ALU.min),
                reads=("mx16",), writes=("thcol",))
        em.step("vector", lambda e: e.tensor_scalar(
            t1col[:], mx16[:, 0:1], -1.0, None, ALU.mult),
            reads=("mx16",), writes=("t1col",))
        em.step("vector", lambda e: e.tensor_scalar(
            maskb[:], sc_tm[:], thcol[:], None, ALU.is_ge),
            reads=("sc_tm", "thcol"), writes=("maskb",))
        em.step("scalar", lambda e: e.activation(
            gtm[:], sc_tm[:], AF.Exp, bias=t1col[:]),
            reads=("sc_tm", "t1col"), writes=("gtm",))
        em.step("vector", lambda e: e.tensor_mul(gtm[:], gtm[:], maskb[:]),
                reads=("gtm", "maskb"), writes=("gtm",))
        em.step("vector", lambda e: e.reduce_sum(thcol[:], gtm[:], axis=AX.X),
                reads=("gtm",), writes=("thcol",))
        em.step("vector", lambda e: e.reciprocal(thcol[:], thcol[:]),
                reads=("thcol",), writes=("thcol",))
        em.step("vector", lambda e: e.tensor_scalar_mul(gtm[:], gtm[:], thcol[:]),
                reads=("gtm", "thcol"), writes=("gtm",))
        em.step("tensor", lambda e: e.transpose(
            pH[5][0:64, 0:128], gtm[:], eyef[:]),
            reads=("gtm", "eyef", "pH5"), writes=("pH5",))
        em.step("vector", lambda e, i=i: lowprec(e, lambda: e.tensor_copy(
            gT[:, i * 128:(i + 1) * 128], pH[5][0:64, 0:128])),
            reads=("pH5",), writes=("gT", "pH5"))
    for nb in range(4):
        for kc in range(ND):
            xs = xstream[kc % 2]
            xst = f"xs{kc % 2}"
            em.dma("scalar", xs[:], xv[:, kc, nb * 512:(nb + 1) * 512],
                   reads=(xst,), writes=(xst,))
            for m in range(8):
                em.step("tensor", lambda e, kc=kc, m=m, xs=xs: e.matmul(
                    pH[m][:, :], a1T[:, kc * KR + m * 128: kc * KR + (m + 1) * 128],
                    xs[:], start=(kc == 0), stop=(kc == ND - 1)),
                    reads=("a1T", xst) + ((f"pH{m}",) if kc == 0 else ()),
                    writes=(f"pH{m}",))
        for m in range(8):
            em.step("vector", lambda e, m=m: e.tensor_copy(hsb[:], pH[m][:, :]),
                    reads=(f"pH{m}", "hsb"), writes=("hsb", f"pH{m}"))
            em.step("tensor", lambda e, m=m, nb=nb: e.matmul(
                pH[m][:, :], e64[:, m * 128:(m + 1) * 128], gT[:, nb * 512:(nb + 1) * 512],
                start=True, stop=True),
                reads=("e64", "gT", f"pH{m}"), writes=(f"pH{m}",))
            em.step("vector", lambda e, m=m, nb=nb: e.tensor_mul(
                hgT[:, m * T + nb * 512: m * T + (nb + 1) * 512], hsb[:], pH[m][:, :]),
                reads=("hsb", f"pH{m}"), writes=("hgT",))
    for nb in range(4):
        for m2 in range(ND):
            pi = 6 + (m2 % 2)
            for kc2 in range(8):
                em.step("tensor", lambda e, m2=m2, kc2=kc2, nb=nb, pi=pi: e.matmul(
                    pH[pi][:, :], b1T[:, kc2 * D + m2 * 128: kc2 * D + (m2 + 1) * 128],
                    hgT[:, kc2 * T + nb * 512: kc2 * T + (nb + 1) * 512],
                    start=(kc2 == 0), stop=(kc2 == 7)),
                    reads=("b1T", "hgT") + ((f"pH{pi}",) if kc2 == 0 else ()),
                    writes=(f"pH{pi}",))
            em.step("scalar", lambda e, pi=pi: e.activation(
                otile[:], pH[pi][:, :], AF.Silu, scale=2.0),
                reads=(f"pH{pi}", "otile"), writes=("otile", f"pH{pi}"))
            em.dma("gpsimd", g1T_o[m2 * 128:(m2 + 1) * 128, nb * 512:(nb + 1) * 512],
                   otile[:], reads=("otile",))

    al.finish()


def l1ffn_inputs(core, xn1T):
    d = const_inputs(L1FFN_CONSTS)
    d.update({"xn1T": xn1T, "A1": core["L1_ffn_A"], "B1": core["L1_ffn_B"],
              "Wr1": core["L1_ffn_Wr"]})
    return d



COMB_CONSTS = ("eyef",)

def emit_COMB(em, io):
    nc = em.nc
    C = io
    x1 = io["x1"]
    s1T = io["s1T"]
    g1T = io["g1T"]
    out_o = io["out"]

    al = Alloc(nc)
    sb, ps = al.sb, al.ps
    eyef = sb("s_eyef", [128, 128], F32)
    x1t = [sb(f"s_x1t{i}", [128, D], F32) for i in range(2)]
    sTt = [sb(f"s_sTt{i}", [128, ND * 128], F32) for i in range(2)]
    gTt = [sb(f"s_gTt{i}", [128, ND * 128], BF16) for i in range(2)]
    dT = sb("s_dT", [128, 128], F32)
    pT = [ps(f"p_T{i}", [128, 512]) for i in range(2)]

    em.dma("sync", eyef[:], C["eyef"][:], writes=("eyef",))
    sv = s1T.rearrange("(c p) t -> p c t", p=128)
    gv = g1T.rearrange("(c p) t -> p c t", p=128)
    for i in range(NT):
        xb, xtag = x1t[i % 2], f"x1t{i % 2}"
        st, stag = sTt[i % 2], f"sTt{i % 2}"
        gt, gtag = gTt[i % 2], f"gTt{i % 2}"
        em.dma("sync", xb[:], x1[i * 128:(i + 1) * 128, :], reads=(xtag,), writes=(xtag,))
        em.dma("gpsimd", st[:].rearrange("p (c t) -> p c t", t=128),
               sv[:, :, i * 128:(i + 1) * 128], reads=(stag,), writes=(stag,))
        em.dma("scalar", gt[:].rearrange("p (c t) -> p c t", t=128),
               gv[:, :, i * 128:(i + 1) * 128], reads=(gtag,), writes=(gtag,))
        for dc in range(ND):
            pb, ptag = (pT[0], "pT0") if dc % 2 == 0 else (pT[1], "pT1")
            em.step("vector", lambda e, dc=dc, st=st, gt=gt: e.tensor_mul(
                dT[:], st[:, dc * 128:(dc + 1) * 128], gt[:, dc * 128:(dc + 1) * 128]),
                reads=(stag, gtag, "dT"), writes=("dT",))
            em.step("tensor", lambda e, pb=pb: e.transpose(pb[:, 0:128], dT[:], eyef[:]),
                    reads=("dT", "eyef", ptag), writes=(ptag,))
            em.step("vector", lambda e, dc=dc, pb=pb, xb=xb: e.tensor_add(
                xb[:, dc * 128:(dc + 1) * 128], xb[:, dc * 128:(dc + 1) * 128], pb[:, 0:128]),
                reads=(ptag, xtag), writes=(xtag, ptag))
        em.dma("sync", out_o[i * 128:(i + 1) * 128, :], xb[:], reads=(xtag,))

    al.finish()


def comb_inputs(x1, s1T, g1T):
    d = const_inputs(COMB_CONSTS)
    d.update({"x1": x1, "s1T": s1T, "g1T": g1T})
    return d



_PROGS = {}

def _prog(name, builder):
    if name not in _PROGS:
        nc = bass.Bass("TRN2", target_bir_lowering=False)
        builder(nc, strict=True)
        _PROGS[name] = nc
    return _PROGS[name]


def _run(name, builder, in_maps):
    nc = _prog(name, builder)
    res = run_bass_kernel_spmd(nc, in_maps, core_ids=list(range(NCORES)))
    return res.results


def kernel(**inputs):
    inputs = {k: np.ascontiguousarray(v) for k, v in inputs.items()}
    x_full = inputs["L0_attn_x"]

    maps = []
    for b in range(NCORES):
        core = dict(inputs)
        core["x0"] = np.ascontiguousarray(x_full[b])
        maps.append(l0_inputs(core))
    r = _run("L0", build_L0, maps)
    x1 = [r[b]["x1"] for b in range(NCORES)]
    xn1T = [r[b]["xn1T"] for b in range(NCORES)]

    maps = [inproj_inputs(inputs, xn1T[b]) for b in range(NCORES)]
    r = _run("INPROJ", build_INPROJ, maps)
    sg = [r[b]["sg"] for b in range(NCORES)]
    cvx = [r[b]["cvx"] for b in range(NCORES)]
    cvB = [r[b]["cvB"] for b in range(NCORES)]
    cvC = [r[b]["cvC"] for b in range(NCORES)]
    dtv = [r[b]["dtv"] for b in range(NCORES)]

    maps = [ssd_inputs(inputs, cvx[b], cvB[b], cvC[b], dtv[b], sg[b])
            for b in range(NCORES)]
    r = _run("SSD", build_SSD, maps)
    ygT = [r[b]["ygT"] for b in range(NCORES)]

    maps = [oproj_inputs(inputs, ygT[b]) for b in range(NCORES)]
    r = _run("OPROJ", build_OPROJ, maps)
    s1T = [r[b]["s1T"] for b in range(NCORES)]

    maps = [l1ffn_inputs(inputs, xn1T[b]) for b in range(NCORES)]
    r = _run("L1FFN", build_L1FFN, maps)
    g1T = [r[b]["g1T"] for b in range(NCORES)]

    maps = [comb_inputs(x1[b], s1T[b], g1T[b]) for b in range(NCORES)]
    r = _run("COMB", build_COMB, maps)
    out = np.stack([r[b]["out"] for b in range(NCORES)], axis=0)
    return out



PHASE_SPECS = {
    "PREFETCH": dict(
        consts=PRE_CONSTS,
        ins={"WpT": ([D, CPROJ], F32), "WoT": ([INTER, D], F32),
             "A1T": ([D, KR], F32), "B1T": ([KR, D], F32)},
        outs={},
        internal={"wbT": ([D, CPROJ], BF16), "woT": ([INTER, D], BF16),
                  "waT": ([D, KR], BF16), "wb1": ([KR, D], BF16)}),
    "L0": dict(
        consts=L0_CONSTS,
        ins={"x0": ([T, D], F32), "Wq": ([D, 128, 32], F32), "bq": ([128, 32], F32),
             "Wk": ([D, 128, 1], F32), "bk": ([128, 1], F32), "Wv": ([D, 128, 1], F32),
             "bv": ([128, 1], F32), "A0T": ([D, 128], F32), "B0T": ([128, D], F32),
             "Wr0": ([8, D], F32), "br0": ([8], F32)},
        outs={"x1": ([T, D], F32), "xn1T": ([D, T], BF16)},
        internal={}),
    "INPROJ": dict(
        consts=INPROJ_CONSTS,
        ins={"xn1T": ([D, T], BF16),
             "convw": ([CONV_DIM, 1, 4], F32), "convb": ([CONV_DIM], F32),
             "dtb": ([H], F32), "nw": ([INTER], F32)},
        outs={"sg": ([INTER, T], BF16), "cvx": ([INTER, T], BF16),
              "cvB": ([GN, T], BF16), "cvC": ([GN, T], BF16), "dtv": ([H, T], F32)},
        internal={}),
    "SSD": dict(
        consts=SSD_CONSTS,
        ins={"cvx": ([INTER, T], BF16), "cvB": ([GN, T], BF16), "cvC": ([GN, T], BF16),
             "dtv": ([H, T], F32), "sg": ([INTER, T], BF16),
             "alog": ([H], F32), "Din": ([H], F32)},
        outs={"ygT": ([INTER, T], BF16)},
        internal={"rowscr": ([128, 128], F32)}),
    "OPROJ": dict(
        consts=OPROJ_CONSTS,
        ins={"ygT": ([INTER, T], BF16)},
        outs={"s1T": ([D, T], F32)},
        internal={}),
    "L1FFN": dict(
        consts=L1FFN_CONSTS,
        ins={"xn1T": ([D, T], BF16), "Wr1T": ([D, NEXP], F32)},
        outs={"g1T": ([D, T], BF16)},
        internal={}),
    "COMB": dict(
        consts=COMB_CONSTS,
        ins={"x1": ([T, D], F32), "s1T": ([D, T], F32), "g1T": ([D, T], BF16)},
        outs={"out": ([T, D], F32)},
        internal={}),
}

EMITTERS = {"PREFETCH": emit_PREFETCH, "L0": emit_L0, "INPROJ": emit_INPROJ,
            "SSD": emit_SSD, "OPROJ": emit_OPROJ, "L1FFN": emit_L1FFN,
            "COMB": emit_COMB}


def full_barrier(em, targets=None):
    if targets is None:
        targets = {ch: c for ch, c in em.count.items() if c > 0}
    else:
        targets = {ch: c for ch, c in targets.items() if c > 0}
    for e in ENGINES:
        kn = em.know[e]
        eff = {}
        for ch2, c in targets.items():
            if ch2 == (e, "c"):
                continue
            if kn[ch2] < c:
                eff[ch2] = c
        for ch2, c in eff.items():
            s = em.snap[ch2][c]
            for g in CHANNELS:
                kn[g] = max(kn[g], s[g])
            kn[ch2] = max(kn[ch2], c)
        ch = (e, "c")
        em.count[ch] += 1
        cnt = em.count[ch]
        em.steps[e].append(((lambda e2: e2.nop()), eff, 1, ch))
        snap = dict(kn)
        snap[ch] = cnt
        em.snap[ch][cnt] = snap


def build_phase(nc, phase, strict=True):
    em = Emitter(nc, strict)
    spec = PHASE_SPECS[phase]
    io = declare_consts(nc, spec["consts"])
    for k, (shape, dt) in spec["ins"].items():
        io[k] = nc.dram_tensor(k, shape, dt, kind="ExternalInput").ap()
    for k, (shape, dt) in spec["outs"].items():
        io[k] = nc.dram_tensor(k, shape, dt, kind="ExternalOutput").ap()
    for k, (shape, dt) in spec["internal"].items():
        io[k] = nc.dram_tensor(k, shape, dt, kind="Internal").ap()
    EMITTERS[phase](em, io)
    em.barrier_step("sync")
    em.realize()
    em.finish()
    return em


PHASE_ORDER = ["L0", "INPROJ", "SSD", "OPROJ", "L1FFN", "COMB"]
DECL_ORDER = ["PREFETCH"] + PHASE_ORDER


def build_FUSED(nc, strict=True):
    em = Emitter(nc, strict)
    io = declare_consts(nc, tuple(CONST_SPECS.keys()))
    declared = {}
    produced = set()
    for ph in PHASE_ORDER:
        for k in PHASE_SPECS[ph]["outs"]:
            produced.add(k)
    produced.discard("out")
    for ph in DECL_ORDER:
        spec = PHASE_SPECS[ph]
        for k, (shape, dt) in spec["ins"].items():
            if k not in declared:
                kind = "Internal" if k in produced else "ExternalInput"
                declared[k] = nc.dram_tensor(k, shape, dt, kind=kind).ap()
        for k, (shape, dt) in spec["outs"].items():
            if k not in declared:
                kind = "ExternalOutput" if k == "out" else "Internal"
                declared[k] = nc.dram_tensor(k, shape, dt, kind=kind).ap()
        for k, (shape, dt) in spec["internal"].items():
            if k not in declared:
                declared[k] = nc.dram_tensor(k, shape, dt, kind="Internal").ap()
    io.update(declared)
    EMITTERS["L0"](em, io)
    snap0 = dict(em.count)
    EMITTERS["PREFETCH"](em, io)
    full_barrier(em, targets=snap0)
    for ph in PHASE_ORDER[1:]:
        EMITTERS[ph](em, io)
        full_barrier(em)
    em.barrier_step("sync")
    em.realize()
    em.finish()
    return em


def kernel(**inputs):
    inputs = {k: np.ascontiguousarray(v) for k, v in inputs.items()}
    x_full = inputs["L0_attn_x"]
    if "FUSED" not in _PROGS:
        nc = bass.Bass("TRN2", target_bir_lowering=False)
        build_FUSED(nc, strict=True)
        _PROGS["FUSED"] = nc
    nc = _PROGS["FUSED"]
    base = const_inputs(tuple(CONST_SPECS.keys()))
    ct = np.ascontiguousarray
    base.update({
        "Wq": inputs["L0_attn_Wq"], "bq": inputs["L0_attn_bq"],
        "Wk": inputs["L0_attn_Wk"], "bk": inputs["L0_attn_bk"],
        "Wv": inputs["L0_attn_Wv"], "bv": inputs["L0_attn_bv"],
        "A0T": ct(inputs["L0_ffn_A"].reshape(128, D).T),
        "B0T": ct(inputs["L0_ffn_B"].transpose(0, 2, 1).reshape(128, D)),
        "Wr0": inputs["L0_ffn_Wr"], "br0": inputs["L0_ffn_br"],
        "WpT": ct(inputs["L1_attn_in_proj_w"].T),
        "convw": inputs["L1_attn_conv_w"], "convb": inputs["L1_attn_conv_b"],
        "dtb": inputs["L1_attn_dt_bias"], "nw": inputs["L1_attn_norm_w"],
        "alog": inputs["L1_attn_A_log"], "Din": inputs["L1_attn_D"],
        "WoT": ct(inputs["L1_attn_out_proj_w"].T),
        "A1T": ct(inputs["L1_ffn_A"].reshape(KR, D).T),
        "B1T": ct(inputs["L1_ffn_B"].transpose(0, 2, 1).reshape(KR, D)),
        "Wr1T": ct(inputs["L1_ffn_Wr"].T),
    })
    in_maps = []
    for b in range(NCORES):
        m = dict(base)
        m["x0"] = np.ascontiguousarray(x_full[b])
        in_maps.append(m)
    res = run_bass_kernel_spmd(nc, in_maps, core_ids=list(range(NCORES)))
    return np.stack([res.results[b]["out"] for b in range(NCORES)], axis=0)


# version 10
# speedup vs baseline: 1.4286x; 1.0557x over previous
import numpy as np
import ml_dtypes
import concourse.bass as bass
import concourse.mybir as mybir
from concourse.bass_utils import run_bass_kernel_spmd

F32 = mybir.dt.float32
BF16 = mybir.dt.bfloat16
AF = mybir.ActivationFunctionType
ALU = mybir.AluOpType
AX = mybir.AxisListType
bf16 = ml_dtypes.bfloat16

NCORES = 8
T = 2048
D = 4096
NT = T // 128
ND = D // 128

H = 128; P = 64; G = 8; N = 128; CS = 128; K4 = 4
INTER = H * P
GN = G * N
CONV_DIM = INTER + 2 * GN
CPROJ = INTER + CONV_DIM + H
NCHUNK = T // CS
NEXP = 64; TOPK = 16; RANK = 16
KR = NEXP * RANK

INV_SQRT_QK = float(1.0 / np.sqrt(128.0))

ENGINES = ("tensor", "vector", "scalar", "gpsimd", "sync")
NDMACH = 4
CHANNELS = ([(e, "c") for e in ENGINES]
            + [(e, f"d{i}") for e in ("sync", "gpsimd", "scalar", "vector")
               for i in range(NDMACH)])


class Emitter:

    def __init__(self, nc, strict=False):
        self.nc = nc
        self.strict = strict
        self.steps = {e: [] for e in ENGINES}
        self.count = {ch: 0 for ch in CHANNELS}
        self.know = {e: {ch: 0 for ch in CHANNELS} for e in ENGINES}
        self.snap = {ch: {0: {c2: 0 for c2 in CHANNELS}} for ch in CHANNELS}
        self.last_writer = {}
        self.readers = {}
        self.dma_idx = {e: 0 for e in ENGINES}
        self.n_waits = 0
        self.audit = []

    def step(self, eng, fn, reads=(), writes=(), dma=False):
        if dma:
            sub = self.dma_idx[eng] % NDMACH
            self.dma_idx[eng] += 1
            ch = (eng, f"d{sub}")
        else:
            ch = (eng, "c")
        waits = {}
        def add_need(src):
            (sch, c) = src
            if sch == ch and not (self.strict and sch[1] == "c" and eng != "tensor"):
                return
            if sch[0] == eng and sch[1] == "c" and not dma and (
                    not self.strict or eng == "tensor"):
                return
            waits[sch] = max(waits.get(sch, 0), c)
        for t_ in reads:
            lw = self.last_writer.get(t_)
            if lw is not None:
                add_need(lw)
        for t_ in writes:
            lw = self.last_writer.get(t_)
            if lw is not None:
                add_need(lw)
            for r in self.readers.get(t_, ()):
                add_need(r)
        if dma and self.count[ch] > 0:
            waits[ch] = max(waits.get(ch, 0), self.count[ch])
        kn = self.know[eng]
        eff = {ch2: c for ch2, c in waits.items() if kn[ch2] < c}
        for ch2, c in eff.items():
            s = self.snap[ch2][c]
            for g in CHANNELS:
                kn[g] = max(kn[g], s[g])
            kn[ch2] = max(kn[ch2], c)
        amount = 16 if dma else 1
        self.count[ch] += amount
        cnt = self.count[ch]
        self.steps[eng].append((fn, eff, amount, ch))
        self.n_waits += len(eff)
        for t_ in writes:
            self.last_writer[t_] = (ch, cnt)
            self.readers[t_] = []
        for t_ in reads:
            self.readers.setdefault(t_, []).append((ch, cnt))
        snap = dict(kn)
        snap[ch] = cnt
        self.snap[ch][cnt] = snap
        self.audit.append((eng, len(self.steps[eng]) - 1, tuple(reads),
                           tuple(writes), ch, cnt, dma))
        return cnt

    def dma(self, queue, out, in_, reads=(), writes=(), **kw):
        self.step(queue, lambda e: e.dma_start(out=out, in_=in_, **kw),
                  reads=reads, writes=writes, dma=True)

    def barrier_step(self, eng="sync"):
        targets = {ch: c for ch, c in self.count.items() if c > 0}
        kn = self.know[eng]
        eff = {ch: c for ch, c in targets.items()
               if kn[ch] < c and not (ch[0] == eng and ch[1] == "c")}
        for ch2, c in eff.items():
            s = self.snap[ch2][c]
            for g in CHANNELS:
                kn[g] = max(kn[g], s[g])
            kn[ch2] = max(kn[ch2], c)
        ch = (eng, "c")
        self.count[ch] += 1
        cnt = self.count[ch]
        self.steps[eng].append((lambda e: e.nop(), eff, 1, ch))
        snap = dict(kn)
        snap[ch] = cnt
        self.snap[ch][cnt] = snap

    def realize(self):
        nc = self.nc
        self._cms = []
        sems = {}
        for ch in CHANNELS:
            cm = nc.semaphore(f"sem_{ch[0]}_{ch[1]}")
            sems[ch] = cm.__enter__()
            self._cms.append(cm)
        block_cm = nc.Block()
        block = block_cm.__enter__()
        self._cms.append(block_cm)

        def emit_for(e):
            def body(eng):
                for (fn, waits, amount, ch) in self.steps[e]:
                    for ch2, c in waits.items():
                        eng.wait_ge(sems[ch2], c)
                    inst = fn(eng)
                    inst.then_inc(sems[ch], amount)
            return body

        for e in ENGINES:
            if self.steps[e]:
                getattr(block, e)(emit_for(e))
        return block

    def finish(self):
        for cm in reversed(self._cms):
            cm.__exit__(None, None, None)



_CONSTS = None

def build_consts():
    global _CONSTS
    if _CONSTS is not None:
        return _CONSTS
    c = {}
    c["eyeb"] = np.eye(128, dtype=bf16)
    c["eyef"] = np.eye(128, dtype=np.float32)
    c["onesf"] = np.ones((1, 128), dtype=np.float32)
    sel = np.zeros((128, 32 * 128), dtype=bf16)
    for q in range(32):
        for f in range(128):
            sel[4 * q + f // 32, q * 128 + f] = 1
    c["selq"] = sel
    e8 = np.zeros((8, 128), dtype=bf16)
    for j in range(128):
        e8[j // 16, j] = 1
    c["e8"] = e8
    e64 = np.zeros((64, 1024), dtype=bf16)
    for j in range(1024):
        e64[j // 16, j] = 1
    c["e64"] = e64
    c["cumU"] = np.triu(np.ones((128, 128), dtype=np.float32))
    c["mask01"] = np.triu(np.ones((128, 128), dtype=np.float32))
    _CONSTS = {k: np.ascontiguousarray(v) for k, v in c.items()}
    return _CONSTS


CONST_SPECS = {
    "eyeb": BF16, "eyef": F32, "onesf": F32, "selq": BF16,
    "e8": BF16, "e64": BF16, "cumU": F32, "mask01": F32,
}


def lowprec(e, f):
    with e.bass.allow_low_precision(reason="bf16 weight fold / operand cast"):
        return f()


def declare_consts(nc, names):
    cvals = build_consts()
    return {k: nc.dram_tensor("c_" + k, list(cvals[k].shape), CONST_SPECS[k],
                              kind="ExternalInput").ap() for k in names}


def const_inputs(names):
    cvals = build_consts()
    return {"c_" + k: cvals[k] for k in names}


class Alloc:
    _n = 0

    def __init__(self, nc):
        self.nc = nc
        self.cms = []
        Alloc._n += 1
        self.prefix = f"a{Alloc._n}_"

    def sb(self, name, shape, dt):
        cm = self.nc.sbuf_tensor(self.prefix + name, shape, dt)
        t = cm.__enter__(); self.cms.append(cm); return t

    def ps(self, name, shape, dt=F32):
        cm = self.nc.psum_tensor(self.prefix + name, shape, dt)
        t = cm.__enter__(); self.cms.append(cm); return t

    def finish(self):
        for cm in reversed(self.cms):
            cm.__exit__(None, None, None)



L0_CONSTS = ("eyeb", "eyef", "selq", "e8")

def emit_L0(em, io, tile_hook=None):
    nc = em.nc
    C = io
    x0 = io["x0"]
    Wq = io["Wq"]
    bq = io["bq"]
    Wk = io["Wk"]
    bk = io["bk"]
    Wv = io["Wv"]
    bv = io["bv"]
    A0T = io["A0T"]
    B0T = io["B0T"]
    Wr0 = io["Wr0"]
    br0 = io["br0"]
    x1_o = io["x1"]
    xn1T_o = io["xn1T"]

    al = Alloc(nc)
    sb, ps = al.sb, al.ps

    eyeb = sb("s_eyeb", [128, 128], BF16)
    eyef = sb("s_eyef", [128, 128], F32)
    selq = sb("s_selq", [128, 32 * 128], BF16)
    e8 = sb("s_e8", [8, 128], BF16)
    wqs = sb("s_wqs", [128, ND * 128], BF16)
    wkT = sb("s_wkT", [128, ND * 128], BF16)
    wvT = sb("s_wvT", [128, ND * 128], BF16)
    a0T = sb("s_a0T", [128, ND * 128], BF16)
    b0T = sb("s_b0T", [128, ND * 128], BF16)
    wr0T = sb("s_wr0T", [128, ND * 8], BF16)
    bqs = sb("s_bqs", [128, 1], F32)
    bkc = sb("s_bkc", [128, 1], F32)
    bvc = sb("s_bvc", [128, 1], F32)
    br0c = sb("s_br0c", [8, 1], F32)
    bq_stage = sb("s_bqstage", [128, 32], F32)
    epscol = sb("s_epscol", [128, 1], F32)

    mat = sb("s_mat", [128, 128 * 128], BF16)
    wq_stage = [mat[:, 0:4096], mat[:, 4096:8192]]
    wr0_stage = mat[0:8, 12288:16384]
    PREP_TAGS = ("mq0", "mq1", "mw")

    x0t = [sb(f"s_x0t{i}", [128, D], F32) for i in range(2)]
    stat = sb("s_stat", [128, 8], F32)
    g0T = sb("s_g0T", [128, ND * 128], F32)
    d0T = sb("s_d0T", [128, ND * 128], F32)
    xnb = sb("s_xnb", [128, D], BF16)
    xnT = sb("s_xnT", [128, ND * 128], BF16)
    a_T = sb("s_aT", [128, 128], F32)
    k_T = sb("s_kT", [128, 128], F32)
    v_T = sb("s_vT", [128, 128], F32)
    a_tm = sb("s_atm", [128, 128], F32)
    k_tm = sb("s_ktm", [128, 128], F32)
    v_tm = sb("s_vtm", [128, 128], F32)
    kmm = sb("s_kmm", [128, 4], F32)
    negm = sb("s_negm", [128, 128], F32)
    s1col = sb("s_s1", [128, 128], F32)
    s2col = sb("s_s2", [128, 128], F32)
    ttg = [sb(f"s_ttg{i}", [128, 128], F32) for i in range(4)]
    o_tm = sb("s_otm", [128, 128], F32)
    o_Tb = sb("s_oTb", [128, 128], BF16)
    rt_f = sb("s_rtf", [8, 128], F32)
    rt_tm = sb("s_rttm", [128, 8], F32)
    rt_scr = sb("s_rtscr", [128, 8], F32)
    g8T = sb("s_g8T", [8, 128], BF16)
    scoreB = sb("s_scoreB", [128, 128], F32)
    glow = sb("s_glow", [128, 128], BF16)
    x1t = sb("s_x1t", [128, D], F32)
    xn1b = sb("s_xn1b", [128, D], BF16)
    xn1Tt = sb("s_xn1Tt", [128, ND * 128], BF16)

    pA = ps("p_A", [128, 512])
    pB = ps("p_B", [128, 512])
    pC = ps("p_C", [128, 512])
    pD = ps("p_D", [128, 512])
    pT1 = ps("p_T1", [128, 512])
    pT2 = ps("p_T2", [128, 512])
    pb1 = ps("p_b1", [128, 1024], BF16)
    pb2 = ps("p_b2", [128, 1024], BF16)

    em.step("gpsimd", lambda e: e.memset(epscol[:], 1e-5), writes=("epscol",))
    em.dma("sync", eyeb[:], C["eyeb"][:], writes=("eyeb",))
    em.dma("sync", eyef[:], C["eyef"][:], writes=("eyef",))
    em.dma("sync", selq[:], C["selq"][:], writes=("selq",))
    em.dma("sync", e8[:], C["e8"][:], writes=("e8",))
    em.dma("gpsimd", bq_stage[:], bq[:], writes=("bq_stage",))
    em.step("vector", lambda e: e.reduce_sum(bqs[:], bq_stage[:], axis=AX.X),
            reads=("bq_stage",), writes=("bqs",))
    em.dma("gpsimd", bkc[:], bk[:], writes=("bkc",))
    em.dma("gpsimd", bvc[:], bv[:], writes=("bvc",))
    em.dma("gpsimd", br0c[:], br0[:].rearrange("h -> h ()"), writes=("br0c",))

    for kc in range(ND):
        wst = wq_stage[kc % 2]
        wtg = f"mq{kc % 2}"
        em.dma("gpsimd", wst[:],
               Wq[kc * 128:(kc + 1) * 128, :, :].rearrange("p d h -> p (d h)"),
               reads=(wtg,), writes=(wtg,))
        em.step("vector", lambda e, kc=kc, wst=wst: lowprec(e, lambda: e.reduce_sum(
            wqs[:, kc * 128:(kc + 1) * 128].rearrange("p (d a) -> p d a", a=1),
            wst[:].rearrange("p (d h) -> p d h", h=32), axis=AX.X)),
            reads=(wtg,), writes=("wqs",))
        em.dma("gpsimd", wkT[:, kc * 128:(kc + 1) * 128],
               Wk[kc * 128:(kc + 1) * 128, :, 0], writes=("wkT",))
        em.dma("gpsimd", wvT[:, kc * 128:(kc + 1) * 128],
               Wv[kc * 128:(kc + 1) * 128, :, 0], writes=("wvT",))

    em.dma("gpsimd", a0T[:].rearrange("p (c h) -> p c h", h=128),
           A0T.rearrange("(c p) h -> p c h", p=128), writes=("a0T",))
    em.dma("gpsimd", b0T[:], B0T[:], writes=("b0T",))
    em.dma("gpsimd", wr0_stage[:], Wr0[:], writes=("mw",))
    for kc in range(ND):
        pt, ptag = (pb1, "pb1") if kc % 2 == 0 else (pb2, "pb2")
        em.step("tensor", lambda e, kc=kc, pt=pt: e.transpose(
            pt[0:128, 0:8], wr0_stage[:, kc * 128:(kc + 1) * 128], eyeb[0:8, 0:8]),
            reads=("mw", "eyeb", ptag), writes=(ptag,))
        em.step("vector", lambda e, kc=kc, pt=pt: e.tensor_copy(
            wr0T[:, kc * 8:(kc + 1) * 8], pt[0:128, 0:8]),
            reads=(ptag,), writes=("wr0T", ptag))

    def emit_layernorm(src, out_bf, tag_src, tag_out, dump, tag_dump):
        em.step("vector", lambda e: e.reduce_sum(stat[:, 0:1], src[:], axis=AX.X),
                reads=(tag_src,), writes=("stat",))
        em.step("scalar", lambda e: e.activation(
            dump[:], src[:], AF.Square, accum_out=stat[:, 1:2]),
            reads=(tag_src,), writes=("stat", tag_dump))
        em.step("vector", lambda e: e.tensor_scalar(
            stat[:, 2:3], stat[:, 0:1], 1.0 / D, None, ALU.mult),
            reads=("stat",), writes=("stat",))
        em.step("vector", lambda e: e.tensor_mul(stat[:, 5:6], stat[:, 2:3], stat[:, 2:3]),
                reads=("stat",), writes=("stat",))
        em.step("vector", lambda e: e.scalar_tensor_tensor(
            stat[:, 3:4], stat[:, 1:2], 1.0 / D, stat[:, 5:6], ALU.mult, ALU.subtract),
            reads=("stat",), writes=("stat",))
        em.step("scalar", lambda e: e.activation(
            stat[:, 6:7], stat[:, 3:4], AF.Sqrt, bias=epscol[:]),
            reads=("stat", "epscol"), writes=("stat",))
        em.step("vector", lambda e: e.reciprocal(stat[:, 4:5], stat[:, 6:7]),
                reads=("stat",), writes=("stat",))
        em.step("vector", lambda e: e.tensor_scalar(
            out_bf[:], src[:], stat[:, 2:3], stat[:, 4:5], ALU.subtract, ALU.mult),
            reads=(tag_src, "stat"), writes=(tag_out,))

    def emit_transpose_tm2fm(src_bf, dst, tag_src, tag_dst):
        for dc in range(ND):
            pt, ptag = (pb1, "pb1") if dc % 2 == 0 else (pb2, "pb2")
            em.step("tensor", lambda e, dc=dc, pt=pt: e.transpose(
                pt[:, 0:128], src_bf[:, dc * 128:(dc + 1) * 128], eyeb[:]),
                reads=(tag_src, "eyeb", ptag), writes=(ptag,))
            em.step("vector", lambda e, dc=dc, pt=pt: e.tensor_copy(
                dst[:, dc * 128:(dc + 1) * 128], pt[:, 0:128]),
                reads=(ptag,), writes=(tag_dst, ptag))

    for i in range(NT):
        if tile_hook is not None:
            tile_hook(i)
        xb = x0t[i % 2]
        xtag = f"x0t{i % 2}"
        em.dma("sync", xb[:], x0[i * 128:(i + 1) * 128, :], writes=(xtag,))
        emit_layernorm(xb, xnb, xtag, "xnb", g0T, "g0T")
        emit_transpose_tm2fm(xnb, xnT, "xnb", "xnT")

        def mm_acc(psum, pslice, lhs_buf, tag_lhs, lhs_w, m_width, ptag):
            for kc in range(ND):
                em.step("tensor", lambda e, kc=kc: e.matmul(
                    psum[0:m_width, pslice[0]:pslice[1]],
                    lhs_buf[:, kc * lhs_w: kc * lhs_w + m_width],
                    xnT[:, kc * 128:(kc + 1) * 128],
                    start=(kc == 0), stop=(kc == ND - 1)),
                    reads=(tag_lhs, "xnT") + ((ptag,) if kc == 0 else ()),
                    writes=(ptag,))
        mm_acc(pA, (0, 128), wqs, "wqs", 128, 128, "pA")
        mm_acc(pB, (0, 128), wkT, "wkT", 128, 128, "pB")
        mm_acc(pC, (0, 128), wvT, "wvT", 128, 128, "pC")
        em.step("vector", lambda e: e.tensor_scalar(
            a_T[:], pA[:, 0:128], bqs[:], INV_SQRT_QK, ALU.add, ALU.mult),
            reads=("pA", "bqs"), writes=("a_T",))
        em.step("vector", lambda e: e.tensor_scalar(
            k_T[:], pB[:, 0:128], bkc[:], None, ALU.add),
            reads=("pB", "bkc"), writes=("k_T",))
        em.step("vector", lambda e: e.tensor_scalar(
            v_T[:], pC[:, 0:128], bvc[:], None, ALU.add),
            reads=("pC", "bvc"), writes=("v_T",))
        for (srcb, dstb, st, dt_) in ((a_T, a_tm, "a_T", "a_tm"),
                                      (k_T, k_tm, "k_T", "k_tm"),
                                      (v_T, v_tm, "v_T", "v_tm")):
            em.step("tensor", lambda e, srcb=srcb: e.transpose(
                pT1[:, 0:128], srcb[:], eyef[:]),
                reads=(st, "eyef", "pT1"), writes=("pT1",))
            em.step("vector", lambda e, dstb=dstb: e.tensor_copy(dstb[:], pT1[:, 0:128]),
                    reads=("pT1",), writes=(dt_, "pT1"))
        em.step("vector", lambda e: e.reduce_max(kmm[:, 0:1], k_tm[:], axis=AX.X),
                reads=("k_tm",), writes=("kmm",))
        em.step("vector", lambda e: e.tensor_reduce(kmm[:, 1:2], k_tm[:], AX.X, ALU.min),
                reads=("k_tm",), writes=("kmm",))
        em.step("vector", lambda e: e.tensor_scalar(
            kmm[:, 2:4], kmm[:, 0:2], -1.0, None, ALU.mult),
            reads=("kmm",), writes=("kmm",))
        em.step("vector", lambda e: e.tensor_scalar_mul(negm[:], a_tm[:], kmm[:, 2:3]),
                reads=("a_tm", "kmm"), writes=("negm",))
        em.step("vector", lambda e: e.scalar_tensor_tensor(
            negm[:], a_tm[:], kmm[:, 3:4], negm[:], ALU.mult, ALU.min),
            reads=("a_tm", "kmm", "negm"), writes=("negm",))
        for d in range(128):
            mtg = f"mat{d % 8}"
            wtags_d = (mtg, "s1col") + (
                PREP_TAGS if (i == 0 and d == 0) else ())
            em.step("scalar", lambda e, d=d: e.activation(
                mat[:, d * 128:(d + 1) * 128], k_tm[:], AF.Exp,
                bias=negm[:, d:d + 1], scale=a_tm[:, d:d + 1],
                accum_out=s1col[:, d:d + 1]),
                reads=("k_tm", "negm", "a_tm"), writes=wtags_d)
            tg = f"ttg{d % 4}"
            em.step("gpsimd", lambda e, d=d: e.tensor_tensor(
                ttg[d % 4][:], mat[:, d * 128:(d + 1) * 128], v_tm[:], ALU.mult),
                reads=(mtg, "v_tm"), writes=(tg,))
            em.step("vector", lambda e, d=d: e.reduce_sum(
                s2col[:, d:d + 1], ttg[d % 4][:], axis=AX.X),
                reads=(tg,), writes=("s2v",))
        em.step("vector", lambda e: e.reciprocal(s1col[:], s1col[:]),
                reads=("s1col",), writes=("s1col",))
        em.step("vector", lambda e: e.tensor_mul(o_tm[:], s2col[:], s1col[:]),
                reads=("s2v", "s1col"), writes=("o_tm",))
        em.step("tensor", lambda e: e.transpose(pT1[:, 0:128], o_tm[:], eyef[:]),
                reads=("o_tm", "eyef", "pT1"), writes=("pT1",))
        em.step("vector", lambda e: e.tensor_copy(o_Tb[:], pT1[:, 0:128]),
                reads=("pT1",), writes=("o_Tb", "pT1"))

        mm_acc(pD, (0, 128), wr0T, "wr0T", 8, 8, "pD")
        em.step("vector", lambda e: e.tensor_scalar(
            rt_f[:], pD[0:8, 0:128], br0c[:], None, ALU.add),
            reads=("pD", "br0c"), writes=("rt_f",))
        em.step("tensor", lambda e: e.transpose(pT1[0:128, 0:8], rt_f[:], eyef[0:8, 0:8]),
                reads=("rt_f", "eyef", "pT1"), writes=("pT1",))
        em.step("vector", lambda e: e.tensor_copy(rt_tm[:], pT1[0:128, 0:8]),
                reads=("pT1",), writes=("rt_tm", "pT1"))
        em.step("vector", lambda e: e.reduce_max(stat[:, 7:8], rt_tm[:], axis=AX.X),
                reads=("rt_tm",), writes=("stat",))
        em.step("vector", lambda e: e.tensor_scalar(
            stat[:, 7:8], stat[:, 7:8], -1.0, None, ALU.mult),
            reads=("stat",), writes=("stat",))
        em.step("scalar", lambda e: e.activation(
            rt_scr[:], rt_tm[:], AF.Exp, bias=stat[:, 7:8], accum_out=stat[:, 6:7]),
            reads=("rt_tm", "stat"), writes=("rt_scr", "stat"))
        em.step("vector", lambda e: e.reciprocal(stat[:, 6:7], stat[:, 6:7]),
                reads=("stat",), writes=("stat",))
        em.step("vector", lambda e: e.tensor_scalar(
            rt_scr[:], rt_scr[:], stat[:, 6:7], 8.0, ALU.mult, ALU.mult),
            reads=("rt_scr", "stat"), writes=("rt_scr",))
        em.step("tensor", lambda e: e.transpose(pT1[0:8, 0:128], rt_scr[:], eyef[:]),
                reads=("rt_scr", "eyef", "pT1"), writes=("pT1",))
        em.step("vector", lambda e: e.tensor_copy(g8T[:], pT1[0:8, 0:128]),
                reads=("pT1",), writes=("g8T", "pT1"))
        em.step("tensor", lambda e: e.matmul(
            pD[:, 128:256], e8[:], g8T[:], start=True, stop=True),
            reads=("e8", "g8T", "pD"), writes=("pD",))
        em.step("vector", lambda e: e.tensor_copy(scoreB[:], pD[:, 128:256]),
                reads=("pD",), writes=("scoreB", "pD"))
        mm_acc(pA, (128, 256), a0T, "a0T", 128, 128, "pA")
        em.step("vector", lambda e: e.tensor_mul(glow[:], pA[:, 128:256], scoreB[:]),
                reads=("pA", "scoreB"), writes=("glow",))
        for oc in range(ND):
            pt, ptg = (pA, "pA") if oc % 2 == 0 else (pB, "pB")
            em.step("tensor", lambda e, oc=oc, pt=pt: e.matmul(
                pt[:, 256:384], b0T[:, oc * 128:(oc + 1) * 128], glow[:],
                start=True, stop=True),
                reads=("b0T", "glow", ptg), writes=(ptg,))
            em.step("scalar", lambda e, oc=oc, pt=pt: e.activation(
                g0T[:, oc * 128:(oc + 1) * 128], pt[:, 256:384], AF.Silu, scale=2.0),
                reads=(ptg,), writes=("g0T",))
        for q in range(ND):
            pt, ptg = (pC, "pC") if q % 2 == 0 else (pD, "pD")
            em.step("tensor", lambda e, q=q, pt=pt: e.matmul(
                pt[:, 256:384], selq[:, q * 128:(q + 1) * 128], o_Tb[:],
                start=True, stop=True),
                reads=("selq", "o_Tb", ptg), writes=(ptg,))
            em.step("vector", lambda e, q=q, pt=pt: e.tensor_mul(
                d0T[:, q * 128:(q + 1) * 128], pt[:, 256:384],
                g0T[:, q * 128:(q + 1) * 128]),
                reads=(ptg, "g0T"), writes=("d0T",))
        for dc in range(ND):
            pt, ptg = (pT1, "pT1") if dc % 2 == 0 else (pT2, "pT2")
            em.step("tensor", lambda e, dc=dc, pt=pt: e.transpose(
                pt[:, 0:128], d0T[:, dc * 128:(dc + 1) * 128], eyef[:]),
                reads=("d0T", "eyef", ptg), writes=(ptg,))
            em.step("vector", lambda e, dc=dc, pt=pt, xb=xb: e.tensor_add(
                x1t[:, dc * 128:(dc + 1) * 128],
                xb[:, dc * 128:(dc + 1) * 128], pt[:, 0:128]),
                reads=(ptg, xtag, "x1t"), writes=("x1t", ptg))
        em.dma("sync", x1_o[i * 128:(i + 1) * 128, :], x1t[:], reads=("x1t",))
        emit_layernorm(x1t, xn1b, "x1t", "xn1b", g0T, "g0T")
        emit_transpose_tm2fm(xn1b, xn1Tt, "xn1b", "xn1Tt")
        em.dma("sync",
               xn1T_o.rearrange("(c p) t -> p c t", p=128)[:, :, i * 128:(i + 1) * 128],
               xn1Tt[:].rearrange("p (c t) -> p c t", t=128),
               reads=("xn1Tt",))

    al.finish()


def fix_wkv_dma():
    pass


def l0_inputs(core):
    d = const_inputs(L0_CONSTS)
    d.update({
        "x0": core["x0"],
        "Wq": core["L0_attn_Wq"], "bq": core["L0_attn_bq"],
        "Wk": core["L0_attn_Wk"], "bk": core["L0_attn_bk"],
        "Wv": core["L0_attn_Wv"], "bv": core["L0_attn_bv"],
        "A0": core["L0_ffn_A"], "B0": core["L0_ffn_B"],
        "Wr0": core["L0_ffn_Wr"], "br0": core["L0_ffn_br"],
    })
    return d


def kernel(**inputs):
    raise NotImplementedError("under construction")



PRE_CONSTS = ()

def prefetch_chunks(em, io):
    WpT = io["WpT"]
    WoT = io["WoT"]
    A1T = io["A1T"]
    B1T = io["B1T"]
    wbT = io["wbT"]
    woT = io["woT"]
    waT = io["waT"]
    wb1 = io["wb1"]
    thunks = []
    for ci in range(4):
        for h in range(4):
            r0 = ci * 1024 + h * 256
            thunks.append(lambda ci=ci, r0=r0: em.dma(
                "gpsimd", wbT[r0:r0 + 256, :], WpT[r0:r0 + 256, :],
                writes=(f"wbT{ci}",)))
    for ci in range(8):
        for h in range(2):
            r0 = ci * 1024 + h * 512
            thunks.append(lambda ci=ci, r0=r0: em.dma(
                "gpsimd", woT[r0:r0 + 512, :], WoT[r0:r0 + 512, :],
                writes=(f"woT{ci}",)))
    thunks.append(lambda: em.dma("gpsimd", waT[:], A1T[:], writes=("waT",)))
    thunks.append(lambda: em.dma("gpsimd", wb1[:], B1T[:], writes=("wb1",)))
    return thunks



INPROJ_CONSTS = ("eyef",)
NM_PROJ = CPROJ // 128
CAST_ROWS = 512

def emit_INPROJ(em, io):
    nc = em.nc
    C = io
    xn1T = io["xn1T"]
    wbT = io["wbT"]
    convw = io["convw"]
    convb = io["convb"]
    dtb = io["dtb"]
    nw = io["nw"]
    sg_o = io["sg"]
    cvx_o = io["cvx"]
    cvB_o = io["cvB"]
    cvC_o = io["cvC"]
    dtv_o = io["dtv"]

    al = Alloc(nc)
    sb, ps = al.sb, al.ps
    eyef = sb("s_eyef", [128, 128], F32)
    xn1sb = sb("s_xn1sb", [128, ND * T], BF16)
    lbuf = [sb(f"s_lbuf{i}", [128, ND * 256], BF16) for i in range(2)]
    cvpad = [sb(f"s_cvpad{i}", [128, T + 3], F32) for i in range(2)]
    acc = [sb(f"s_acc{i}", [128, T], F32) for i in range(2)]
    otile = [sb(f"s_otile{i}", [128, T], BF16) for i in range(2)]
    cw = sb("s_cw", [128, 80 * 4], F32)
    cb = sb("s_cb", [128, 80], F32)
    nwc = sb("s_nwc", [128, 64], F32)
    dtbc = sb("s_dtbc", [128, 1], F32)
    onec = sb("s_onec", [128, 1], F32)
    stage = sb("s_stage", [128, 128], F32)
    pP = [ps(f"p_P{i}", [128, 512]) for i in range(8)]

    em.dma("sync", eyef[:], C["eyef"][:], writes=("eyef",))
    for i in range(2):
        em.step("gpsimd", lambda e, i=i: e.memset(cvpad[i][:, 0:3], 0.0),
                writes=(f"cvp{i}",))
    em.step("gpsimd", lambda e: e.memset(onec[:], 1.0), writes=("onec",))
    xv = xn1T.rearrange("(c p) t -> p c t", p=128)
    for j in range(4):
        em.dma("sync", xn1sb[:].rearrange("p (c t) -> p c t", t=T)[:, j * 8:(j + 1) * 8, :],
               xv[:, j * 8:(j + 1) * 8, :], writes=("xn1sb",))
    for j in range(80):
        em.dma("scalar", cw[:, j * 4:(j + 1) * 4],
               convw[:, 0, :].rearrange("(n p) j -> p n j", p=128)[:, j, :],
               writes=("cw",))
    em.dma("gpsimd", stage[0:80, :], convb[:].rearrange("(n p) -> n p", p=128),
           writes=("stage",))
    em.step("tensor", lambda e: e.transpose(pP[0][0:128, 0:80], stage[0:80, :], eyef[0:80, 0:80]),
            reads=("stage", "eyef", "pP0"), writes=("pP0",))
    em.step("vector", lambda e: e.tensor_copy(cb[:], pP[0][0:128, 0:80]),
            reads=("pP0",), writes=("cb", "pP0"))
    em.dma("gpsimd", stage[0:1, 0:128], dtb[:].rearrange("h -> () h"), writes=("stage",))
    em.step("tensor", lambda e: e.transpose(pP[0][0:128, 0:1], stage[0:1, 0:128], eyef[0:1, 0:1]),
            reads=("stage", "eyef", "pP0"), writes=("pP0",))
    em.step("vector", lambda e: e.tensor_copy(dtbc[:], pP[0][0:128, 0:1]),
            reads=("pP0",), writes=("dtbc", "pP0"))
    em.dma("gpsimd", stage[0:64, :], nw[:].rearrange("(n p) -> n p", p=128),
           writes=("stage",))
    em.step("tensor", lambda e: e.transpose(pP[0][0:128, 0:64], stage[0:64, :], eyef[0:64, 0:64]),
            reads=("stage", "eyef", "pP0"), writes=("pP0",))
    em.step("vector", lambda e: e.tensor_copy(nwc[:], pP[0][0:128, 0:64]),
            reads=("pP0",), writes=("nwc", "pP0"))

    ngrp = (NM_PROJ + 1) // 2
    for g in range(ngrp):
        gm = 2 if (g + 1) * 2 <= NM_PROJ else NM_PROJ - g * 2
        lb = lbuf[g % 2]
        ltag = f"lbuf{g % 2}"
        if gm == 2:
            for kc0 in range(0, ND, 8):
                q = "sync" if (kc0 // 8) % 2 == 0 else "scalar"
                em.dma(q,
                       lb[:, kc0 * 256:(kc0 + 8) * 256].rearrange(
                           "p (f m) -> p f m", f=8),
                       wbT[kc0 * 128:(kc0 + 8) * 128,
                           g * 256: g * 256 + 256].rearrange(
                           "(f p) m -> p f m", p=128),
                       reads=(f"wbT{kc0 // 8}",) + ((ltag,) if kc0 == 0 else ()),
                       writes=(ltag,))
        else:
            for kc in range(ND):
                em.dma("sync", lb[:, kc * 256: kc * 256 + gm * 128],
                       wbT[kc * 128:(kc + 1) * 128, g * 256: g * 256 + gm * 128],
                       reads=(f"wbT{kc // 8}",) + ((ltag,) if kc < 2 else ()),
                       writes=(ltag,))
        for mi in range(gm):
            m = g * 2 + mi
            cv = cvpad[m % 2]
            cvt = f"cvp{m % 2}"
            ac = acc[m % 2]
            act = f"acc{m % 2}"
            ot = otile[m % 2]
            ott = f"otile{m % 2}"
            for kc in range(ND):
                for nb in range(4):
                    pi = (m % 2) * 4 + nb
                    em.step("tensor", lambda e, kc=kc, nb=nb, lb=lb, mi=mi, pi=pi: e.matmul(
                        pP[pi][:, :], lb[:, kc * 256 + mi * 128: kc * 256 + mi * 128 + 128],
                        xn1sb[:, kc * T + nb * 512: kc * T + (nb + 1) * 512],
                        start=(kc == 0), stop=(kc == ND - 1)),
                        reads=(ltag, "xn1sb") + ((f"pP{pi}",) if kc == 0 else ()),
                        writes=(f"pP{pi}",))
            for nb in range(4):
                pi = (m % 2) * 4 + nb
                if nb < 2:
                    em.step("vector", lambda e, nb=nb, pi=pi, cv=cv: e.tensor_copy(
                        cv[:, 3 + nb * 512: 3 + (nb + 1) * 512], pP[pi][:, :]),
                        reads=(f"pP{pi}",), writes=(cvt, f"pP{pi}"))
                else:
                    em.step("scalar", lambda e, nb=nb, pi=pi, cv=cv: e.copy(
                        cv[:, 3 + nb * 512: 3 + (nb + 1) * 512], pP[pi][:, :]),
                        reads=(f"pP{pi}",), writes=(cvt, f"pP{pi}"))
            if m < 64:
                em.step("scalar", lambda e, cv=cv, ac=ac: e.activation(
                    ac[:], cv[:, 3:3 + T], AF.Silu),
                    reads=(cvt,), writes=(act,))
                em.step("vector", lambda e, m=m, ot=ot, ac=ac: e.tensor_scalar_mul(
                    ot[:], ac[:], nwc[:, m:m + 1]),
                    reads=(act, "nwc"), writes=(ott,))
                em.dma("gpsimd", sg_o[m * 128:(m + 1) * 128, :], ot[:],
                       reads=(ott,))
            elif m < 144:
                j = m - 64
                em.step("vector", lambda e, j=j, cv=cv, ac=ac: e.tensor_scalar_mul(
                    ac[:], cv[:, 3:3 + T], cw[:, j * 4 + 3: j * 4 + 4]),
                    reads=(cvt, "cw"), writes=(act,))
                for tap in (2, 1, 0):
                    em.step("vector", lambda e, j=j, tap=tap, cv=cv, ac=ac: e.scalar_tensor_tensor(
                        ac[:], cv[:, tap: tap + T], cw[:, j * 4 + tap: j * 4 + tap + 1],
                        ac[:], ALU.mult, ALU.add),
                        reads=(cvt, "cw", act), writes=(act,))
                em.step("scalar", lambda e, j=j, ac=ac, ot=ot: e.activation(
                    ot[:], ac[:], AF.Silu, bias=cb[:, j:j + 1]),
                    reads=(act, "cb"), writes=(ott,))
                if j < 64:
                    em.dma("gpsimd", cvx_o[j * 128:(j + 1) * 128, :], ot[:],
                           reads=(ott,))
                elif j < 72:
                    em.dma("gpsimd", cvB_o[(j - 64) * 128:(j - 63) * 128, :], ot[:],
                           reads=(ott,))
                else:
                    em.dma("gpsimd", cvC_o[(j - 72) * 128:(j - 71) * 128, :], ot[:],
                           reads=(ott,))
            else:
                rl = acc[1]
                rlt = "acc1"
                em.step("vector", lambda e, cv=cv, ac=ac: e.tensor_scalar_add(
                    ac[:], cv[:, 3:3 + T], dtbc[:]),
                    reads=(cvt, "dtbc"), writes=(act,))
                em.step("scalar", lambda e, ac=ac, rl=rl: e.activation(rl[:], ac[:], AF.Relu),
                        reads=(act,), writes=(rlt,))
                em.step("scalar", lambda e, ac=ac: e.activation(ac[:], ac[:], AF.Abs),
                        reads=(act,), writes=(act,))
                em.step("scalar", lambda e, ac=ac: e.activation(ac[:], ac[:], AF.Exp, scale=-1.0),
                        reads=(act,), writes=(act,))
                em.step("scalar", lambda e, ac=ac: e.activation(ac[:], ac[:], AF.Ln, bias=onec[:]),
                        reads=(act, "onec"), writes=(act,))
                em.step("vector", lambda e, ac=ac, rl=rl: e.tensor_add(ac[:], ac[:], rl[:]),
                        reads=(act, rlt), writes=(act,))
                em.dma("gpsimd", dtv_o[:], ac[:], reads=(act,))

    al.finish()


def inproj_inputs(core, xn1T):
    d = const_inputs(INPROJ_CONSTS)
    d.update({
        "xn1T": xn1T,
        "Wp": core["L1_attn_in_proj_w"],
        "convw": core["L1_attn_conv_w"],
        "convb": core["L1_attn_conv_b"],
        "dtb": core["L1_attn_dt_bias"],
        "nw": core["L1_attn_norm_w"],
    })
    return d



SSD_CONSTS = ("eyef", "eyeb", "onesf", "cumU", "mask01")

def emit_SSD(em, io):
    nc = em.nc
    C = io
    cvx = io["cvx"]
    cvB = io["cvB"]
    cvC = io["cvC"]
    dtv = io["dtv"]
    sg = io["sg"]
    alog = io["alog"]
    Din = io["Din"]
    ygT_o = io["ygT"]
    rowscr = io["rowscr"]

    al = Alloc(nc)
    sb, ps = al.sb, al.ps
    eyef = sb("s_eyef", [128, 128], F32)
    eyeb = sb("s_eyeb", [128, 128], BF16)
    onesf = sb("s_onesf", [1, 128], F32)
    cumU = sb("s_cumU", [128, 128], F32)
    mask01 = sb("s_mask01", [128, 128], F32)
    onescb = sb("s_onescb", [128, 1], BF16)
    onescf = sb("s_onescf", [128, 1], F32)
    eps1 = sb("s_eps1", [1, 1], F32)
    acol = sb("s_acol", [128, 1], F32)
    dball = sb("s_dball", [128, 128], F32)
    dtv_sb = sb("s_dtvsb", [128, T], F32)
    adt = sb("s_adt", [128, T], F32)
    s_state = sb("s_state", [128, H * 64], F32)
    xcm = sb("s_xcm", [128, 64 * 128], BF16)
    xtk = sb("s_xtk", [128, 64 * 128], BF16)
    bcm = sb("s_bcm", [128, 8 * 128], BF16)
    ccm = sb("s_ccm", [128, 8 * 128], BF16)
    bgT = sb("s_bgT", [128, 8 * 128], BF16)
    sgt = sb("s_sgt", [128, 64 * 128], BF16)
    ybuf = sb("s_ybuf", [128, 64 * 128], F32)
    y2b = sb("s_y2b", [128, 64 * 128], BF16)
    mgmT = sb("s_mgmT", [128, 8 * 128], BF16)
    adtT = sb("s_adtT", [128, 128], F32)
    csT = sb("s_csT", [128, 128], F32)
    acs = sb("s_acs", [128, 128], F32)
    dbfm = sb("s_dbfm", [128, 128], F32)
    dbT = sb("s_dbT", [128, 128], F32)
    dtT = sb("s_dtT", [128, 128], F32)
    gbc = sb("s_gbc", [128, 128], F32)
    rstdB = sb("s_rstdB", [128, 128], F32)
    rr = sb("s_rr", [1, 128], F32)
    csr4 = sb("s_csr4", [1, 512], F32)
    sd4 = sb("s_sd4", [1, 512], F32)
    grow = sb("s_grow", [1, 128], F32)
    sdB4 = sb("s_sdB4", [128, 512], F32)
    dscr = [sb(f"s_dscr{i}", [128, 128], F32) for i in range(2)]
    ltb = [sb(f"s_ltb{i}", [128, 128], BF16) for i in range(2)]
    mtb = [sb(f"s_mtb{i}", [128, 128], BF16) for i in range(2)]
    xdt = [sb(f"s_xdt{i}", [128, 64], BF16) for i in range(2)]
    bdb = [sb(f"s_bdb{i}", [128, 128], BF16) for i in range(2)]
    sbst = [sb(f"s_sbst{i}", [128, 64], BF16) for i in range(2)]
    o1b = [sb(f"s_o1b{i}", [128, 128], F32) for i in range(2)]
    o2b = [sb(f"s_o2b{i}", [128, 128], F32) for i in range(2)]
    ytmp = sb("s_ytmp", [128, 128], F32)
    stage = sb("s_stage", [1, 128], F32)

    b0 = ps("p_b0", [128, 512])
    b1 = ps("p_b1", [128, 512])
    b2 = ps("p_b2", [128, 512])
    b5 = ps("p_b5", [128, 512])
    b6 = ps("p_b6", [128, 512])
    b7 = ps("p_b7", [128, 1024], BF16)
    b8 = ps("p_b8", [128, 1024], BF16)
    b3 = ps("p_b3", [128, 512])

    em.dma("sync", eyef[:], C["eyef"][:], writes=("eyef",))
    em.dma("sync", eyeb[:], C["eyeb"][:], writes=("eyeb",))
    em.dma("sync", onesf[:], C["onesf"][:], writes=("onesf",))
    em.dma("sync", cumU[:], C["cumU"][:], writes=("cumU",))
    em.dma("sync", mask01[:], C["mask01"][:], writes=("mask01",))
    em.step("gpsimd", lambda e: e.memset(onescb[:], 1.0), writes=("onescb",))
    em.step("gpsimd", lambda e: e.memset(onescf[:], 1.0), writes=("onescf",))
    em.step("gpsimd", lambda e: e.memset(eps1[:], 1e-5), writes=("eps1",))
    for j in range(8):
        em.step("gpsimd", lambda e, j=j: e.memset(s_state[:, j * 1024:(j + 1) * 1024], 0.0),
                writes=("s_state",))
    em.dma("gpsimd", dtv_sb[:], dtv[:], writes=("dtvsb",))
    em.dma("gpsimd", stage[:], alog[:].rearrange("h -> () h"), writes=("stage",))
    em.step("tensor", lambda e: e.transpose(b0[0:128, 0:1], stage[:], eyef[0:1, 0:1]),
            reads=("stage", "eyef", "b0"), writes=("b0",))
    em.step("scalar", lambda e: e.activation(acol[:], b0[0:128, 0:1], AF.Exp),
            reads=("b0",), writes=("acol",))
    em.step("vector", lambda e: e.tensor_scalar_mul(acol[:], acol[:], -1.0),
            reads=("acol",), writes=("acol",))
    em.step("vector", lambda e: e.tensor_scalar_mul(adt[:], dtv_sb[:], acol[:]),
            reads=("dtvsb", "acol"), writes=("adt",))
    em.dma("gpsimd", stage[:], Din[:].rearrange("h -> () h"), writes=("stage",))
    em.step("tensor", lambda e: e.matmul(b0[:, 128:256], onesf[:], stage[:],
                                         start=True, stop=True),
            reads=("onesf", "stage", "b0"), writes=("b0",))
    em.step("vector", lambda e: e.tensor_copy(dball[:], b0[:, 128:256]),
            reads=("b0",), writes=("dball", "b0"))

    cvxv = cvx.rearrange("(c p) t -> p c t", p=128)
    cvBv = cvB.rearrange("(g p) t -> p g t", p=128)
    cvCv = cvC.rearrange("(g p) t -> p g t", p=128)
    sgv = sg.rearrange("(c p) t -> p c t", p=128)
    ygv = ygT_o.rearrange("(c p) t -> p c t", p=128)

    for c in range(NCHUNK):
        tsl = slice(c * CS, (c + 1) * CS)
        em.dma("sync", xcm[:].rearrange("p (a t) -> p a t", t=128), cvxv[:, :, tsl],
               reads=("xcm",), writes=("xcm",))
        em.dma("gpsimd", bcm[:].rearrange("p (a t) -> p a t", t=128), cvBv[:, :, tsl],
               reads=("bcm",), writes=("bcm",))
        em.dma("gpsimd", ccm[:].rearrange("p (a t) -> p a t", t=128), cvCv[:, :, tsl],
               reads=("ccm",), writes=("ccm",))
        em.dma("scalar", sgt[:].rearrange("p (a t) -> p a t", t=128), sgv[:, :, tsl],
               reads=("sgt",), writes=("sgt",))
        em.step("tensor", lambda e, c=c: e.transpose(b0[:, 0:128], adt[:, c * CS:(c + 1) * CS], eyef[:]),
                reads=("adt", "eyef", "b0"), writes=("b0",))
        em.step("vector", lambda e: e.tensor_copy(adtT[:], b0[:, 0:128]),
                reads=("b0",), writes=("adtT", "b0"))
        em.step("tensor", lambda e: e.matmul(b0[:, 128:256], cumU[:], adtT[:], start=True, stop=True),
                reads=("cumU", "adtT", "b0"), writes=("b0",))
        em.step("vector", lambda e: e.tensor_copy(csT[:], b0[:, 128:256]),
                reads=("b0",), writes=("csT", "b0"))
        em.step("tensor", lambda e: e.matmul(b0[:, 256:384], adtT[:], cumU[:], start=True, stop=True),
                reads=("adtT", "cumU", "b0"), writes=("b0",))
        em.step("vector", lambda e: e.tensor_copy(acs[:], b0[:, 256:384]),
                reads=("b0",), writes=("acs", "b0"))
        em.dma("gpsimd", rowscr[:], acs[:], reads=("acs",), writes=("rowscr",))
        em.step("tensor", lambda e: e.matmul(b3[0:1, 128:256], onescf[:], adtT[:],
                                             start=True, stop=True),
                reads=("onescf", "adtT", "b3"), writes=("b3",))
        em.step("scalar", lambda e: e.activation(grow[:], b3[0:1, 128:256], AF.Exp),
                reads=("b3",), writes=("grow",))
        em.step("scalar", lambda e: e.activation(dbfm[:], acs[:], AF.Exp,
                                                 scale=-1.0, bias=acs[:, 127:128]),
                reads=("acs",), writes=("dbfm",))
        em.step("tensor", lambda e: e.transpose(b0[:, 384:512], dbfm[:], eyef[:]),
                reads=("dbfm", "eyef", "b0"), writes=("b0",))
        em.step("vector", lambda e: e.tensor_copy(dbT[:], b0[:, 384:512]),
                reads=("b0",), writes=("dbT", "b0"))
        em.step("tensor", lambda e, c=c: e.transpose(b0[:, 0:128], dtv_sb[:, c * CS:(c + 1) * CS], eyef[:]),
                reads=("dtvsb", "eyef", "b0"), writes=("b0",))
        em.step("vector", lambda e: e.tensor_copy(dtT[:], b0[:, 0:128]),
                reads=("b0",), writes=("dtT", "b0"))
        em.step("tensor", lambda e: e.matmul(
            b3[:, 0:128], onesf[:], grow[:], start=True, stop=True),
            reads=("onesf", "grow", "b3"), writes=("b3",))
        em.step("vector", lambda e: e.tensor_copy(gbc[:], b3[:, 0:128]),
                reads=("b3",), writes=("gbc", "b3"))
        for g in range(8):
            pg, ptag = (b1, "b1") if g < 4 else (b2, "b2")
            off = (g % 4) * 128
            em.step("tensor", lambda e, g=g, pg=pg, off=off: e.matmul(
                pg[:, off:off + 128], bcm[:, g * 128:(g + 1) * 128],
                ccm[:, g * 128:(g + 1) * 128], start=True, stop=True),
                reads=("bcm", "ccm", ptag), writes=(ptag,))
        for g in range(8):
            pg, ptag = (b1, "b1") if g < 4 else (b2, "b2")
            off = (g % 4) * 128
            em.step("vector", lambda e, g=g, pg=pg, off=off: e.tensor_mul(
                mgmT[:, g * 128:(g + 1) * 128], pg[:, off:off + 128], mask01[:]),
                reads=(ptag, "mask01"), writes=("mgmT",))
        for c2 in range(64):
            pb, ptag = (b7, "b7") if c2 % 2 == 0 else (b8, "b8")
            em.step("tensor", lambda e, c2=c2, pb=pb: e.transpose(
                pb[:, 0:128], xcm[:, c2 * 128:(c2 + 1) * 128], eyeb[:]),
                reads=("xcm", "eyeb", ptag), writes=(ptag,))
            em.step("vector", lambda e, c2=c2, pb=pb: e.tensor_copy(
                xtk[:, c2 * 128:(c2 + 1) * 128], pb[:, 0:128]),
                reads=(ptag,), writes=("xtk", ptag))
        for g in range(8):
            pb, ptag = (b7, "b7") if g % 2 == 0 else (b8, "b8")
            em.step("tensor", lambda e, g=g, pb=pb: e.transpose(
                pb[:, 128:256], bcm[:, g * 128:(g + 1) * 128], eyeb[:]),
                reads=("bcm", "eyeb", ptag), writes=(ptag,))
            em.step("vector", lambda e, g=g, pb=pb: e.tensor_copy(
                bgT[:, g * 128:(g + 1) * 128], pb[:, 128:256]),
                reads=(ptag,), writes=("bgT", ptag))
        for hg in range(32):
            em.dma("gpsimd", csr4[:],
                   rowscr[hg * 4:(hg + 1) * 4, :].rearrange("p n -> () (p n)"),
                   reads=("rowscr", "csr4"), writes=("csr4",))
            em.step("scalar", lambda e: e.activation(sd4[:], csr4[:], AF.Exp),
                    reads=("csr4",), writes=("sd4",))
            em.step("tensor", lambda e: e.matmul(
                b1[:, :], onesf[:], csr4[:], start=True, stop=True),
                reads=("onesf", "csr4", "b1"), writes=("b1",))
            em.step("tensor", lambda e: e.matmul(
                b2[:, :], onesf[:], sd4[:], start=True, stop=True),
                reads=("onesf", "sd4", "b2"), writes=("b2",))
            em.step("scalar", lambda e: e.copy(sdB4[:], b2[:, :]),
                    reads=("b2",), writes=("sdB4",))
            for j in range(4):
                h = hg * 4 + j
                g = h // 16
                pM, mtag = (b5, "b5") if h % 2 == 0 else (b6, "b6")
                pr = h % 2
                dv, lv, mv, xv2, bv, sv = dscr[pr], ltb[pr], mtb[pr], xdt[pr], bdb[pr], sbst[pr]
                dtag, lttag, mttag, xtag2, btag, stag2 = (f"dscr{pr}", f"ltb{pr}", f"mtb{pr}",
                                                          f"xdt{pr}", f"bdb{pr}", f"sbst{pr}")
                em.step("vector", lambda e, j=j, dv=dv, h=h: e.tensor_scalar(
                    dv[:], b1[:, j * 128:(j + 1) * 128], csT[:, h:h + 1], 0.0,
                    ALU.subtract, ALU.min),
                    reads=("b1", "csT", dtag), writes=(dtag,))
                em.step("scalar", lambda e, dv=dv, lv=lv: e.activation(lv[:], dv[:], AF.Exp),
                        reads=(dtag,), writes=(lttag,))
                em.step("vector", lambda e, lv=lv, mv=mv, g=g: e.tensor_mul(
                    mv[:], lv[:], mgmT[:, g * 128:(g + 1) * 128]),
                    reads=(lttag, "mgmT"), writes=(mttag,))
                em.step("vector", lambda e, xv2=xv2, h=h: e.tensor_scalar_mul(
                    xv2[:], xtk[:, h * 64:(h + 1) * 64], dtT[:, h:h + 1]),
                    reads=("xtk", "dtT", xtag2), writes=(xtag2,))
                em.step("vector", lambda e, bv=bv, g=g, h=h: e.tensor_scalar_mul(
                    bv[:], bgT[:, g * 128:(g + 1) * 128], dbT[:, h:h + 1]),
                    reads=("bgT", "dbT", btag), writes=(btag,))
                if c > 0:
                    em.step("vector", lambda e, sv=sv, h=h: lowprec(e, lambda: e.tensor_copy(
                        sv[:], s_state[:, h * 64:(h + 1) * 64])),
                        reads=("s_state", stag2), writes=(stag2,))
                B_ = (h % 2) * 64
                em.step("tensor", lambda e, pM=pM, xv2=xv2, mv=mv, B_=B_: e.matmul(
                    pM[B_:B_ + 64, 0:128], xv2[:], mv[:], start=True, stop=True),
                    reads=(xtag2, mttag, mtag), writes=(mtag,))
                if c > 0:
                    em.step("tensor", lambda e, pM=pM, sv=sv, g=g, B_=B_: e.matmul(
                        pM[B_:B_ + 64, 128:256], sv[:], ccm[:, g * 128:(g + 1) * 128],
                        start=True, stop=True),
                        reads=(stag2, "ccm", mtag), writes=(mtag,))
                em.step("tensor", lambda e, pM=pM, bv=bv, xv2=xv2: e.matmul(
                    pM[0:128, 256:320], bv[:], xv2[:], start=True, stop=True),
                    reads=(btag, xtag2, mtag), writes=(mtag,))
                o1v, o2v = o1b[pr], o2b[pr]
                o1tag, o2tag = f"o1b{pr}", f"o2b{pr}"
                if c > 0:
                    em.step("vector", lambda e, o1v=o1v, pM=pM, j=j, B_=B_: e.tensor_mul(
                        o1v[B_:B_ + 64, :], pM[B_:B_ + 64, 128:256],
                        sdB4[B_:B_ + 64, j * 128:(j + 1) * 128]),
                        reads=(mtag, "sdB4", o1tag), writes=(o1tag,))
                    em.step("vector", lambda e, o1v=o1v, o2v=o2v, pM=pM, B_=B_: e.tensor_add(
                        o2v[B_:B_ + 64, :], o1v[B_:B_ + 64, :], pM[B_:B_ + 64, 0:128]),
                        reads=(o1tag, mtag, o2tag), writes=(o2tag,))
                else:
                    em.step("vector", lambda e, o2v=o2v, pM=pM, B_=B_: e.tensor_copy(
                        o2v[B_:B_ + 64, :], pM[B_:B_ + 64, 0:128]),
                        reads=(mtag, o2tag), writes=(o2tag,))
                em.step("vector", lambda e, h=h, o2v=o2v, B_=B_: e.scalar_tensor_tensor(
                    ybuf[B_:B_ + 64, (h // 2) * 128:(h // 2) * 128 + 128],
                    xcm[B_:B_ + 64, (h // 2) * 128:(h // 2) * 128 + 128],
                    dball[B_:B_ + 64, h:h + 1], o2v[B_:B_ + 64, :], ALU.mult, ALU.add),
                    reads=("xcm", "dball", o2tag, "ybuf"), writes=("ybuf",))
                em.step("vector", lambda e, h=h, pM=pM: e.scalar_tensor_tensor(
                    s_state[:, h * 64:(h + 1) * 64], s_state[:, h * 64:(h + 1) * 64],
                    gbc[:, h:h + 1], pM[0:128, 256:320], ALU.mult, ALU.add),
                    reads=("s_state", "gbc", mtag), writes=("s_state",))
        em.step("scalar", lambda e: e.activation(y2b[:], ybuf[:], AF.Square),
                reads=("ybuf",), writes=("y2b",))
        for c2 in range(64):
            em.step("tensor", lambda e, c2=c2: e.matmul(
                b0[0:1, 0:128], onescb[:], y2b[:, c2 * 128:(c2 + 1) * 128],
                start=(c2 == 0), stop=(c2 == 63)),
                reads=("onescb", "y2b") + (("b0",) if c2 == 0 else ()), writes=("b0",))
        em.step("scalar", lambda e: e.activation(rr[:], b0[0:1, 0:128], AF.Sqrt,
                                                 scale=1.0 / INTER, bias=eps1[:]),
                reads=("b0", "eps1"), writes=("rr",))
        em.step("vector", lambda e: e.reciprocal(rr[:], rr[:]),
                reads=("rr",), writes=("rr",))
        em.step("tensor", lambda e: e.matmul(b0[:, 128:256], onesf[:], rr[:],
                                             start=True, stop=True),
                reads=("onesf", "rr", "b0"), writes=("b0",))
        em.step("vector", lambda e: e.tensor_copy(rstdB[:], b0[:, 128:256]),
                reads=("b0",), writes=("rstdB", "b0"))
        for c2 in range(64):
            em.step("vector", lambda e, c2=c2: e.tensor_mul(
                ytmp[:], ybuf[:, c2 * 128:(c2 + 1) * 128], rstdB[:]),
                reads=("ybuf", "rstdB", "ytmp"), writes=("ytmp",))
            em.step("vector", lambda e, c2=c2: e.tensor_mul(
                y2b[:, c2 * 128:(c2 + 1) * 128], ytmp[:], sgt[:, c2 * 128:(c2 + 1) * 128]),
                reads=("ytmp", "sgt"), writes=("y2b",))
        em.dma("sync", ygv[:, :, tsl], y2b[:].rearrange("p (a t) -> p a t", t=128),
               reads=("y2b",))

    al.finish()


def ssd_inputs(core, cvx, cvB, cvC, dtv, sg):
    d = const_inputs(SSD_CONSTS)
    d.update({
        "cvx": cvx, "cvB": cvB, "cvC": cvC, "dtv": dtv, "sg": sg,
        "alog": core["L1_attn_A_log"], "Din": core["L1_attn_D"],
    })
    return d



NKO = INTER // 128
OPROJ_CONSTS = ()

def emit_OPROJ(em, io):
    nc = em.nc
    ygT = io["ygT"]
    woT = io["woT"]
    s1T_o = io["s1T"]

    al = Alloc(nc)
    sb, ps = al.sb, al.ps
    ygb = sb("s_ygb", [128, NKO * 1024], BF16)
    lbuf = [sb(f"s_lb{i}", [128, 32 * 256], BF16) for i in range(2)]
    ev = [sb(f"s_ev{i}", [128, 1024], F32) for i in range(2)]
    pO = [ps(f"p_O{i}", [128, 512]) for i in range(8)]
    ygvv = ygT.rearrange("(c p) t -> p c t", p=128)

    segs = []
    for th in range(2):
        for mg in range(16):
            for sg_ in range(2):
                segs.append((th, mg, sg_))

    for th in range(2):
        tsl = slice(th * 1024, (th + 1) * 1024)
        for j in range(8):
            em.dma("sync", ygb[:].rearrange("p (c t) -> p c t", t=1024)[:, j * 8:(j + 1) * 8, :],
                   ygvv[:, j * 8:(j + 1) * 8, tsl], reads=("ygb",), writes=("ygb",))
        for mg in range(16):
            for sg_ in range(2):
                si = (th * 32 + mg * 2 + sg_)
                lb = lbuf[si % 2]
                ltag = f"lb{si % 2}"
                for kc0 in range(sg_ * 32, sg_ * 32 + 32, 8):
                    q = "sync" if ((kc0 // 8) % 2 == 0) else "scalar"
                    em.dma(q,
                           lb[:, (kc0 % 32) * 256:((kc0 % 32) + 8) * 256].rearrange(
                               "p (f m) -> p f m", f=8),
                           woT[kc0 * 128:(kc0 + 8) * 128,
                               mg * 256:(mg + 1) * 256].rearrange(
                               "(f p) m -> p f m", p=128),
                           reads=(f"woT{kc0 // 8}",) + ((ltag,) if kc0 % 32 == 0 else ()),
                           writes=(ltag,))
                for mi in range(2):
                    m = mg * 2 + mi
                    for kc in range(sg_ * 32, sg_ * 32 + 32):
                        for nb in range(2):
                            pi = (mg % 2) * 4 + mi * 2 + nb
                            em.step("tensor", lambda e, kc=kc, nb=nb, lb=lb, mi=mi, pi=pi: e.matmul(
                                pO[pi][:, :],
                                lb[:, (kc % 32) * 256 + mi * 128:(kc % 32) * 256 + mi * 128 + 128],
                                ygb[:, kc * 1024 + nb * 512: kc * 1024 + (nb + 1) * 512],
                                start=(kc == 0), stop=(kc == NKO - 1)),
                                reads=(ltag, "ygb") + ((f"pO{pi}",) if kc == 0 else ()),
                                writes=(f"pO{pi}",))
            for mi in range(2):
                m = mg * 2 + mi
                evb = ev[m % 2]
                evt = f"ev{m % 2}"
                for nb in range(2):
                    pi = (mg % 2) * 4 + mi * 2 + nb
                    if nb == 0:
                        em.step("vector", lambda e, pi=pi, nb=nb, evb=evb: e.tensor_copy(
                            evb[:, nb * 512:(nb + 1) * 512], pO[pi][:, :]),
                            reads=(f"pO{pi}", evt), writes=(evt, f"pO{pi}"))
                    else:
                        em.step("scalar", lambda e, pi=pi, nb=nb, evb=evb: e.copy(
                            evb[:, nb * 512:(nb + 1) * 512], pO[pi][:, :]),
                            reads=(f"pO{pi}", evt), writes=(evt, f"pO{pi}"))
                em.dma("gpsimd", s1T_o[m * 128:(m + 1) * 128, tsl], evb[:],
                       reads=(evt,))

    al.finish()


def oproj_inputs(core, ygT):
    return {"ygT": ygT, "Wo": core["L1_attn_out_proj_w"]}



L1FFN_CONSTS = ("eyef", "e64")

def emit_L1FFN(em, io):
    nc = em.nc
    C = io
    xn1T = io["xn1T"]
    waT = io["waT"]
    wb1 = io["wb1"]
    Wr1T = io["Wr1T"]
    g1T_o = io["g1T"]

    al = Alloc(nc)
    sb, ps = al.sb, al.ps
    eyef = sb("s_eyef", [128, 128], F32)
    e64 = sb("s_e64", [64, 1024], BF16)
    a1T = sb("s_a1T", [128, ND * KR], BF16)
    b1T = sb("s_b1T", [128, 8 * D], BF16)
    wr1T = sb("s_wr1T", [128, ND * 64], BF16)
    hgT = sb("s_hgT", [128, 8 * T], BF16)
    hsb = sb("s_hsb", [128, 512], F32)
    otile = sb("s_ot", [128, 512], BF16)
    xstream = [sb(f"s_xs{i}", [128, 512], BF16) for i in range(2)]
    sc_f = sb("s_scf", [64, T], F32)
    sc_tm = sb("s_sctm", [128, 64], F32)
    mx16 = sb("s_mx16", [128, 16], F32)
    mrep = sb("s_mrep", [128, 64], F32)
    thcol = sb("s_thcol", [128, 1], F32)
    t1col = sb("s_t1col", [128, 1], F32)
    maskb = sb("s_maskb", [128, 64], F32)
    gtm = sb("s_gtm", [128, 64], F32)
    gT = sb("s_gT", [64, T], BF16)
    pH = [ps(f"p_H{i}", [128, 512]) for i in range(8)]

    em.dma("sync", eyef[:], C["eyef"][:], writes=("eyef",))
    em.dma("sync", e64[:], C["e64"][:], writes=("e64",))
    a1dst = a1T[:].rearrange("p (c k) -> p c k", k=KR)
    a1src = waT.rearrange("(c p) k -> p c k", p=128)
    for j in range(4):
        em.dma("sync" if j % 2 == 0 else "scalar",
               a1dst[:, j * 8:(j + 1) * 8, :], a1src[:, j * 8:(j + 1) * 8, :],
               reads=("waT",), writes=("a1T",))
    wrdst = wr1T[:].rearrange("p (c e) -> p c e", e=64)
    wrsrc = Wr1T.rearrange("(c p) e -> p c e", p=128)
    for j in range(2):
        em.dma("gpsimd", wrdst[:, j * 16:(j + 1) * 16, :], wrsrc[:, j * 16:(j + 1) * 16, :],
               writes=("wr1T",))
    b1dst = b1T[:].rearrange("p (c d) -> p c d", d=D)
    b1src = wb1.rearrange("(c p) d -> p c d", p=128)
    for j in range(2):
        em.dma("sync" if j % 2 == 0 else "scalar",
               b1dst[:, j * 4:(j + 1) * 4, :], b1src[:, j * 4:(j + 1) * 4, :],
               reads=("wb1",), writes=("b1T",))

    xv = xn1T.rearrange("(c p) t -> p c t", p=128)
    for kc in range(ND):
        xs = xstream[kc % 2]
        xst = f"xs{kc % 2}"
        for nb in range(4):
            em.dma("scalar", xs[:], xv[:, kc, nb * 512:(nb + 1) * 512],
                   reads=(xst,), writes=(xst,))
            em.step("tensor", lambda e, kc=kc, nb=nb, xs=xs: e.matmul(
                pH[nb][0:64, :], wr1T[:, kc * 64:(kc + 1) * 64], xs[:],
                start=(kc == 0), stop=(kc == ND - 1)),
                reads=("wr1T", xst) + ((f"pH{nb}",) if kc == 0 else ()),
                writes=(f"pH{nb}",))
    for nb in range(4):
        em.step("vector", lambda e, nb=nb: e.tensor_copy(
            sc_f[0:64, nb * 512:(nb + 1) * 512], pH[nb][0:64, :]),
            reads=(f"pH{nb}", "sc_f"), writes=("sc_f", f"pH{nb}"))
    for i in range(NT):
        nb = i // 4
        off = (i % 4) * 128
        em.step("tensor", lambda e, nb=nb, off=off: e.transpose(
            pH[4][0:128, 0:64], sc_f[0:64, nb * 512 + off: nb * 512 + off + 128],
            eyef[0:64, 0:64]),
            reads=("sc_f", "eyef", "pH4"), writes=("pH4",))
        em.step("vector", lambda e: e.tensor_copy(sc_tm[:], pH[4][0:128, 0:64]),
                reads=("pH4",), writes=("sc_tm", "pH4"))
        em.step("vector", lambda e: e.max(out=mx16[:, 0:8], in_=sc_tm[:]),
                reads=("sc_tm",), writes=("mx16",))
        em.step("vector", lambda e: e.match_replace(
            out=mrep[:], in_to_replace=mx16[:, 0:8], in_values=sc_tm[:], imm_value=-1e30),
            reads=("sc_tm", "mx16"), writes=("mrep",))
        em.step("vector", lambda e: e.max(out=mx16[:, 8:16], in_=mrep[:]),
                reads=("mrep",), writes=("mx16",))
        em.step("vector", lambda e: e.tensor_reduce(thcol[:], mx16[:], AX.X, ALU.min),
                reads=("mx16",), writes=("thcol",))
        em.step("vector", lambda e: e.tensor_scalar(
            t1col[:], mx16[:, 0:1], -1.0, None, ALU.mult),
            reads=("mx16",), writes=("t1col",))
        em.step("vector", lambda e: e.tensor_scalar(
            maskb[:], sc_tm[:], thcol[:], None, ALU.is_ge),
            reads=("sc_tm", "thcol"), writes=("maskb",))
        em.step("scalar", lambda e: e.activation(
            gtm[:], sc_tm[:], AF.Exp, bias=t1col[:]),
            reads=("sc_tm", "t1col"), writes=("gtm",))
        em.step("vector", lambda e: e.tensor_mul(gtm[:], gtm[:], maskb[:]),
                reads=("gtm", "maskb"), writes=("gtm",))
        em.step("vector", lambda e: e.reduce_sum(thcol[:], gtm[:], axis=AX.X),
                reads=("gtm",), writes=("thcol",))
        em.step("vector", lambda e: e.reciprocal(thcol[:], thcol[:]),
                reads=("thcol",), writes=("thcol",))
        em.step("vector", lambda e: e.tensor_scalar_mul(gtm[:], gtm[:], thcol[:]),
                reads=("gtm", "thcol"), writes=("gtm",))
        em.step("tensor", lambda e: e.transpose(
            pH[5][0:64, 0:128], gtm[:], eyef[:]),
            reads=("gtm", "eyef", "pH5"), writes=("pH5",))
        em.step("vector", lambda e, i=i: lowprec(e, lambda: e.tensor_copy(
            gT[:, i * 128:(i + 1) * 128], pH[5][0:64, 0:128])),
            reads=("pH5",), writes=("gT", "pH5"))
    for nb in range(4):
        for kc in range(ND):
            xs = xstream[kc % 2]
            xst = f"xs{kc % 2}"
            em.dma("scalar", xs[:], xv[:, kc, nb * 512:(nb + 1) * 512],
                   reads=(xst,), writes=(xst,))
            for m in range(8):
                em.step("tensor", lambda e, kc=kc, m=m, xs=xs: e.matmul(
                    pH[m][:, :], a1T[:, kc * KR + m * 128: kc * KR + (m + 1) * 128],
                    xs[:], start=(kc == 0), stop=(kc == ND - 1)),
                    reads=("a1T", xst) + ((f"pH{m}",) if kc == 0 else ()),
                    writes=(f"pH{m}",))
        for m in range(8):
            em.step("vector", lambda e, m=m: e.tensor_copy(hsb[:], pH[m][:, :]),
                    reads=(f"pH{m}", "hsb"), writes=("hsb", f"pH{m}"))
            em.step("tensor", lambda e, m=m, nb=nb: e.matmul(
                pH[m][:, :], e64[:, m * 128:(m + 1) * 128], gT[:, nb * 512:(nb + 1) * 512],
                start=True, stop=True),
                reads=("e64", "gT", f"pH{m}"), writes=(f"pH{m}",))
            em.step("vector", lambda e, m=m, nb=nb: e.tensor_mul(
                hgT[:, m * T + nb * 512: m * T + (nb + 1) * 512], hsb[:], pH[m][:, :]),
                reads=("hsb", f"pH{m}"), writes=("hgT",))
    for nb in range(4):
        for m2 in range(ND):
            pi = 6 + (m2 % 2)
            for kc2 in range(8):
                em.step("tensor", lambda e, m2=m2, kc2=kc2, nb=nb, pi=pi: e.matmul(
                    pH[pi][:, :], b1T[:, kc2 * D + m2 * 128: kc2 * D + (m2 + 1) * 128],
                    hgT[:, kc2 * T + nb * 512: kc2 * T + (nb + 1) * 512],
                    start=(kc2 == 0), stop=(kc2 == 7)),
                    reads=("b1T", "hgT") + ((f"pH{pi}",) if kc2 == 0 else ()),
                    writes=(f"pH{pi}",))
            em.step("scalar", lambda e, pi=pi: e.activation(
                otile[:], pH[pi][:, :], AF.Silu, scale=2.0),
                reads=(f"pH{pi}", "otile"), writes=("otile", f"pH{pi}"))
            em.dma("gpsimd", g1T_o[m2 * 128:(m2 + 1) * 128, nb * 512:(nb + 1) * 512],
                   otile[:], reads=("otile",))

    al.finish()


def l1ffn_inputs(core, xn1T):
    d = const_inputs(L1FFN_CONSTS)
    d.update({"xn1T": xn1T, "A1": core["L1_ffn_A"], "B1": core["L1_ffn_B"],
              "Wr1": core["L1_ffn_Wr"]})
    return d



COMB_CONSTS = ("eyef",)

def emit_COMB(em, io):
    nc = em.nc
    C = io
    x1 = io["x1"]
    s1T = io["s1T"]
    g1T = io["g1T"]
    out_o = io["out"]

    al = Alloc(nc)
    sb, ps = al.sb, al.ps
    eyef = sb("s_eyef", [128, 128], F32)
    x1t = [sb(f"s_x1t{i}", [128, D], F32) for i in range(2)]
    sTt = [sb(f"s_sTt{i}", [128, ND * 128], F32) for i in range(2)]
    gTt = [sb(f"s_gTt{i}", [128, ND * 128], BF16) for i in range(2)]
    dT = sb("s_dT", [128, 128], F32)
    pT = [ps(f"p_T{i}", [128, 512]) for i in range(2)]

    em.dma("sync", eyef[:], C["eyef"][:], writes=("eyef",))
    sv = s1T.rearrange("(c p) t -> p c t", p=128)
    gv = g1T.rearrange("(c p) t -> p c t", p=128)
    for i in range(NT):
        xb, xtag = x1t[i % 2], f"x1t{i % 2}"
        st, stag = sTt[i % 2], f"sTt{i % 2}"
        gt, gtag = gTt[i % 2], f"gTt{i % 2}"
        em.dma("sync", xb[:], x1[i * 128:(i + 1) * 128, :], reads=(xtag,), writes=(xtag,))
        em.dma("gpsimd", st[:].rearrange("p (c t) -> p c t", t=128),
               sv[:, :, i * 128:(i + 1) * 128], reads=(stag,), writes=(stag,))
        em.dma("scalar", gt[:].rearrange("p (c t) -> p c t", t=128),
               gv[:, :, i * 128:(i + 1) * 128], reads=(gtag,), writes=(gtag,))
        for dc in range(ND):
            pb, ptag = (pT[0], "pT0") if dc % 2 == 0 else (pT[1], "pT1")
            em.step("vector", lambda e, dc=dc, st=st, gt=gt: e.tensor_mul(
                dT[:], st[:, dc * 128:(dc + 1) * 128], gt[:, dc * 128:(dc + 1) * 128]),
                reads=(stag, gtag, "dT"), writes=("dT",))
            em.step("tensor", lambda e, pb=pb: e.transpose(pb[:, 0:128], dT[:], eyef[:]),
                    reads=("dT", "eyef", ptag), writes=(ptag,))
            em.step("vector", lambda e, dc=dc, pb=pb, xb=xb: e.tensor_add(
                xb[:, dc * 128:(dc + 1) * 128], xb[:, dc * 128:(dc + 1) * 128], pb[:, 0:128]),
                reads=(ptag, xtag), writes=(xtag, ptag))
        em.dma("sync", out_o[i * 128:(i + 1) * 128, :], xb[:], reads=(xtag,))

    al.finish()


def comb_inputs(x1, s1T, g1T):
    d = const_inputs(COMB_CONSTS)
    d.update({"x1": x1, "s1T": s1T, "g1T": g1T})
    return d



_PROGS = {}

def _prog(name, builder):
    if name not in _PROGS:
        nc = bass.Bass("TRN2", target_bir_lowering=False)
        builder(nc, strict=True)
        _PROGS[name] = nc
    return _PROGS[name]


def _run(name, builder, in_maps):
    nc = _prog(name, builder)
    res = run_bass_kernel_spmd(nc, in_maps, core_ids=list(range(NCORES)))
    return res.results


def kernel(**inputs):
    inputs = {k: np.ascontiguousarray(v) for k, v in inputs.items()}
    x_full = inputs["L0_attn_x"]

    maps = []
    for b in range(NCORES):
        core = dict(inputs)
        core["x0"] = np.ascontiguousarray(x_full[b])
        maps.append(l0_inputs(core))
    r = _run("L0", build_L0, maps)
    x1 = [r[b]["x1"] for b in range(NCORES)]
    xn1T = [r[b]["xn1T"] for b in range(NCORES)]

    maps = [inproj_inputs(inputs, xn1T[b]) for b in range(NCORES)]
    r = _run("INPROJ", build_INPROJ, maps)
    sg = [r[b]["sg"] for b in range(NCORES)]
    cvx = [r[b]["cvx"] for b in range(NCORES)]
    cvB = [r[b]["cvB"] for b in range(NCORES)]
    cvC = [r[b]["cvC"] for b in range(NCORES)]
    dtv = [r[b]["dtv"] for b in range(NCORES)]

    maps = [ssd_inputs(inputs, cvx[b], cvB[b], cvC[b], dtv[b], sg[b])
            for b in range(NCORES)]
    r = _run("SSD", build_SSD, maps)
    ygT = [r[b]["ygT"] for b in range(NCORES)]

    maps = [oproj_inputs(inputs, ygT[b]) for b in range(NCORES)]
    r = _run("OPROJ", build_OPROJ, maps)
    s1T = [r[b]["s1T"] for b in range(NCORES)]

    maps = [l1ffn_inputs(inputs, xn1T[b]) for b in range(NCORES)]
    r = _run("L1FFN", build_L1FFN, maps)
    g1T = [r[b]["g1T"] for b in range(NCORES)]

    maps = [comb_inputs(x1[b], s1T[b], g1T[b]) for b in range(NCORES)]
    r = _run("COMB", build_COMB, maps)
    out = np.stack([r[b]["out"] for b in range(NCORES)], axis=0)
    return out



PHASE_SPECS = {
    "PREFETCH": dict(
        consts=PRE_CONSTS,
        ins={"WpT": ([D, CPROJ], F32), "WoT": ([INTER, D], F32),
             "A1T": ([D, KR], F32), "B1T": ([KR, D], F32)},
        outs={},
        internal={"wbT": ([D, CPROJ], BF16), "woT": ([INTER, D], BF16),
                  "waT": ([D, KR], BF16), "wb1": ([KR, D], BF16)}),
    "L0": dict(
        consts=L0_CONSTS,
        ins={"x0": ([T, D], F32), "Wq": ([D, 128, 32], F32), "bq": ([128, 32], F32),
             "Wk": ([D, 128, 1], F32), "bk": ([128, 1], F32), "Wv": ([D, 128, 1], F32),
             "bv": ([128, 1], F32), "A0T": ([D, 128], F32), "B0T": ([128, D], F32),
             "Wr0": ([8, D], F32), "br0": ([8], F32)},
        outs={"x1": ([T, D], F32), "xn1T": ([D, T], BF16)},
        internal={}),
    "INPROJ": dict(
        consts=INPROJ_CONSTS,
        ins={"xn1T": ([D, T], BF16),
             "convw": ([CONV_DIM, 1, 4], F32), "convb": ([CONV_DIM], F32),
             "dtb": ([H], F32), "nw": ([INTER], F32)},
        outs={"sg": ([INTER, T], BF16), "cvx": ([INTER, T], BF16),
              "cvB": ([GN, T], BF16), "cvC": ([GN, T], BF16), "dtv": ([H, T], F32)},
        internal={}),
    "SSD": dict(
        consts=SSD_CONSTS,
        ins={"cvx": ([INTER, T], BF16), "cvB": ([GN, T], BF16), "cvC": ([GN, T], BF16),
             "dtv": ([H, T], F32), "sg": ([INTER, T], BF16),
             "alog": ([H], F32), "Din": ([H], F32)},
        outs={"ygT": ([INTER, T], BF16)},
        internal={"rowscr": ([128, 128], F32)}),
    "OPROJ": dict(
        consts=OPROJ_CONSTS,
        ins={"ygT": ([INTER, T], BF16)},
        outs={"s1T": ([D, T], F32)},
        internal={}),
    "L1FFN": dict(
        consts=L1FFN_CONSTS,
        ins={"xn1T": ([D, T], BF16), "Wr1T": ([D, NEXP], F32)},
        outs={"g1T": ([D, T], BF16)},
        internal={}),
    "COMB": dict(
        consts=COMB_CONSTS,
        ins={"x1": ([T, D], F32), "s1T": ([D, T], F32), "g1T": ([D, T], BF16)},
        outs={"out": ([T, D], F32)},
        internal={}),
}

EMITTERS = {"L0": emit_L0, "INPROJ": emit_INPROJ,
            "SSD": emit_SSD, "OPROJ": emit_OPROJ, "L1FFN": emit_L1FFN,
            "COMB": emit_COMB}


def full_barrier(em, targets=None):
    if targets is None:
        targets = {ch: c for ch, c in em.count.items() if c > 0}
    else:
        targets = {ch: c for ch, c in targets.items() if c > 0}
    for e in ENGINES:
        kn = em.know[e]
        eff = {}
        for ch2, c in targets.items():
            if ch2 == (e, "c"):
                continue
            if kn[ch2] < c:
                eff[ch2] = c
        for ch2, c in eff.items():
            s = em.snap[ch2][c]
            for g in CHANNELS:
                kn[g] = max(kn[g], s[g])
            kn[ch2] = max(kn[ch2], c)
        ch = (e, "c")
        em.count[ch] += 1
        cnt = em.count[ch]
        em.steps[e].append(((lambda e2: e2.nop()), eff, 1, ch))
        snap = dict(kn)
        snap[ch] = cnt
        em.snap[ch][cnt] = snap


def build_phase(nc, phase, strict=True):
    em = Emitter(nc, strict)
    spec = PHASE_SPECS[phase]
    io = declare_consts(nc, spec["consts"])
    for k, (shape, dt) in spec["ins"].items():
        io[k] = nc.dram_tensor(k, shape, dt, kind="ExternalInput").ap()
    for k, (shape, dt) in spec["outs"].items():
        io[k] = nc.dram_tensor(k, shape, dt, kind="ExternalOutput").ap()
    for k, (shape, dt) in spec["internal"].items():
        io[k] = nc.dram_tensor(k, shape, dt, kind="Internal").ap()
    EMITTERS[phase](em, io)
    em.barrier_step("sync")
    em.realize()
    em.finish()
    return em


PHASE_ORDER = ["L0", "INPROJ", "SSD", "OPROJ", "L1FFN", "COMB"]
DECL_ORDER = ["PREFETCH"] + PHASE_ORDER


def build_FUSED(nc, strict=True):
    em = Emitter(nc, strict)
    io = declare_consts(nc, tuple(CONST_SPECS.keys()))
    declared = {}
    produced = set()
    for ph in PHASE_ORDER:
        for k in PHASE_SPECS[ph]["outs"]:
            produced.add(k)
    produced.discard("out")
    for ph in DECL_ORDER:
        spec = PHASE_SPECS[ph]
        for k, (shape, dt) in spec["ins"].items():
            if k not in declared:
                kind = "Internal" if k in produced else "ExternalInput"
                declared[k] = nc.dram_tensor(k, shape, dt, kind=kind).ap()
        for k, (shape, dt) in spec["outs"].items():
            if k not in declared:
                kind = "ExternalOutput" if k == "out" else "Internal"
                declared[k] = nc.dram_tensor(k, shape, dt, kind=kind).ap()
        for k, (shape, dt) in spec["internal"].items():
            if k not in declared:
                declared[k] = nc.dram_tensor(k, shape, dt, kind="Internal").ap()
    io.update(declared)
    thunks = prefetch_chunks(em, io)

    def hook(i):
        take = 3 if i < NT - 1 else len(thunks)
        for _ in range(min(take, len(thunks))):
            thunks.pop(0)()

    emit_L0(em, io, tile_hook=hook)
    snap0 = dict(em.count)
    full_barrier(em, targets=snap0)
    for ph in PHASE_ORDER[1:]:
        EMITTERS[ph](em, io)
        full_barrier(em)
    em.barrier_step("sync")
    em.realize()
    em.finish()
    return em


def kernel(**inputs):
    inputs = {k: np.ascontiguousarray(v) for k, v in inputs.items()}
    x_full = inputs["L0_attn_x"]
    if "FUSED" not in _PROGS:
        nc = bass.Bass("TRN2", target_bir_lowering=False)
        build_FUSED(nc, strict=True)
        _PROGS["FUSED"] = nc
    nc = _PROGS["FUSED"]
    base = const_inputs(tuple(CONST_SPECS.keys()))
    ct = np.ascontiguousarray
    base.update({
        "Wq": inputs["L0_attn_Wq"], "bq": inputs["L0_attn_bq"],
        "Wk": inputs["L0_attn_Wk"], "bk": inputs["L0_attn_bk"],
        "Wv": inputs["L0_attn_Wv"], "bv": inputs["L0_attn_bv"],
        "A0T": ct(inputs["L0_ffn_A"].reshape(128, D).T),
        "B0T": ct(inputs["L0_ffn_B"].transpose(0, 2, 1).reshape(128, D)),
        "Wr0": inputs["L0_ffn_Wr"], "br0": inputs["L0_ffn_br"],
        "WpT": ct(inputs["L1_attn_in_proj_w"].T),
        "convw": inputs["L1_attn_conv_w"], "convb": inputs["L1_attn_conv_b"],
        "dtb": inputs["L1_attn_dt_bias"], "nw": inputs["L1_attn_norm_w"],
        "alog": inputs["L1_attn_A_log"], "Din": inputs["L1_attn_D"],
        "WoT": ct(inputs["L1_attn_out_proj_w"].T),
        "A1T": ct(inputs["L1_ffn_A"].reshape(KR, D).T),
        "B1T": ct(inputs["L1_ffn_B"].transpose(0, 2, 1).reshape(KR, D)),
        "Wr1T": ct(inputs["L1_ffn_Wr"].T),
    })
    in_maps = []
    for b in range(NCORES):
        m = dict(base)
        m["x0"] = np.ascontiguousarray(x_full[b])
        in_maps.append(m)
    res = run_bass_kernel_spmd(nc, in_maps, core_ids=list(range(NCORES)))
    return np.stack([res.results[b]["out"] for b in range(NCORES)], axis=0)


# version 11
# speedup vs baseline: 1.6256x; 1.1379x over previous
import numpy as np
import ml_dtypes
import concourse.bass as bass
import concourse.mybir as mybir
from concourse.bass_utils import run_bass_kernel_spmd

F32 = mybir.dt.float32
BF16 = mybir.dt.bfloat16
AF = mybir.ActivationFunctionType
ALU = mybir.AluOpType
AX = mybir.AxisListType
bf16 = ml_dtypes.bfloat16

NCORES = 8
T = 2048
D = 4096
NT = T // 128
ND = D // 128

H = 128; P = 64; G = 8; N = 128; CS = 128; K4 = 4
INTER = H * P
GN = G * N
CONV_DIM = INTER + 2 * GN
CPROJ = INTER + CONV_DIM + H
NCHUNK = T // CS
NEXP = 64; TOPK = 16; RANK = 16
KR = NEXP * RANK

INV_SQRT_QK = float(1.0 / np.sqrt(128.0))

ENGINES = ("tensor", "vector", "scalar", "gpsimd", "sync")
NDMACH = 4
CHANNELS = ([(e, "c") for e in ENGINES]
            + [(e, f"d{i}") for e in ("sync", "gpsimd", "scalar", "vector")
               for i in range(NDMACH)])


class Emitter:

    def __init__(self, nc, strict=False):
        self.nc = nc
        self.strict = strict
        self.steps = {e: [] for e in ENGINES}
        self.count = {ch: 0 for ch in CHANNELS}
        self.know = {e: {ch: 0 for ch in CHANNELS} for e in ENGINES}
        self.snap = {ch: {0: {c2: 0 for c2 in CHANNELS}} for ch in CHANNELS}
        self.last_writer = {}
        self.readers = {}
        self.dma_idx = {e: 0 for e in ENGINES}
        self.n_waits = 0
        self.audit = []

    def step(self, eng, fn, reads=(), writes=(), dma=False):
        if dma:
            sub = self.dma_idx[eng] % NDMACH
            self.dma_idx[eng] += 1
            ch = (eng, f"d{sub}")
        else:
            ch = (eng, "c")
        waits = {}
        def add_need(src):
            (sch, c) = src
            if sch == ch and not (self.strict and sch[1] == "c" and eng != "tensor"):
                return
            if sch[0] == eng and sch[1] == "c" and not dma and (
                    not self.strict or eng == "tensor"):
                return
            waits[sch] = max(waits.get(sch, 0), c)
        for t_ in reads:
            lw = self.last_writer.get(t_)
            if lw is not None:
                add_need(lw)
        for t_ in writes:
            lw = self.last_writer.get(t_)
            if lw is not None:
                add_need(lw)
            for r in self.readers.get(t_, ()):
                add_need(r)
        if dma and self.count[ch] > 0:
            waits[ch] = max(waits.get(ch, 0), self.count[ch])
        kn = self.know[eng]
        eff = {ch2: c for ch2, c in waits.items() if kn[ch2] < c}
        for ch2, c in eff.items():
            s = self.snap[ch2][c]
            for g in CHANNELS:
                kn[g] = max(kn[g], s[g])
            kn[ch2] = max(kn[ch2], c)
        amount = 16 if dma else 1
        self.count[ch] += amount
        cnt = self.count[ch]
        self.steps[eng].append((fn, eff, amount, ch))
        self.n_waits += len(eff)
        for t_ in writes:
            self.last_writer[t_] = (ch, cnt)
            self.readers[t_] = []
        for t_ in reads:
            self.readers.setdefault(t_, []).append((ch, cnt))
        snap = dict(kn)
        snap[ch] = cnt
        self.snap[ch][cnt] = snap
        self.audit.append((eng, len(self.steps[eng]) - 1, tuple(reads),
                           tuple(writes), ch, cnt, dma))
        return cnt

    def dma(self, queue, out, in_, reads=(), writes=(), **kw):
        self.step(queue, lambda e: e.dma_start(out=out, in_=in_, **kw),
                  reads=reads, writes=writes, dma=True)

    def barrier_step(self, eng="sync"):
        targets = {ch: c for ch, c in self.count.items() if c > 0}
        kn = self.know[eng]
        eff = {ch: c for ch, c in targets.items()
               if kn[ch] < c and not (ch[0] == eng and ch[1] == "c")}
        for ch2, c in eff.items():
            s = self.snap[ch2][c]
            for g in CHANNELS:
                kn[g] = max(kn[g], s[g])
            kn[ch2] = max(kn[ch2], c)
        ch = (eng, "c")
        self.count[ch] += 1
        cnt = self.count[ch]
        self.steps[eng].append((lambda e: e.nop(), eff, 1, ch))
        snap = dict(kn)
        snap[ch] = cnt
        self.snap[ch][cnt] = snap

    def realize(self):
        nc = self.nc
        self._cms = []
        sems = {}
        for ch in CHANNELS:
            cm = nc.semaphore(f"sem_{ch[0]}_{ch[1]}")
            sems[ch] = cm.__enter__()
            self._cms.append(cm)
        block_cm = nc.Block()
        block = block_cm.__enter__()
        self._cms.append(block_cm)

        def emit_for(e):
            def body(eng):
                for (fn, waits, amount, ch) in self.steps[e]:
                    for ch2, c in waits.items():
                        eng.wait_ge(sems[ch2], c)
                    inst = fn(eng)
                    inst.then_inc(sems[ch], amount)
            return body

        for e in ENGINES:
            if self.steps[e]:
                getattr(block, e)(emit_for(e))
        return block

    def finish(self):
        for cm in reversed(self._cms):
            cm.__exit__(None, None, None)



_CONSTS = None

def build_consts():
    global _CONSTS
    if _CONSTS is not None:
        return _CONSTS
    c = {}
    c["eyeb"] = np.eye(128, dtype=bf16)
    c["eyef"] = np.eye(128, dtype=np.float32)
    c["onesf"] = np.ones((1, 128), dtype=np.float32)
    sel = np.zeros((128, 32 * 128), dtype=bf16)
    for q in range(32):
        for f in range(128):
            sel[4 * q + f // 32, q * 128 + f] = 1
    c["selq"] = sel
    s64 = np.zeros((128, 128 * 64), dtype=bf16)
    for h in range(128):
        s64[h, h * 64:(h + 1) * 64] = 1
    c["sel64"] = s64
    ind2 = np.zeros((2, 128), dtype=np.float32)
    ind2[0, 0:64] = 1.0
    ind2[1, 64:128] = 1.0
    c["ind2"] = ind2
    e8 = np.zeros((8, 128), dtype=bf16)
    for j in range(128):
        e8[j // 16, j] = 1
    c["e8"] = e8
    e64 = np.zeros((64, 1024), dtype=bf16)
    for j in range(1024):
        e64[j // 16, j] = 1
    c["e64"] = e64
    c["cumU"] = np.triu(np.ones((128, 128), dtype=np.float32))
    c["mask01"] = np.triu(np.ones((128, 128), dtype=np.float32))
    _CONSTS = {k: np.ascontiguousarray(v) for k, v in c.items()}
    return _CONSTS


CONST_SPECS = {
    "eyeb": BF16, "eyef": F32, "onesf": F32, "selq": BF16,
    "e8": BF16, "e64": BF16, "cumU": F32, "mask01": F32,
    "sel64": BF16, "ind2": F32,
}


def lowprec(e, f):
    with e.bass.allow_low_precision(reason="bf16 weight fold / operand cast"):
        return f()


def declare_consts(nc, names):
    cvals = build_consts()
    return {k: nc.dram_tensor("c_" + k, list(cvals[k].shape), CONST_SPECS[k],
                              kind="ExternalInput").ap() for k in names}


def const_inputs(names):
    cvals = build_consts()
    return {"c_" + k: cvals[k] for k in names}


class Alloc:
    _n = 0

    def __init__(self, nc):
        self.nc = nc
        self.cms = []
        Alloc._n += 1
        self.prefix = f"a{Alloc._n}_"

    def sb(self, name, shape, dt):
        cm = self.nc.sbuf_tensor(self.prefix + name, shape, dt)
        t = cm.__enter__(); self.cms.append(cm); return t

    def ps(self, name, shape, dt=F32):
        cm = self.nc.psum_tensor(self.prefix + name, shape, dt)
        t = cm.__enter__(); self.cms.append(cm); return t

    def finish(self):
        for cm in reversed(self.cms):
            cm.__exit__(None, None, None)



L0_CONSTS = ("eyeb", "eyef", "selq", "e8")

def emit_L0(em, io, tile_hook=None):
    nc = em.nc
    C = io
    x0 = io["x0"]
    Wq = io["Wq"]
    bq = io["bq"]
    Wk = io["Wk"]
    bk = io["bk"]
    Wv = io["Wv"]
    bv = io["bv"]
    A0T = io["A0T"]
    B0T = io["B0T"]
    Wr0 = io["Wr0"]
    br0 = io["br0"]
    x1_o = io["x1"]
    xn1T_o = io["xn1T"]

    al = Alloc(nc)
    sb, ps = al.sb, al.ps

    eyeb = sb("s_eyeb", [128, 128], BF16)
    eyef = sb("s_eyef", [128, 128], F32)
    selq = sb("s_selq", [128, 32 * 128], BF16)
    e8 = sb("s_e8", [8, 128], BF16)
    wqs = sb("s_wqs", [128, ND * 128], BF16)
    wkT = sb("s_wkT", [128, ND * 128], BF16)
    wvT = sb("s_wvT", [128, ND * 128], BF16)
    a0T = sb("s_a0T", [128, ND * 128], BF16)
    b0T = sb("s_b0T", [128, ND * 128], BF16)
    wr0T = sb("s_wr0T", [128, ND * 8], BF16)
    bqs = sb("s_bqs", [128, 1], F32)
    bkc = sb("s_bkc", [128, 1], F32)
    bvc = sb("s_bvc", [128, 1], F32)
    br0c = sb("s_br0c", [8, 1], F32)
    bq_stage = sb("s_bqstage", [128, 32], F32)
    epscol = sb("s_epscol", [128, 1], F32)

    mat = sb("s_mat", [128, 128 * 128], BF16)
    wq_stage = [mat[:, 0:4096], mat[:, 4096:8192]]
    wr0_stage = mat[0:8, 12288:16384]
    PREP_TAGS = ("mq0", "mq1", "mw")

    x0t = [sb(f"s_x0t{i}", [128, D], F32) for i in range(2)]
    stat = sb("s_stat", [128, 8], F32)
    g0T = sb("s_g0T", [128, ND * 128], F32)
    d0T = sb("s_d0T", [128, ND * 128], F32)
    xnb = sb("s_xnb", [128, D], BF16)
    xnT = sb("s_xnT", [128, ND * 128], BF16)
    a_T = sb("s_aT", [128, 128], F32)
    k_T = sb("s_kT", [128, 128], F32)
    v_T = sb("s_vT", [128, 128], F32)
    a_tm = sb("s_atm", [128, 128], F32)
    k_tm = sb("s_ktm", [128, 128], F32)
    v_tm = sb("s_vtm", [128, 128], F32)
    kmm = sb("s_kmm", [128, 4], F32)
    negm = sb("s_negm", [128, 128], F32)
    s1col = sb("s_s1", [128, 128], F32)
    s2col = sb("s_s2", [128, 128], F32)
    ttg = [sb(f"s_ttg{i}", [128, 128], F32) for i in range(4)]
    o_tm = sb("s_otm", [128, 128], F32)
    o_Tb = sb("s_oTb", [128, 128], BF16)
    rt_f = sb("s_rtf", [8, 128], F32)
    rt_tm = sb("s_rttm", [128, 8], F32)
    rt_scr = sb("s_rtscr", [128, 8], F32)
    g8T = sb("s_g8T", [8, 128], BF16)
    scoreB = sb("s_scoreB", [128, 128], F32)
    glow = sb("s_glow", [128, 128], BF16)
    x1t = sb("s_x1t", [128, D], F32)
    xn1b = sb("s_xn1b", [128, D], BF16)
    xn1Tt = sb("s_xn1Tt", [128, ND * 128], BF16)

    pA = ps("p_A", [128, 512])
    pB = ps("p_B", [128, 512])
    pC = ps("p_C", [128, 512])
    pD = ps("p_D", [128, 512])
    pT1 = ps("p_T1", [128, 512])
    pT2 = ps("p_T2", [128, 512])
    pb1 = ps("p_b1", [128, 1024], BF16)
    pb2 = ps("p_b2", [128, 1024], BF16)

    em.step("gpsimd", lambda e: e.memset(epscol[:], 1e-5), writes=("epscol",))
    em.dma("sync", eyeb[:], C["eyeb"][:], writes=("eyeb",))
    em.dma("sync", eyef[:], C["eyef"][:], writes=("eyef",))
    em.dma("sync", selq[:], C["selq"][:], writes=("selq",))
    em.dma("sync", e8[:], C["e8"][:], writes=("e8",))
    em.dma("gpsimd", bq_stage[:], bq[:], writes=("bq_stage",))
    em.step("vector", lambda e: e.reduce_sum(bqs[:], bq_stage[:], axis=AX.X),
            reads=("bq_stage",), writes=("bqs",))
    em.dma("gpsimd", bkc[:], bk[:], writes=("bkc",))
    em.dma("gpsimd", bvc[:], bv[:], writes=("bvc",))
    em.dma("gpsimd", br0c[:], br0[:].rearrange("h -> h ()"), writes=("br0c",))

    for kc in range(ND):
        wst = wq_stage[kc % 2]
        wtg = f"mq{kc % 2}"
        em.dma("gpsimd", wst[:],
               Wq[kc * 128:(kc + 1) * 128, :, :].rearrange("p d h -> p (d h)"),
               reads=(wtg,), writes=(wtg,))
        em.step("vector", lambda e, kc=kc, wst=wst: lowprec(e, lambda: e.reduce_sum(
            wqs[:, kc * 128:(kc + 1) * 128].rearrange("p (d a) -> p d a", a=1),
            wst[:].rearrange("p (d h) -> p d h", h=32), axis=AX.X)),
            reads=(wtg,), writes=("wqs",))
        em.dma("gpsimd", wkT[:, kc * 128:(kc + 1) * 128],
               Wk[kc * 128:(kc + 1) * 128, :, 0], writes=("wkT",))
        em.dma("gpsimd", wvT[:, kc * 128:(kc + 1) * 128],
               Wv[kc * 128:(kc + 1) * 128, :, 0], writes=("wvT",))

    em.dma("gpsimd", a0T[:].rearrange("p (c h) -> p c h", h=128),
           A0T.rearrange("(c p) h -> p c h", p=128), writes=("a0T",))
    em.dma("gpsimd", b0T[:], B0T[:], writes=("b0T",))
    em.dma("gpsimd", wr0_stage[:], Wr0[:], writes=("mw",))
    for kc in range(ND):
        pt, ptag = (pb1, "pb1") if kc % 2 == 0 else (pb2, "pb2")
        em.step("tensor", lambda e, kc=kc, pt=pt: e.transpose(
            pt[0:128, 0:8], wr0_stage[:, kc * 128:(kc + 1) * 128], eyeb[0:8, 0:8]),
            reads=("mw", "eyeb", ptag), writes=(ptag,))
        em.step("vector", lambda e, kc=kc, pt=pt: e.tensor_copy(
            wr0T[:, kc * 8:(kc + 1) * 8], pt[0:128, 0:8]),
            reads=(ptag,), writes=("wr0T", ptag))

    def emit_layernorm(src, out_bf, tag_src, tag_out, dump, tag_dump):
        em.step("vector", lambda e: e.reduce_sum(stat[:, 0:1], src[:], axis=AX.X),
                reads=(tag_src,), writes=("stat",))
        em.step("scalar", lambda e: e.activation(
            dump[:], src[:], AF.Square, accum_out=stat[:, 1:2]),
            reads=(tag_src,), writes=("stat", tag_dump))
        em.step("vector", lambda e: e.tensor_scalar(
            stat[:, 2:3], stat[:, 0:1], 1.0 / D, None, ALU.mult),
            reads=("stat",), writes=("stat",))
        em.step("vector", lambda e: e.tensor_mul(stat[:, 5:6], stat[:, 2:3], stat[:, 2:3]),
                reads=("stat",), writes=("stat",))
        em.step("vector", lambda e: e.scalar_tensor_tensor(
            stat[:, 3:4], stat[:, 1:2], 1.0 / D, stat[:, 5:6], ALU.mult, ALU.subtract),
            reads=("stat",), writes=("stat",))
        em.step("scalar", lambda e: e.activation(
            stat[:, 6:7], stat[:, 3:4], AF.Sqrt, bias=epscol[:]),
            reads=("stat", "epscol"), writes=("stat",))
        em.step("vector", lambda e: e.reciprocal(stat[:, 4:5], stat[:, 6:7]),
                reads=("stat",), writes=("stat",))
        em.step("vector", lambda e: e.tensor_scalar(
            out_bf[:], src[:], stat[:, 2:3], stat[:, 4:5], ALU.subtract, ALU.mult),
            reads=(tag_src, "stat"), writes=(tag_out,))

    def emit_transpose_tm2fm(src_bf, dst, tag_src, tag_dst):
        for dc in range(ND):
            pt, ptag = (pb1, "pb1") if dc % 2 == 0 else (pb2, "pb2")
            em.step("tensor", lambda e, dc=dc, pt=pt: e.transpose(
                pt[:, 0:128], src_bf[:, dc * 128:(dc + 1) * 128], eyeb[:]),
                reads=(tag_src, "eyeb", ptag), writes=(ptag,))
            em.step("vector", lambda e, dc=dc, pt=pt: e.tensor_copy(
                dst[:, dc * 128:(dc + 1) * 128], pt[:, 0:128]),
                reads=(ptag,), writes=(tag_dst, ptag))

    for i in range(NT):
        if tile_hook is not None:
            tile_hook(i)
        xb = x0t[i % 2]
        xtag = f"x0t{i % 2}"
        em.dma("sync", xb[:], x0[i * 128:(i + 1) * 128, :], writes=(xtag,))
        emit_layernorm(xb, xnb, xtag, "xnb", g0T, "g0T")
        emit_transpose_tm2fm(xnb, xnT, "xnb", "xnT")

        def mm_acc(psum, pslice, lhs_buf, tag_lhs, lhs_w, m_width, ptag):
            for kc in range(ND):
                em.step("tensor", lambda e, kc=kc: e.matmul(
                    psum[0:m_width, pslice[0]:pslice[1]],
                    lhs_buf[:, kc * lhs_w: kc * lhs_w + m_width],
                    xnT[:, kc * 128:(kc + 1) * 128],
                    start=(kc == 0), stop=(kc == ND - 1)),
                    reads=(tag_lhs, "xnT") + ((ptag,) if kc == 0 else ()),
                    writes=(ptag,))
        mm_acc(pA, (0, 128), wqs, "wqs", 128, 128, "pA")
        mm_acc(pB, (0, 128), wkT, "wkT", 128, 128, "pB")
        mm_acc(pC, (0, 128), wvT, "wvT", 128, 128, "pC")
        em.step("vector", lambda e: e.tensor_scalar(
            a_T[:], pA[:, 0:128], bqs[:], INV_SQRT_QK, ALU.add, ALU.mult),
            reads=("pA", "bqs"), writes=("a_T",))
        em.step("vector", lambda e: e.tensor_scalar(
            k_T[:], pB[:, 0:128], bkc[:], None, ALU.add),
            reads=("pB", "bkc"), writes=("k_T",))
        em.step("vector", lambda e: e.tensor_scalar(
            v_T[:], pC[:, 0:128], bvc[:], None, ALU.add),
            reads=("pC", "bvc"), writes=("v_T",))
        for (srcb, dstb, st, dt_) in ((a_T, a_tm, "a_T", "a_tm"),
                                      (k_T, k_tm, "k_T", "k_tm"),
                                      (v_T, v_tm, "v_T", "v_tm")):
            em.step("tensor", lambda e, srcb=srcb: e.transpose(
                pT1[:, 0:128], srcb[:], eyef[:]),
                reads=(st, "eyef", "pT1"), writes=("pT1",))
            em.step("vector", lambda e, dstb=dstb: e.tensor_copy(dstb[:], pT1[:, 0:128]),
                    reads=("pT1",), writes=(dt_, "pT1"))
        em.step("vector", lambda e: e.reduce_max(kmm[:, 0:1], k_tm[:], axis=AX.X),
                reads=("k_tm",), writes=("kmm",))
        em.step("vector", lambda e: e.tensor_reduce(kmm[:, 1:2], k_tm[:], AX.X, ALU.min),
                reads=("k_tm",), writes=("kmm",))
        em.step("vector", lambda e: e.tensor_scalar(
            kmm[:, 2:4], kmm[:, 0:2], -1.0, None, ALU.mult),
            reads=("kmm",), writes=("kmm",))
        em.step("vector", lambda e: e.tensor_scalar_mul(negm[:], a_tm[:], kmm[:, 2:3]),
                reads=("a_tm", "kmm"), writes=("negm",))
        em.step("vector", lambda e: e.scalar_tensor_tensor(
            negm[:], a_tm[:], kmm[:, 3:4], negm[:], ALU.mult, ALU.min),
            reads=("a_tm", "kmm", "negm"), writes=("negm",))
        for d in range(128):
            mtg = f"mat{d % 8}"
            wtags_d = (mtg, "s1col") + (
                PREP_TAGS if (i == 0 and d == 0) else ())
            em.step("scalar", lambda e, d=d: e.activation(
                mat[:, d * 128:(d + 1) * 128], k_tm[:], AF.Exp,
                bias=negm[:, d:d + 1], scale=a_tm[:, d:d + 1],
                accum_out=s1col[:, d:d + 1]),
                reads=("k_tm", "negm", "a_tm"), writes=wtags_d)
            tg = f"ttg{d % 4}"
            em.step("gpsimd", lambda e, d=d: e.tensor_tensor(
                ttg[d % 4][:], mat[:, d * 128:(d + 1) * 128], v_tm[:], ALU.mult),
                reads=(mtg, "v_tm"), writes=(tg,))
            em.step("vector", lambda e, d=d: e.reduce_sum(
                s2col[:, d:d + 1], ttg[d % 4][:], axis=AX.X),
                reads=(tg,), writes=("s2v",))
        em.step("vector", lambda e: e.reciprocal(s1col[:], s1col[:]),
                reads=("s1col",), writes=("s1col",))
        em.step("vector", lambda e: e.tensor_mul(o_tm[:], s2col[:], s1col[:]),
                reads=("s2v", "s1col"), writes=("o_tm",))
        em.step("tensor", lambda e: e.transpose(pT1[:, 0:128], o_tm[:], eyef[:]),
                reads=("o_tm", "eyef", "pT1"), writes=("pT1",))
        em.step("vector", lambda e: e.tensor_copy(o_Tb[:], pT1[:, 0:128]),
                reads=("pT1",), writes=("o_Tb", "pT1"))

        mm_acc(pD, (0, 128), wr0T, "wr0T", 8, 8, "pD")
        em.step("vector", lambda e: e.tensor_scalar(
            rt_f[:], pD[0:8, 0:128], br0c[:], None, ALU.add),
            reads=("pD", "br0c"), writes=("rt_f",))
        em.step("tensor", lambda e: e.transpose(pT1[0:128, 0:8], rt_f[:], eyef[0:8, 0:8]),
                reads=("rt_f", "eyef", "pT1"), writes=("pT1",))
        em.step("vector", lambda e: e.tensor_copy(rt_tm[:], pT1[0:128, 0:8]),
                reads=("pT1",), writes=("rt_tm", "pT1"))
        em.step("vector", lambda e: e.reduce_max(stat[:, 7:8], rt_tm[:], axis=AX.X),
                reads=("rt_tm",), writes=("stat",))
        em.step("vector", lambda e: e.tensor_scalar(
            stat[:, 7:8], stat[:, 7:8], -1.0, None, ALU.mult),
            reads=("stat",), writes=("stat",))
        em.step("scalar", lambda e: e.activation(
            rt_scr[:], rt_tm[:], AF.Exp, bias=stat[:, 7:8], accum_out=stat[:, 6:7]),
            reads=("rt_tm", "stat"), writes=("rt_scr", "stat"))
        em.step("vector", lambda e: e.reciprocal(stat[:, 6:7], stat[:, 6:7]),
                reads=("stat",), writes=("stat",))
        em.step("vector", lambda e: e.tensor_scalar(
            rt_scr[:], rt_scr[:], stat[:, 6:7], 8.0, ALU.mult, ALU.mult),
            reads=("rt_scr", "stat"), writes=("rt_scr",))
        em.step("tensor", lambda e: e.transpose(pT1[0:8, 0:128], rt_scr[:], eyef[:]),
                reads=("rt_scr", "eyef", "pT1"), writes=("pT1",))
        em.step("vector", lambda e: e.tensor_copy(g8T[:], pT1[0:8, 0:128]),
                reads=("pT1",), writes=("g8T", "pT1"))
        em.step("tensor", lambda e: e.matmul(
            pD[:, 128:256], e8[:], g8T[:], start=True, stop=True),
            reads=("e8", "g8T", "pD"), writes=("pD",))
        em.step("vector", lambda e: e.tensor_copy(scoreB[:], pD[:, 128:256]),
                reads=("pD",), writes=("scoreB", "pD"))
        mm_acc(pA, (128, 256), a0T, "a0T", 128, 128, "pA")
        em.step("vector", lambda e: e.tensor_mul(glow[:], pA[:, 128:256], scoreB[:]),
                reads=("pA", "scoreB"), writes=("glow",))
        for oc in range(ND):
            pt, ptg = (pA, "pA") if oc % 2 == 0 else (pB, "pB")
            em.step("tensor", lambda e, oc=oc, pt=pt: e.matmul(
                pt[:, 256:384], b0T[:, oc * 128:(oc + 1) * 128], glow[:],
                start=True, stop=True),
                reads=("b0T", "glow", ptg), writes=(ptg,))
            em.step("scalar", lambda e, oc=oc, pt=pt: e.activation(
                g0T[:, oc * 128:(oc + 1) * 128], pt[:, 256:384], AF.Silu, scale=2.0),
                reads=(ptg,), writes=("g0T",))
        for q in range(ND):
            pt, ptg = (pC, "pC") if q % 2 == 0 else (pD, "pD")
            em.step("tensor", lambda e, q=q, pt=pt: e.matmul(
                pt[:, 256:384], selq[:, q * 128:(q + 1) * 128], o_Tb[:],
                start=True, stop=True),
                reads=("selq", "o_Tb", ptg), writes=(ptg,))
            em.step("vector", lambda e, q=q, pt=pt: e.tensor_mul(
                d0T[:, q * 128:(q + 1) * 128], pt[:, 256:384],
                g0T[:, q * 128:(q + 1) * 128]),
                reads=(ptg, "g0T"), writes=("d0T",))
        for dc in range(ND):
            pt, ptg = (pT1, "pT1") if dc % 2 == 0 else (pT2, "pT2")
            em.step("tensor", lambda e, dc=dc, pt=pt: e.transpose(
                pt[:, 0:128], d0T[:, dc * 128:(dc + 1) * 128], eyef[:]),
                reads=("d0T", "eyef", ptg), writes=(ptg,))
            em.step("vector", lambda e, dc=dc, pt=pt, xb=xb: e.tensor_add(
                x1t[:, dc * 128:(dc + 1) * 128],
                xb[:, dc * 128:(dc + 1) * 128], pt[:, 0:128]),
                reads=(ptg, xtag, "x1t"), writes=("x1t", ptg))
        em.dma("sync", x1_o[i * 128:(i + 1) * 128, :], x1t[:], reads=("x1t",))
        emit_layernorm(x1t, xn1b, "x1t", "xn1b", g0T, "g0T")
        emit_transpose_tm2fm(xn1b, xn1Tt, "xn1b", "xn1Tt")
        em.dma("sync",
               xn1T_o.rearrange("(c p) t -> p c t", p=128)[:, :, i * 128:(i + 1) * 128],
               xn1Tt[:].rearrange("p (c t) -> p c t", t=128),
               reads=("xn1Tt",))

    al.finish()


def fix_wkv_dma():
    pass


def l0_inputs(core):
    d = const_inputs(L0_CONSTS)
    d.update({
        "x0": core["x0"],
        "Wq": core["L0_attn_Wq"], "bq": core["L0_attn_bq"],
        "Wk": core["L0_attn_Wk"], "bk": core["L0_attn_bk"],
        "Wv": core["L0_attn_Wv"], "bv": core["L0_attn_bv"],
        "A0": core["L0_ffn_A"], "B0": core["L0_ffn_B"],
        "Wr0": core["L0_ffn_Wr"], "br0": core["L0_ffn_br"],
    })
    return d


def kernel(**inputs):
    raise NotImplementedError("under construction")



PRE_CONSTS = ()

def prefetch_chunks(em, io):
    WpT = io["WpT"]
    WoT = io["WoT"]
    A1T = io["A1T"]
    B1T = io["B1T"]
    wbT = io["wbT"]
    woT = io["woT"]
    waT = io["waT"]
    wb1 = io["wb1"]
    thunks = []
    for ci in range(4):
        for h in range(4):
            r0 = ci * 1024 + h * 256
            thunks.append(lambda ci=ci, r0=r0: em.dma(
                "gpsimd", wbT[r0:r0 + 256, :], WpT[r0:r0 + 256, :],
                writes=(f"wbT{ci}",)))
    for ci in range(8):
        for h in range(2):
            r0 = ci * 1024 + h * 512
            thunks.append(lambda ci=ci, r0=r0: em.dma(
                "gpsimd", woT[r0:r0 + 512, :], WoT[r0:r0 + 512, :],
                writes=(f"woT{ci}",)))
    thunks.append(lambda: em.dma("gpsimd", waT[:], A1T[:], writes=("waT",)))
    thunks.append(lambda: em.dma("gpsimd", wb1[:], B1T[:], writes=("wb1",)))
    return thunks



INPROJ_CONSTS = ("eyef",)
NM_PROJ = CPROJ // 128
CAST_ROWS = 512

def emit_INPROJ(em, io):
    nc = em.nc
    C = io
    xn1T = io["xn1T"]
    wbT = io["wbT"]
    convw = io["convw"]
    convb = io["convb"]
    dtb = io["dtb"]
    nw = io["nw"]
    sg_o = io["sg"]
    cvx_o = io["cvx"]
    cvB_o = io["cvB"]
    cvC_o = io["cvC"]
    dtv_o = io["dtv"]

    al = Alloc(nc)
    sb, ps = al.sb, al.ps
    eyef = sb("s_eyef", [128, 128], F32)
    xn1sb = sb("s_xn1sb", [128, ND * T], BF16)
    lbuf = [sb(f"s_lbuf{i}", [128, ND * 256], BF16) for i in range(2)]
    cvpad = [sb(f"s_cvpad{i}", [128, T + 3], F32) for i in range(2)]
    acc = [sb(f"s_acc{i}", [128, T], F32) for i in range(2)]
    otile = [sb(f"s_otile{i}", [128, T], BF16) for i in range(2)]
    cw = sb("s_cw", [128, 80 * 4], F32)
    cb = sb("s_cb", [128, 80], F32)
    nwc = sb("s_nwc", [128, 64], F32)
    dtbc = sb("s_dtbc", [128, 1], F32)
    onec = sb("s_onec", [128, 1], F32)
    stage = sb("s_stage", [128, 128], F32)
    pP = [ps(f"p_P{i}", [128, 512]) for i in range(8)]

    em.dma("sync", eyef[:], C["eyef"][:], writes=("eyef",))
    for i in range(2):
        em.step("gpsimd", lambda e, i=i: e.memset(cvpad[i][:, 0:3], 0.0),
                writes=(f"cvp{i}",))
    em.step("gpsimd", lambda e: e.memset(onec[:], 1.0), writes=("onec",))
    xv = xn1T.rearrange("(c p) t -> p c t", p=128)
    for j in range(4):
        em.dma("sync", xn1sb[:].rearrange("p (c t) -> p c t", t=T)[:, j * 8:(j + 1) * 8, :],
               xv[:, j * 8:(j + 1) * 8, :], writes=("xn1sb",))
    for j in range(80):
        em.dma("scalar", cw[:, j * 4:(j + 1) * 4],
               convw[:, 0, :].rearrange("(n p) j -> p n j", p=128)[:, j, :],
               writes=("cw",))
    em.dma("gpsimd", stage[0:80, :], convb[:].rearrange("(n p) -> n p", p=128),
           writes=("stage",))
    em.step("tensor", lambda e: e.transpose(pP[0][0:128, 0:80], stage[0:80, :], eyef[0:80, 0:80]),
            reads=("stage", "eyef", "pP0"), writes=("pP0",))
    em.step("vector", lambda e: e.tensor_copy(cb[:], pP[0][0:128, 0:80]),
            reads=("pP0",), writes=("cb", "pP0"))
    em.dma("gpsimd", stage[0:1, 0:128], dtb[:].rearrange("h -> () h"), writes=("stage",))
    em.step("tensor", lambda e: e.transpose(pP[0][0:128, 0:1], stage[0:1, 0:128], eyef[0:1, 0:1]),
            reads=("stage", "eyef", "pP0"), writes=("pP0",))
    em.step("vector", lambda e: e.tensor_copy(dtbc[:], pP[0][0:128, 0:1]),
            reads=("pP0",), writes=("dtbc", "pP0"))
    em.dma("gpsimd", stage[0:64, :], nw[:].rearrange("(n p) -> n p", p=128),
           writes=("stage",))
    em.step("tensor", lambda e: e.transpose(pP[0][0:128, 0:64], stage[0:64, :], eyef[0:64, 0:64]),
            reads=("stage", "eyef", "pP0"), writes=("pP0",))
    em.step("vector", lambda e: e.tensor_copy(nwc[:], pP[0][0:128, 0:64]),
            reads=("pP0",), writes=("nwc", "pP0"))

    ngrp = (NM_PROJ + 1) // 2
    for g in range(ngrp):
        gm = 2 if (g + 1) * 2 <= NM_PROJ else NM_PROJ - g * 2
        lb = lbuf[g % 2]
        ltag = f"lbuf{g % 2}"
        if gm == 2:
            for kc0 in range(0, ND, 8):
                q = "sync" if (kc0 // 8) % 2 == 0 else "scalar"
                em.dma(q,
                       lb[:, kc0 * 256:(kc0 + 8) * 256].rearrange(
                           "p (f m) -> p f m", f=8),
                       wbT[kc0 * 128:(kc0 + 8) * 128,
                           g * 256: g * 256 + 256].rearrange(
                           "(f p) m -> p f m", p=128),
                       reads=(f"wbT{kc0 // 8}",) + ((ltag,) if kc0 == 0 else ()),
                       writes=(ltag,))
        else:
            for kc in range(ND):
                em.dma("sync", lb[:, kc * 256: kc * 256 + gm * 128],
                       wbT[kc * 128:(kc + 1) * 128, g * 256: g * 256 + gm * 128],
                       reads=(f"wbT{kc // 8}",) + ((ltag,) if kc < 2 else ()),
                       writes=(ltag,))
        for mi in range(gm):
            m = g * 2 + mi
            cv = cvpad[m % 2]
            cvt = f"cvp{m % 2}"
            ac = acc[m % 2]
            act = f"acc{m % 2}"
            ot = otile[m % 2]
            ott = f"otile{m % 2}"
            for kc in range(ND):
                for nb in range(4):
                    pi = (m % 2) * 4 + nb
                    em.step("tensor", lambda e, kc=kc, nb=nb, lb=lb, mi=mi, pi=pi: e.matmul(
                        pP[pi][:, :], lb[:, kc * 256 + mi * 128: kc * 256 + mi * 128 + 128],
                        xn1sb[:, kc * T + nb * 512: kc * T + (nb + 1) * 512],
                        start=(kc == 0), stop=(kc == ND - 1)),
                        reads=(ltag, "xn1sb") + ((f"pP{pi}",) if kc == 0 else ()),
                        writes=(f"pP{pi}",))
            for nb in range(4):
                pi = (m % 2) * 4 + nb
                if nb < 2:
                    em.step("vector", lambda e, nb=nb, pi=pi, cv=cv: e.tensor_copy(
                        cv[:, 3 + nb * 512: 3 + (nb + 1) * 512], pP[pi][:, :]),
                        reads=(f"pP{pi}",), writes=(cvt, f"pP{pi}"))
                else:
                    em.step("scalar", lambda e, nb=nb, pi=pi, cv=cv: e.copy(
                        cv[:, 3 + nb * 512: 3 + (nb + 1) * 512], pP[pi][:, :]),
                        reads=(f"pP{pi}",), writes=(cvt, f"pP{pi}"))
            if m < 64:
                em.step("scalar", lambda e, cv=cv, ac=ac: e.activation(
                    ac[:], cv[:, 3:3 + T], AF.Silu),
                    reads=(cvt,), writes=(act,))
                em.step("vector", lambda e, m=m, ot=ot, ac=ac: e.tensor_scalar_mul(
                    ot[:], ac[:], nwc[:, m:m + 1]),
                    reads=(act, "nwc"), writes=(ott,))
                em.dma("gpsimd", sg_o[m * 128:(m + 1) * 128, :], ot[:],
                       reads=(ott,))
            elif m < 144:
                j = m - 64
                em.step("vector", lambda e, j=j, cv=cv, ac=ac: e.tensor_scalar_mul(
                    ac[:], cv[:, 3:3 + T], cw[:, j * 4 + 3: j * 4 + 4]),
                    reads=(cvt, "cw"), writes=(act,))
                for tap in (2, 1, 0):
                    em.step("vector", lambda e, j=j, tap=tap, cv=cv, ac=ac: e.scalar_tensor_tensor(
                        ac[:], cv[:, tap: tap + T], cw[:, j * 4 + tap: j * 4 + tap + 1],
                        ac[:], ALU.mult, ALU.add),
                        reads=(cvt, "cw", act), writes=(act,))
                em.step("scalar", lambda e, j=j, ac=ac, ot=ot: e.activation(
                    ot[:], ac[:], AF.Silu, bias=cb[:, j:j + 1]),
                    reads=(act, "cb"), writes=(ott,))
                if j < 64:
                    em.dma("gpsimd", cvx_o[j * 128:(j + 1) * 128, :], ot[:],
                           reads=(ott,))
                elif j < 72:
                    em.dma("gpsimd", cvB_o[(j - 64) * 128:(j - 63) * 128, :], ot[:],
                           reads=(ott,))
                else:
                    em.dma("gpsimd", cvC_o[(j - 72) * 128:(j - 71) * 128, :], ot[:],
                           reads=(ott,))
            else:
                rl = acc[1]
                rlt = "acc1"
                em.step("vector", lambda e, cv=cv, ac=ac: e.tensor_scalar_add(
                    ac[:], cv[:, 3:3 + T], dtbc[:]),
                    reads=(cvt, "dtbc"), writes=(act,))
                em.step("scalar", lambda e, ac=ac, rl=rl: e.activation(rl[:], ac[:], AF.Relu),
                        reads=(act,), writes=(rlt,))
                em.step("scalar", lambda e, ac=ac: e.activation(ac[:], ac[:], AF.Abs),
                        reads=(act,), writes=(act,))
                em.step("scalar", lambda e, ac=ac: e.activation(ac[:], ac[:], AF.Exp, scale=-1.0),
                        reads=(act,), writes=(act,))
                em.step("scalar", lambda e, ac=ac: e.activation(ac[:], ac[:], AF.Ln, bias=onec[:]),
                        reads=(act, "onec"), writes=(act,))
                em.step("vector", lambda e, ac=ac, rl=rl: e.tensor_add(ac[:], ac[:], rl[:]),
                        reads=(act, rlt), writes=(act,))
                em.dma("gpsimd", dtv_o[:], ac[:], reads=(act,))

    al.finish()


def inproj_inputs(core, xn1T):
    d = const_inputs(INPROJ_CONSTS)
    d.update({
        "xn1T": xn1T,
        "Wp": core["L1_attn_in_proj_w"],
        "convw": core["L1_attn_conv_w"],
        "convb": core["L1_attn_conv_b"],
        "dtb": core["L1_attn_dt_bias"],
        "nw": core["L1_attn_norm_w"],
    })
    return d



SSD_CONSTS = ("eyef", "eyeb", "onesf", "cumU", "mask01", "sel64", "ind2")
SSD_CHUNKS = NCHUNK
SSD_STAGE = 9
SSD_ST_SUB = 2

def emit_SSD(em, io):
    nc = em.nc
    C = io
    cvx = io["cvx"]
    cvB = io["cvB"]
    cvC = io["cvC"]
    dtv = io["dtv"]
    sg = io["sg"]
    alog = io["alog"]
    Din = io["Din"]
    ygT_o = io["ygT"]
    rowscr = io["rowscr"]

    al = Alloc(nc)
    sb, ps = al.sb, al.ps
    eyef = sb("s_eyef", [128, 128], F32)
    eyeb = sb("s_eyeb", [128, 128], BF16)
    onesf = sb("s_onesf", [1, 128], F32)
    cumU = sb("s_cumU", [128, 128], F32)
    mask01 = sb("s_mask01", [128, 128], F32)
    sel64 = sb("s_sel64", [128, 8192], BF16)
    ind2 = sb("s_ind2", [2, 128], F32)
    onescb = sb("s_onescb", [128, 1], BF16)
    onescf = sb("s_onescf", [128, 1], F32)
    eps1 = sb("s_eps1", [1, 1], F32)
    acol = sb("s_acol", [128, 1], F32)
    dball = sb("s_dball", [128, 128], F32)
    dballP = sb("s_dballP", [128, 64], F32)
    dtv_sb = sb("s_dtvsb", [128, T], F32)
    adtc = sb("s_adtc", [128, 128], F32)
    s_state = sb("s_state", [128, H * 64], F32)
    xcm = sb("s_xcm", [128, 64 * 128], BF16)
    xtk = sb("s_xtk", [128, 8192], BF16)
    xdt = sb("s_xdt", [128, 8192], BF16)
    xst = sb("s_xst", [128, 8192], BF16)
    bcm = sb("s_bcm", [128, 8 * 128], BF16)
    ccm = sb("s_ccm", [128, 8 * 128], BF16)
    bgT = sb("s_bgT", [128, 8 * 128], BF16)
    sgt = sb("s_sgt", [128, 64 * 128], BF16)
    ybuf = sb("s_ybuf", [128, 64 * 128], F32)
    y2b = xdt
    mgmT = sb("s_mgmT", [128, 8 * 128], BF16)
    adtT = sb("s_adtT", [128, 128], F32)
    negcsT = sb("s_negcsT", [128, 128], F32)
    acs = sb("s_acs", [128, 128], F32)
    dbfm = sb("s_dbfm", [128, 128], F32)
    w1b = sb("s_w1b", [128, 128], BF16)
    w2b = sb("s_w2b", [128, 128], BF16)
    gbc = sb("s_gbc", [128, 128], F32)
    grow = sb("s_grow", [1, 128], F32)
    csr4 = sb("s_csr4", [1, 512], F32)
    csr2 = sb("s_csr2", [2, 256], F32)
    sdm = [sb(f"s_sdm{i}", [128, 256], F32) for i in range(2)]
    dvb = [sb(f"s_dv{i}", [128, 128], F32) for i in range(2)]
    lvb = [sb(f"s_lv{i}", [128, 128], BF16) for i in range(2)]
    mvb = [sb(f"s_mv{i}", [128, 128], BF16) for i in range(2)]
    o1b = [sb(f"s_o1{i}", [128, 128], BF16) for i in range(2)]
    svb = [sb(f"s_sv{i}", [128, 128], BF16) for i in range(2)]
    rstdB = sb("s_rstdB", [128, 128], F32)
    ytmp = [sb(f"s_ytmp{i}", [128, 128], F32) for i in range(2)]
    ytmpg = sb("s_ytmpg", [128, 128], F32)
    stage = sb("s_stage", [1, 128], F32)

    pYo = [ps(f"p_Yo{i}", [128, 512]) for i in range(2)]
    pYd = [ps(f"p_Yd{i}", [128, 512]) for i in range(2)]
    pB1 = ps("p_B1", [128, 512])
    pE = ps("p_E", [128, 512])
    pT7 = [ps(f"p_T7{i}", [128, 256], BF16) for i in range(2)]
    EXPB = [(pE, "pE"), (pB1, "pB1")]

    em.dma("sync", eyef[:], C["eyef"][:], writes=("eyef",))
    em.dma("sync", eyeb[:], C["eyeb"][:], writes=("eyeb",))
    em.dma("sync", onesf[:], C["onesf"][:], writes=("onesf",))
    em.dma("sync", cumU[:], C["cumU"][:], writes=("cumU",))
    em.dma("sync", mask01[:], C["mask01"][:], writes=("mask01",))
    em.dma("sync", sel64[:], C["sel64"][:], writes=("sel64",))
    em.dma("sync", ind2[:], C["ind2"][:], writes=("ind2",))
    em.step("gpsimd", lambda e: e.memset(onescb[:], 1.0), writes=("onescb",))
    em.step("gpsimd", lambda e: e.memset(onescf[:], 1.0), writes=("onescf",))
    em.step("gpsimd", lambda e: e.memset(eps1[:], 1e-5), writes=("eps1",))
    for j in range(8):
        em.step("gpsimd", lambda e, j=j: e.memset(s_state[:, j * 1024:(j + 1) * 1024], 0.0),
                writes=(f"sst{j}a", f"sst{j}b"))
    em.dma("sync", dtv_sb[:], dtv[:], writes=("dtvsb",))
    em.dma("sync", stage[:], alog[:].rearrange("h -> () h"), writes=("stage",))
    em.step("tensor", lambda e: e.transpose(pE[0:128, 0:1], stage[:], eyef[0:1, 0:1]),
            reads=("stage", "eyef", "pE"), writes=("pE",))
    em.step("scalar", lambda e: e.activation(acol[:], pE[0:128, 0:1], AF.Exp),
            reads=("pE",), writes=("acol",))
    em.step("vector", lambda e: e.tensor_scalar_mul(acol[:], acol[:], -1.0),
            reads=("acol",), writes=("acol",))
    em.dma("sync", stage[:], Din[:].rearrange("h -> () h"), writes=("stage",))
    em.step("tensor", lambda e: e.matmul(pE[:, 128:256], onesf[:], stage[:],
                                         start=True, stop=True),
            reads=("onesf", "stage", "pE"), writes=("pE",))
    em.step("vector", lambda e: e.tensor_copy(dball[:], pE[:, 128:256]),
            reads=("pE",), writes=("dball", "pE"))
    dbv = dball[:].rearrange("p (j two) -> p j two", two=2)
    em.step("vector", lambda e: e.tensor_copy(dballP[0:64, :], dbv[0:64, :, 0]),
            reads=("dball",), writes=("dballP",))
    em.step("vector", lambda e: e.tensor_copy(dballP[64:128, :], dbv[64:128, :, 1]),
            reads=("dball",), writes=("dballP",))

    cvxv = cvx.rearrange("(c p) t -> p c t", p=128)
    cvBv = cvB.rearrange("(g p) t -> p g t", p=128)
    cvCv = cvC.rearrange("(g p) t -> p g t", p=128)
    sgv = sg.rearrange("(c p) t -> p c t", p=128)
    ygv = ygT_o.rearrange("(c p) t -> p c t", p=128)

    for c in range(SSD_CHUNKS):
        tsl = slice(c * CS, (c + 1) * CS)
        em.dma("sync", xcm[:].rearrange("p (a t) -> p a t", t=128), cvxv[:, :, tsl],
               reads=("xcm",), writes=("xcm",))
        em.dma("sync", bcm[:].rearrange("p (a t) -> p a t", t=128), cvBv[:, :, tsl],
               reads=("bcm",), writes=("bcm",))
        em.dma("scalar", ccm[:].rearrange("p (a t) -> p a t", t=128), cvCv[:, :, tsl],
               reads=("ccm",), writes=("ccm",))
        em.dma("scalar", sgt[:].rearrange("p (a t) -> p a t", t=128), sgv[:, :, tsl],
               reads=("sgt",), writes=("sgt",))
        if SSD_STAGE >= 1:
            em.step("vector", lambda e, c=c: e.tensor_scalar_mul(
                adtc[:], dtv_sb[:, c * CS:(c + 1) * CS], acol[:]),
                reads=("dtvsb", "acol", "adtc"), writes=("adtc",))
            em.step("tensor", lambda e: e.transpose(
                pE[:, 0:128], adtc[:], eyef[:]),
                reads=("adtc", "eyef", "pE"), writes=("pE",))
            em.step("vector", lambda e: e.tensor_copy(adtT[:], pE[:, 0:128]),
                    reads=("pE",), writes=("adtT", "pE"))
            em.step("tensor", lambda e: e.matmul(pE[:, 128:256], cumU[:], adtT[:],
                                                 start=True, stop=True),
                    reads=("cumU", "adtT", "pE"), writes=("pE",))
            em.step("vector", lambda e: e.tensor_scalar(
                negcsT[:], pE[:, 128:256], -1.0, None, ALU.mult),
                reads=("pE",), writes=("negcsT", "pE"))
            em.step("tensor", lambda e: e.matmul(pE[:, 256:384], adtT[:], cumU[:],
                                                 start=True, stop=True),
                    reads=("adtT", "cumU", "pE"), writes=("pE",))
            em.step("vector", lambda e: e.tensor_copy(acs[:], pE[:, 256:384]),
                    reads=("pE",), writes=("acs", "pE"))
            em.step("tensor", lambda e: e.matmul(pE[0:1, 384:512], onescf[:], adtT[:],
                                                 start=True, stop=True),
                    reads=("onescf", "adtT", "pE"), writes=("pE",))
            em.step("scalar", lambda e: e.activation(grow[:], pE[0:1, 384:512], AF.Exp),
                    reads=("pE",), writes=("grow",))
            em.step("scalar", lambda e: e.activation(dbfm[:], acs[:], AF.Exp,
                                                     scale=-1.0, bias=acs[:, 127:128]),
                    reads=("acs",), writes=("dbfm",))
            em.step("vector", lambda e, c=c: lowprec(e, lambda: e.tensor_copy(
                w1b[:], dtv_sb[:, c * CS:(c + 1) * CS])),
                reads=("dtvsb",), writes=("w1b",))
            em.step("vector", lambda e, c=c: lowprec(e, lambda: e.tensor_mul(
                w2b[:], dtv_sb[:, c * CS:(c + 1) * CS], dbfm[:])),
                reads=("dtvsb", "dbfm"), writes=("w2b",))
            em.step("tensor", lambda e: e.matmul(pE[:, 0:128], onesf[:], grow[:],
                                                 start=True, stop=True),
                    reads=("onesf", "grow", "pE"), writes=("pE",))
            em.step("vector", lambda e: e.tensor_copy(gbc[:], pE[:, 0:128]),
                    reads=("pE",), writes=("gbc", "pE"))
            em.dma("sync", rowscr[:], acs[:], reads=("acs",), writes=("rowscr",))
        if SSD_STAGE >= 2:
            for t_ in range(64):
                bk = pT7[t_ % 2]
                reg = 0
                ptg = f"pT7{t_ % 2}"
                em.step("tensor", lambda e, t_=t_, reg=reg, bk=bk: e.transpose(
                    bk[:, reg:reg + 128], xcm[:, t_ * 128:(t_ + 1) * 128], eyeb[:]),
                    reads=("xcm", "eyeb", ptg), writes=(ptg,))
                if t_ % 2 == 0:
                    em.step("vector", lambda e, t_=t_, reg=reg, bk=bk: e.tensor_copy(
                        xtk[:, t_ * 128:(t_ + 1) * 128], bk[:, reg:reg + 128]),
                        reads=(ptg,), writes=(f"xtk{t_ // 16}v",))
                else:
                    em.step("scalar", lambda e, t_=t_, reg=reg, bk=bk: e.activation(
                        xtk[:, t_ * 128:(t_ + 1) * 128], bk[:, reg:reg + 128],
                        AF.Identity),
                        reads=(ptg,), writes=(f"xtk{t_ // 16}s",))
            for g in range(8):
                bk = pT7[g % 2]
                reg = 0
                ptg = f"pT7{g % 2}"
                em.step("tensor", lambda e, g=g, reg=reg, bk=bk: e.transpose(
                    bk[:, reg:reg + 128], bcm[:, g * 128:(g + 1) * 128], eyeb[:]),
                    reads=("bcm", "eyeb", ptg), writes=(ptg,))
                em.step("vector", lambda e, g=g, reg=reg, bk=bk: e.tensor_copy(
                    bgT[:, g * 128:(g + 1) * 128], bk[:, reg:reg + 128]),
                    reads=(ptg,), writes=("bgT",))
        if SSD_STAGE >= 3:
            for g in range(8):
                reg = (g % 4) * 128
                em.step("tensor", lambda e, g=g, reg=reg: e.matmul(
                    pE[:, reg:reg + 128], bcm[:, g * 128:(g + 1) * 128],
                    ccm[:, g * 128:(g + 1) * 128], start=True, stop=True),
                    reads=("bcm", "ccm", "pE"), writes=("pE",))
                em.step("vector", lambda e, g=g, reg=reg: lowprec(e, lambda: e.tensor_mul(
                    mgmT[:, g * 128:(g + 1) * 128], pE[:, reg:reg + 128], mask01[:])),
                    reads=("pE", "mask01"), writes=("mgmT",))
        if SSD_STAGE >= 4:
            for j in range(16):
                bk1, tg1 = EXPB[0]
                bk2, tg2 = EXPB[1]
                em.step("tensor", lambda e, j=j, bk1=bk1: e.matmul(
                    bk1[:, :], w1b[:], sel64[:, j * 512:(j + 1) * 512],
                    start=True, stop=True),
                    reads=("w1b", "sel64", tg1), writes=(tg1,))
                em.step("vector", lambda e, j=j, bk1=bk1: lowprec(e, lambda: e.tensor_mul(
                    xdt[:, j * 512:(j + 1) * 512], bk1[:, :],
                    xtk[:, j * 512:(j + 1) * 512])),
                    reads=(tg1, f"xtk{j // 4}v", f"xtk{j // 4}s"),
                    writes=(f"xdt{j // 4}", "y2bv", "y2bg", tg1))
                em.step("tensor", lambda e, j=j, bk2=bk2: e.matmul(
                    bk2[:, :], w2b[:], sel64[:, j * 512:(j + 1) * 512],
                    start=True, stop=True),
                    reads=("w2b", "sel64", tg2), writes=(tg2,))
                em.step("vector", lambda e, j=j, bk2=bk2: lowprec(e, lambda: e.tensor_mul(
                    xst[:, j * 512:(j + 1) * 512], bk2[:, :],
                    xtk[:, j * 512:(j + 1) * 512])),
                    reads=(tg2, f"xtk{j // 4}v", f"xtk{j // 4}s"), writes=(f"xst{j // 4}", tg2))
        if SSD_STAGE >= 5:
            for hg in range(32):
                g = hg // 4
                em.dma("sync", csr4[:],
                       rowscr[hg * 4:(hg + 1) * 4, :].rearrange("p n -> () (p n)"),
                       reads=("rowscr", "csr4"), writes=("csr4",))
                em.dma("sync", csr2[:].rearrange("two (q n) -> two q n", n=128),
                       rowscr.rearrange("(q two) n -> two q n", two=2)[:, hg * 2:hg * 2 + 2, :],
                       reads=("rowscr", "csr2"), writes=("csr2",))
                em.step("tensor", lambda e: e.matmul(pB1[:, :], onesf[:], csr4[:],
                                                     start=True, stop=True),
                        reads=("onesf", "csr4", "pB1"), writes=("pB1",))
                mreg = (hg % 2) * 256
                em.step("tensor", lambda e, mreg=mreg: e.matmul(
                    pE[:, mreg:mreg + 256], ind2[:], csr2[:], start=True, stop=True),
                    reads=("ind2", "csr2", "pE"), writes=("pE",))
                em.step("scalar", lambda e, hg=hg, mreg=mreg: e.activation(
                    sdm[hg % 2][:], pE[:, mreg:mreg + 256], AF.Exp),
                    reads=("pE",), writes=(f"sdm{hg % 2}",))
                for pp2 in range(2):
                    pp = hg * 2 + pp2
                    set_ = (pp // 4) % 2
                    slot = pp % 4
                    off = slot * 128
                    oT, dT = f"pYo{set_}_{slot}", f"pYd{set_}_{slot}"
                    for jj in range(2):
                        j = 2 * pp2 + jj
                        h = hg * 4 + j
                        em.step("vector", lambda e, j=j, h=h: e.tensor_scalar(
                            dvb[j % 2][:], pB1[:, j * 128:(j + 1) * 128],
                            negcsT[:, h:h + 1], 0.0, ALU.add, ALU.min),
                            reads=("pB1", "negcsT", f"dv{j % 2}"), writes=(f"dv{j % 2}",))
                        em.step("scalar", lambda e, j=j: e.activation(
                            lvb[j % 2][:], dvb[j % 2][:], AF.Exp),
                            reads=(f"dv{j % 2}",), writes=(f"lv{j % 2}",))
                        eng = "vector" if jj == 0 else "gpsimd"
                        em.step(eng, lambda e, j=j, g=g: e.tensor_tensor(
                            mvb[j % 2][:], lvb[j % 2][:], mgmT[:, g * 128:(g + 1) * 128],
                            ALU.mult),
                            reads=(f"lv{j % 2}", "mgmT", f"mv{j % 2}"), writes=(f"mv{j % 2}",))
                    h0 = pp * 2
                    h1 = h0 + 1
                    j0 = 2 * pp2
                    j1 = j0 + 1
                    if c > 0:
                        em.step("vector", lambda e, pp=pp: lowprec(e, lambda: e.tensor_copy(
                            svb[pp % 2][:], s_state[:, pp * 128:(pp + 1) * 128])),
                            reads=(f"sst{pp // 8}a", f"sst{pp // 8}b", f"sv{pp % 2}"),
                            writes=(f"sv{pp % 2}",))
                        em.step("tensor", lambda e, pp=pp, set_=set_, off=off, g=g: e.matmul(
                            pYo[set_][:, off:off + 128], svb[pp % 2][:],
                            ccm[:, g * 128:(g + 1) * 128], start=True, stop=True),
                            reads=(f"sv{pp % 2}", "ccm", oT), writes=(oT,))
                    em.step("tensor", lambda e, h0=h0, j0=j0, set_=set_, off=off, c=c: e.matmul(
                        pYd[set_][0:64, off:off + 128], xdt[:, h0 * 64:(h0 + 1) * 64],
                        mvb[j0 % 2][:], start=True, stop=(c == 0)),
                        reads=(f"xdt{h0 // 32}", f"mv{j0 % 2}", dT), writes=(dT,))
                    em.step("tensor", lambda e, h1=h1, j1=j1, set_=set_, off=off, c=c: e.matmul(
                        pYd[set_][64:128, off:off + 128], xdt[:, h1 * 64:(h1 + 1) * 64],
                        mvb[j1 % 2][:], start=True, stop=(c == 0)),
                        reads=(f"xdt{h1 // 32}", f"mv{j1 % 2}", dT), writes=(dT,))
                    if c > 0:
                        em.step("vector", lambda e, pp=pp, pp2=pp2, set_=set_, off=off, hg=hg: lowprec(
                            e, lambda: e.tensor_mul(
                                o1b[pp % 2][:], pYo[set_][:, off:off + 128],
                                sdm[hg % 2][:, pp2 * 128:(pp2 + 1) * 128])),
                            reads=(oT, f"sdm{hg % 2}", f"o1{pp % 2}"), writes=(f"o1{pp % 2}", oT))
                        em.step("tensor", lambda e, pp=pp, set_=set_, off=off: e.matmul(
                            pYd[set_][:, off:off + 128], eyeb[:], o1b[pp % 2][:],
                            start=False, stop=True),
                            reads=("eyeb", f"o1{pp % 2}", dT), writes=(dT,))
                    em.step("vector", lambda e, pp=pp, set_=set_, off=off: e.scalar_tensor_tensor(
                        ybuf[:, pp * 128:(pp + 1) * 128], xcm[:, pp * 128:(pp + 1) * 128],
                        dballP[:, pp:pp + 1], pYd[set_][:, off:off + 128],
                        ALU.mult, ALU.add),
                        reads=("xcm", "dballP", dT, "ybuf"), writes=("ybuf", dT))
        if SSD_STAGE >= 6:
            for g in range(8):
                pA_ = pYo[g % 2]
                pB_ = pYd[g % 2]
                tAs = tuple(f"pYo{g % 2}_{s}" for s in range(4))
                tBs = tuple(f"pYd{g % 2}_{s}" for s in range(4))
                stg = (f"sst{g}a", f"sst{g}b")
                em.step("tensor", lambda e, g=g, pA_=pA_: e.matmul(
                    pA_[:, :], bgT[:, g * 128:(g + 1) * 128],
                    xst[:, g * 1024:g * 1024 + 512], start=True, stop=True),
                    reads=("bgT", f"xst{g // 2}") + tAs, writes=tAs)
                em.step("tensor", lambda e, g=g, pB_=pB_: e.matmul(
                    pB_[:, :], bgT[:, g * 128:(g + 1) * 128],
                    xst[:, g * 1024 + 512:(g + 1) * 1024], start=True, stop=True),
                    reads=("bgT", f"xst{g // 2}") + tBs, writes=tBs)
                for hh in range(16 if SSD_ST_SUB >= 2 or SSD_ST_SUB == 3 else 0):
                    h = g * 16 + hh
                    bank = pA_ if hh < 8 else pB_
                    btags = tAs if hh < 8 else tBs
                    reg = (hh % 8) * 64
                    em.step("vector", lambda e, h=h, bank=bank, reg=reg: e.scalar_tensor_tensor(
                        s_state[:, h * 64:(h + 1) * 64], s_state[:, h * 64:(h + 1) * 64],
                        gbc[:, h:h + 1], bank[:, reg:reg + 64], ALU.mult, ALU.add),
                        reads=btags + ("gbc",) + stg,
                        writes=(f"sst{g}a" if hh % 2 == 0 else f"sst{g}b",))
        if SSD_STAGE >= 7:
            em.step("scalar", lambda e: lowprec(e, lambda: e.activation(
                y2b[:], ybuf[:], AF.Square)),
                reads=("ybuf",),
                writes=("y2bv", "y2bg", "xdt0", "xdt1", "xdt2", "xdt3"))
            for c2 in range(64):
                em.step("tensor", lambda e, c2=c2: e.matmul(
                    pE[0:1, 0:128], onescb[:], y2b[:, c2 * 128:(c2 + 1) * 128],
                    start=(c2 == 0), stop=(c2 == 63)),
                    reads=("onescb", "y2bv", "y2bg") + (("pE",) if c2 == 0 else ()),
                    writes=("pE",))
            em.step("scalar", lambda e: e.activation(grow[:], pE[0:1, 0:128], AF.Sqrt,
                                                     scale=1.0 / INTER, bias=eps1[:]),
                    reads=("pE", "eps1"), writes=("grow",))
            em.step("vector", lambda e: e.reciprocal(grow[:], grow[:]),
                    reads=("grow",), writes=("grow",))
            em.step("tensor", lambda e: e.matmul(pE[:, 128:256], onesf[:], grow[:],
                                                 start=True, stop=True),
                    reads=("onesf", "grow", "pE"), writes=("pE",))
            em.step("vector", lambda e: e.tensor_copy(rstdB[:], pE[:, 128:256]),
                    reads=("pE",), writes=("rstdB", "pE"))
            for c2 in range(64):
                if c2 % 3 == 2:
                    em.step("gpsimd", lambda e, c2=c2: e.tensor_mul(
                        ytmpg[:], ybuf[:, c2 * 128:(c2 + 1) * 128], rstdB[:]),
                        reads=("ybuf", "rstdB", "ytmpg"), writes=("ytmpg",))
                    em.step("gpsimd", lambda e, c2=c2: lowprec(e, lambda: e.tensor_mul(
                        y2b[:, c2 * 128:(c2 + 1) * 128], ytmpg[:],
                        sgt[:, c2 * 128:(c2 + 1) * 128])),
                        reads=("ytmpg", "sgt"), writes=("y2bg",))
                else:
                    em.step("vector", lambda e, c2=c2: e.tensor_mul(
                        ytmp[c2 % 2][:], ybuf[:, c2 * 128:(c2 + 1) * 128], rstdB[:]),
                        reads=("ybuf", "rstdB", f"ytmp{c2 % 2}"), writes=(f"ytmp{c2 % 2}",))
                    em.step("vector", lambda e, c2=c2: lowprec(e, lambda: e.tensor_mul(
                        y2b[:, c2 * 128:(c2 + 1) * 128], ytmp[c2 % 2][:],
                        sgt[:, c2 * 128:(c2 + 1) * 128])),
                        reads=(f"ytmp{c2 % 2}", "sgt"), writes=("y2bv",))
        em.dma("sync", ygv[:, :, tsl], y2b[:].rearrange("p (a t) -> p a t", t=128),
               reads=("y2bv", "y2bg"))

    al.finish()


def ssd_inputs(core, cvx, cvB, cvC, dtv, sg):
    d = const_inputs(SSD_CONSTS)
    d.update({
        "cvx": cvx, "cvB": cvB, "cvC": cvC, "dtv": dtv, "sg": sg,
        "alog": core["L1_attn_A_log"], "Din": core["L1_attn_D"],
    })
    return d



NKO = INTER // 128
OPROJ_CONSTS = ()

def emit_OPROJ(em, io):
    nc = em.nc
    ygT = io["ygT"]
    woT = io["woT"]
    s1T_o = io["s1T"]

    al = Alloc(nc)
    sb, ps = al.sb, al.ps
    ygb = sb("s_ygb", [128, NKO * 1024], BF16)
    lbuf = [sb(f"s_lb{i}", [128, 32 * 256], BF16) for i in range(2)]
    ev = [sb(f"s_ev{i}", [128, 1024], F32) for i in range(2)]
    pO = [ps(f"p_O{i}", [128, 512]) for i in range(8)]
    ygvv = ygT.rearrange("(c p) t -> p c t", p=128)

    segs = []
    for th in range(2):
        for mg in range(16):
            for sg_ in range(2):
                segs.append((th, mg, sg_))

    for th in range(2):
        tsl = slice(th * 1024, (th + 1) * 1024)
        for j in range(8):
            em.dma("sync", ygb[:].rearrange("p (c t) -> p c t", t=1024)[:, j * 8:(j + 1) * 8, :],
                   ygvv[:, j * 8:(j + 1) * 8, tsl], reads=("ygb",), writes=("ygb",))
        for mg in range(16):
            for sg_ in range(2):
                si = (th * 32 + mg * 2 + sg_)
                lb = lbuf[si % 2]
                ltag = f"lb{si % 2}"
                for kc0 in range(sg_ * 32, sg_ * 32 + 32, 8):
                    q = "sync" if ((kc0 // 8) % 2 == 0) else "scalar"
                    em.dma(q,
                           lb[:, (kc0 % 32) * 256:((kc0 % 32) + 8) * 256].rearrange(
                               "p (f m) -> p f m", f=8),
                           woT[kc0 * 128:(kc0 + 8) * 128,
                               mg * 256:(mg + 1) * 256].rearrange(
                               "(f p) m -> p f m", p=128),
                           reads=(f"woT{kc0 // 8}",) + ((ltag,) if kc0 % 32 == 0 else ()),
                           writes=(ltag,))
                for mi in range(2):
                    m = mg * 2 + mi
                    for kc in range(sg_ * 32, sg_ * 32 + 32):
                        for nb in range(2):
                            pi = (mg % 2) * 4 + mi * 2 + nb
                            em.step("tensor", lambda e, kc=kc, nb=nb, lb=lb, mi=mi, pi=pi: e.matmul(
                                pO[pi][:, :],
                                lb[:, (kc % 32) * 256 + mi * 128:(kc % 32) * 256 + mi * 128 + 128],
                                ygb[:, kc * 1024 + nb * 512: kc * 1024 + (nb + 1) * 512],
                                start=(kc == 0), stop=(kc == NKO - 1)),
                                reads=(ltag, "ygb") + ((f"pO{pi}",) if kc == 0 else ()),
                                writes=(f"pO{pi}",))
            for mi in range(2):
                m = mg * 2 + mi
                evb = ev[m % 2]
                evt = f"ev{m % 2}"
                for nb in range(2):
                    pi = (mg % 2) * 4 + mi * 2 + nb
                    if nb == 0:
                        em.step("vector", lambda e, pi=pi, nb=nb, evb=evb: e.tensor_copy(
                            evb[:, nb * 512:(nb + 1) * 512], pO[pi][:, :]),
                            reads=(f"pO{pi}", evt), writes=(evt, f"pO{pi}"))
                    else:
                        em.step("scalar", lambda e, pi=pi, nb=nb, evb=evb: e.copy(
                            evb[:, nb * 512:(nb + 1) * 512], pO[pi][:, :]),
                            reads=(f"pO{pi}", evt), writes=(evt, f"pO{pi}"))
                em.dma("gpsimd", s1T_o[m * 128:(m + 1) * 128, tsl], evb[:],
                       reads=(evt,))

    al.finish()


def oproj_inputs(core, ygT):
    return {"ygT": ygT, "Wo": core["L1_attn_out_proj_w"]}



L1FFN_CONSTS = ("eyef", "e64")

def emit_L1FFN(em, io):
    nc = em.nc
    C = io
    xn1T = io["xn1T"]
    waT = io["waT"]
    wb1 = io["wb1"]
    Wr1T = io["Wr1T"]
    g1T_o = io["g1T"]

    al = Alloc(nc)
    sb, ps = al.sb, al.ps
    eyef = sb("s_eyef", [128, 128], F32)
    e64 = sb("s_e64", [64, 1024], BF16)
    a1T = sb("s_a1T", [128, ND * KR], BF16)
    b1T = sb("s_b1T", [128, 8 * D], BF16)
    wr1T = sb("s_wr1T", [128, ND * 64], BF16)
    hgT = sb("s_hgT", [128, 8 * T], BF16)
    hsb = sb("s_hsb", [128, 512], F32)
    otile = sb("s_ot", [128, 512], BF16)
    xstream = [sb(f"s_xs{i}", [128, 512], BF16) for i in range(2)]
    sc_f = sb("s_scf", [64, T], F32)
    sc_tm = sb("s_sctm", [128, 64], F32)
    mx16 = sb("s_mx16", [128, 16], F32)
    mrep = sb("s_mrep", [128, 64], F32)
    thcol = sb("s_thcol", [128, 1], F32)
    t1col = sb("s_t1col", [128, 1], F32)
    maskb = sb("s_maskb", [128, 64], F32)
    gtm = sb("s_gtm", [128, 64], F32)
    gT = sb("s_gT", [64, T], BF16)
    pH = [ps(f"p_H{i}", [128, 512]) for i in range(8)]

    em.dma("sync", eyef[:], C["eyef"][:], writes=("eyef",))
    em.dma("sync", e64[:], C["e64"][:], writes=("e64",))
    a1dst = a1T[:].rearrange("p (c k) -> p c k", k=KR)
    a1src = waT.rearrange("(c p) k -> p c k", p=128)
    for j in range(4):
        em.dma("sync" if j % 2 == 0 else "scalar",
               a1dst[:, j * 8:(j + 1) * 8, :], a1src[:, j * 8:(j + 1) * 8, :],
               reads=("waT",), writes=("a1T",))
    wrdst = wr1T[:].rearrange("p (c e) -> p c e", e=64)
    wrsrc = Wr1T.rearrange("(c p) e -> p c e", p=128)
    for j in range(2):
        em.dma("gpsimd", wrdst[:, j * 16:(j + 1) * 16, :], wrsrc[:, j * 16:(j + 1) * 16, :],
               writes=("wr1T",))
    b1dst = b1T[:].rearrange("p (c d) -> p c d", d=D)
    b1src = wb1.rearrange("(c p) d -> p c d", p=128)
    for j in range(2):
        em.dma("sync" if j % 2 == 0 else "scalar",
               b1dst[:, j * 4:(j + 1) * 4, :], b1src[:, j * 4:(j + 1) * 4, :],
               reads=("wb1",), writes=("b1T",))

    xv = xn1T.rearrange("(c p) t -> p c t", p=128)
    for kc in range(ND):
        xs = xstream[kc % 2]
        xst = f"xs{kc % 2}"
        for nb in range(4):
            em.dma("scalar", xs[:], xv[:, kc, nb * 512:(nb + 1) * 512],
                   reads=(xst,), writes=(xst,))
            em.step("tensor", lambda e, kc=kc, nb=nb, xs=xs: e.matmul(
                pH[nb][0:64, :], wr1T[:, kc * 64:(kc + 1) * 64], xs[:],
                start=(kc == 0), stop=(kc == ND - 1)),
                reads=("wr1T", xst) + ((f"pH{nb}",) if kc == 0 else ()),
                writes=(f"pH{nb}",))
    for nb in range(4):
        em.step("vector", lambda e, nb=nb: e.tensor_copy(
            sc_f[0:64, nb * 512:(nb + 1) * 512], pH[nb][0:64, :]),
            reads=(f"pH{nb}", "sc_f"), writes=("sc_f", f"pH{nb}"))
    for i in range(NT):
        nb = i // 4
        off = (i % 4) * 128
        em.step("tensor", lambda e, nb=nb, off=off: e.transpose(
            pH[4][0:128, 0:64], sc_f[0:64, nb * 512 + off: nb * 512 + off + 128],
            eyef[0:64, 0:64]),
            reads=("sc_f", "eyef", "pH4"), writes=("pH4",))
        em.step("vector", lambda e: e.tensor_copy(sc_tm[:], pH[4][0:128, 0:64]),
                reads=("pH4",), writes=("sc_tm", "pH4"))
        em.step("vector", lambda e: e.max(out=mx16[:, 0:8], in_=sc_tm[:]),
                reads=("sc_tm",), writes=("mx16",))
        em.step("vector", lambda e: e.match_replace(
            out=mrep[:], in_to_replace=mx16[:, 0:8], in_values=sc_tm[:], imm_value=-1e30),
            reads=("sc_tm", "mx16"), writes=("mrep",))
        em.step("vector", lambda e: e.max(out=mx16[:, 8:16], in_=mrep[:]),
                reads=("mrep",), writes=("mx16",))
        em.step("vector", lambda e: e.tensor_reduce(thcol[:], mx16[:], AX.X, ALU.min),
                reads=("mx16",), writes=("thcol",))
        em.step("vector", lambda e: e.tensor_scalar(
            t1col[:], mx16[:, 0:1], -1.0, None, ALU.mult),
            reads=("mx16",), writes=("t1col",))
        em.step("vector", lambda e: e.tensor_scalar(
            maskb[:], sc_tm[:], thcol[:], None, ALU.is_ge),
            reads=("sc_tm", "thcol"), writes=("maskb",))
        em.step("scalar", lambda e: e.activation(
            gtm[:], sc_tm[:], AF.Exp, bias=t1col[:]),
            reads=("sc_tm", "t1col"), writes=("gtm",))
        em.step("vector", lambda e: e.tensor_mul(gtm[:], gtm[:], maskb[:]),
                reads=("gtm", "maskb"), writes=("gtm",))
        em.step("vector", lambda e: e.reduce_sum(thcol[:], gtm[:], axis=AX.X),
                reads=("gtm",), writes=("thcol",))
        em.step("vector", lambda e: e.reciprocal(thcol[:], thcol[:]),
                reads=("thcol",), writes=("thcol",))
        em.step("vector", lambda e: e.tensor_scalar_mul(gtm[:], gtm[:], thcol[:]),
                reads=("gtm", "thcol"), writes=("gtm",))
        em.step("tensor", lambda e: e.transpose(
            pH[5][0:64, 0:128], gtm[:], eyef[:]),
            reads=("gtm", "eyef", "pH5"), writes=("pH5",))
        em.step("vector", lambda e, i=i: lowprec(e, lambda: e.tensor_copy(
            gT[:, i * 128:(i + 1) * 128], pH[5][0:64, 0:128])),
            reads=("pH5",), writes=("gT", "pH5"))
    for nb in range(4):
        for kc in range(ND):
            xs = xstream[kc % 2]
            xst = f"xs{kc % 2}"
            em.dma("scalar", xs[:], xv[:, kc, nb * 512:(nb + 1) * 512],
                   reads=(xst,), writes=(xst,))
            for m in range(8):
                em.step("tensor", lambda e, kc=kc, m=m, xs=xs: e.matmul(
                    pH[m][:, :], a1T[:, kc * KR + m * 128: kc * KR + (m + 1) * 128],
                    xs[:], start=(kc == 0), stop=(kc == ND - 1)),
                    reads=("a1T", xst) + ((f"pH{m}",) if kc == 0 else ()),
                    writes=(f"pH{m}",))
        for m in range(8):
            em.step("vector", lambda e, m=m: e.tensor_copy(hsb[:], pH[m][:, :]),
                    reads=(f"pH{m}", "hsb"), writes=("hsb", f"pH{m}"))
            em.step("tensor", lambda e, m=m, nb=nb: e.matmul(
                pH[m][:, :], e64[:, m * 128:(m + 1) * 128], gT[:, nb * 512:(nb + 1) * 512],
                start=True, stop=True),
                reads=("e64", "gT", f"pH{m}"), writes=(f"pH{m}",))
            em.step("vector", lambda e, m=m, nb=nb: e.tensor_mul(
                hgT[:, m * T + nb * 512: m * T + (nb + 1) * 512], hsb[:], pH[m][:, :]),
                reads=("hsb", f"pH{m}"), writes=("hgT",))
    for nb in range(4):
        for m2 in range(ND):
            pi = 6 + (m2 % 2)
            for kc2 in range(8):
                em.step("tensor", lambda e, m2=m2, kc2=kc2, nb=nb, pi=pi: e.matmul(
                    pH[pi][:, :], b1T[:, kc2 * D + m2 * 128: kc2 * D + (m2 + 1) * 128],
                    hgT[:, kc2 * T + nb * 512: kc2 * T + (nb + 1) * 512],
                    start=(kc2 == 0), stop=(kc2 == 7)),
                    reads=("b1T", "hgT") + ((f"pH{pi}",) if kc2 == 0 else ()),
                    writes=(f"pH{pi}",))
            em.step("scalar", lambda e, pi=pi: e.activation(
                otile[:], pH[pi][:, :], AF.Silu, scale=2.0),
                reads=(f"pH{pi}", "otile"), writes=("otile", f"pH{pi}"))
            em.dma("gpsimd", g1T_o[m2 * 128:(m2 + 1) * 128, nb * 512:(nb + 1) * 512],
                   otile[:], reads=("otile",))

    al.finish()


def l1ffn_inputs(core, xn1T):
    d = const_inputs(L1FFN_CONSTS)
    d.update({"xn1T": xn1T, "A1": core["L1_ffn_A"], "B1": core["L1_ffn_B"],
              "Wr1": core["L1_ffn_Wr"]})
    return d



COMB_CONSTS = ("eyef",)

def emit_COMB(em, io):
    nc = em.nc
    C = io
    x1 = io["x1"]
    s1T = io["s1T"]
    g1T = io["g1T"]
    out_o = io["out"]

    al = Alloc(nc)
    sb, ps = al.sb, al.ps
    eyef = sb("s_eyef", [128, 128], F32)
    x1t = [sb(f"s_x1t{i}", [128, D], F32) for i in range(2)]
    sTt = [sb(f"s_sTt{i}", [128, ND * 128], F32) for i in range(2)]
    gTt = [sb(f"s_gTt{i}", [128, ND * 128], BF16) for i in range(2)]
    dT = sb("s_dT", [128, 128], F32)
    pT = [ps(f"p_T{i}", [128, 512]) for i in range(2)]

    em.dma("sync", eyef[:], C["eyef"][:], writes=("eyef",))
    sv = s1T.rearrange("(c p) t -> p c t", p=128)
    gv = g1T.rearrange("(c p) t -> p c t", p=128)
    for i in range(NT):
        xb, xtag = x1t[i % 2], f"x1t{i % 2}"
        st, stag = sTt[i % 2], f"sTt{i % 2}"
        gt, gtag = gTt[i % 2], f"gTt{i % 2}"
        em.dma("sync", xb[:], x1[i * 128:(i + 1) * 128, :], reads=(xtag,), writes=(xtag,))
        em.dma("gpsimd", st[:].rearrange("p (c t) -> p c t", t=128),
               sv[:, :, i * 128:(i + 1) * 128], reads=(stag,), writes=(stag,))
        em.dma("scalar", gt[:].rearrange("p (c t) -> p c t", t=128),
               gv[:, :, i * 128:(i + 1) * 128], reads=(gtag,), writes=(gtag,))
        for dc in range(ND):
            pb, ptag = (pT[0], "pT0") if dc % 2 == 0 else (pT[1], "pT1")
            em.step("vector", lambda e, dc=dc, st=st, gt=gt: e.tensor_mul(
                dT[:], st[:, dc * 128:(dc + 1) * 128], gt[:, dc * 128:(dc + 1) * 128]),
                reads=(stag, gtag, "dT"), writes=("dT",))
            em.step("tensor", lambda e, pb=pb: e.transpose(pb[:, 0:128], dT[:], eyef[:]),
                    reads=("dT", "eyef", ptag), writes=(ptag,))
            em.step("vector", lambda e, dc=dc, pb=pb, xb=xb: e.tensor_add(
                xb[:, dc * 128:(dc + 1) * 128], xb[:, dc * 128:(dc + 1) * 128], pb[:, 0:128]),
                reads=(ptag, xtag), writes=(xtag, ptag))
        em.dma("sync", out_o[i * 128:(i + 1) * 128, :], xb[:], reads=(xtag,))

    al.finish()


def comb_inputs(x1, s1T, g1T):
    d = const_inputs(COMB_CONSTS)
    d.update({"x1": x1, "s1T": s1T, "g1T": g1T})
    return d



_PROGS = {}

def _prog(name, builder):
    if name not in _PROGS:
        nc = bass.Bass("TRN2", target_bir_lowering=False)
        builder(nc, strict=True)
        _PROGS[name] = nc
    return _PROGS[name]


def _run(name, builder, in_maps):
    nc = _prog(name, builder)
    res = run_bass_kernel_spmd(nc, in_maps, core_ids=list(range(NCORES)))
    return res.results


def kernel(**inputs):
    inputs = {k: np.ascontiguousarray(v) for k, v in inputs.items()}
    x_full = inputs["L0_attn_x"]

    maps = []
    for b in range(NCORES):
        core = dict(inputs)
        core["x0"] = np.ascontiguousarray(x_full[b])
        maps.append(l0_inputs(core))
    r = _run("L0", build_L0, maps)
    x1 = [r[b]["x1"] for b in range(NCORES)]
    xn1T = [r[b]["xn1T"] for b in range(NCORES)]

    maps = [inproj_inputs(inputs, xn1T[b]) for b in range(NCORES)]
    r = _run("INPROJ", build_INPROJ, maps)
    sg = [r[b]["sg"] for b in range(NCORES)]
    cvx = [r[b]["cvx"] for b in range(NCORES)]
    cvB = [r[b]["cvB"] for b in range(NCORES)]
    cvC = [r[b]["cvC"] for b in range(NCORES)]
    dtv = [r[b]["dtv"] for b in range(NCORES)]

    maps = [ssd_inputs(inputs, cvx[b], cvB[b], cvC[b], dtv[b], sg[b])
            for b in range(NCORES)]
    r = _run("SSD", build_SSD, maps)
    ygT = [r[b]["ygT"] for b in range(NCORES)]

    maps = [oproj_inputs(inputs, ygT[b]) for b in range(NCORES)]
    r = _run("OPROJ", build_OPROJ, maps)
    s1T = [r[b]["s1T"] for b in range(NCORES)]

    maps = [l1ffn_inputs(inputs, xn1T[b]) for b in range(NCORES)]
    r = _run("L1FFN", build_L1FFN, maps)
    g1T = [r[b]["g1T"] for b in range(NCORES)]

    maps = [comb_inputs(x1[b], s1T[b], g1T[b]) for b in range(NCORES)]
    r = _run("COMB", build_COMB, maps)
    out = np.stack([r[b]["out"] for b in range(NCORES)], axis=0)
    return out



PHASE_SPECS = {
    "PREFETCH": dict(
        consts=PRE_CONSTS,
        ins={"WpT": ([D, CPROJ], F32), "WoT": ([INTER, D], F32),
             "A1T": ([D, KR], F32), "B1T": ([KR, D], F32)},
        outs={},
        internal={"wbT": ([D, CPROJ], BF16), "woT": ([INTER, D], BF16),
                  "waT": ([D, KR], BF16), "wb1": ([KR, D], BF16)}),
    "L0": dict(
        consts=L0_CONSTS,
        ins={"x0": ([T, D], F32), "Wq": ([D, 128, 32], F32), "bq": ([128, 32], F32),
             "Wk": ([D, 128, 1], F32), "bk": ([128, 1], F32), "Wv": ([D, 128, 1], F32),
             "bv": ([128, 1], F32), "A0T": ([D, 128], F32), "B0T": ([128, D], F32),
             "Wr0": ([8, D], F32), "br0": ([8], F32)},
        outs={"x1": ([T, D], F32), "xn1T": ([D, T], BF16)},
        internal={}),
    "INPROJ": dict(
        consts=INPROJ_CONSTS,
        ins={"xn1T": ([D, T], BF16),
             "convw": ([CONV_DIM, 1, 4], F32), "convb": ([CONV_DIM], F32),
             "dtb": ([H], F32), "nw": ([INTER], F32)},
        outs={"sg": ([INTER, T], BF16), "cvx": ([INTER, T], BF16),
              "cvB": ([GN, T], BF16), "cvC": ([GN, T], BF16), "dtv": ([H, T], F32)},
        internal={}),
    "SSD": dict(
        consts=SSD_CONSTS,
        ins={"cvx": ([INTER, T], BF16), "cvB": ([GN, T], BF16), "cvC": ([GN, T], BF16),
             "dtv": ([H, T], F32), "sg": ([INTER, T], BF16),
             "alog": ([H], F32), "Din": ([H], F32)},
        outs={"ygT": ([INTER, T], BF16)},
        internal={"rowscr": ([128, 128], F32)}),
    "OPROJ": dict(
        consts=OPROJ_CONSTS,
        ins={"ygT": ([INTER, T], BF16)},
        outs={"s1T": ([D, T], F32)},
        internal={}),
    "L1FFN": dict(
        consts=L1FFN_CONSTS,
        ins={"xn1T": ([D, T], BF16), "Wr1T": ([D, NEXP], F32)},
        outs={"g1T": ([D, T], BF16)},
        internal={}),
    "COMB": dict(
        consts=COMB_CONSTS,
        ins={"x1": ([T, D], F32), "s1T": ([D, T], F32), "g1T": ([D, T], BF16)},
        outs={"out": ([T, D], F32)},
        internal={}),
}

EMITTERS = {"L0": emit_L0, "INPROJ": emit_INPROJ,
            "SSD": emit_SSD, "OPROJ": emit_OPROJ, "L1FFN": emit_L1FFN,
            "COMB": emit_COMB}


def full_barrier(em, targets=None):
    if targets is None:
        targets = {ch: c for ch, c in em.count.items() if c > 0}
    else:
        targets = {ch: c for ch, c in targets.items() if c > 0}
    for e in ENGINES:
        kn = em.know[e]
        eff = {}
        for ch2, c in targets.items():
            if ch2 == (e, "c"):
                continue
            if kn[ch2] < c:
                eff[ch2] = c
        for ch2, c in eff.items():
            s = em.snap[ch2][c]
            for g in CHANNELS:
                kn[g] = max(kn[g], s[g])
            kn[ch2] = max(kn[ch2], c)
        ch = (e, "c")
        em.count[ch] += 1
        cnt = em.count[ch]
        em.steps[e].append(((lambda e2: e2.nop()), eff, 1, ch))
        snap = dict(kn)
        snap[ch] = cnt
        em.snap[ch][cnt] = snap


def build_phase(nc, phase, strict=True):
    em = Emitter(nc, strict)
    spec = PHASE_SPECS[phase]
    io = declare_consts(nc, spec["consts"])
    for k, (shape, dt) in spec["ins"].items():
        io[k] = nc.dram_tensor(k, shape, dt, kind="ExternalInput").ap()
    for k, (shape, dt) in spec["outs"].items():
        io[k] = nc.dram_tensor(k, shape, dt, kind="ExternalOutput").ap()
    for k, (shape, dt) in spec["internal"].items():
        io[k] = nc.dram_tensor(k, shape, dt, kind="Internal").ap()
    EMITTERS[phase](em, io)
    em.barrier_step("sync")
    em.realize()
    em.finish()
    return em


PHASE_ORDER = ["L0", "INPROJ", "SSD", "OPROJ", "L1FFN", "COMB"]
DECL_ORDER = ["PREFETCH"] + PHASE_ORDER


def build_FUSED(nc, strict=True):
    em = Emitter(nc, strict)
    io = declare_consts(nc, tuple(CONST_SPECS.keys()))
    declared = {}
    produced = set()
    for ph in PHASE_ORDER:
        for k in PHASE_SPECS[ph]["outs"]:
            produced.add(k)
    produced.discard("out")
    for ph in DECL_ORDER:
        spec = PHASE_SPECS[ph]
        for k, (shape, dt) in spec["ins"].items():
            if k not in declared:
                kind = "Internal" if k in produced else "ExternalInput"
                declared[k] = nc.dram_tensor(k, shape, dt, kind=kind).ap()
        for k, (shape, dt) in spec["outs"].items():
            if k not in declared:
                kind = "ExternalOutput" if k == "out" else "Internal"
                declared[k] = nc.dram_tensor(k, shape, dt, kind=kind).ap()
        for k, (shape, dt) in spec["internal"].items():
            if k not in declared:
                declared[k] = nc.dram_tensor(k, shape, dt, kind="Internal").ap()
    io.update(declared)
    thunks = prefetch_chunks(em, io)

    def hook(i):
        take = 3 if i < NT - 1 else len(thunks)
        for _ in range(min(take, len(thunks))):
            thunks.pop(0)()

    emit_L0(em, io, tile_hook=hook)
    snap0 = dict(em.count)
    full_barrier(em, targets=snap0)
    for ph in PHASE_ORDER[1:]:
        EMITTERS[ph](em, io)
        full_barrier(em)
    em.barrier_step("sync")
    em.realize()
    em.finish()
    return em


def kernel(**inputs):
    inputs = {k: np.ascontiguousarray(v) for k, v in inputs.items()}
    x_full = inputs["L0_attn_x"]
    if "FUSED" not in _PROGS:
        nc = bass.Bass("TRN2", target_bir_lowering=False)
        build_FUSED(nc, strict=True)
        _PROGS["FUSED"] = nc
    nc = _PROGS["FUSED"]
    base = const_inputs(tuple(CONST_SPECS.keys()))
    ct = np.ascontiguousarray
    base.update({
        "Wq": inputs["L0_attn_Wq"], "bq": inputs["L0_attn_bq"],
        "Wk": inputs["L0_attn_Wk"], "bk": inputs["L0_attn_bk"],
        "Wv": inputs["L0_attn_Wv"], "bv": inputs["L0_attn_bv"],
        "A0T": ct(inputs["L0_ffn_A"].reshape(128, D).T),
        "B0T": ct(inputs["L0_ffn_B"].transpose(0, 2, 1).reshape(128, D)),
        "Wr0": inputs["L0_ffn_Wr"], "br0": inputs["L0_ffn_br"],
        "WpT": ct(inputs["L1_attn_in_proj_w"].T),
        "convw": inputs["L1_attn_conv_w"], "convb": inputs["L1_attn_conv_b"],
        "dtb": inputs["L1_attn_dt_bias"], "nw": inputs["L1_attn_norm_w"],
        "alog": inputs["L1_attn_A_log"], "Din": inputs["L1_attn_D"],
        "WoT": ct(inputs["L1_attn_out_proj_w"].T),
        "A1T": ct(inputs["L1_ffn_A"].reshape(KR, D).T),
        "B1T": ct(inputs["L1_ffn_B"].transpose(0, 2, 1).reshape(KR, D)),
        "Wr1T": ct(inputs["L1_ffn_Wr"].T),
    })
    in_maps = []
    for b in range(NCORES):
        m = dict(base)
        m["x0"] = np.ascontiguousarray(x_full[b])
        in_maps.append(m)
    res = run_bass_kernel_spmd(nc, in_maps, core_ids=list(range(NCORES)))
    return np.stack([res.results[b]["out"] for b in range(NCORES)], axis=0)
